# version 01
# baseline (speedup 1.0000x reference)
import numpy as np
from contextlib import ExitStack

import concourse.bass as bass
import concourse.mybir as mybir
from concourse.bass_utils import run_bass_kernel_spmd
from concourse.aot_env import allow_ici_all_to_all

F32 = mybir.dt.float32
U32 = mybir.dt.uint32
AF = mybir.ActivationFunctionType
ALU = mybir.AluOpType
AX = mybir.AxisListType

NCORE = 8
B, T, D = 4, 4096, 1024
SLICE = 512
TOK = B * SLICE
NT = TOK // 128
DC = D // 128
EPS = 1e-5

NE, F = 8, 4096
FSL = F // NCORE

SUBLAYERS = (1, 2, 3, 4)
STAGE = 99
DEBUG_DUMP = ()



def _repl(v, p=128):
    v = np.ascontiguousarray(np.asarray(v), dtype=np.float32)
    return np.broadcast_to(v.reshape(1, -1), (p, v.size)).copy()


def _colsplit(v, tile=128):
    v = np.asarray(v, dtype=np.float32)
    return np.ascontiguousarray(v.reshape(-1, tile).T)


def _rope_tables():
    half = 32
    inv = 10000.0 ** (-(np.arange(half, dtype=np.float64)) * 2.0 / 64.0)
    ang = np.arange(T, dtype=np.float64)[None, :] * inv[:, None]
    out = []
    for m in (np.cos(ang), np.sin(ang)):
        hi = m.astype(np.float32)
        lo = (m - hi.astype(np.float64)).astype(np.float32)
        out.append(np.tile(hi, (4, 1)))
        out.append(np.tile(lo, (4, 1)))
    return out


def _rope_perm():
    perm = np.zeros(512, dtype=np.int64)
    for h in range(8):
        for j in range(64):
            p = (h // 4) * 256 + (j % 2) * 128 + (h % 4) * 32 + j // 2
            perm[p] = h * 64 + j
    return perm


def make_in_map(c, inp):
    hg = c % 2
    hb = c % 2
    d = {}
    x = np.asarray(inp["L0_attn_a_x"], dtype=np.float32)
    d["x0"] = np.ascontiguousarray(
        x[:, c * SLICE:(c + 1) * SLICE, :].reshape(TOK, D))
    for nm, key in (("wq", "L0_attn_a_Wq"), ("wk", "L0_attn_a_Wk"),
                    ("wv", "L0_attn_a_Wv"), ("wg", "L0_attn_a_Wg")):
        W = np.asarray(inp[key], dtype=np.float32)
        d[nm] = np.ascontiguousarray(W[:, hg * 512:(hg + 1) * 512])
    d["wgk1"] = np.ascontiguousarray(np.asarray(inp["L0_attn_a_Wgk1"], np.float32))
    d["wgk2"] = np.ascontiguousarray(
        np.asarray(inp["L0_attn_a_Wgk2"], np.float32)[:, hg * 512:(hg + 1) * 512])
    d["bgk2"] = _colsplit(np.asarray(inp["L0_attn_a_bgk2"], np.float32)
                          [hg * 512:(hg + 1) * 512])
    d["rmsw"] = _repl(inp["L0_attn_a_rms_w"])
    d["wo"] = np.ascontiguousarray(
        np.asarray(inp["L0_attn_a_Wo"], np.float32)[hg * 512:(hg + 1) * 512, :])
    d["ln1w"] = _repl(inp["L0_attn_a_ln1_w"]); d["ln1b"] = _repl(inp["L0_attn_a_ln1_b"])
    d["ln2w"] = _repl(inp["L0_attn_a_ln2_w"]); d["ln2b"] = _repl(inp["L0_attn_a_ln2_b"])
    d["w1"] = np.ascontiguousarray(np.asarray(inp["L0_attn_a_W1"], np.float32))
    d["b1c"] = _colsplit(inp["L0_attn_a_b1"])
    d["w2"] = np.ascontiguousarray(np.asarray(inp["L0_attn_a_W2"], np.float32))
    d["b2"] = _repl(inp["L0_attn_a_b2"])
    fs = slice(c * FSL, (c + 1) * FSL)
    wv = np.asarray(inp["L0_ffn_wv"], np.float32)[:, :, fs]
    w_ = np.asarray(inp["L0_ffn_w"], np.float32)[:, :, fs]
    w1m = np.asarray(inp["L0_ffn_w1"], np.float32)[:, fs, :]
    d["mwv"] = np.ascontiguousarray(
        wv.reshape(NE, 8, 128, FSL).transpose(0, 2, 1, 3).reshape(NE * 128, 8 * FSL))
    d["mw"] = np.ascontiguousarray(
        w_.reshape(NE, 8, 128, FSL).transpose(0, 2, 1, 3).reshape(NE * 128, 8 * FSL))
    d["mw1"] = np.ascontiguousarray(
        w1m.reshape(NE, 4, 128, D).transpose(0, 2, 1, 3).reshape(NE * 128, 4 * D))
    sv = np.asarray(inp["L0_ffn_sv"], np.float32)[:, 0, fs]
    s_ = np.asarray(inp["L0_ffn_s"], np.float32)[:, 0, fs]
    d["msv"] = np.concatenate([_colsplit(sv[e]) for e in range(NE)], axis=1)
    d["ms"] = np.concatenate([_colsplit(s_[e]) for e in range(NE)], axis=1)
    s1 = np.asarray(inp["L0_ffn_s1"], np.float32)[:, 0, :]
    d["ms1"] = np.concatenate([_repl(s1[e]) for e in range(NE)], axis=1)
    d["rw"] = np.ascontiguousarray(np.asarray(inp["L0_ffn_router_w"], np.float32))
    perm = _rope_perm()
    rows = hb * 512 + np.arange(512)
    d["bwq"] = np.ascontiguousarray(
        np.asarray(inp["L0_attn_b_Wq"], np.float32)[rows][perm].T)
    d["bwk"] = np.ascontiguousarray(
        np.asarray(inp["L0_attn_b_Wk"], np.float32)[rows][perm].T)
    d["bwv"] = np.ascontiguousarray(
        np.asarray(inp["L0_attn_b_Wv"], np.float32)[rows].T)
    d["bwo"] = np.ascontiguousarray(
        np.asarray(inp["L0_attn_b_Wo"], np.float32)[:, rows].T)
    ch, cl, sh, sl = _rope_tables()
    d["rch"], d["rcl"], d["rsh"], d["rsl"] = ch, cl, sh, sl
    d["ident"] = np.eye(128, dtype=np.float32)
    st = np.arange(128)
    d["trimask"] = (st[:, None] <= st[None, :]).astype(np.float32)
    dm = np.zeros((4, 128, 512), dtype=np.float32)
    for r in range(4):
        dm[r] = (np.arange(512)[None, :] <= (r * 128 + st)[:, None])
    d["diagmask"] = np.ascontiguousarray(dm.reshape(512, 512))
    d["dmneg"] = np.ascontiguousarray(((dm - 1.0) * 1e30).reshape(512, 512))
    return d



class KB:
    def __init__(self, nc, es):
        self.nc = nc
        self.t, self.v, self.s, self.g, self.y = \
            nc.tensor, nc.vector, nc.scalar, nc.gpsimd, nc.sync
        self.engines = [self.t, self.v, self.s, self.g, self.y]
        self._sems, self._cnt = {}, {}
        for nm in ("bar", "dS", "dG", "dC", "dV", "cc",
                   "pe", "ve", "se", "ge"):
            self._sems[nm] = es.enter_context(nc.semaphore(nm))
            self._cnt[nm] = 0
        self._qmap = {id(self.y): "dS", id(self.g): "dG",
                      id(self.s): "dC", id(self.v): "dV"}

    def sem(self, nm):
        return self._sems[nm]

    def inc(self, instr, nm, k=1):
        if nm in ("pe", "ve", "se", "ge"):
            eng = {"pe": self.t, "ve": self.v, "se": self.s,
                   "ge": self.g}[nm]
            eng.drain().then_inc(self._sems[nm], k)
        else:
            instr.then_inc(self._sems[nm], k)
        self._cnt[nm] += k
        return instr

    def dma(self, eng, out, in_, **kw):
        return self.inc(eng.dma_start(out=out, in_=in_, **kw),
                        self._qmap[id(eng)], 16)

    def wait_at(self, eng, nm, val):
        if val:
            eng.wait_ge(self._sems[nm], val)

    def wait(self, eng, *names):
        for nm in names:
            if self._cnt[nm]:
                eng.wait_ge(self._sems[nm], self._cnt[nm])

    def bar(self, dmas=True):
        b = self._sems["bar"]
        for e in self.engines:
            e.drain().then_inc(b, 1)
        self._cnt["bar"] += len(self.engines)
        tgt = self._cnt["bar"]
        for e in self.engines:
            e.wait_ge(b, tgt)
            if dmas:
                self.wait(e, "dS", "dG", "dC", "dV")


class Arena:
    def __init__(self, tens, ncols):
        self.t = tens
        self.n = ncols
        self.off = 0
        self.marks = []

    def take(self, cols):
        assert self.off + cols <= self.n, (self.off, cols, self.n)
        ap = self.t[:, self.off:self.off + cols]
        self.off += cols
        return ap

    def push(self):
        self.marks.append(self.off)

    def pop(self):
        self.off = self.marks.pop()


def emit_ln(k, src, dst, wsp, sc, affine=None, n_free=D):
    v, s = k.v, k.s
    eps = sc[:, 0:1]
    nmu = sc[:, 4:5]; ssq = sc[:, 5:6]; rst = sc[:, 6:7]
    inv = 1.0 / n_free
    k.wait(v, "se")
    v.reduce_sum(nmu, src, axis=AX.X)
    v.drain()
    k.inc(v.tensor_scalar_mul(nmu, nmu, -inv), "ve")
    k.wait(s, "ve")
    s.activation(out=wsp, in_=src, func=AF.Identity, bias=nmu)
    s.activation(out=dst, in_=wsp, func=AF.Square, accum_out=ssq)
    s.drain()
    s.activation(out=ssq, in_=ssq, func=AF.Identity, scale=inv, bias=eps)
    k.inc(s.activation(out=ssq, in_=ssq, func=AF.Sqrt), "se")
    k.wait(v, "se")
    k.inc(v.reciprocal(rst, ssq), "ve")
    if affine is None:
        k.inc(v.tensor_scalar_mul(dst, wsp, rst), "ve")
    else:
        w_ap, b_ap = affine
        k.inc(v.scalar_tensor_tensor(dst, wsp, rst, w_ap, ALU.mult, ALU.mult),
              "ve")
        if b_ap is not None:
            k.inc(v.tensor_tensor(dst, dst, b_ap, ALU.add), "ve")



def build_program(sublayers=None):
    sublayers = SUBLAYERS if sublayers is None else sublayers
    nc = bass.Bass("TRN2", target_bir_lowering=False, debug=False,
                   detect_race_conditions=False)
    dram = {}

    def din(name, shape, dt=F32):
        dram[name] = nc.dram_tensor(name, shape, dt, kind="ExternalInput").ap()

    def dint(name, shape, dt=F32):
        dram[name] = nc.dram_tensor(name, shape, dt).ap()

    for nm, sh in (("x0", [TOK, D]),
                   ("wq", [D, 512]), ("wk", [D, 512]), ("wv", [D, 512]),
                   ("wg", [D, 512]), ("wgk1", [D, 16]), ("wgk2", [16, 512]),
                   ("bgk2", [128, 4]), ("rmsw", [128, 256]), ("wo", [512, D]),
                   ("ln1w", [128, D]), ("ln1b", [128, D]),
                   ("ln2w", [128, D]), ("ln2b", [128, D]),
                   ("w1", [D, 3072]), ("b1c", [128, 24]),
                   ("w2", [3072, D]), ("b2", [128, D]),
                   ("mwv", [NE * 128, 8 * FSL]), ("mw", [NE * 128, 8 * FSL]),
                   ("mw1", [NE * 128, 4 * D]),
                   ("msv", [128, NE * 4]), ("ms", [128, NE * 4]),
                   ("ms1", [128, NE * D]), ("rw", [D, NE]),
                   ("bwq", [D, 512]), ("bwk", [D, 512]), ("bwv", [D, 512]),
                   ("bwo", [512, D]),
                   ("rch", [128, T]), ("rcl", [128, T]),
                   ("rsh", [128, T]), ("rsl", [128, T]),
                   ("ident", [128, 128]), ("trimask", [128, 128]),
                   ("diagmask", [512, 512]), ("dmneg", [512, 512])):
        din(nm, sh)
    yout = nc.dram_tensor("y", [TOK, D], F32, kind="ExternalOutput").ap()

    dint("xh", [TOK, D])
    dint("xh2", [TOK, D])
    dint("xh3", [TOK, D])
    dint("uh", [TOK, D])
    dint("v1h", [TOK, D])
    dint("ag_u_i", [TOK, D])
    dint("ag_u_o", [NCORE * TOK, D])
    dint("ag_w_i", [TOK, NE])
    dint("ag_w_o", [NCORE * TOK, NE])
    dint("rs_i", [NCORE * TOK, D])
    dint("rs_o", [TOK, D])
    dint("a2a_i", [NCORE, 4, 128, D])
    dint("a2a_o", [NCORE, 4, 128, D])
    dint("p2a_i", [NCORE, 4, 128, D])
    dint("p2a_o", [NCORE, 4, 128, D])

    es = ExitStack()
    with es:
        k = KB(nc, es)
        t, v, s, g, y = k.t, k.v, k.s, k.g, k.y

        cst = es.enter_context(nc.sbuf_tensor("cst", [128, 6144], F32))
        bigt = es.enter_context(nc.sbuf_tensor("big", [128, 45056], F32))
        A = Arena(bigt, 45056)
        pss = [es.enter_context(nc.psum_tensor("ps%d" % i, [128, 512], F32))
               for i in range(8)]

        ln1w = cst[:, 0:D]; ln1b = cst[:, D:2 * D]
        ln2w = cst[:, 2 * D:3 * D]; ln2b = cst[:, 3 * D:4 * D]
        rmsw = cst[:, 4 * D:4 * D + 256]
        ident = cst[:, 4352:4480]
        tri = cst[:, 4480:4608]
        zc = cst[:, 4608:4736]
        bgk2 = cst[:, 4736:4740]
        sc = cst[:, 4740:4804]
        b1c = cst[:, 4804:4828]
        ln_aff1 = (ln1w, ln1b)
        ln_aff2 = (ln2w, ln2b)

        for dst, srcn in ((ln1w, "ln1w"), (ln1b, "ln1b"), (ln2w, "ln2w"),
                          (ln2b, "ln2b"), (rmsw, "rmsw"), (ident, "ident"),
                          (tri, "trimask"), (bgk2, "bgk2"), (b1c, "b1c")):
            k.dma(y, dst, dram[srcn][:, :])
        g.memset(zc, 0.0)
        g.memset(sc[:, 2:64], 0.0)
        g.memset(sc[:, 0:1], EPS)
        g.memset(sc[:, 1:2], 1.0 / 16.0)
        k.bar()

        ctx = dict(k=k, nc=nc, dram=dram, A=A, pss=pss, sc=sc, ident=ident,
                   tri=tri, zc=zc, rmsw=rmsw, bgk2=bgk2, b1c=b1c,
                   ln_aff1=ln_aff1, ln_aff2=ln_aff2)

        xin = dram["x0"]
        if 1 in sublayers:
            emit_attn_a(ctx, xin, dram["xh"])
            xin = dram["xh"]
            k.bar()
        if 2 in sublayers:
            emit_moe(ctx, xin, dram["xh2"])
            xin = dram["xh2"]
            k.bar()
        if 3 in sublayers:
            emit_attn_b(ctx, xin, dram["xh3"])
            xin = dram["xh3"]
            k.bar()
        if 4 in sublayers:
            emit_moe(ctx, xin, dram["xh"])
            xin = dram["xh"]
            k.bar()

        A.push()
        tb = A.take(D)
        for i in range(NT):
            k.dma(y, tb, xin[i * 128:(i + 1) * 128, :])
            k.wait(y, "dS")
            k.dma(y, yout[i * 128:(i + 1) * 128, :], tb)
            k.wait(y, "dS")
        for nm in DEBUG_DUMP:
            src_ap = dram[nm]
            flat = int(np.prod(src_ap.shape))
            dbg = nc.dram_tensor("dbg_" + nm, [flat // 1024, 1024], F32,
                                 kind="ExternalOutput").ap()
            s2 = src_ap.rearrange(
                " ".join("abcd"[:len(src_ap.shape)]) + " -> (" +
                " ".join("abcd"[:len(src_ap.shape)]) + ")"
            ).rearrange("(r c) -> r c", c=1024)
            for i in range(flat // 1024 // 128):
                k.dma(y, tb, s2[i * 128:(i + 1) * 128, :])
                k.wait(y, "dS")
                k.dma(y, dbg[i * 128:(i + 1) * 128, :], tb)
                k.wait(y, "dS")
        A.pop()
        k.bar()
    return nc



def emit_attn_a(ctx, xin, xh):
    k = ctx["k"]; dram = ctx["dram"]; A = ctx["A"]; pss = ctx["pss"]
    sc = ctx["sc"]; ident = ctx["ident"]
    t, v, s, g, y = k.t, k.v, k.s, k.g, k.y
    uh, v1h = dram["uh"], dram["v1h"]
    a2a_i, a2a_o = dram["a2a_i"], dram["a2a_o"]
    p2a_i, p2a_o = dram["p2a_i"], dram["p2a_o"]

    A.push()
    xt = A.take(D); ut = A.take(D); at = A.take(D); wsp = A.take(D)
    aT = A.take(D)
    for i in range(NT):
        sq, sub = i // 4, i % 4
        k.dma(y, xt, xin[i * 128:(i + 1) * 128, :])
        k.wait(v, "dS"); k.wait(s, "dS")
        emit_ln(k, xt, ut, wsp, sc, affine=None)
        emit_ln(k, ut, at, wsp, sc, affine=ctx["ln_aff1"])
        k.wait(y, "ve")
        k.dma(y, uh[i * 128:(i + 1) * 128, :], ut)
        for dch in range(DC):
            pb = pss[dch % 2]
            k.wait(t, "ve")
            k.inc(t.transpose(pb[:, 0:128], at[:, dch * 128:(dch + 1) * 128],
                              ident), "pe")
            k.wait(v, "pe")
            k.inc(v.tensor_copy(aT[:, dch * 128:(dch + 1) * 128], pb[:, 0:128]),
                  "ve")
        k.wait(y, "ve")
        for r in (2 * sq, 2 * sq + 1):
            k.dma(y, a2a_i[r, sub, :, :], aT)
    A.pop()
    k.bar()
    if STAGE < 2:
        return

    with allow_ici_all_to_all():
        k.inc(g.collective_compute(
            "AllToAll", ALU.bypass, replica_groups=[list(range(NCORE))],
            ins=[a2a_i.opt()], outs=[a2a_o.opt()]), "cc")
    g.wait_ge(k.sem("cc"), k._cnt["cc"])
    k.bar(dmas=False)
    if STAGE < 3:
        return

    emit_gla(ctx)
    k.bar()
    if STAGE < 4:
        return

    with allow_ici_all_to_all():
        k.inc(g.collective_compute(
            "AllToAll", ALU.bypass, replica_groups=[list(range(NCORE))],
            ins=[p2a_i.opt()], outs=[p2a_o.opt()]), "cc")
    g.wait_ge(k.sem("cc"), k._cnt["cc"])
    k.bar(dmas=False)

    A.push()
    pa = A.take(D); pb_ = A.take(D); utl = A.take(D); vt = A.take(D)
    for i in range(NT):
        sq, sub = i // 4, i % 4
        k.dma(y, pa, p2a_o[2 * sq, sub, :, :])
        k.dma(y, pb_, p2a_o[2 * sq + 1, sub, :, :])
        k.dma(y, utl, uh[i * 128:(i + 1) * 128, :])
        k.wait(v, "dS")
        k.inc(v.tensor_tensor(pa, pa, pb_, ALU.add), "ve")
        for dch in range(DC):
            pb = pss[dch % 2]
            k.wait(t, "ve")
            k.inc(t.transpose(pb[:, 0:128], pa[:, dch * 128:(dch + 1) * 128],
                              ident), "pe")
            k.wait(v, "pe")
            k.inc(v.tensor_tensor(vt[:, dch * 128:(dch + 1) * 128],
                                  utl[:, dch * 128:(dch + 1) * 128],
                                  pb[:, 0:128], ALU.add), "ve")
        k.wait(y, "ve")
        k.dma(y, v1h[i * 128:(i + 1) * 128, :], vt)
    A.pop()
    k.bar()
    if STAGE < 5:
        return

    emit_ffn_a(ctx, xin, xh)


def emit_gla(ctx):
    k = ctx["k"]; dram = ctx["dram"]; A = ctx["A"]; pss = ctx["pss"]
    sc = ctx["sc"]; ident = ctx["ident"]; tri = ctx["tri"]; zc = ctx["zc"]
    rmsw = ctx["rmsw"]; bgk2 = ctx["bgk2"]
    t, v, s, g, y = k.t, k.v, k.s, k.g, k.y
    a2a_o, p2a_i = dram["a2a_o"], dram["p2a_i"]
    one16 = sc[:, 1:2]

    A.push()
    wq = A.take(8 * 512); wk = A.take(8 * 512)
    wvw = A.take(8 * 512); wgw = A.take(8 * 512)
    wo = A.take(4 * D)
    wgk1 = A.take(8 * 16)
    wgk2 = A.take(512)
    for dch in range(DC):
        k.dma(y, wq[:, dch * 512:(dch + 1) * 512],
              dram["wq"][dch * 128:(dch + 1) * 128, :])
        k.dma(y, wk[:, dch * 512:(dch + 1) * 512],
              dram["wk"][dch * 128:(dch + 1) * 128, :])
        k.dma(y, wvw[:, dch * 512:(dch + 1) * 512],
              dram["wv"][dch * 128:(dch + 1) * 128, :])
        k.dma(y, wgw[:, dch * 512:(dch + 1) * 512],
              dram["wg"][dch * 128:(dch + 1) * 128, :])
        k.dma(y, wgk1[:, dch * 16:(dch + 1) * 16],
              dram["wgk1"][dch * 128:(dch + 1) * 128, :])
    for kc in range(4):
        k.dma(y, wo[:, kc * D:(kc + 1) * D],
              dram["wo"][kc * 128:(kc + 1) * 128, :])
    k.dma(y, wgk2[0:16, :], dram["wgk2"][:, :])

    xc = A.take(4 * D)
    qT = A.take(4 * 512); kT = A.take(4 * 512)
    lsT = A.take(4 * 512)
    vts = A.take(4 * 512); gts = A.take(4 * 512)
    hst = A.take(4 * 256)
    u16 = A.take(512)
    wrk = A.take(2048)
    out_t = A.take(D)
    gsc = A.take(16)

    g.memset(hst[:, :], 0.0)
    k.bar()
    if STAGE < 3.05:
        A.pop()
        return

    for sup in range(T // 512):
        for sb_ in range(4):
            k.dma(y, xc[:, sb_ * D:(sb_ + 1) * D], a2a_o[sup, sb_, :, :])
        k.bar(dmas=True)
        if STAGE < 3.08:
            continue

        for dt_ in range(4):
            pq, pk = pss[0], pss[1]
            for ww, pp in ((wq, pq), (wk, pk)):
                for cc2 in range(4):
                    for kc in range(DC):
                        l_ = ww[:, kc * 512 + dt_ * 128: kc * 512 + (dt_ + 1) * 128]
                        r_ = xc[:, cc2 * D + kc * 128: cc2 * D + (kc + 1) * 128]
                        t.matmul(pp[:, cc2 * 128:(cc2 + 1) * 128], l_, r_,
                                 start=(kc == 0), stop=(kc == DC - 1))
            k.inc(t.drain(), "pe")
            k.wait(v, "pe")
            k.inc(v.tensor_copy(qT[:, dt_ * 512:(dt_ + 1) * 512], pq[:, :]), "ve")
            k.inc(v.tensor_copy(kT[:, dt_ * 512:(dt_ + 1) * 512], pk[:, :]), "ve")
            k.wait(t, "ve")
        if STAGE < 3.12:
            continue
        for cc2 in range(4):
            pv_, pg = pss[2], pss[3]
            for ww, pp in ((wvw, pv_), (wgw, pg)):
                for kc in range(DC):
                    l_ = xc[:, cc2 * D + kc * 128: cc2 * D + (kc + 1) * 128]
                    t.matmul(pp[:, :], l_, ww[:, kc * 512:(kc + 1) * 512],
                             start=(kc == 0), stop=(kc == DC - 1))
            k.inc(t.drain(), "pe")
            k.wait(v, "pe")
            k.inc(v.tensor_copy(vts[:, cc2 * 512:(cc2 + 1) * 512], pv_[:, :]), "ve")
            k.inc(v.tensor_copy(gts[:, cc2 * 512:(cc2 + 1) * 512], pg[:, :]), "ve")
            k.wait(t, "ve")
        if STAGE < 3.2:
            continue
        pu = pss[4]
        for cc2 in range(4):
            for kc in range(DC):
                t.matmul(pu[0:16, cc2 * 128:(cc2 + 1) * 128],
                         wgk1[:, kc * 16:(kc + 1) * 16],
                         xc[:, cc2 * D + kc * 128: cc2 * D + (kc + 1) * 128],
                         start=(kc == 0), stop=(kc == DC - 1))
        k.inc(t.drain(), "pe")
        k.wait(v, "pe")
        k.inc(v.tensor_copy(u16[0:16, :], pu[0:16, :]), "ve")
        k.wait(t, "ve")
        for dt_ in range(4):
            pz = pss[5]
            t.matmul(pz[:, :], wgk2[0:16, dt_ * 128:(dt_ + 1) * 128],
                     u16[0:16, :], start=True, stop=True)
            k.inc(t.drain(), "pe")
            k.wait(s, "pe")
            s.activation(out=wrk[:, 0:512], in_=pz[:, :], func=AF.Sigmoid,
                         bias=bgk2[:, dt_:dt_ + 1])
            k.inc(s.activation(out=lsT[:, dt_ * 512:(dt_ + 1) * 512],
                               in_=wrk[:, 0:512], func=AF.Ln), "se")
            k.wait(t, "se")
        k.bar(dmas=False)
        if STAGE < 3.3:
            continue

        for cc2 in range(4):
            pw = [pss[6], pss[7]]
            for h in range(2):
                dts = (2 * h, 2 * h + 1)
                Gt = [wrk[:, 0:128], wrk[:, 128:256]]
                E = [wrk[:, 256:384], wrk[:, 384:512]]
                En = [wrk[:, 512:640], wrk[:, 640:768]]
                Er = [wrk[:, 768:896], wrk[:, 896:1024]]
                qd = [wrk[:, 1024:1152], wrk[:, 1152:1280]]
                kda = [wrk[:, 1280:1408], wrk[:, 1408:1536]]
                kdh = [wrk[:, 1536:1664], wrk[:, 1664:1792]]
                am = wrk[:, 1792:1920]
                kdhs = [wrk[:, 1920:2048], None]
                ge16 = [gsc[:, 0:1], gsc[:, 1:2]]
                edec = [gsc[:, 2:3], gsc[:, 3:4]]
                ssq2 = gsc[:, 4:5]; rstd = gsc[:, 5:6]

                for j, dt_ in enumerate(dts):
                    lsl = lsT[:, dt_ * 512 + cc2 * 128: dt_ * 512 + (cc2 + 1) * 128]
                    k.wait(v, "se")
                    k.inc(v.tensor_tensor_scan(Gt[j], lsl, zc, 0.0,
                                               ALU.add, ALU.add), "ve")
                    k.wait(s, "ve")
                    s.activation(out=ge16[j], in_=Gt[j][:, 127:128],
                                 func=AF.Identity, scale=1.0 / 16.0)
                    s.activation(out=E[j], in_=Gt[j], func=AF.Exp,
                                 scale=1.0 / 16.0)
                    s.activation(out=En[j], in_=Gt[j], func=AF.Exp,
                                 scale=-1.0 / 16.0)
                    s.activation(out=Er[j], in_=Gt[j], func=AF.Exp,
                                 scale=-1.0 / 16.0, bias=ge16[j])
                    k.inc(s.activation(out=edec[j], in_=Gt[j][:, 127:128],
                                       func=AF.Exp, scale=1.0 / 16.0), "se")
                    k.wait(v, "se")
                    qsl = qT[:, dt_ * 512 + cc2 * 128: dt_ * 512 + (cc2 + 1) * 128]
                    ksl = kT[:, dt_ * 512 + cc2 * 128: dt_ * 512 + (cc2 + 1) * 128]
                    k.inc(v.scalar_tensor_tensor(qd[j], qsl, one16, E[j],
                                                 ALU.mult, ALU.mult), "ve")
                    k.inc(v.tensor_tensor(kda[j], ksl, En[j], ALU.mult), "ve")
                    k.inc(v.tensor_tensor(kdh[j], ksl, Er[j], ALU.mult), "ve")
                kdhs[1] = Gt[0]
                k.wait(t, "ve")

                pA = pss[0]
                t.matmul(pA[:, 0:128], kda[0], qd[0], start=True, stop=False)
                k.inc(t.matmul(pA[:, 0:128], kda[1], qd[1],
                               start=False, stop=True), "pe")
                pT = pss[1]
                k.inc(t.transpose(pT[:, 0:128], kdh[0], ident), "pe")
                k.inc(t.transpose(pT[:, 128:256], kdh[1], ident), "pe")
                k.wait(v, "pe")
                k.inc(v.tensor_tensor(am, pA[:, 0:128], tri, ALU.mult), "ve")
                k.inc(v.tensor_copy(kdhs[0], pT[:, 0:128]), "ve")
                k.inc(v.tensor_copy(kdhs[1], pT[:, 128:256]), "ve")
                k.wait(t, "ve")

                po = pss[2]
                vsl = vts[:, cc2 * 512 + h * 256: cc2 * 512 + (h + 1) * 256]
                t.matmul(po[:, 0:256], qd[0],
                         hst[:, (2 * h) * 256:(2 * h + 1) * 256],
                         start=True, stop=False)
                t.matmul(po[:, 0:256], qd[1],
                         hst[:, (2 * h + 1) * 256:(2 * h + 2) * 256],
                         start=False, stop=False)
                t.matmul(po[:, 0:256], am, vsl, start=False, stop=True)
                ph = pss[3]
                t.matmul(ph[:, 0:256], kdhs[0], vsl, start=True, stop=True)
                k.inc(t.matmul(ph[:, 256:512], kdhs[1], vsl,
                               start=True, stop=True), "pe")
                k.wait(v, "pe"); k.wait(s, "pe")
                for j in range(2):
                    k.inc(v.scalar_tensor_tensor(
                        hst[:, (2 * h + j) * 256:(2 * h + j + 1) * 256],
                        hst[:, (2 * h + j) * 256:(2 * h + j + 1) * 256],
                        edec[j], ph[:, j * 256:(j + 1) * 256],
                        ALU.mult, ALU.add), "ve")
                o2 = wrk[:, 256:512]
                k.inc(s.activation(out=o2, in_=po[:, 0:256], func=AF.Square,
                                   accum_out=ssq2), "se")
                s.drain()
                s.activation(out=ssq2, in_=ssq2, func=AF.Identity,
                             scale=1.0 / 256, bias=sc[:, 0:1])
                k.inc(s.activation(out=ssq2, in_=ssq2, func=AF.Sqrt), "se")
                k.wait(v, "se")
                k.inc(v.reciprocal(rstd, ssq2), "ve")
                gsl = gts[:, cc2 * 512 + h * 256: cc2 * 512 + (h + 1) * 256]
                sg = wrk[:, 512:768]
                k.wait(s, "ve")
                k.inc(s.activation(out=sg, in_=gsl, func=AF.Silu), "se")
                k.wait(v, "se")
                t1 = wrk[:, 768:1024]
                k.inc(v.scalar_tensor_tensor(t1, po[:, 0:256], rstd, rmsw,
                                             ALU.mult, ALU.mult), "ve")
                go = wrk[:, 1024:1280]
                k.inc(v.tensor_tensor(go, t1, sg, ALU.mult), "ve")
                k.wait(t, "ve")
                pgt = pss[4]
                k.inc(t.transpose(pgt[:, 0:128], go[:, 0:128], ident), "pe")
                k.inc(t.transpose(pgt[:, 128:256], go[:, 128:256], ident), "pe")
                k.wait(v, "pe")
                goT0 = wrk[:, 1280:1408]; goT1 = wrk[:, 1408:1536]
                k.inc(v.tensor_copy(goT0, pgt[:, 0:128]), "ve")
                k.inc(v.tensor_copy(goT1, pgt[:, 128:256]), "ve")
                k.wait(t, "ve")
                kc0 = 2 * h
                for dti in range(DC):
                    pwb = pw[dti // 4][:, (dti % 4) * 128:((dti % 4) + 1) * 128]
                    wl0 = wo[:, kc0 * D + dti * 128: kc0 * D + (dti + 1) * 128]
                    wl1 = wo[:, (kc0 + 1) * D + dti * 128:
                             (kc0 + 1) * D + (dti + 1) * 128]
                    t.matmul(pwb, wl0, goT0, start=True, stop=False)
                    t.matmul(pwb, wl1, goT1, start=False, stop=True)
                k.inc(t.drain(), "pe")
                k.wait(v, "pe")
                for dti in range(DC):
                    pwb = pw[dti // 4][:, (dti % 4) * 128:((dti % 4) + 1) * 128]
                    ot = out_t[:, dti * 128:(dti + 1) * 128]
                    if h == 0:
                        k.inc(v.tensor_copy(ot, pwb), "ve")
                    else:
                        k.inc(v.tensor_tensor(ot, ot, pwb, ALU.add), "ve")
                k.wait(t, "ve")
            k.wait(y, "ve")
            k.dma(y, p2a_i[sup, cc2, :, :], out_t)
            k.wait(t, "ve")
        k.bar(dmas=False)
    A.pop()


def emit_ffn_a(ctx, xin, xh):
    k = ctx["k"]; dram = ctx["dram"]; A = ctx["A"]; pss = ctx["pss"]
    sc = ctx["sc"]; ident = ctx["ident"]; b1c = ctx["b1c"]
    t, v, s, g, y = k.t, k.v, k.s, k.g, k.y
    v1h = dram["v1h"]
    A.push()
    v1c = A.take(4 * D)
    bT = A.take(4 * D)
    fT = A.take(24 * 512)
    w1b = A.take(2 * 8 * 128)
    w2b = A.take(2 * D)
    b2t = A.take(D)
    wsp = A.take(D)
    k.dma(y, b2t, dram["b2"][:, :])

    for sup in range(TOK // 512):
        for sb_ in range(4):
            i = sup * 4 + sb_
            k.dma(y, v1c[:, sb_ * D:(sb_ + 1) * D],
                  v1h[i * 128:(i + 1) * 128, :])
        k.bar(dmas=True)
        for sb_ in range(4):
            emit_ln(k, v1c[:, sb_ * D:(sb_ + 1) * D], wsp,
                    fT[:, 0:D], sc, affine=ctx["ln_aff2"])
            for dch in range(DC):
                pb = pss[dch % 2]
                k.wait(t, "ve")
                k.inc(t.transpose(pb[:, 0:128],
                                  wsp[:, dch * 128:(dch + 1) * 128], ident),
                      "pe")
                k.wait(v, "pe")
                k.inc(v.tensor_copy(
                    bT[:, dch * 512 + sb_ * 128: dch * 512 + (sb_ + 1) * 128],
                    pb[:, 0:128]), "ve")
            k.wait(t, "ve"); k.wait(s, "ve")
        for fc in range(24):
            buf = w1b[:, (fc % 2) * 1024:(fc % 2) * 1024 + 1024]
            k.wait(y, "pe")
            for dch in range(DC):
                k.dma(y, buf[:, dch * 128:(dch + 1) * 128],
                      dram["w1"][dch * 128:(dch + 1) * 128,
                                 fc * 128:(fc + 1) * 128])
            k.wait(t, "dS")
            pf = pss[2 + (fc % 2)]
            for dch in range(DC):
                t.matmul(pf[:, :], buf[:, dch * 128:(dch + 1) * 128],
                         bT[:, dch * 512:(dch + 1) * 512],
                         start=(dch == 0), stop=(dch == DC - 1))
            k.inc(t.drain(), "pe")
            k.wait(s, "pe"); k.wait(v, "pe")
            hb = fT[:, fc * 512:(fc + 1) * 512]
            k.inc(s.activation(out=hb, in_=pf[:, :], func=AF.Identity,
                               bias=b1c[:, fc:fc + 1]), "se")
            k.wait(v, "se")
            k.inc(v.tensor_scalar_mul(wsp[:, 0:512], hb, 0.01), "ve")
            k.inc(v.tensor_tensor(hb, hb, wsp[:, 0:512], ALU.max), "ve")
            k.wait(s, "ve")
            k.inc(s.activation(out=hb, in_=hb, func=AF.Square), "se")
            k.wait(t, "se")
        for sb_ in range(4):
            i = sup * 4 + sb_
            p0, p1 = pss[4], pss[5]
            for half, pp in ((0, p0), (1, p1)):
                for kf in range(24):
                    buf = w2b[:, (kf % 2) * 512:(kf % 2) * 512 + 512]
                    k.wait(y, "pe")
                    k.dma(y, buf, dram["w2"][kf * 128:(kf + 1) * 128,
                                             half * 512:(half + 1) * 512])
                    k.wait(t, "dS")
                    l_ = fT[:, kf * 512 + sb_ * 128: kf * 512 + (sb_ + 1) * 128]
                    k.inc(t.matmul(pp[:, :], l_, buf,
                                   start=(kf == 0), stop=(kf == 23)), "pe")
            k.inc(t.drain(), "pe")
            k.dma(y, wsp, xin[i * 128:(i + 1) * 128, :])
            k.wait(v, "pe"); k.wait(v, "dS")
            k.inc(v.tensor_tensor(wsp[:, 0:512], wsp[:, 0:512], p0[:, :],
                                  ALU.add), "ve")
            k.inc(v.tensor_tensor(wsp[:, 512:1024], wsp[:, 512:1024], p1[:, :],
                                  ALU.add), "ve")
            k.inc(v.tensor_tensor(wsp, wsp, b2t, ALU.add), "ve")
            k.inc(v.tensor_tensor(wsp, wsp, v1c[:, sb_ * D:(sb_ + 1) * D],
                                  ALU.add), "ve")
            k.wait(y, "ve")
            k.dma(y, xh[i * 128:(i + 1) * 128, :], wsp)
            k.wait(t, "ve")
    A.pop()







def emit_attn_b(ctx, xin, xout):
    k = ctx["k"]; dram = ctx["dram"]; A = ctx["A"]; pss = ctx["pss"]
    sc = ctx["sc"]; ident = ctx["ident"]
    t, v, s, g, y = k.t, k.v, k.s, k.g, k.y
    a2a_i, a2a_o = dram["a2a_i"], dram["a2a_o"]
    p2a_i, p2a_o = dram["p2a_i"], dram["p2a_o"]

    A.push()
    xt = A.take(D); ut = A.take(D); aT = A.take(D)
    for i in range(NT):
        sq, sub = i // 4, i % 4
        k.dma(y, xt, xin[i * 128:(i + 1) * 128, :])
        k.wait(v, "dS"); k.wait(s, "dS")
        emit_ln(k, xt, ut, aT, sc, affine=None)
        for dch in range(DC):
            pb = pss[dch % 2]
            k.wait(t, "ve")
            k.inc(t.transpose(pb[:, 0:128], ut[:, dch * 128:(dch + 1) * 128],
                              ident), "pe")
            k.wait(v, "pe")
            k.inc(v.tensor_copy(aT[:, dch * 128:(dch + 1) * 128], pb[:, 0:128]),
                  "ve")
        k.wait(y, "ve")
        for r in (2 * sq, 2 * sq + 1):
            k.dma(y, a2a_i[r, sub, :, :], aT)
    A.pop()
    k.bar()

    with allow_ici_all_to_all():
        k.inc(g.collective_compute(
            "AllToAll", ALU.bypass, replica_groups=[list(range(NCORE))],
            ins=[a2a_i.opt()], outs=[a2a_o.opt()]), "cc")
    g.wait_ge(k.sem("cc"), k._cnt["cc"])
    k.bar(dmas=False)

    A.push()
    qh = A.take(2 * T)
    kh = A.take(2 * T)
    vt_ = A.take(8 * 1024)
    xc = A.take(4 * D)
    rt = A.take(4 * 512)
    wq_ = A.take(8 * 256); wk_ = A.take(8 * 256); wv_ = A.take(8 * 256)
    wo_ = A.take(2 * D)
    rp1 = A.take(512); rp2 = A.take(512)
    dneg = A.take(4 * 512)
    out_t = A.take(D); out_t2 = A.take(D)
    Pt = A.take(512)
    oacc = A.take(128)
    fsc = A.take(16)
    for r_ in range(4):
        k.dma(y, dneg[:, r_ * 512:(r_ + 1) * 512],
              dram["dmneg"][r_ * 128:(r_ + 1) * 128, :])
    k.bar()

    for wave in range(2):
        wc = wave * 256
        for dch in range(DC):
            k.dma(y, wq_[:, dch * 256:(dch + 1) * 256],
                  dram["bwq"][dch * 128:(dch + 1) * 128, wc:wc + 256])
            k.dma(y, wk_[:, dch * 256:(dch + 1) * 256],
                  dram["bwk"][dch * 128:(dch + 1) * 128, wc:wc + 256])
            k.dma(y, wv_[:, dch * 256:(dch + 1) * 256],
                  dram["bwv"][dch * 128:(dch + 1) * 128, wc:wc + 256])
        for kc in range(2):
            k.dma(y, wo_[:, kc * D:(kc + 1) * D],
                  dram["bwo"][wc + kc * 128: wc + (kc + 1) * 128, :])
        k.bar()

        for sup in range(T // 512):
            k.wait(y, "pe"); k.wait(y, "ve"); k.wait(y, "se")
            for sb_ in range(4):
                k.dma(y, xc[:, sb_ * D:(sb_ + 1) * D], a2a_o[sup, sb_, :, :])
            for tb in range(4):
                k.dma(y, rt[:, tb * 512:(tb + 1) * 512],
                      dram[("rch", "rcl", "rsh", "rsl")[tb]]
                      [:, sup * 512:(sup + 1) * 512])
            k.bar(dmas=True)
            CH = rt[:, 0 * 512:1 * 512]
            CL = rt[:, 1 * 512:2 * 512]
            SH = rt[:, 2 * 512:3 * 512]
            SL = rt[:, 3 * 512:4 * 512]
            for (ws, dst, isq) in ((wq_, qh, True), (wk_, kh, False)):
                px = [pss[0], pss[1]]
                for dt_ in (0, 1):
                    for cc2 in range(4):
                        for kc in range(DC):
                            l_ = ws[:, kc * 256 + dt_ * 128:
                                    kc * 256 + (dt_ + 1) * 128]
                            r_ = xc[:, cc2 * D + kc * 128:
                                    cc2 * D + (kc + 1) * 128]
                            t.matmul(px[dt_][:, cc2 * 128:(cc2 + 1) * 128],
                                     l_, r_, start=(kc == 0),
                                     stop=(kc == DC - 1))
                k.inc(t.drain(), "pe")
                k.wait(v, "pe")
                x1, x2 = px[0], px[1]
                k.inc(v.tensor_tensor(rp1, x1[:, :], CH, ALU.mult), "ve")
                v.tensor_tensor(rp2, x2[:, :], SH, ALU.mult)
                v.tensor_tensor(rp1, rp1, rp2, ALU.subtract)
                v.tensor_tensor(rp2, x1[:, :], CL, ALU.mult)
                v.tensor_tensor(rp1, rp1, rp2, ALU.add)
                v.tensor_tensor(rp2, x2[:, :], SL, ALU.mult)
                k.inc(v.tensor_tensor(rp1, rp1, rp2, ALU.subtract), "ve")
                for h in range(4):
                    k.inc(v.tensor_copy(
                        dst[(h % 2) * 64:(h % 2) * 64 + 32,
                            (h // 2) * T + sup * 512:(h // 2) * T + (sup + 1) * 512],
                        rp1[32 * h:32 * h + 32, :]), "ve")
                v.tensor_tensor(rp1, x1[:, :], SH, ALU.mult)
                v.tensor_tensor(rp2, x2[:, :], CH, ALU.mult)
                v.tensor_tensor(rp1, rp1, rp2, ALU.add)
                v.tensor_tensor(rp2, x1[:, :], SL, ALU.mult)
                v.tensor_tensor(rp1, rp1, rp2, ALU.add)
                v.tensor_tensor(rp2, x2[:, :], CL, ALU.mult)
                k.inc(v.tensor_tensor(rp1, rp1, rp2, ALU.add), "ve")
                for h in range(4):
                    k.inc(v.tensor_copy(
                        dst[(h % 2) * 64 + 32:(h % 2) * 64 + 64,
                            (h // 2) * T + sup * 512:(h // 2) * T + (sup + 1) * 512],
                        rp1[32 * h:32 * h + 32, :]), "ve")
                k.wait(t, "ve")
            if DEBUG_DUMP and wave == 0 and sup == 0:
                for j in range(4):
                    k.dma(y, dram["rs_i"][(24 + j) * 128:(25 + j) * 128, :],
                          xc[:, j * D:(j + 1) * D])
                for j in range(2):
                    k.dma(y, dram["rs_i"][(28 + j) * 128:(29 + j) * 128, :],
                          wv_[:, j * D:(j + 1) * D])
                k.wait(y, "dS")
            for cc2 in range(4):
                pv_ = pss[2]
                for kc in range(DC):
                    t.matmul(pv_[:, 0:256],
                             xc[:, cc2 * D + kc * 128: cc2 * D + (kc + 1) * 128],
                             wv_[:, kc * 256:(kc + 1) * 256],
                             start=(kc == 0), stop=(kc == DC - 1))
                k.inc(t.drain(), "pe")
                k.wait(v, "pe")
                k.inc(v.tensor_copy(
                    vt_[:, (sup * 4 + cc2) * 256:(sup * 4 + cc2 + 1) * 256],
                    pv_[:, 0:256]), "ve")
                k.wait(t, "ve")
        k.bar()
        if DEBUG_DUMP and wave == 0:
            for j in range(8):
                k.dma(y, dram["rs_i"][j * 128:(j + 1) * 128, :],
                      vt_[:, j * D:(j + 1) * D])
                k.dma(y, dram["rs_i"][(8 + j) * 128:(9 + j) * 128, :],
                      qh[:, j * D:(j + 1) * D])
                k.dma(y, dram["rs_i"][(16 + j) * 128:(17 + j) * 128, :],
                      kh[:, j * D:(j + 1) * D])
            k.wait(y, "dS")

        mneg = fsc[:, 8:9]
        g.memset(mneg, -1e30)
        k.bar(dmas=False)
        for qt in range(NT * 2):
            for hp in range(2):
                macc = [fsc[:, 0:1], fsc[:, 1:2]]
                lacc = [fsc[:, 2:3], fsc[:, 3:4]]
                nmr = fsc[:, 4:5]
                nnm8 = fsc[:, 5:6]
                alph = fsc[:, 6:7]
                rsum = fsc[:, 7:8]
                rl = fsc[:, 9:10]
                k.wait(v, "se"); k.wait(v, "pe")
                v.memset(oacc[:, :], 0.0)
                for ha in range(2):
                    v.tensor_copy(macc[ha], mneg)
                    v.memset(lacc[ha], 0.0)
                k.inc(v.drain(), "ve")
                nsb = qt // 4 + 1
                for sb_ in range(nsb):
                    diag = (sb_ == qt // 4)
                    for ha in range(2):
                        pS = pss[ha]
                        k.wait(t, "ve")
                        k.inc(t.matmul(
                            pS[:, :],
                            qh[64 * ha:64 * ha + 64,
                               hp * T + qt * 128:hp * T + (qt + 1) * 128],
                            kh[64 * ha:64 * ha + 64,
                               hp * T + sb_ * 512:hp * T + (sb_ + 1) * 512],
                            start=True, stop=True), "pe")
                        k.wait(v, "pe")
                        Ssrc = pS[:, :]
                        if diag:
                            r_ = qt % 4
                            k.inc(v.tensor_tensor(
                                Pt, pS[:, :], dneg[:, r_ * 512:(r_ + 1) * 512],
                                ALU.add), "ve")
                            Ssrc = Pt
                        v.reduce_max(nmr, Ssrc, axis=AX.X)
                        v.drain()
                        v.tensor_tensor(nmr, nmr, macc[ha], ALU.max)
                        v.drain()
                        k.inc(v.tensor_scalar_mul(nnm8, nmr, -0.125), "ve")
                        k.wait(s, "ve")
                        s.activation(out=alph, in_=macc[ha], func=AF.Exp,
                                     scale=0.125, bias=nnm8)
                        s.drain()
                        k.inc(s.activation(out=Pt, in_=Ssrc, func=AF.Exp,
                                           scale=0.125, bias=nnm8,
                                           accum_out=rsum), "se")
                        k.wait(v, "se")
                        v.tensor_copy(macc[ha], nmr)
                        v.drain()
                        v.scalar_tensor_tensor(lacc[ha], lacc[ha], alph,
                                               rsum, ALU.mult, ALU.add)
                        k.inc(v.drain(), "ve")
                        pT = pss[2]
                        for i4 in range(4):
                            k.wait(t, "ve")
                            k.inc(t.transpose(pT[:, i4 * 128:(i4 + 1) * 128],
                                              Pt[:, i4 * 128:(i4 + 1) * 128],
                                              ident), "pe")
                        k.wait(v, "pe")
                        for i4 in range(4):
                            k.inc(v.tensor_copy(
                                Pt[:, i4 * 128:(i4 + 1) * 128],
                                pT[:, i4 * 128:(i4 + 1) * 128]), "ve")
                        k.wait(t, "ve")
                        po = pss[3]
                        for i4 in range(4):
                            t.matmul(po[:, 0:64],
                                     Pt[:, i4 * 128:(i4 + 1) * 128],
                                     vt_[:, (sb_ * 4 + i4) * 256 +
                                         (2 * hp + ha) * 64:
                                         (sb_ * 4 + i4) * 256 +
                                         (2 * hp + ha) * 64 + 64],
                                     start=(i4 == 0), stop=(i4 == 3))
                        k.inc(t.drain(), "pe")
                        k.wait(v, "pe")
                        k.inc(v.scalar_tensor_tensor(
                            oacc[:, ha * 64:(ha + 1) * 64],
                            oacc[:, ha * 64:(ha + 1) * 64],
                            alph, po[:, 0:64], ALU.mult, ALU.add), "ve")
                for ha in range(2):
                    v.reciprocal(rl, lacc[ha])
                    v.drain()
                    k.inc(v.tensor_scalar_mul(oacc[:, ha * 64:(ha + 1) * 64],
                                              oacc[:, ha * 64:(ha + 1) * 64],
                                              rl), "ve")
                k.wait(t, "ve")
                pT = pss[4]
                k.inc(t.transpose(pT[:, 0:128], oacc[:, 0:128], ident), "pe")
                k.wait(v, "pe")
                k.inc(v.tensor_copy(out_t2[:, 0:128], pT[:, 0:128]), "ve")
                k.wait(t, "ve")
                pw = [pss[5], pss[6]]
                for dti in range(DC):
                    pwb = pw[dti // 4][:, (dti % 4) * 128:((dti % 4) + 1) * 128]
                    t.matmul(pwb, wo_[:, hp * D + dti * 128:
                                      hp * D + (dti + 1) * 128],
                             out_t2[:, 0:128], start=True, stop=True)
                k.inc(t.drain(), "pe")
                k.wait(v, "pe")
                for dti in range(DC):
                    pwb = pw[dti // 4][:, (dti % 4) * 128:((dti % 4) + 1) * 128]
                    ot = out_t[:, dti * 128:(dti + 1) * 128]
                    if hp == 0:
                        k.inc(v.tensor_copy(ot, pwb), "ve")
                    else:
                        k.inc(v.tensor_tensor(ot, ot, pwb, ALU.add), "ve")
                k.wait(t, "ve")
            sup_, cc2_ = qt // 4, qt % 4
            if wave == 0:
                k.wait(y, "ve")
                k.dma(y, p2a_i[sup_, cc2_, :, :], out_t)
            else:
                k.wait(y, "pe")
                k.dma(y, out_t2, p2a_i[sup_, cc2_, :, :])
                k.wait(v, "dS")
                k.inc(v.tensor_tensor(out_t, out_t, out_t2, ALU.add), "ve")
                k.wait(y, "ve")
                k.dma(y, p2a_i[sup_, cc2_, :, :], out_t)
    A.pop()
    k.bar()

    with allow_ici_all_to_all():
        k.inc(g.collective_compute(
            "AllToAll", ALU.bypass, replica_groups=[list(range(NCORE))],
            ins=[p2a_i.opt()], outs=[p2a_o.opt()]), "cc")
    g.wait_ge(k.sem("cc"), k._cnt["cc"])
    k.bar(dmas=False)

    A.push()
    pa = A.take(D); pb_ = A.take(D); xtl = A.take(D); vt2 = A.take(D)
    for i in range(NT):
        sq, sub = i // 4, i % 4
        k.dma(y, pa, p2a_o[2 * sq, sub, :, :])
        k.dma(y, pb_, p2a_o[2 * sq + 1, sub, :, :])
        k.dma(y, xtl, xin[i * 128:(i + 1) * 128, :])
        k.wait(v, "dS")
        k.inc(v.tensor_tensor(pa, pa, pb_, ALU.add), "ve")
        for dch in range(DC):
            pb = pss[dch % 2]
            k.wait(t, "ve")
            k.inc(t.transpose(pb[:, 0:128], pa[:, dch * 128:(dch + 1) * 128],
                              ident), "pe")
            k.wait(v, "pe")
            k.inc(v.tensor_tensor(vt2[:, dch * 128:(dch + 1) * 128],
                                  xtl[:, dch * 128:(dch + 1) * 128],
                                  pb[:, 0:128], ALU.add), "ve")
        k.wait(y, "ve")
        k.dma(y, xout[i * 128:(i + 1) * 128, :], vt2)
    A.pop()



def emit_moe(ctx, xin, xout):
    k = ctx["k"]; dram = ctx["dram"]; A = ctx["A"]; pss = ctx["pss"]
    sc = ctx["sc"]; ident = ctx["ident"]; zc = ctx["zc"]
    t, v, s, g, y = k.t, k.v, k.s, k.g, k.y
    GTOK = NCORE * TOK

    A.push()
    xt = A.take(D); ut = A.take(D); wsp = A.take(D); uT = A.take(D)
    rws = A.take(64)
    lg = A.take(8)
    pr = A.take(8); e1 = A.take(8); pm_ = A.take(8); sels = A.take(8)
    c1 = A.take(8); wmt = A.take(8)
    rsc = A.take(8)
    for dch in range(DC):
        k.dma(y, rws[:, dch * 8:(dch + 1) * 8],
              dram["rw"][dch * 128:(dch + 1) * 128, :])
    k.bar()
    for i in range(NT):
        k.dma(y, xt, xin[i * 128:(i + 1) * 128, :])
        k.wait(v, "dS"); k.wait(s, "dS")
        emit_ln(k, xt, ut, wsp, sc, affine=None)
        k.wait(y, "ve")
        k.dma(y, dram["ag_u_i"][i * 128:(i + 1) * 128, :], ut)
        for dch in range(DC):
            pb = pss[dch % 2]
            k.wait(t, "ve")
            k.inc(t.transpose(pb[:, 0:128], ut[:, dch * 128:(dch + 1) * 128],
                              ident), "pe")
            k.wait(v, "pe")
            k.inc(v.tensor_copy(uT[:, dch * 128:(dch + 1) * 128], pb[:, 0:128]),
                  "ve")
        k.wait(t, "ve")
        pl = pss[2]
        for dch in range(DC):
            t.matmul(pl[:, 0:8], uT[:, dch * 128:(dch + 1) * 128],
                     rws[:, dch * 8:(dch + 1) * 8],
                     start=(dch == 0), stop=(dch == DC - 1))
        k.inc(t.drain(), "pe")
        k.wait(v, "pe")
        k.inc(v.tensor_copy(lg, pl[:, 0:8]), "ve")
        rmax = rsc[:, 0:1]; nmax = rsc[:, 1:2]; ssum = rsc[:, 2:3]
        rs_ = rsc[:, 3:4]; m1 = rsc[:, 4:5]; m2 = rsc[:, 5:6]
        v.reduce_max(rmax, lg, axis=AX.X)
        v.drain()
        k.inc(v.tensor_scalar_mul(nmax, rmax, -1.0), "ve")
        k.wait(s, "ve")
        s.activation(out=pr, in_=lg, func=AF.Exp, bias=nmax, accum_out=ssum)
        k.inc(s.drain(), "se")
        k.wait(v, "se")
        v.reciprocal(rs_, ssum)
        v.drain()
        v.tensor_scalar_mul(pr, pr, rs_)
        v.drain()
        v.reduce_max(m1, pr, axis=AX.X)
        v.drain()
        v.tensor_scalar(e1, pr, m1, None, ALU.is_ge)
        v.drain()
        v.tensor_tensor_scan(c1, e1, zc[:, 0:8], 0.0, ALU.add, ALU.add)
        v.drain()
        v.tensor_scalar(c1, c1, 1.0, None, ALU.is_le)
        v.drain()
        v.tensor_tensor(sels, e1, c1, ALU.mult)
        v.drain()
        v.tensor_tensor(pm_, pr, sels, ALU.mult)
        v.drain()
        v.tensor_tensor(pm_, pr, pm_, ALU.subtract)
        v.drain()
        v.reduce_max(m2, pm_, axis=AX.X)
        v.drain()
        v.tensor_scalar(e1, pm_, m2, None, ALU.is_ge)
        v.drain()
        v.tensor_tensor_scan(c1, e1, zc[:, 0:8], 0.0, ALU.add, ALU.add)
        v.drain()
        v.tensor_scalar(c1, c1, 1.0, None, ALU.is_le)
        v.drain()
        v.tensor_tensor(e1, e1, c1, ALU.mult)
        v.drain()
        v.tensor_tensor(sels, sels, e1, ALU.add)
        v.drain()
        k.inc(v.tensor_tensor(wmt, pr, sels, ALU.mult), "ve")
        k.wait(y, "ve")
        k.dma(y, dram["ag_w_i"][i * 128:(i + 1) * 128, :], wmt)
    A.pop()
    k.bar()

    with allow_ici_all_to_all():
        k.inc(g.collective_compute(
            "AllGather", ALU.bypass, replica_groups=[list(range(NCORE))],
            ins=[dram["ag_u_i"].opt()], outs=[dram["ag_u_o"].opt()]), "cc")
        k.inc(g.collective_compute(
            "AllGather", ALU.bypass, replica_groups=[list(range(NCORE))],
            ins=[dram["ag_w_i"].opt()], outs=[dram["ag_w_o"].opt()]), "cc")
    g.wait_ge(k.sem("cc"), k._cnt["cc"])
    k.bar(dmas=False)

    A.push()
    msv = A.take(NE * 4); ms_ = A.take(NE * 4)
    k.dma(y, msv, dram["msv"][:, :])
    k.dma(y, ms_, dram["ms"][:, :])
    ctx["msv"], ctx["ms"] = msv, ms_
    k.bar()
    xT = A.take(8 * 512)
    vT = A.take(4 * 512)
    yT = A.take(4 * 512)
    acc = A.take(4 * D)
    wm = A.take(32)
    xtl = A.take(D)
    wvb = [A.take(8 * 512), A.take(8 * 512)]
    wb_ = [A.take(8 * 512), A.take(8 * 512)]
    w1b = [A.take(4 * D), A.take(4 * D)]
    ms1b = [A.take(D), A.take(D)]
    pe_at_parity = [0, 0]
    ve_at_parity = [0, 0]
    for scg in range(GTOK // 512):
        for sb_ in range(4):
            gi = scg * 4 + sb_
            k.dma(y, xtl, dram["ag_u_o"][gi * 128:(gi + 1) * 128, :])
            k.dma(y, wm[:, sb_ * 8:(sb_ + 1) * 8],
                  dram["ag_w_o"][gi * 128:(gi + 1) * 128, :])
            k.wait(t, "dS")
            for dch in range(DC):
                pb = pss[6 + dch % 2]
                k.wait(t, "ve")
                k.inc(t.transpose(pb[:, 0:128], xtl[:, dch * 128:(dch + 1) * 128],
                                  ident), "pe")
                k.wait(v, "pe")
                k.inc(v.tensor_copy(xT[:, dch * 512 + sb_ * 128:
                                       dch * 512 + (sb_ + 1) * 128],
                                    pb[:, 0:128]), "ve")
            k.wait(t, "ve")
            k.wait(y, "ve")
        for e in range(NE):
            par = e % 2
            k.wait_at(y, "pe", pe_at_parity[par])
            k.wait_at(y, "ve", ve_at_parity[par])
            k.dma(y, wvb[par], dram["mwv"][e * 128:(e + 1) * 128, :])
            k.dma(y, wb_[par], dram["mw"][e * 128:(e + 1) * 128, :])
            k.dma(y, w1b[par], dram["mw1"][e * 128:(e + 1) * 128, :])
            k.dma(y, ms1b[par], dram["ms1"][:, e * D:(e + 1) * D])
            k.wait(t, "dS")
            for src_w, dst, scl, fn in (
                    (wvb[par], vT, "msv", AF.Identity),
                    (wb_[par], yT, "ms", AF.Gelu_apprx_tanh)):
                for ft in range(4):
                    pf = pss[ft % 2]
                    for kc in range(DC):
                        t.matmul(pf[:, :],
                                 src_w[:, kc * 512 + ft * 128:
                                       kc * 512 + (ft + 1) * 128],
                                 xT[:, kc * 512:(kc + 1) * 512],
                                 start=(kc == 0), stop=(kc == DC - 1))
                    k.inc(t.drain(), "pe")
                    k.wait(s, "pe")
                    k.inc(s.activation(out=dst[:, ft * 512:(ft + 1) * 512],
                                       in_=pf[:, :], func=fn,
                                       scale=ctx[scl][:, e * 4 + ft:e * 4 + ft + 1]),
                          "se")
                    k.wait(t, "se")
            k.wait(v, "se")
            for ft in range(4):
                k.inc(v.tensor_tensor(vT[:, ft * 512:(ft + 1) * 512],
                                      vT[:, ft * 512:(ft + 1) * 512],
                                      yT[:, ft * 512:(ft + 1) * 512],
                                      ALU.mult), "ve")
            k.wait(t, "ve")
            for sb_ in range(4):
                for half in range(2):
                    ph_ = pss[2 + half]
                    for kf in range(4):
                        t.matmul(ph_[:, :],
                                 vT[:, kf * 512 + sb_ * 128:
                                    kf * 512 + (sb_ + 1) * 128],
                                 w1b[par][:, kf * D + half * 512:
                                          kf * D + half * 512 + 512],
                                 start=(kf == 0), stop=(kf == 3))
                k.inc(t.drain(), "pe")
                if sb_ == 3:
                    pe_at_parity[par] = k._cnt["pe"]
                k.wait(v, "pe")
                gate = wm[:, sb_ * 8 + e: sb_ * 8 + e + 1]
                for half in range(2):
                    ph_ = pss[2 + half]
                    tmp = xtl[:, half * 512:(half + 1) * 512]
                    a_ = acc[:, sb_ * D + half * 512: sb_ * D + half * 512 + 512]
                    v.tensor_tensor(tmp, ph_[:, :],
                                    ms1b[par][:, half * 512:(half + 1) * 512],
                                    ALU.mult)
                    if e == 0:
                        k.inc(v.tensor_scalar_mul(a_, tmp, gate), "ve")
                    else:
                        k.inc(v.scalar_tensor_tensor(a_, tmp, gate, a_,
                                                     ALU.mult, ALU.add), "ve")
                k.wait(t, "ve")
            ve_at_parity[par] = k._cnt["ve"]
        k.wait(y, "ve")
        for sb_ in range(4):
            gi = scg * 4 + sb_
            k.dma(y, dram["rs_i"][gi * 128:(gi + 1) * 128, :],
                  acc[:, sb_ * D:(sb_ + 1) * D])
    A.pop()
    k.bar()

    with allow_ici_all_to_all():
        k.inc(g.collective_compute(
            "ReduceScatter", ALU.add, replica_groups=[list(range(NCORE))],
            ins=[dram["rs_i"].opt()], outs=[dram["rs_o"].opt()]), "cc")
    g.wait_ge(k.sem("cc"), k._cnt["cc"])
    k.bar(dmas=False)

    A.push()
    xa = A.take(D); ra = A.take(D)
    for i in range(NT):
        k.dma(y, xa, xin[i * 128:(i + 1) * 128, :])
        k.dma(y, ra, dram["rs_o"][i * 128:(i + 1) * 128, :])
        k.wait(v, "dS")
        k.inc(v.tensor_tensor(xa, xa, ra, ALU.add), "ve")
        k.wait(y, "ve")
        k.dma(y, xout[i * 128:(i + 1) * 128, :], xa)
    A.pop()



def kernel(**inputs):
    nc = build_program()
    in_maps = [make_in_map(c, inputs) for c in range(NCORE)]
    res = run_bass_kernel_spmd(nc, in_maps, core_ids=list(range(NCORE)))
    out = np.zeros((B, T, D), dtype=np.float32)
    for c in range(NCORE):
        yc = np.asarray(res.results[c]["y"]).reshape(B, SLICE, D)
        out[:, c * SLICE:(c + 1) * SLICE, :] = yc
    return out


# version 02
# speedup vs baseline: 1.2872x; 1.2872x over previous
import numpy as np
from contextlib import ExitStack

import concourse.bass as bass
import concourse.mybir as mybir
from concourse.bass_utils import run_bass_kernel_spmd
from concourse.aot_env import allow_ici_all_to_all

F32 = mybir.dt.float32
U32 = mybir.dt.uint32
AF = mybir.ActivationFunctionType
ALU = mybir.AluOpType
AX = mybir.AxisListType

NCORE = 8
B, T, D = 4, 4096, 1024
SLICE = 512
TOK = B * SLICE
NT = TOK // 128
DC = D // 128
EPS = 1e-5

NE, F = 8, 4096
FSL = F // NCORE

SUBLAYERS = (1, 2, 3, 4)
MOE_SPARSE = True
NCH = 40
NSLOT = NCH * 128
STAGE = 99
DEBUG_DUMP = ()



def _repl(v, p=128):
    v = np.ascontiguousarray(np.asarray(v), dtype=np.float32)
    return np.broadcast_to(v.reshape(1, -1), (p, v.size)).copy()


def _colsplit(v, tile=128):
    v = np.asarray(v, dtype=np.float32)
    return np.ascontiguousarray(v.reshape(-1, tile).T)


def _rope_tables():
    half = 32
    inv = 10000.0 ** (-(np.arange(half, dtype=np.float64)) * 2.0 / 64.0)
    ang = np.arange(T, dtype=np.float64)[None, :] * inv[:, None]
    out = []
    for m in (np.cos(ang), np.sin(ang)):
        hi = m.astype(np.float32)
        lo = (m - hi.astype(np.float64)).astype(np.float32)
        out.append(np.tile(hi, (4, 1)))
        out.append(np.tile(lo, (4, 1)))
    return out


def _rope_perm():
    perm = np.zeros(512, dtype=np.int64)
    for h in range(8):
        for j in range(64):
            p = (h // 4) * 256 + (j % 2) * 128 + (h % 4) * 32 + j // 2
            perm[p] = h * 64 + j
    return perm


def make_in_map(c, inp):
    hg = c % 2
    hb = c % 2
    d = {}
    x = np.asarray(inp["L0_attn_a_x"], dtype=np.float32)
    d["x0"] = np.ascontiguousarray(
        x[:, c * SLICE:(c + 1) * SLICE, :].reshape(TOK, D))
    for nm, key in (("wq", "L0_attn_a_Wq"), ("wk", "L0_attn_a_Wk"),
                    ("wv", "L0_attn_a_Wv"), ("wg", "L0_attn_a_Wg")):
        W = np.asarray(inp[key], dtype=np.float32)
        d[nm] = np.ascontiguousarray(W[:, hg * 512:(hg + 1) * 512])
    d["wgk1"] = np.ascontiguousarray(np.asarray(inp["L0_attn_a_Wgk1"], np.float32))
    d["wgk2"] = np.ascontiguousarray(
        np.asarray(inp["L0_attn_a_Wgk2"], np.float32)[:, hg * 512:(hg + 1) * 512])
    d["bgk2"] = _colsplit(np.asarray(inp["L0_attn_a_bgk2"], np.float32)
                          [hg * 512:(hg + 1) * 512])
    d["rmsw"] = _repl(inp["L0_attn_a_rms_w"])
    d["wo"] = np.ascontiguousarray(
        np.asarray(inp["L0_attn_a_Wo"], np.float32)[hg * 512:(hg + 1) * 512, :])
    d["ln1w"] = _repl(inp["L0_attn_a_ln1_w"]); d["ln1b"] = _repl(inp["L0_attn_a_ln1_b"])
    d["ln2w"] = _repl(inp["L0_attn_a_ln2_w"]); d["ln2b"] = _repl(inp["L0_attn_a_ln2_b"])
    d["w1"] = np.ascontiguousarray(np.asarray(inp["L0_attn_a_W1"], np.float32))
    d["b1c"] = _colsplit(inp["L0_attn_a_b1"])
    d["w2"] = np.ascontiguousarray(np.asarray(inp["L0_attn_a_W2"], np.float32))
    d["b2"] = _repl(inp["L0_attn_a_b2"])
    fs = slice(c * FSL, (c + 1) * FSL)
    wv = np.asarray(inp["L0_ffn_wv"], np.float32)[:, :, fs]
    w_ = np.asarray(inp["L0_ffn_w"], np.float32)[:, :, fs]
    w1m = np.asarray(inp["L0_ffn_w1"], np.float32)[:, fs, :]
    sv = np.asarray(inp["L0_ffn_sv"], np.float32)[:, 0, fs]
    s_ = np.asarray(inp["L0_ffn_s"], np.float32)[:, 0, fs]
    mwv = wv.reshape(NE, 8, 128, FSL).transpose(0, 2, 1, 3).reshape(NE * 128, 8 * FSL)
    mw = w_.reshape(NE, 8, 128, FSL).transpose(0, 2, 1, 3).reshape(NE * 128, 8 * FSL)
    svc = sv.reshape(NE, 4, 128).transpose(0, 2, 1).reshape(NE * 128, 4)
    ssc = s_.reshape(NE, 4, 128).transpose(0, 2, 1).reshape(NE * 128, 4)
    d["mwv"] = np.ascontiguousarray(np.concatenate([mwv, svc], axis=1))
    d["mw"] = np.ascontiguousarray(np.concatenate([mw, ssc], axis=1))
    d["mw1"] = np.ascontiguousarray(
        w1m.reshape(NE, 4, 128, D).transpose(0, 2, 1, 3).reshape(NE * 128, 4 * D))
    d["msv"] = np.concatenate([_colsplit(sv[e]) for e in range(NE)], axis=1)
    d["ms"] = np.concatenate([_colsplit(s_[e]) for e in range(NE)], axis=1)
    s1 = np.asarray(inp["L0_ffn_s1"], np.float32)[:, 0, :]
    d["ms1"] = np.concatenate([_repl(s1[e]) for e in range(NE)], axis=1)
    d["ms1g"] = np.ascontiguousarray(
        np.repeat(s1, 128, axis=0))
    d["rw"] = np.ascontiguousarray(np.asarray(inp["L0_ffn_router_w"], np.float32))
    p_ = np.arange(128, dtype=np.float32)[:, None]
    d["tokid"] = np.ascontiguousarray(
        (c * TOK + np.arange(NT, dtype=np.float32)[None, :] * 128 + p_))
    pr = np.zeros((128, 2 * NT), dtype=np.float32)
    for kk_ in range(2):
        for i_ in range(NT):
            pr[:, kk_ * NT + i_] = c * 2 * TOK + kk_ * TOK + i_ * 128 + p_[:, 0]
    d["pairrow"] = np.ascontiguousarray(pr)
    d["jrow"] = np.ascontiguousarray(
        (np.arange(NCH, dtype=np.float32) * 128)[None, :])
    lt = (np.arange(128)[:, None] < np.arange(128)[None, :]).astype(np.float32)
    d["ltstrict"] = np.ascontiguousarray(lt)
    d["iota8"] = np.broadcast_to(np.arange(8, dtype=np.float32), (128, 8)).copy()
    perm = _rope_perm()
    rows = hb * 512 + np.arange(512)
    d["bwq"] = np.ascontiguousarray(
        np.asarray(inp["L0_attn_b_Wq"], np.float32)[rows][perm].T)
    d["bwk"] = np.ascontiguousarray(
        np.asarray(inp["L0_attn_b_Wk"], np.float32)[rows][perm].T)
    d["bwv"] = np.ascontiguousarray(
        np.asarray(inp["L0_attn_b_Wv"], np.float32)[rows].T)
    d["bwo"] = np.ascontiguousarray(
        np.asarray(inp["L0_attn_b_Wo"], np.float32)[:, rows].T)
    ch, cl, sh, sl = _rope_tables()
    d["rch"], d["rcl"], d["rsh"], d["rsl"] = ch, cl, sh, sl
    d["ident"] = np.eye(128, dtype=np.float32)
    st = np.arange(128)
    d["trimask"] = (st[:, None] <= st[None, :]).astype(np.float32)
    dm = np.zeros((4, 128, 512), dtype=np.float32)
    for r in range(4):
        dm[r] = (np.arange(512)[None, :] <= (r * 128 + st)[:, None])
    d["diagmask"] = np.ascontiguousarray(dm.reshape(512, 512))
    d["dmneg"] = np.ascontiguousarray(((dm - 1.0) * 1e30).reshape(512, 512))
    return d



class KB:
    def __init__(self, nc, es):
        self.nc = nc
        self.t, self.v, self.s, self.g, self.y = \
            nc.tensor, nc.vector, nc.scalar, nc.gpsimd, nc.sync
        self.engines = [self.t, self.v, self.s, self.g, self.y]
        self._sems, self._cnt = {}, {}
        for nm in ("bar", "dS", "dG", "dC", "dV", "cc",
                   "pe", "ve", "se", "ge"):
            self._sems[nm] = es.enter_context(nc.semaphore(nm))
            self._cnt[nm] = 0
        self._qmap = {id(self.y): "dS", id(self.g): "dG",
                      id(self.s): "dC", id(self.v): "dV"}

    def sem(self, nm):
        return self._sems[nm]

    def inc(self, instr, nm, k=1):
        if nm in ("pe", "ve", "se", "ge"):
            eng = {"pe": self.t, "ve": self.v, "se": self.s,
                   "ge": self.g}[nm]
            eng.drain().then_inc(self._sems[nm], k)
        else:
            instr.then_inc(self._sems[nm], k)
        self._cnt[nm] += k
        return instr

    def dma(self, eng, out, in_, **kw):
        return self.inc(eng.dma_start(out=out, in_=in_, **kw),
                        self._qmap[id(eng)], 16)

    def wait_at(self, eng, nm, val):
        if val:
            eng.wait_ge(self._sems[nm], val)

    def wait(self, eng, *names):
        for nm in names:
            if self._cnt[nm]:
                eng.wait_ge(self._sems[nm], self._cnt[nm])

    def bar(self, dmas=True):
        b = self._sems["bar"]
        for e in self.engines:
            e.drain().then_inc(b, 1)
        self._cnt["bar"] += len(self.engines)
        tgt = self._cnt["bar"]
        for e in self.engines:
            e.wait_ge(b, tgt)
            if dmas:
                self.wait(e, "dS", "dG", "dC", "dV")


class Arena:
    def __init__(self, tens, ncols):
        self.t = tens
        self.n = ncols
        self.off = 0
        self.marks = []

    def take(self, cols):
        assert self.off + cols <= self.n, (self.off, cols, self.n)
        ap = self.t[:, self.off:self.off + cols]
        self.off += cols
        return ap

    def push(self):
        self.marks.append(self.off)

    def pop(self):
        self.off = self.marks.pop()


def emit_ln(k, src, dst, wsp, sc, affine=None, n_free=D):
    v, s = k.v, k.s
    eps = sc[:, 0:1]
    nmu = sc[:, 4:5]; ssq = sc[:, 5:6]; rst = sc[:, 6:7]
    inv = 1.0 / n_free
    k.wait(v, "se")
    v.reduce_sum(nmu, src, axis=AX.X)
    v.drain()
    k.inc(v.tensor_scalar_mul(nmu, nmu, -inv), "ve")
    k.wait(s, "ve")
    s.activation(out=wsp, in_=src, func=AF.Identity, bias=nmu)
    s.activation(out=dst, in_=wsp, func=AF.Square, accum_out=ssq)
    s.drain()
    s.activation(out=ssq, in_=ssq, func=AF.Identity, scale=inv, bias=eps)
    k.inc(s.activation(out=ssq, in_=ssq, func=AF.Sqrt), "se")
    k.wait(v, "se")
    k.inc(v.reciprocal(rst, ssq), "ve")
    if affine is None:
        k.inc(v.tensor_scalar_mul(dst, wsp, rst), "ve")
    else:
        w_ap, b_ap = affine
        k.inc(v.scalar_tensor_tensor(dst, wsp, rst, w_ap, ALU.mult, ALU.mult),
              "ve")
        if b_ap is not None:
            k.inc(v.tensor_tensor(dst, dst, b_ap, ALU.add), "ve")



def build_program(sublayers=None):
    sublayers = SUBLAYERS if sublayers is None else sublayers
    nc = bass.Bass("TRN2", target_bir_lowering=False, debug=False,
                   detect_race_conditions=False)
    dram = {}

    def din(name, shape, dt=F32):
        dram[name] = nc.dram_tensor(name, shape, dt, kind="ExternalInput").ap()

    def dint(name, shape, dt=F32):
        dram[name] = nc.dram_tensor(name, shape, dt).ap()

    for nm, sh in (("x0", [TOK, D]),
                   ("wq", [D, 512]), ("wk", [D, 512]), ("wv", [D, 512]),
                   ("wg", [D, 512]), ("wgk1", [D, 16]), ("wgk2", [16, 512]),
                   ("bgk2", [128, 4]), ("rmsw", [128, 256]), ("wo", [512, D]),
                   ("ln1w", [128, D]), ("ln1b", [128, D]),
                   ("ln2w", [128, D]), ("ln2b", [128, D]),
                   ("w1", [D, 3072]), ("b1c", [128, 24]),
                   ("w2", [3072, D]), ("b2", [128, D]),
                   ("mwv", [NE * 128, 8 * FSL + 4]), ("mw", [NE * 128, 8 * FSL + 4]),
                   ("mw1", [NE * 128, 4 * D]), ("ms1g", [NE * 128, D]),
                   ("tokid", [128, NT]), ("pairrow", [128, 2 * NT]),
                   ("jrow", [1, NCH]), ("ltstrict", [128, 128]),
                   ("iota8", [128, 8]),
                   ("msv", [128, NE * 4]), ("ms", [128, NE * 4]),
                   ("ms1", [128, NE * D]), ("rw", [D, NE]),
                   ("bwq", [D, 512]), ("bwk", [D, 512]), ("bwv", [D, 512]),
                   ("bwo", [512, D]),
                   ("rch", [128, T]), ("rcl", [128, T]),
                   ("rsh", [128, T]), ("rsl", [128, T]),
                   ("ident", [128, 128]), ("trimask", [128, 128]),
                   ("diagmask", [512, 512]), ("dmneg", [512, 512])):
        din(nm, sh)
    yout = nc.dram_tensor("y", [TOK, D], F32, kind="ExternalOutput").ap()

    dint("xh", [TOK, D])
    dint("xh2", [TOK, D])
    dint("xh3", [TOK, D])
    dint("uh", [TOK, D])
    dint("v1h", [TOK, D])
    dint("ag_u_i", [TOK, D])
    dint("ag_u_o", [NCORE * TOK + 128, D])
    dint("ag_w_i", [TOK, NE])
    dint("ag_w_o", [NCORE * TOK, NE])
    dint("srt4", [NSLOT, 4])
    dint("cnt_i", [1, NE])
    dint("ag4", [NCORE * NSLOT, 4])
    dint("ag_cnt", [NCORE, NE])
    dint("rs_i", [2 * NCORE * TOK, D])
    dint("rs_o", [2 * TOK, D])
    dint("a2a_i", [NCORE, 4, 128, D])
    dint("a2a_o", [NCORE, 4, 128, D])
    dint("p2a_i", [NCORE, 4, 128, D])
    dint("p2a_o", [NCORE, 4, 128, D])

    es = ExitStack()
    with es:
        k = KB(nc, es)
        t, v, s, g, y = k.t, k.v, k.s, k.g, k.y

        cst = es.enter_context(nc.sbuf_tensor("cst", [128, 6144], F32))
        bigt = es.enter_context(nc.sbuf_tensor("big", [128, 45056], F32))
        A = Arena(bigt, 45056)
        pss = [es.enter_context(nc.psum_tensor("ps%d" % i, [128, 512], F32))
               for i in range(8)]

        ln1w = cst[:, 0:D]; ln1b = cst[:, D:2 * D]
        ln2w = cst[:, 2 * D:3 * D]; ln2b = cst[:, 3 * D:4 * D]
        rmsw = cst[:, 4 * D:4 * D + 256]
        ident = cst[:, 4352:4480]
        tri = cst[:, 4480:4608]
        zc = cst[:, 4608:4736]
        bgk2 = cst[:, 4736:4740]
        sc = cst[:, 4740:4804]
        b1c = cst[:, 4804:4828]
        ln_aff1 = (ln1w, ln1b)
        ln_aff2 = (ln2w, ln2b)

        for dst, srcn in ((ln1w, "ln1w"), (ln1b, "ln1b"), (ln2w, "ln2w"),
                          (ln2b, "ln2b"), (rmsw, "rmsw"), (ident, "ident"),
                          (tri, "trimask"), (bgk2, "bgk2"), (b1c, "b1c")):
            k.dma(y, dst, dram[srcn][:, :])
        g.memset(zc, 0.0)
        g.memset(sc[:, 2:64], 0.0)
        g.memset(sc[:, 0:1], EPS)
        g.memset(sc[:, 1:2], 1.0 / 16.0)
        g.memset(sc[:, 2:3], 1.0)
        k.bar()

        ctx = dict(k=k, nc=nc, dram=dram, A=A, pss=pss, sc=sc, ident=ident,
                   tri=tri, zc=zc, rmsw=rmsw, bgk2=bgk2, b1c=b1c,
                   ln_aff1=ln_aff1, ln_aff2=ln_aff2)

        xin = dram["x0"]
        if 1 in sublayers:
            emit_attn_a(ctx, xin, dram["xh"])
            xin = dram["xh"]
            k.bar()
        if 2 in sublayers:
            emit_moe(ctx, xin, dram["xh2"])
            xin = dram["xh2"]
            k.bar()
        if 3 in sublayers:
            emit_attn_b(ctx, xin, dram["xh3"])
            xin = dram["xh3"]
            k.bar()
        if 4 in sublayers:
            emit_moe(ctx, xin, dram["xh"])
            xin = dram["xh"]
            k.bar()

        A.push()
        tb = A.take(D)
        for i in range(NT):
            k.dma(y, tb, xin[i * 128:(i + 1) * 128, :])
            k.wait(y, "dS")
            k.dma(y, yout[i * 128:(i + 1) * 128, :], tb)
            k.wait(y, "dS")
        for nm in DEBUG_DUMP:
            src_ap = dram[nm]
            flat = int(np.prod(src_ap.shape))
            dbg = nc.dram_tensor("dbg_" + nm, [flat // 1024, 1024], F32,
                                 kind="ExternalOutput").ap()
            s2 = src_ap.rearrange(
                " ".join("abcd"[:len(src_ap.shape)]) + " -> (" +
                " ".join("abcd"[:len(src_ap.shape)]) + ")"
            ).rearrange("(r c) -> r c", c=1024)
            for i in range(flat // 1024 // 128):
                k.dma(y, tb, s2[i * 128:(i + 1) * 128, :])
                k.wait(y, "dS")
                k.dma(y, dbg[i * 128:(i + 1) * 128, :], tb)
                k.wait(y, "dS")
        A.pop()
        k.bar()
    return nc



def emit_attn_a(ctx, xin, xh):
    k = ctx["k"]; dram = ctx["dram"]; A = ctx["A"]; pss = ctx["pss"]
    sc = ctx["sc"]; ident = ctx["ident"]
    t, v, s, g, y = k.t, k.v, k.s, k.g, k.y
    uh, v1h = dram["uh"], dram["v1h"]
    a2a_i, a2a_o = dram["a2a_i"], dram["a2a_o"]
    p2a_i, p2a_o = dram["p2a_i"], dram["p2a_o"]

    A.push()
    xt = A.take(D); ut = A.take(D); at = A.take(D); wsp = A.take(D)
    aT = A.take(D)
    for i in range(NT):
        sq, sub = i // 4, i % 4
        k.dma(y, xt, xin[i * 128:(i + 1) * 128, :])
        k.wait(v, "dS"); k.wait(s, "dS")
        emit_ln(k, xt, ut, wsp, sc, affine=None)
        emit_ln(k, ut, at, wsp, sc, affine=ctx["ln_aff1"])
        k.wait(y, "ve")
        k.dma(y, uh[i * 128:(i + 1) * 128, :], ut)
        for dch in range(DC):
            pb = pss[dch % 2]
            k.wait(t, "ve")
            k.inc(t.transpose(pb[:, 0:128], at[:, dch * 128:(dch + 1) * 128],
                              ident), "pe")
            k.wait(v, "pe")
            k.inc(v.tensor_copy(aT[:, dch * 128:(dch + 1) * 128], pb[:, 0:128]),
                  "ve")
        k.wait(y, "ve")
        for r in (2 * sq, 2 * sq + 1):
            k.dma(y, a2a_i[r, sub, :, :], aT)
    A.pop()
    k.bar()
    if STAGE < 2:
        return

    with allow_ici_all_to_all():
        k.inc(g.collective_compute(
            "AllToAll", ALU.bypass, replica_groups=[list(range(NCORE))],
            ins=[a2a_i.opt()], outs=[a2a_o.opt()]), "cc")
    g.wait_ge(k.sem("cc"), k._cnt["cc"])
    k.bar(dmas=False)
    if STAGE < 3:
        return

    emit_gla(ctx)
    k.bar()
    if STAGE < 4:
        return

    with allow_ici_all_to_all():
        k.inc(g.collective_compute(
            "AllToAll", ALU.bypass, replica_groups=[list(range(NCORE))],
            ins=[p2a_i.opt()], outs=[p2a_o.opt()]), "cc")
    g.wait_ge(k.sem("cc"), k._cnt["cc"])
    k.bar(dmas=False)

    A.push()
    pa = A.take(D); pb_ = A.take(D); utl = A.take(D); vt = A.take(D)
    for i in range(NT):
        sq, sub = i // 4, i % 4
        k.dma(y, pa, p2a_o[2 * sq, sub, :, :])
        k.dma(y, pb_, p2a_o[2 * sq + 1, sub, :, :])
        k.dma(y, utl, uh[i * 128:(i + 1) * 128, :])
        k.wait(v, "dS")
        k.inc(v.tensor_tensor(pa, pa, pb_, ALU.add), "ve")
        for dch in range(DC):
            pb = pss[dch % 2]
            k.wait(t, "ve")
            k.inc(t.transpose(pb[:, 0:128], pa[:, dch * 128:(dch + 1) * 128],
                              ident), "pe")
            k.wait(v, "pe")
            k.inc(v.tensor_tensor(vt[:, dch * 128:(dch + 1) * 128],
                                  utl[:, dch * 128:(dch + 1) * 128],
                                  pb[:, 0:128], ALU.add), "ve")
        k.wait(y, "ve")
        k.dma(y, v1h[i * 128:(i + 1) * 128, :], vt)
    A.pop()
    k.bar()
    if STAGE < 5:
        return

    emit_ffn_a(ctx, xin, xh)


def emit_gla(ctx):
    k = ctx["k"]; dram = ctx["dram"]; A = ctx["A"]; pss = ctx["pss"]
    sc = ctx["sc"]; ident = ctx["ident"]; tri = ctx["tri"]; zc = ctx["zc"]
    rmsw = ctx["rmsw"]; bgk2 = ctx["bgk2"]
    t, v, s, g, y = k.t, k.v, k.s, k.g, k.y
    a2a_o, p2a_i = dram["a2a_o"], dram["p2a_i"]
    one16 = sc[:, 1:2]

    A.push()
    wq = A.take(8 * 512); wk = A.take(8 * 512)
    wvw = A.take(8 * 512); wgw = A.take(8 * 512)
    wo = A.take(4 * D)
    wgk1 = A.take(8 * 16)
    wgk2 = A.take(512)
    for dch in range(DC):
        k.dma(y, wq[:, dch * 512:(dch + 1) * 512],
              dram["wq"][dch * 128:(dch + 1) * 128, :])
        k.dma(y, wk[:, dch * 512:(dch + 1) * 512],
              dram["wk"][dch * 128:(dch + 1) * 128, :])
        k.dma(y, wvw[:, dch * 512:(dch + 1) * 512],
              dram["wv"][dch * 128:(dch + 1) * 128, :])
        k.dma(y, wgw[:, dch * 512:(dch + 1) * 512],
              dram["wg"][dch * 128:(dch + 1) * 128, :])
        k.dma(y, wgk1[:, dch * 16:(dch + 1) * 16],
              dram["wgk1"][dch * 128:(dch + 1) * 128, :])
    for kc in range(4):
        k.dma(y, wo[:, kc * D:(kc + 1) * D],
              dram["wo"][kc * 128:(kc + 1) * 128, :])
    k.dma(y, wgk2[0:16, :], dram["wgk2"][:, :])

    xc = A.take(4 * D)
    qT = A.take(4 * 512); kT = A.take(4 * 512)
    lsT = A.take(4 * 512)
    vts = A.take(4 * 512); gts = A.take(4 * 512)
    hst = A.take(4 * 256)
    u16 = A.take(512)
    wrk = A.take(2048)
    out_t = A.take(D)
    gsc = A.take(16)

    g.memset(hst[:, :], 0.0)
    k.bar()
    if STAGE < 3.05:
        A.pop()
        return

    for sup in range(T // 512):
        for sb_ in range(4):
            k.dma(y, xc[:, sb_ * D:(sb_ + 1) * D], a2a_o[sup, sb_, :, :])
        k.bar(dmas=True)
        if STAGE < 3.08:
            continue

        for dt_ in range(4):
            pq, pk = pss[0], pss[1]
            for ww, pp in ((wq, pq), (wk, pk)):
                for cc2 in range(4):
                    for kc in range(DC):
                        l_ = ww[:, kc * 512 + dt_ * 128: kc * 512 + (dt_ + 1) * 128]
                        r_ = xc[:, cc2 * D + kc * 128: cc2 * D + (kc + 1) * 128]
                        t.matmul(pp[:, cc2 * 128:(cc2 + 1) * 128], l_, r_,
                                 start=(kc == 0), stop=(kc == DC - 1))
            k.inc(t.drain(), "pe")
            k.wait(v, "pe")
            k.inc(v.tensor_copy(qT[:, dt_ * 512:(dt_ + 1) * 512], pq[:, :]), "ve")
            k.inc(v.tensor_copy(kT[:, dt_ * 512:(dt_ + 1) * 512], pk[:, :]), "ve")
            k.wait(t, "ve")
        if STAGE < 3.12:
            continue
        for cc2 in range(4):
            pv_, pg = pss[2], pss[3]
            for ww, pp in ((wvw, pv_), (wgw, pg)):
                for kc in range(DC):
                    l_ = xc[:, cc2 * D + kc * 128: cc2 * D + (kc + 1) * 128]
                    t.matmul(pp[:, :], l_, ww[:, kc * 512:(kc + 1) * 512],
                             start=(kc == 0), stop=(kc == DC - 1))
            k.inc(t.drain(), "pe")
            k.wait(v, "pe")
            k.inc(v.tensor_copy(vts[:, cc2 * 512:(cc2 + 1) * 512], pv_[:, :]), "ve")
            k.inc(v.tensor_copy(gts[:, cc2 * 512:(cc2 + 1) * 512], pg[:, :]), "ve")
            k.wait(t, "ve")
        if STAGE < 3.2:
            continue
        pu = pss[4]
        for cc2 in range(4):
            for kc in range(DC):
                t.matmul(pu[0:16, cc2 * 128:(cc2 + 1) * 128],
                         wgk1[:, kc * 16:(kc + 1) * 16],
                         xc[:, cc2 * D + kc * 128: cc2 * D + (kc + 1) * 128],
                         start=(kc == 0), stop=(kc == DC - 1))
        k.inc(t.drain(), "pe")
        k.wait(v, "pe")
        k.inc(v.tensor_copy(u16[0:16, :], pu[0:16, :]), "ve")
        k.wait(t, "ve")
        for dt_ in range(4):
            pz = pss[5]
            t.matmul(pz[:, :], wgk2[0:16, dt_ * 128:(dt_ + 1) * 128],
                     u16[0:16, :], start=True, stop=True)
            k.inc(t.drain(), "pe")
            k.wait(s, "pe")
            s.activation(out=wrk[:, 0:512], in_=pz[:, :], func=AF.Sigmoid,
                         bias=bgk2[:, dt_:dt_ + 1])
            k.inc(s.activation(out=lsT[:, dt_ * 512:(dt_ + 1) * 512],
                               in_=wrk[:, 0:512], func=AF.Ln), "se")
            k.wait(t, "se")
        k.bar(dmas=False)
        if STAGE < 3.3:
            continue

        for cc2 in range(4):
            pw = [pss[6], pss[7]]
            for h in range(2):
                dts = (2 * h, 2 * h + 1)
                Gt = [wrk[:, 0:128], wrk[:, 128:256]]
                E = [wrk[:, 256:384], wrk[:, 384:512]]
                En = [wrk[:, 512:640], wrk[:, 640:768]]
                Er = [wrk[:, 768:896], wrk[:, 896:1024]]
                qd = [wrk[:, 1024:1152], wrk[:, 1152:1280]]
                kda = [wrk[:, 1280:1408], wrk[:, 1408:1536]]
                kdh = [wrk[:, 1536:1664], wrk[:, 1664:1792]]
                am = wrk[:, 1792:1920]
                kdhs = [wrk[:, 1920:2048], None]
                ge16 = [gsc[:, 0:1], gsc[:, 1:2]]
                edec = [gsc[:, 2:3], gsc[:, 3:4]]
                ssq2 = gsc[:, 4:5]; rstd = gsc[:, 5:6]

                for j, dt_ in enumerate(dts):
                    lsl = lsT[:, dt_ * 512 + cc2 * 128: dt_ * 512 + (cc2 + 1) * 128]
                    k.wait(v, "se")
                    k.inc(v.tensor_tensor_scan(Gt[j], lsl, zc, 0.0,
                                               ALU.add, ALU.add), "ve")
                    k.wait(s, "ve")
                    s.activation(out=ge16[j], in_=Gt[j][:, 127:128],
                                 func=AF.Identity, scale=1.0 / 16.0)
                    s.activation(out=E[j], in_=Gt[j], func=AF.Exp,
                                 scale=1.0 / 16.0)
                    s.activation(out=En[j], in_=Gt[j], func=AF.Exp,
                                 scale=-1.0 / 16.0)
                    s.activation(out=Er[j], in_=Gt[j], func=AF.Exp,
                                 scale=-1.0 / 16.0, bias=ge16[j])
                    k.inc(s.activation(out=edec[j], in_=Gt[j][:, 127:128],
                                       func=AF.Exp, scale=1.0 / 16.0), "se")
                    k.wait(v, "se")
                    qsl = qT[:, dt_ * 512 + cc2 * 128: dt_ * 512 + (cc2 + 1) * 128]
                    ksl = kT[:, dt_ * 512 + cc2 * 128: dt_ * 512 + (cc2 + 1) * 128]
                    k.inc(v.scalar_tensor_tensor(qd[j], qsl, one16, E[j],
                                                 ALU.mult, ALU.mult), "ve")
                    k.inc(v.tensor_tensor(kda[j], ksl, En[j], ALU.mult), "ve")
                    k.inc(v.tensor_tensor(kdh[j], ksl, Er[j], ALU.mult), "ve")
                kdhs[1] = Gt[0]
                k.wait(t, "ve")

                pA = pss[0]
                t.matmul(pA[:, 0:128], kda[0], qd[0], start=True, stop=False)
                k.inc(t.matmul(pA[:, 0:128], kda[1], qd[1],
                               start=False, stop=True), "pe")
                pT = pss[1]
                k.inc(t.transpose(pT[:, 0:128], kdh[0], ident), "pe")
                k.inc(t.transpose(pT[:, 128:256], kdh[1], ident), "pe")
                k.wait(v, "pe")
                k.inc(v.tensor_tensor(am, pA[:, 0:128], tri, ALU.mult), "ve")
                k.inc(v.tensor_copy(kdhs[0], pT[:, 0:128]), "ve")
                k.inc(v.tensor_copy(kdhs[1], pT[:, 128:256]), "ve")
                k.wait(t, "ve")

                po = pss[2]
                vsl = vts[:, cc2 * 512 + h * 256: cc2 * 512 + (h + 1) * 256]
                t.matmul(po[:, 0:256], qd[0],
                         hst[:, (2 * h) * 256:(2 * h + 1) * 256],
                         start=True, stop=False)
                t.matmul(po[:, 0:256], qd[1],
                         hst[:, (2 * h + 1) * 256:(2 * h + 2) * 256],
                         start=False, stop=False)
                t.matmul(po[:, 0:256], am, vsl, start=False, stop=True)
                ph = pss[3]
                t.matmul(ph[:, 0:256], kdhs[0], vsl, start=True, stop=True)
                k.inc(t.matmul(ph[:, 256:512], kdhs[1], vsl,
                               start=True, stop=True), "pe")
                k.wait(v, "pe"); k.wait(s, "pe")
                for j in range(2):
                    k.inc(v.scalar_tensor_tensor(
                        hst[:, (2 * h + j) * 256:(2 * h + j + 1) * 256],
                        hst[:, (2 * h + j) * 256:(2 * h + j + 1) * 256],
                        edec[j], ph[:, j * 256:(j + 1) * 256],
                        ALU.mult, ALU.add), "ve")
                o2 = wrk[:, 256:512]
                k.inc(s.activation(out=o2, in_=po[:, 0:256], func=AF.Square,
                                   accum_out=ssq2), "se")
                s.drain()
                s.activation(out=ssq2, in_=ssq2, func=AF.Identity,
                             scale=1.0 / 256, bias=sc[:, 0:1])
                k.inc(s.activation(out=ssq2, in_=ssq2, func=AF.Sqrt), "se")
                k.wait(v, "se")
                k.inc(v.reciprocal(rstd, ssq2), "ve")
                gsl = gts[:, cc2 * 512 + h * 256: cc2 * 512 + (h + 1) * 256]
                sg = wrk[:, 512:768]
                k.wait(s, "ve")
                k.inc(s.activation(out=sg, in_=gsl, func=AF.Silu), "se")
                k.wait(v, "se")
                t1 = wrk[:, 768:1024]
                k.inc(v.scalar_tensor_tensor(t1, po[:, 0:256], rstd, rmsw,
                                             ALU.mult, ALU.mult), "ve")
                go = wrk[:, 1024:1280]
                k.inc(v.tensor_tensor(go, t1, sg, ALU.mult), "ve")
                k.wait(t, "ve")
                pgt = pss[4]
                k.inc(t.transpose(pgt[:, 0:128], go[:, 0:128], ident), "pe")
                k.inc(t.transpose(pgt[:, 128:256], go[:, 128:256], ident), "pe")
                k.wait(v, "pe")
                goT0 = wrk[:, 1280:1408]; goT1 = wrk[:, 1408:1536]
                k.inc(v.tensor_copy(goT0, pgt[:, 0:128]), "ve")
                k.inc(v.tensor_copy(goT1, pgt[:, 128:256]), "ve")
                k.wait(t, "ve")
                kc0 = 2 * h
                for dti in range(DC):
                    pwb = pw[dti // 4][:, (dti % 4) * 128:((dti % 4) + 1) * 128]
                    wl0 = wo[:, kc0 * D + dti * 128: kc0 * D + (dti + 1) * 128]
                    wl1 = wo[:, (kc0 + 1) * D + dti * 128:
                             (kc0 + 1) * D + (dti + 1) * 128]
                    t.matmul(pwb, wl0, goT0, start=True, stop=False)
                    t.matmul(pwb, wl1, goT1, start=False, stop=True)
                k.inc(t.drain(), "pe")
                k.wait(v, "pe")
                for dti in range(DC):
                    pwb = pw[dti // 4][:, (dti % 4) * 128:((dti % 4) + 1) * 128]
                    ot = out_t[:, dti * 128:(dti + 1) * 128]
                    if h == 0:
                        k.inc(v.tensor_copy(ot, pwb), "ve")
                    else:
                        k.inc(v.tensor_tensor(ot, ot, pwb, ALU.add), "ve")
                k.wait(t, "ve")
            k.wait(y, "ve")
            k.dma(y, p2a_i[sup, cc2, :, :], out_t)
            k.wait(t, "ve")
        k.bar(dmas=False)
    A.pop()


def emit_ffn_a(ctx, xin, xh):
    k = ctx["k"]; dram = ctx["dram"]; A = ctx["A"]; pss = ctx["pss"]
    sc = ctx["sc"]; ident = ctx["ident"]; b1c = ctx["b1c"]
    t, v, s, g, y = k.t, k.v, k.s, k.g, k.y
    v1h = dram["v1h"]
    A.push()
    v1c = A.take(4 * D)
    bT = A.take(4 * D)
    fT = A.take(24 * 512)
    w1b = A.take(2 * 8 * 128)
    w2b = A.take(2 * D)
    b2t = A.take(D)
    wsp = A.take(D)
    k.dma(y, b2t, dram["b2"][:, :])

    for sup in range(TOK // 512):
        for sb_ in range(4):
            i = sup * 4 + sb_
            k.dma(y, v1c[:, sb_ * D:(sb_ + 1) * D],
                  v1h[i * 128:(i + 1) * 128, :])
        k.bar(dmas=True)
        for sb_ in range(4):
            emit_ln(k, v1c[:, sb_ * D:(sb_ + 1) * D], wsp,
                    fT[:, 0:D], sc, affine=ctx["ln_aff2"])
            for dch in range(DC):
                pb = pss[dch % 2]
                k.wait(t, "ve")
                k.inc(t.transpose(pb[:, 0:128],
                                  wsp[:, dch * 128:(dch + 1) * 128], ident),
                      "pe")
                k.wait(v, "pe")
                k.inc(v.tensor_copy(
                    bT[:, dch * 512 + sb_ * 128: dch * 512 + (sb_ + 1) * 128],
                    pb[:, 0:128]), "ve")
            k.wait(t, "ve"); k.wait(s, "ve")
        for fc in range(24):
            buf = w1b[:, (fc % 2) * 1024:(fc % 2) * 1024 + 1024]
            k.wait(y, "pe")
            for dch in range(DC):
                k.dma(y, buf[:, dch * 128:(dch + 1) * 128],
                      dram["w1"][dch * 128:(dch + 1) * 128,
                                 fc * 128:(fc + 1) * 128])
            k.wait(t, "dS")
            pf = pss[2 + (fc % 2)]
            for dch in range(DC):
                t.matmul(pf[:, :], buf[:, dch * 128:(dch + 1) * 128],
                         bT[:, dch * 512:(dch + 1) * 512],
                         start=(dch == 0), stop=(dch == DC - 1))
            k.inc(t.drain(), "pe")
            k.wait(s, "pe"); k.wait(v, "pe")
            hb = fT[:, fc * 512:(fc + 1) * 512]
            k.inc(s.activation(out=hb, in_=pf[:, :], func=AF.Identity,
                               bias=b1c[:, fc:fc + 1]), "se")
            k.wait(v, "se")
            k.inc(v.tensor_scalar_mul(wsp[:, 0:512], hb, 0.01), "ve")
            k.inc(v.tensor_tensor(hb, hb, wsp[:, 0:512], ALU.max), "ve")
            k.wait(s, "ve")
            k.inc(s.activation(out=hb, in_=hb, func=AF.Square), "se")
            k.wait(t, "se")
        for sb_ in range(4):
            i = sup * 4 + sb_
            p0, p1 = pss[4], pss[5]
            for half, pp in ((0, p0), (1, p1)):
                for kf in range(24):
                    buf = w2b[:, (kf % 2) * 512:(kf % 2) * 512 + 512]
                    k.wait(y, "pe")
                    k.dma(y, buf, dram["w2"][kf * 128:(kf + 1) * 128,
                                             half * 512:(half + 1) * 512])
                    k.wait(t, "dS")
                    l_ = fT[:, kf * 512 + sb_ * 128: kf * 512 + (sb_ + 1) * 128]
                    k.inc(t.matmul(pp[:, :], l_, buf,
                                   start=(kf == 0), stop=(kf == 23)), "pe")
            k.inc(t.drain(), "pe")
            k.dma(y, wsp, xin[i * 128:(i + 1) * 128, :])
            k.wait(v, "pe"); k.wait(v, "dS")
            k.inc(v.tensor_tensor(wsp[:, 0:512], wsp[:, 0:512], p0[:, :],
                                  ALU.add), "ve")
            k.inc(v.tensor_tensor(wsp[:, 512:1024], wsp[:, 512:1024], p1[:, :],
                                  ALU.add), "ve")
            k.inc(v.tensor_tensor(wsp, wsp, b2t, ALU.add), "ve")
            k.inc(v.tensor_tensor(wsp, wsp, v1c[:, sb_ * D:(sb_ + 1) * D],
                                  ALU.add), "ve")
            k.wait(y, "ve")
            k.dma(y, xh[i * 128:(i + 1) * 128, :], wsp)
            k.wait(t, "ve")
    A.pop()








def emit_moe_sparse(ctx, xin, xout):
    k = ctx["k"]; dram = ctx["dram"]; A = ctx["A"]; pss = ctx["pss"]
    sc = ctx["sc"]; ident = ctx["ident"]
    t, v, s, g, y = k.t, k.v, k.s, k.g, k.y
    WR = 8 * FSL + 4

    A.push()
    for j in range(DC):
        k.dma(y, dram["ag_u_o"][NCORE * TOK:NCORE * TOK + 128,
                                j * 128:(j + 1) * 128], ctx["zc"])
    NJ = NCORE * NCH
    meta = A.take(NJ * 4)
    tokf = A.take(NJ); prf = A.take(NJ); gf_ = A.take(NJ)
    toku = A.take(NJ); pru = A.take(NJ)
    widxf = A.take(NJ); widxu = A.take(NJ)
    jrow = A.take(NCH)
    cnr = A.take(8); padr = A.take(8); cumx = A.take(8); tmp8 = A.take(8)
    tmp40 = A.take(NCH)
    ech = A.take(NCH); iotu = A.take(1); iot = A.take(1); onesr = A.take(128)
    k.dma(y, meta, dram["ag4"].rearrange("(a p) f -> p a f", p=128))
    k.dma(y, jrow[0:1, :], dram["jrow"][:, :])
    g.iota(iotu.bitcast(U32), pattern=[[1, 1]], base=0, channel_multiplier=1)
    v.memset(onesr[0:1, :], 1.0)
    k.bar()
    mview = meta.rearrange("p (a f) -> p a f", f=4)
    v.tensor_copy(tokf, mview[:, :, 0])
    v.tensor_copy(prf, mview[:, :, 1])
    v.tensor_copy(gf_, mview[:, :, 2])
    v.drain()
    v.tensor_copy(iot, iotu.bitcast(U32))
    v.tensor_copy(toku.bitcast(U32), tokf)
    v.tensor_copy(pru.bitcast(U32), prf)
    k.inc(v.drain(), "ve")
    for r in range(NCORE):
        k.wait(y, "ve")
        k.dma(y, cnr[0:1, :], dram["ag_cnt"][r:r + 1, :])
        k.wait(v, "dS")
        v.tensor_scalar(padr[0:1, :], cnr[0:1, :], 63.5, None, ALU.add)
        v.drain()
        v.tensor_scalar(padr[0:1, :], padr[0:1, :], 1.0 / 128.0, None, ALU.mult)
        v.drain()
        v.tensor_scalar(padr[0:1, :], padr[0:1, :], 8388608.0, None, ALU.add)
        v.drain()
        v.tensor_scalar(padr[0:1, :], padr[0:1, :], 8388608.0, None,
                        ALU.subtract)
        v.drain()
        v.tensor_scalar(padr[0:1, :], padr[0:1, :], 128.0, None, ALU.mult)
        v.drain()
        v.tensor_tensor_scan(cumx[0:1, :], padr[0:1, :], padr[0:1, :],
                             0.0, ALU.add, ALU.bypass)
        v.drain()
        v.tensor_tensor(cumx[0:1, :], cumx[0:1, :], padr[0:1, :], ALU.subtract)
        v.drain()
        v.memset(ech[0:1, :], 0.0)
        v.drain()
        for e in range(NE):
            v.tensor_scalar(tmp40[0:1, :], jrow[0:1, :],
                            cumx[0:1, e:e + 1], None, ALU.is_ge)
            v.drain()
            v.tensor_tensor(ech[0:1, :], ech[0:1, :], tmp40[0:1, :],
                            ALU.add)
            v.drain()
        v.tensor_scalar(ech[0:1, :], ech[0:1, :], -1.0, None, ALU.add)
        v.drain()
        k.inc(v.tensor_scalar(ech[0:1, :], ech[0:1, :], 128.0, None,
                              ALU.mult), "ve")
        k.wait(t, "ve")
        pm = pss[6]
        t.matmul(pm[:, 0:NCH], onesr[0:1, :], ech[0:1, :],
                 start=True, stop=True)
        k.inc(t.drain(), "pe")
        k.wait(v, "pe")
        k.inc(v.tensor_scalar(widxf[:, r * NCH:(r + 1) * NCH], pm[:, 0:NCH],
                              iot[:, 0:1], None, ALU.add), "ve")
    v.tensor_copy(widxu.bitcast(U32), widxf)
    k.inc(v.drain(), "ve")
    k.bar()

    wvb = [A.take(WR), A.take(WR)]
    wb_ = [A.take(WR), A.take(WR)]
    w1b = [A.take(4 * D), A.take(4 * D)]
    s1b = [A.take(D), A.take(D)]
    xgb = [A.take(D), A.take(D)]
    xTb = [A.take(D), A.take(D)]
    vT = A.take(4 * 128); yT = A.take(4 * 128)
    ott = [A.take(D), A.take(D)]
    breg = g.alloc_register()
    g.reg_mov(breg, 2 * NCORE * TOK - 1)
    pe_par = [0, 0]
    ve_par = [0, 0]
    sc_par = [0, 0]
    for J in range(NCORE * NCH):
        par = J % 2
        k.wait_at(g, "pe", pe_par[par])
        k.wait_at(g, "ve", ve_par[par])
        k.inc(g.indirect_dma_start(
            out=xgb[par][:, :], out_offset=None, in_=dram["ag_u_o"][:, :],
            in_offset=bass.IndirectOffsetOnAxis(ap=toku[:, J:J + 1].bitcast(U32), axis=0)),
            "dG", 16)
        k.inc(g.indirect_dma_start(
            out=wvb[par][:, :], out_offset=None, in_=dram["mwv"][:, :],
            in_offset=bass.IndirectOffsetOnAxis(ap=widxu[:, J:J + 1].bitcast(U32), axis=0)),
            "dG", 16)
        k.inc(g.indirect_dma_start(
            out=wb_[par][:, :], out_offset=None, in_=dram["mw"][:, :],
            in_offset=bass.IndirectOffsetOnAxis(ap=widxu[:, J:J + 1].bitcast(U32), axis=0)),
            "dG", 16)
        k.inc(g.indirect_dma_start(
            out=w1b[par][:, :], out_offset=None, in_=dram["mw1"][:, :],
            in_offset=bass.IndirectOffsetOnAxis(ap=widxu[:, J:J + 1].bitcast(U32), axis=0)),
            "dG", 16)
        k.inc(g.indirect_dma_start(
            out=s1b[par][:, :], out_offset=None, in_=dram["ms1g"][:, :],
            in_offset=bass.IndirectOffsetOnAxis(ap=widxu[:, J:J + 1].bitcast(U32), axis=0)),
            "dG", 16)
        k.wait_at(t, "dG", k._cnt["dG"])
        for dch in range(DC):
            pb = pss[6 + dch % 2]
            k.wait(t, "ve")
            k.inc(t.transpose(pb[:, 0:128], xgb[par][:, dch * 128:(dch + 1) * 128],
                              ident), "pe")
            k.wait(v, "pe")
            k.inc(v.tensor_copy(xTb[par][:, dch * 128:(dch + 1) * 128],
                                pb[:, 0:128]), "ve")
        k.wait(t, "ve")
        for src_w, dst in ((wvb[par], vT), (wb_[par], yT)):
            fn = AF.Identity if dst is vT else AF.Gelu_apprx_tanh
            for ft in range(4):
                pf = pss[ft % 2]
                for kc in range(DC):
                    t.matmul(pf[:, 0:128],
                             src_w[:, kc * 512 + ft * 128:
                                   kc * 512 + (ft + 1) * 128],
                             xTb[par][:, kc * 128:(kc + 1) * 128],
                             start=(kc == 0), stop=(kc == DC - 1))
                k.inc(t.drain(), "pe")
                k.wait(s, "pe")
                k.inc(s.activation(out=dst[:, ft * 128:(ft + 1) * 128],
                                   in_=pf[:, 0:128], func=fn,
                                   scale=src_w[:, 8 * FSL + ft:8 * FSL + ft + 1]),
                      "se")
                k.wait(t, "se")
        k.wait(v, "se")
        for ft in range(4):
            k.inc(v.tensor_tensor(vT[:, ft * 128:(ft + 1) * 128],
                                  vT[:, ft * 128:(ft + 1) * 128],
                                  yT[:, ft * 128:(ft + 1) * 128],
                                  ALU.mult), "ve")
        k.wait(t, "ve")
        for half in range(2):
            ph_ = pss[2 + half]
            for kf in range(4):
                t.matmul(ph_[:, :],
                         vT[:, kf * 128:(kf + 1) * 128],
                         w1b[par][:, kf * D + half * 512:
                                  kf * D + half * 512 + 512],
                         start=(kf == 0), stop=(kf == 3))
        k.inc(t.drain(), "pe")
        pe_par[par] = k._cnt["pe"]
        k.wait(v, "pe")
        k.wait_at(v, "dG", sc_par[par])
        for half in range(2):
            ph_ = pss[2 + half]
            o_ = ott[par][:, half * 512:(half + 1) * 512]
            v.tensor_tensor(o_, ph_[:, :],
                            s1b[par][:, half * 512:(half + 1) * 512],
                            ALU.mult)
        k.inc(v.tensor_scalar(ott[par][:, :], ott[par][:, :],
                              gf_[:, J:J + 1], None, ALU.mult), "ve")
        ve_par[par] = k._cnt["ve"]
        k.wait(g, "ve")
        k.inc(g.indirect_dma_start(
            out=dram["rs_i"][:, :],
            out_offset=bass.IndirectOffsetOnAxis(ap=pru[:, J:J + 1].bitcast(U32), axis=0),
            in_=ott[par][:, :], in_offset=None,
            bounds_check=breg, oob_is_err=False), "dG", 16)
        sc_par[par] = k._cnt["dG"]
        k.wait(t, "ve")
    A.pop()
    k.bar()

    with allow_ici_all_to_all():
        k.inc(g.collective_compute(
            "ReduceScatter", ALU.add, replica_groups=[list(range(NCORE))],
            ins=[dram["rs_i"].opt()], outs=[dram["rs_o"].opt()]), "cc")
    g.wait_ge(k.sem("cc"), k._cnt["cc"])
    k.bar(dmas=False)

    A.push()
    xa = A.take(D); ra = A.take(D); rb = A.take(D)
    for i in range(NT):
        k.dma(y, xa, xin[i * 128:(i + 1) * 128, :])
        k.dma(y, ra, dram["rs_o"][i * 128:(i + 1) * 128, :])
        k.dma(y, rb, dram["rs_o"][TOK + i * 128:TOK + (i + 1) * 128, :])
        k.wait(v, "dS")
        v.tensor_tensor(xa, xa, ra, ALU.add)
        k.inc(v.tensor_tensor(xa, xa, rb, ALU.add), "ve")
        k.wait(y, "ve")
        k.dma(y, xout[i * 128:(i + 1) * 128, :], xa)
    A.pop()



def emit_attn_b(ctx, xin, xout):
    k = ctx["k"]; dram = ctx["dram"]; A = ctx["A"]; pss = ctx["pss"]
    sc = ctx["sc"]; ident = ctx["ident"]
    t, v, s, g, y = k.t, k.v, k.s, k.g, k.y
    a2a_i, a2a_o = dram["a2a_i"], dram["a2a_o"]
    p2a_i, p2a_o = dram["p2a_i"], dram["p2a_o"]

    A.push()
    xt = A.take(D); ut = A.take(D); aT = A.take(D)
    for i in range(NT):
        sq, sub = i // 4, i % 4
        k.dma(y, xt, xin[i * 128:(i + 1) * 128, :])
        k.wait(v, "dS"); k.wait(s, "dS")
        emit_ln(k, xt, ut, aT, sc, affine=None)
        for dch in range(DC):
            pb = pss[dch % 2]
            k.wait(t, "ve")
            k.inc(t.transpose(pb[:, 0:128], ut[:, dch * 128:(dch + 1) * 128],
                              ident), "pe")
            k.wait(v, "pe")
            k.inc(v.tensor_copy(aT[:, dch * 128:(dch + 1) * 128], pb[:, 0:128]),
                  "ve")
        k.wait(y, "ve")
        for r in (2 * sq, 2 * sq + 1):
            k.dma(y, a2a_i[r, sub, :, :], aT)
    A.pop()
    k.bar()

    with allow_ici_all_to_all():
        k.inc(g.collective_compute(
            "AllToAll", ALU.bypass, replica_groups=[list(range(NCORE))],
            ins=[a2a_i.opt()], outs=[a2a_o.opt()]), "cc")
    g.wait_ge(k.sem("cc"), k._cnt["cc"])
    k.bar(dmas=False)

    A.push()
    qh = A.take(2 * T)
    kh = A.take(2 * T)
    vt_ = A.take(8 * 1024)
    xc = A.take(4 * D)
    rt = A.take(4 * 512)
    wq_ = A.take(8 * 256); wk_ = A.take(8 * 256); wv_ = A.take(8 * 256)
    wo_ = A.take(2 * D)
    rp1 = A.take(512); rp2 = A.take(512)
    dneg = A.take(4 * 512)
    out_t = A.take(D); out_t2 = A.take(D)
    Pt = A.take(512)
    oacc = A.take(128)
    fsc = A.take(16)
    for r_ in range(4):
        k.dma(y, dneg[:, r_ * 512:(r_ + 1) * 512],
              dram["dmneg"][r_ * 128:(r_ + 1) * 128, :])
    k.bar()

    for wave in range(2):
        wc = wave * 256
        for dch in range(DC):
            k.dma(y, wq_[:, dch * 256:(dch + 1) * 256],
                  dram["bwq"][dch * 128:(dch + 1) * 128, wc:wc + 256])
            k.dma(y, wk_[:, dch * 256:(dch + 1) * 256],
                  dram["bwk"][dch * 128:(dch + 1) * 128, wc:wc + 256])
            k.dma(y, wv_[:, dch * 256:(dch + 1) * 256],
                  dram["bwv"][dch * 128:(dch + 1) * 128, wc:wc + 256])
        for kc in range(2):
            k.dma(y, wo_[:, kc * D:(kc + 1) * D],
                  dram["bwo"][wc + kc * 128: wc + (kc + 1) * 128, :])
        k.bar()

        for sup in range(T // 512):
            k.wait(y, "pe"); k.wait(y, "ve"); k.wait(y, "se")
            for sb_ in range(4):
                k.dma(y, xc[:, sb_ * D:(sb_ + 1) * D], a2a_o[sup, sb_, :, :])
            for tb in range(4):
                k.dma(y, rt[:, tb * 512:(tb + 1) * 512],
                      dram[("rch", "rcl", "rsh", "rsl")[tb]]
                      [:, sup * 512:(sup + 1) * 512])
            k.bar(dmas=True)
            CH = rt[:, 0 * 512:1 * 512]
            CL = rt[:, 1 * 512:2 * 512]
            SH = rt[:, 2 * 512:3 * 512]
            SL = rt[:, 3 * 512:4 * 512]
            for (ws, dst, isq) in ((wq_, qh, True), (wk_, kh, False)):
                px = [pss[0], pss[1]]
                for dt_ in (0, 1):
                    for cc2 in range(4):
                        for kc in range(DC):
                            l_ = ws[:, kc * 256 + dt_ * 128:
                                    kc * 256 + (dt_ + 1) * 128]
                            r_ = xc[:, cc2 * D + kc * 128:
                                    cc2 * D + (kc + 1) * 128]
                            t.matmul(px[dt_][:, cc2 * 128:(cc2 + 1) * 128],
                                     l_, r_, start=(kc == 0),
                                     stop=(kc == DC - 1))
                k.inc(t.drain(), "pe")
                k.wait(v, "pe")
                x1, x2 = px[0], px[1]
                k.inc(v.tensor_tensor(rp1, x1[:, :], CH, ALU.mult), "ve")
                v.tensor_tensor(rp2, x2[:, :], SH, ALU.mult)
                v.tensor_tensor(rp1, rp1, rp2, ALU.subtract)
                v.tensor_tensor(rp2, x1[:, :], CL, ALU.mult)
                v.tensor_tensor(rp1, rp1, rp2, ALU.add)
                v.tensor_tensor(rp2, x2[:, :], SL, ALU.mult)
                k.inc(v.tensor_tensor(rp1, rp1, rp2, ALU.subtract), "ve")
                for h in range(4):
                    k.inc(v.tensor_copy(
                        dst[(h % 2) * 64:(h % 2) * 64 + 32,
                            (h // 2) * T + sup * 512:(h // 2) * T + (sup + 1) * 512],
                        rp1[32 * h:32 * h + 32, :]), "ve")
                v.tensor_tensor(rp1, x1[:, :], SH, ALU.mult)
                v.tensor_tensor(rp2, x2[:, :], CH, ALU.mult)
                v.tensor_tensor(rp1, rp1, rp2, ALU.add)
                v.tensor_tensor(rp2, x1[:, :], SL, ALU.mult)
                v.tensor_tensor(rp1, rp1, rp2, ALU.add)
                v.tensor_tensor(rp2, x2[:, :], CL, ALU.mult)
                k.inc(v.tensor_tensor(rp1, rp1, rp2, ALU.add), "ve")
                for h in range(4):
                    k.inc(v.tensor_copy(
                        dst[(h % 2) * 64 + 32:(h % 2) * 64 + 64,
                            (h // 2) * T + sup * 512:(h // 2) * T + (sup + 1) * 512],
                        rp1[32 * h:32 * h + 32, :]), "ve")
                k.wait(t, "ve")
            if DEBUG_DUMP and wave == 0 and sup == 0:
                for j in range(4):
                    k.dma(y, dram["rs_i"][(24 + j) * 128:(25 + j) * 128, :],
                          xc[:, j * D:(j + 1) * D])
                for j in range(2):
                    k.dma(y, dram["rs_i"][(28 + j) * 128:(29 + j) * 128, :],
                          wv_[:, j * D:(j + 1) * D])
                k.wait(y, "dS")
            for cc2 in range(4):
                pv_ = pss[2]
                for kc in range(DC):
                    t.matmul(pv_[:, 0:256],
                             xc[:, cc2 * D + kc * 128: cc2 * D + (kc + 1) * 128],
                             wv_[:, kc * 256:(kc + 1) * 256],
                             start=(kc == 0), stop=(kc == DC - 1))
                k.inc(t.drain(), "pe")
                k.wait(v, "pe")
                k.inc(v.tensor_copy(
                    vt_[:, (sup * 4 + cc2) * 256:(sup * 4 + cc2 + 1) * 256],
                    pv_[:, 0:256]), "ve")
                k.wait(t, "ve")
        k.bar()
        if DEBUG_DUMP and wave == 0:
            for j in range(8):
                k.dma(y, dram["rs_i"][j * 128:(j + 1) * 128, :],
                      vt_[:, j * D:(j + 1) * D])
                k.dma(y, dram["rs_i"][(8 + j) * 128:(9 + j) * 128, :],
                      qh[:, j * D:(j + 1) * D])
                k.dma(y, dram["rs_i"][(16 + j) * 128:(17 + j) * 128, :],
                      kh[:, j * D:(j + 1) * D])
            k.wait(y, "dS")

        mneg = fsc[:, 8:9]
        g.memset(mneg, -1e30)
        k.bar(dmas=False)
        for qt in range(NT * 2):
            for hp in range(2):
                macc = [fsc[:, 0:1], fsc[:, 1:2]]
                lacc = [fsc[:, 2:3], fsc[:, 3:4]]
                nmr = fsc[:, 4:5]
                nnm8 = fsc[:, 5:6]
                alph = fsc[:, 6:7]
                rsum = fsc[:, 7:8]
                rl = fsc[:, 9:10]
                k.wait(v, "se"); k.wait(v, "pe")
                v.memset(oacc[:, :], 0.0)
                for ha in range(2):
                    v.tensor_copy(macc[ha], mneg)
                    v.memset(lacc[ha], 0.0)
                k.inc(v.drain(), "ve")
                nsb = qt // 4 + 1
                for sb_ in range(nsb):
                    diag = (sb_ == qt // 4)
                    for ha in range(2):
                        pS = pss[ha]
                        k.wait(t, "ve")
                        k.inc(t.matmul(
                            pS[:, :],
                            qh[64 * ha:64 * ha + 64,
                               hp * T + qt * 128:hp * T + (qt + 1) * 128],
                            kh[64 * ha:64 * ha + 64,
                               hp * T + sb_ * 512:hp * T + (sb_ + 1) * 512],
                            start=True, stop=True), "pe")
                        k.wait(v, "pe")
                        Ssrc = pS[:, :]
                        if diag:
                            r_ = qt % 4
                            k.inc(v.tensor_tensor(
                                Pt, pS[:, :], dneg[:, r_ * 512:(r_ + 1) * 512],
                                ALU.add), "ve")
                            Ssrc = Pt
                        v.reduce_max(nmr, Ssrc, axis=AX.X)
                        v.drain()
                        v.tensor_tensor(nmr, nmr, macc[ha], ALU.max)
                        v.drain()
                        k.inc(v.tensor_scalar_mul(nnm8, nmr, -0.125), "ve")
                        k.wait(s, "ve")
                        s.activation(out=alph, in_=macc[ha], func=AF.Exp,
                                     scale=0.125, bias=nnm8)
                        s.drain()
                        k.inc(s.activation(out=Pt, in_=Ssrc, func=AF.Exp,
                                           scale=0.125, bias=nnm8,
                                           accum_out=rsum), "se")
                        k.wait(v, "se")
                        v.tensor_copy(macc[ha], nmr)
                        v.drain()
                        v.scalar_tensor_tensor(lacc[ha], lacc[ha], alph,
                                               rsum, ALU.mult, ALU.add)
                        k.inc(v.drain(), "ve")
                        pT = pss[2]
                        for i4 in range(4):
                            k.wait(t, "ve")
                            k.inc(t.transpose(pT[:, i4 * 128:(i4 + 1) * 128],
                                              Pt[:, i4 * 128:(i4 + 1) * 128],
                                              ident), "pe")
                        k.wait(v, "pe")
                        for i4 in range(4):
                            k.inc(v.tensor_copy(
                                Pt[:, i4 * 128:(i4 + 1) * 128],
                                pT[:, i4 * 128:(i4 + 1) * 128]), "ve")
                        k.wait(t, "ve")
                        po = pss[3]
                        for i4 in range(4):
                            t.matmul(po[:, 0:64],
                                     Pt[:, i4 * 128:(i4 + 1) * 128],
                                     vt_[:, (sb_ * 4 + i4) * 256 +
                                         (2 * hp + ha) * 64:
                                         (sb_ * 4 + i4) * 256 +
                                         (2 * hp + ha) * 64 + 64],
                                     start=(i4 == 0), stop=(i4 == 3))
                        k.inc(t.drain(), "pe")
                        k.wait(v, "pe")
                        k.inc(v.scalar_tensor_tensor(
                            oacc[:, ha * 64:(ha + 1) * 64],
                            oacc[:, ha * 64:(ha + 1) * 64],
                            alph, po[:, 0:64], ALU.mult, ALU.add), "ve")
                for ha in range(2):
                    v.reciprocal(rl, lacc[ha])
                    v.drain()
                    k.inc(v.tensor_scalar_mul(oacc[:, ha * 64:(ha + 1) * 64],
                                              oacc[:, ha * 64:(ha + 1) * 64],
                                              rl), "ve")
                k.wait(t, "ve")
                pT = pss[4]
                k.inc(t.transpose(pT[:, 0:128], oacc[:, 0:128], ident), "pe")
                k.wait(v, "pe")
                k.inc(v.tensor_copy(out_t2[:, 0:128], pT[:, 0:128]), "ve")
                k.wait(t, "ve")
                pw = [pss[5], pss[6]]
                for dti in range(DC):
                    pwb = pw[dti // 4][:, (dti % 4) * 128:((dti % 4) + 1) * 128]
                    t.matmul(pwb, wo_[:, hp * D + dti * 128:
                                      hp * D + (dti + 1) * 128],
                             out_t2[:, 0:128], start=True, stop=True)
                k.inc(t.drain(), "pe")
                k.wait(v, "pe")
                for dti in range(DC):
                    pwb = pw[dti // 4][:, (dti % 4) * 128:((dti % 4) + 1) * 128]
                    ot = out_t[:, dti * 128:(dti + 1) * 128]
                    if hp == 0:
                        k.inc(v.tensor_copy(ot, pwb), "ve")
                    else:
                        k.inc(v.tensor_tensor(ot, ot, pwb, ALU.add), "ve")
                k.wait(t, "ve")
            sup_, cc2_ = qt // 4, qt % 4
            if wave == 0:
                k.wait(y, "ve")
                k.dma(y, p2a_i[sup_, cc2_, :, :], out_t)
            else:
                k.wait(y, "pe")
                k.dma(y, out_t2, p2a_i[sup_, cc2_, :, :])
                k.wait(v, "dS")
                k.inc(v.tensor_tensor(out_t, out_t, out_t2, ALU.add), "ve")
                k.wait(y, "ve")
                k.dma(y, p2a_i[sup_, cc2_, :, :], out_t)
    A.pop()
    k.bar()

    with allow_ici_all_to_all():
        k.inc(g.collective_compute(
            "AllToAll", ALU.bypass, replica_groups=[list(range(NCORE))],
            ins=[p2a_i.opt()], outs=[p2a_o.opt()]), "cc")
    g.wait_ge(k.sem("cc"), k._cnt["cc"])
    k.bar(dmas=False)

    A.push()
    pa = A.take(D); pb_ = A.take(D); xtl = A.take(D); vt2 = A.take(D)
    for i in range(NT):
        sq, sub = i // 4, i % 4
        k.dma(y, pa, p2a_o[2 * sq, sub, :, :])
        k.dma(y, pb_, p2a_o[2 * sq + 1, sub, :, :])
        k.dma(y, xtl, xin[i * 128:(i + 1) * 128, :])
        k.wait(v, "dS")
        k.inc(v.tensor_tensor(pa, pa, pb_, ALU.add), "ve")
        for dch in range(DC):
            pb = pss[dch % 2]
            k.wait(t, "ve")
            k.inc(t.transpose(pb[:, 0:128], pa[:, dch * 128:(dch + 1) * 128],
                              ident), "pe")
            k.wait(v, "pe")
            k.inc(v.tensor_tensor(vt2[:, dch * 128:(dch + 1) * 128],
                                  xtl[:, dch * 128:(dch + 1) * 128],
                                  pb[:, 0:128], ALU.add), "ve")
        k.wait(y, "ve")
        k.dma(y, xout[i * 128:(i + 1) * 128, :], vt2)
    A.pop()



def emit_moe_sort(ctx, A, SEL1c, SEL2c, G1c, G2c, cntrow):
    k = ctx["k"]; dram = ctx["dram"]; pss = ctx["pss"]; sc = ctx["sc"]
    t, v, s, g, y = k.t, k.v, k.s, k.g, k.y
    lt = A.take(128)
    onesr = A.take(128)
    tokc = A.take(NT)
    prc = A.take(2 * NT)
    s12 = A.take(8)
    pre = A.take(8)
    brow = A.take(8); base = A.take(8); runt = A.take(8); pad = A.take(8)
    sl1 = A.take(1); sl2 = A.take(1)
    su = A.take(2)
    stg4 = A.take(8)
    dflt = A.take(8)
    sbreg = g.alloc_register()
    g.reg_mov(sbreg, NSLOT - 1)
    k.dma(y, lt, dram["ltstrict"][:, :])
    k.dma(y, tokc, dram["tokid"][:, :])
    k.dma(y, prc, dram["pairrow"][:, :])
    v.memset(stg4, 0.0)
    v.memset(onesr[0:1, :], 1.0)
    v.memset(runt[0:1, :], 0.0)
    k.inc(v.drain(), "ve")
    k.wait(v, "ve")
    v.tensor_scalar(pad[0:1, :], cntrow[0:1, :], 63.5, None, ALU.add)
    v.drain()
    v.tensor_scalar(pad[0:1, :], pad[0:1, :], 1.0 / 128.0, None, ALU.mult)
    v.drain()
    v.tensor_scalar(pad[0:1, :], pad[0:1, :], 8388608.0, None, ALU.add)
    v.drain()
    v.tensor_scalar(pad[0:1, :], pad[0:1, :], 8388608.0, None, ALU.subtract)
    v.drain()
    v.tensor_scalar(pad[0:1, :], pad[0:1, :], 128.0, None, ALU.mult)
    v.drain()
    v.tensor_tensor_scan(base[0:1, :], pad[0:1, :], pad[0:1, :],
                         0.0, ALU.add, ALU.bypass)
    v.drain()
    v.tensor_tensor(base[0:1, :], base[0:1, :], pad[0:1, :], ALU.subtract)
    k.inc(v.drain(), "ve")
    k.wait(y, "ve")
    k.dma(y, dram["cnt_i"][:, :], cntrow[0:1, :])
    g.memset(dflt[:, 0:1], float(NCORE * TOK))
    g.memset(dflt[:, 1:2], 1048576.0)
    g.memset(dflt[:, 2:4], 0.0)
    k.inc(g.drain(), "ge")
    k.wait(y, "ge")
    for a_ in range(NCH):
        k.dma(y, dram["srt4"][a_ * 128:(a_ + 1) * 128, :], dflt[:, 0:4])
    k.bar(dmas=True)
    for i in range(NT):
        se1 = SEL1c[:, i * 8:(i + 1) * 8]
        se2 = SEL2c[:, i * 8:(i + 1) * 8]
        k.wait(v, "ge")
        v.tensor_tensor(s12, se1, se2, ALU.add)
        v.tensor_tensor(brow[0:1, :], runt[0:1, :], base[0:1, :], ALU.add)
        k.inc(v.drain(), "ve")
        k.wait(t, "ve")
        pp = pss[6]
        t.matmul(pp[:, 0:8], lt, s12, start=True, stop=False)
        t.matmul(pp[:, 0:8], onesr[0:1, :], brow[0:1, :],
                 start=False, stop=True)
        pc_ = pss[7]
        t.matmul(pc_[0:1, 0:8], sc[:, 2:3], s12, start=True, stop=True)
        k.inc(t.drain(), "pe")
        k.wait(v, "pe")
        v.tensor_copy(pre, pp[:, 0:8])
        v.tensor_tensor(runt[0:1, :], runt[0:1, :], pc_[0:1, 0:8], ALU.add)
        v.drain()
        v.tensor_tensor(s12, se1, pre, ALU.mult)
        v.drain()
        v.reduce_sum(sl1, s12, axis=AX.X)
        v.drain()
        v.tensor_tensor(s12, se2, pre, ALU.mult)
        v.drain()
        v.reduce_sum(sl2, s12, axis=AX.X)
        v.drain()
        k.wait_at(v, "dG", k._cnt["dG"])
        v.tensor_copy(su[:, 0:1].bitcast(U32), sl1)
        v.tensor_copy(su[:, 1:2].bitcast(U32), sl2)
        v.drain()
        for kk_, gc in ((0, G1c), (1, G2c)):
            st4 = stg4[:, kk_ * 4:(kk_ + 1) * 4]
            v.tensor_copy(st4[:, 0:1], tokc[:, i:i + 1])
            v.tensor_copy(st4[:, 1:2], prc[:, kk_ * NT + i:kk_ * NT + i + 1])
            v.tensor_copy(st4[:, 2:3], gc[:, i:i + 1])
        k.inc(v.drain(), "ve")
        k.wait(t, "ve")
        k.wait(g, "ve")
        for kk_ in (0, 1):
            idx = su[:, kk_:kk_ + 1].bitcast(U32)
            k.inc(g.indirect_dma_start(
                out=dram["srt4"][:, :],
                out_offset=bass.IndirectOffsetOnAxis(ap=idx, axis=0),
                in_=stg4[:, kk_ * 4:(kk_ + 1) * 4], in_offset=None,
                bounds_check=sbreg, oob_is_err=False), "dG", 16)
    k.bar()


def emit_moe(ctx, xin, xout):
    k = ctx["k"]; dram = ctx["dram"]; A = ctx["A"]; pss = ctx["pss"]
    sc = ctx["sc"]; ident = ctx["ident"]; zc = ctx["zc"]
    t, v, s, g, y = k.t, k.v, k.s, k.g, k.y
    GTOK = NCORE * TOK

    A.push()
    xt = A.take(D); ut = A.take(D); wsp = A.take(D); uT = A.take(D)
    rws = A.take(64)
    lg = A.take(8)
    pr = A.take(8); e1 = A.take(8); pm_ = A.take(8); sels = A.take(8)
    c1 = A.take(8); wmt = A.take(8)
    rsc = A.take(8)
    SEL1c = A.take(NT * 8); SEL2c = A.take(NT * 8)
    G1c = A.take(NT); G2c = A.take(NT)
    cntrow = A.take(8)
    for dch in range(DC):
        k.dma(y, rws[:, dch * 8:(dch + 1) * 8],
              dram["rw"][dch * 128:(dch + 1) * 128, :])
    v.memset(cntrow[0:1, :], 0.0)
    k.bar()
    for i in range(NT):
        k.dma(y, xt, xin[i * 128:(i + 1) * 128, :])
        k.wait(v, "dS"); k.wait(s, "dS")
        emit_ln(k, xt, ut, wsp, sc, affine=None)
        k.wait(y, "ve")
        k.dma(y, dram["ag_u_i"][i * 128:(i + 1) * 128, :], ut)
        for dch in range(DC):
            pb = pss[dch % 2]
            k.wait(t, "ve")
            k.inc(t.transpose(pb[:, 0:128], ut[:, dch * 128:(dch + 1) * 128],
                              ident), "pe")
            k.wait(v, "pe")
            k.inc(v.tensor_copy(uT[:, dch * 128:(dch + 1) * 128], pb[:, 0:128]),
                  "ve")
        k.wait(t, "ve")
        pl = pss[2]
        for dch in range(DC):
            t.matmul(pl[:, 0:8], uT[:, dch * 128:(dch + 1) * 128],
                     rws[:, dch * 8:(dch + 1) * 8],
                     start=(dch == 0), stop=(dch == DC - 1))
        k.inc(t.drain(), "pe")
        k.wait(v, "pe")
        k.inc(v.tensor_copy(lg, pl[:, 0:8]), "ve")
        rmax = rsc[:, 0:1]; nmax = rsc[:, 1:2]; ssum = rsc[:, 2:3]
        rs_ = rsc[:, 3:4]; m1 = rsc[:, 4:5]; m2 = rsc[:, 5:6]
        v.reduce_max(rmax, lg, axis=AX.X)
        v.drain()
        k.inc(v.tensor_scalar_mul(nmax, rmax, -1.0), "ve")
        k.wait(s, "ve")
        s.activation(out=pr, in_=lg, func=AF.Exp, bias=nmax, accum_out=ssum)
        k.inc(s.drain(), "se")
        k.wait(v, "se")
        v.reciprocal(rs_, ssum)
        v.drain()
        v.tensor_scalar_mul(pr, pr, rs_)
        v.drain()
        v.reduce_max(m1, pr, axis=AX.X)
        v.drain()
        v.tensor_scalar(e1, pr, m1, None, ALU.is_ge)
        v.drain()
        v.tensor_tensor_scan(c1, e1, zc[:, 0:8], 0.0, ALU.add, ALU.add)
        v.drain()
        v.tensor_scalar(c1, c1, 1.0, None, ALU.is_le)
        v.drain()
        v.tensor_tensor(sels, e1, c1, ALU.mult)
        v.drain()
        if MOE_SPARSE:
            v.tensor_copy(SEL1c[:, i * 8:(i + 1) * 8], sels)
            v.tensor_copy(G1c[:, i:i + 1], m1)
            v.drain()
        v.tensor_tensor(pm_, pr, sels, ALU.mult)
        v.drain()
        v.tensor_tensor(pm_, pr, pm_, ALU.subtract)
        v.drain()
        v.reduce_max(m2, pm_, axis=AX.X)
        v.drain()
        v.tensor_scalar(e1, pm_, m2, None, ALU.is_ge)
        v.drain()
        v.tensor_tensor_scan(c1, e1, zc[:, 0:8], 0.0, ALU.add, ALU.add)
        v.drain()
        v.tensor_scalar(c1, c1, 1.0, None, ALU.is_le)
        v.drain()
        v.tensor_tensor(e1, e1, c1, ALU.mult)
        v.drain()
        if MOE_SPARSE:
            v.tensor_copy(SEL2c[:, i * 8:(i + 1) * 8], e1)
            v.tensor_copy(G2c[:, i:i + 1], m2)
            v.drain()
        v.tensor_tensor(sels, sels, e1, ALU.add)
        v.drain()
        k.inc(v.tensor_tensor(wmt, pr, sels, ALU.mult), "ve")
        if MOE_SPARSE:
            k.wait(t, "ve")
            pc_ = pss[5]
            t.matmul(pc_[0:1, 0:8], sc[:, 2:3], sels, start=True, stop=True)
            k.inc(t.drain(), "pe")
            k.wait(v, "pe")
            k.inc(v.tensor_tensor(cntrow[0:1, :], cntrow[0:1, :],
                                  pc_[0:1, 0:8], ALU.add), "ve")
            v.drain()
        k.wait(y, "ve")
        if not MOE_SPARSE:
            k.dma(y, dram["ag_w_i"][i * 128:(i + 1) * 128, :], wmt)
    if MOE_SPARSE:
        emit_moe_sort(ctx, A, SEL1c, SEL2c, G1c, G2c, cntrow)
    A.pop()
    k.bar()

    with allow_ici_all_to_all():
        k.inc(g.collective_compute(
            "AllGather", ALU.bypass, replica_groups=[list(range(NCORE))],
            ins=[dram["ag_u_i"].opt()],
            outs=[dram["ag_u_o"][0:NCORE * TOK, :].opt()]), "cc")
        if MOE_SPARSE:
            for a_, b_ in (("srt4", "ag4"), ("cnt_i", "ag_cnt")):
                k.inc(g.collective_compute(
                    "AllGather", ALU.bypass,
                    replica_groups=[list(range(NCORE))],
                    ins=[dram[a_].opt()], outs=[dram[b_].opt()]), "cc")
        else:
            k.inc(g.collective_compute(
                "AllGather", ALU.bypass, replica_groups=[list(range(NCORE))],
                ins=[dram["ag_w_i"].opt()], outs=[dram["ag_w_o"].opt()]), "cc")
    g.wait_ge(k.sem("cc"), k._cnt["cc"])
    k.bar(dmas=False)
    if MOE_SPARSE:
        emit_moe_sparse(ctx, xin, xout)
        return

    A.push()
    msv = A.take(NE * 4); ms_ = A.take(NE * 4)
    k.dma(y, msv, dram["msv"][:, :])
    k.dma(y, ms_, dram["ms"][:, :])
    ctx["msv"], ctx["ms"] = msv, ms_
    k.bar()
    xT = A.take(8 * 512)
    vT = A.take(4 * 512)
    yT = A.take(4 * 512)
    acc = A.take(4 * D)
    wm = A.take(32)
    xtl = A.take(D)
    wvb = [A.take(8 * 512), A.take(8 * 512)]
    wb_ = [A.take(8 * 512), A.take(8 * 512)]
    w1b = [A.take(4 * D), A.take(4 * D)]
    ms1b = [A.take(D), A.take(D)]
    pe_at_parity = [0, 0]
    ve_at_parity = [0, 0]
    for scg in range(GTOK // 512):
        for sb_ in range(4):
            gi = scg * 4 + sb_
            k.dma(y, xtl, dram["ag_u_o"][gi * 128:(gi + 1) * 128, :])
            k.dma(y, wm[:, sb_ * 8:(sb_ + 1) * 8],
                  dram["ag_w_o"][gi * 128:(gi + 1) * 128, :])
            k.wait(t, "dS")
            for dch in range(DC):
                pb = pss[6 + dch % 2]
                k.wait(t, "ve")
                k.inc(t.transpose(pb[:, 0:128], xtl[:, dch * 128:(dch + 1) * 128],
                                  ident), "pe")
                k.wait(v, "pe")
                k.inc(v.tensor_copy(xT[:, dch * 512 + sb_ * 128:
                                       dch * 512 + (sb_ + 1) * 128],
                                    pb[:, 0:128]), "ve")
            k.wait(t, "ve")
            k.wait(y, "ve")
        for e in range(NE):
            par = e % 2
            k.wait_at(y, "pe", pe_at_parity[par])
            k.wait_at(y, "ve", ve_at_parity[par])
            k.dma(y, wvb[par], dram["mwv"][e * 128:(e + 1) * 128, :])
            k.dma(y, wb_[par], dram["mw"][e * 128:(e + 1) * 128, :])
            k.dma(y, w1b[par], dram["mw1"][e * 128:(e + 1) * 128, :])
            k.dma(y, ms1b[par], dram["ms1"][:, e * D:(e + 1) * D])
            k.wait(t, "dS")
            for src_w, dst, scl, fn in (
                    (wvb[par], vT, "msv", AF.Identity),
                    (wb_[par], yT, "ms", AF.Gelu_apprx_tanh)):
                for ft in range(4):
                    pf = pss[ft % 2]
                    for kc in range(DC):
                        t.matmul(pf[:, :],
                                 src_w[:, kc * 512 + ft * 128:
                                       kc * 512 + (ft + 1) * 128],
                                 xT[:, kc * 512:(kc + 1) * 512],
                                 start=(kc == 0), stop=(kc == DC - 1))
                    k.inc(t.drain(), "pe")
                    k.wait(s, "pe")
                    k.inc(s.activation(out=dst[:, ft * 512:(ft + 1) * 512],
                                       in_=pf[:, :], func=fn,
                                       scale=ctx[scl][:, e * 4 + ft:e * 4 + ft + 1]),
                          "se")
                    k.wait(t, "se")
            k.wait(v, "se")
            for ft in range(4):
                k.inc(v.tensor_tensor(vT[:, ft * 512:(ft + 1) * 512],
                                      vT[:, ft * 512:(ft + 1) * 512],
                                      yT[:, ft * 512:(ft + 1) * 512],
                                      ALU.mult), "ve")
            k.wait(t, "ve")
            for sb_ in range(4):
                for half in range(2):
                    ph_ = pss[2 + half]
                    for kf in range(4):
                        t.matmul(ph_[:, :],
                                 vT[:, kf * 512 + sb_ * 128:
                                    kf * 512 + (sb_ + 1) * 128],
                                 w1b[par][:, kf * D + half * 512:
                                          kf * D + half * 512 + 512],
                                 start=(kf == 0), stop=(kf == 3))
                k.inc(t.drain(), "pe")
                if sb_ == 3:
                    pe_at_parity[par] = k._cnt["pe"]
                k.wait(v, "pe")
                gate = wm[:, sb_ * 8 + e: sb_ * 8 + e + 1]
                for half in range(2):
                    ph_ = pss[2 + half]
                    tmp = xtl[:, half * 512:(half + 1) * 512]
                    a_ = acc[:, sb_ * D + half * 512: sb_ * D + half * 512 + 512]
                    v.tensor_tensor(tmp, ph_[:, :],
                                    ms1b[par][:, half * 512:(half + 1) * 512],
                                    ALU.mult)
                    if e == 0:
                        k.inc(v.tensor_scalar_mul(a_, tmp, gate), "ve")
                    else:
                        k.inc(v.scalar_tensor_tensor(a_, tmp, gate, a_,
                                                     ALU.mult, ALU.add), "ve")
                k.wait(t, "ve")
            ve_at_parity[par] = k._cnt["ve"]
        k.wait(y, "ve")
        for sb_ in range(4):
            gi = scg * 4 + sb_
            k.dma(y, dram["rs_i"][gi * 128:(gi + 1) * 128, :],
                  acc[:, sb_ * D:(sb_ + 1) * D])
    A.pop()
    k.bar()

    with allow_ici_all_to_all():
        k.inc(g.collective_compute(
            "ReduceScatter", ALU.add, replica_groups=[list(range(NCORE))],
            ins=[dram["rs_i"][0:NCORE * TOK, :].opt()],
            outs=[dram["rs_o"][0:TOK, :].opt()]), "cc")
    g.wait_ge(k.sem("cc"), k._cnt["cc"])
    k.bar(dmas=False)

    A.push()
    xa = A.take(D); ra = A.take(D)
    for i in range(NT):
        k.dma(y, xa, xin[i * 128:(i + 1) * 128, :])
        k.dma(y, ra, dram["rs_o"][i * 128:(i + 1) * 128, :])
        k.wait(v, "dS")
        k.inc(v.tensor_tensor(xa, xa, ra, ALU.add), "ve")
        k.wait(y, "ve")
        k.dma(y, xout[i * 128:(i + 1) * 128, :], xa)
    A.pop()



def kernel(**inputs):
    nc = build_program()
    in_maps = [make_in_map(c, inputs) for c in range(NCORE)]
    res = run_bass_kernel_spmd(nc, in_maps, core_ids=list(range(NCORE)))
    out = np.zeros((B, T, D), dtype=np.float32)
    for c in range(NCORE):
        yc = np.asarray(res.results[c]["y"]).reshape(B, SLICE, D)
        out[:, c * SLICE:(c + 1) * SLICE, :] = yc
    return out


# version 03
# speedup vs baseline: 1.3586x; 1.0555x over previous
import numpy as np
from contextlib import ExitStack

import concourse.bass as bass
import concourse.mybir as mybir
from concourse.bass_utils import run_bass_kernel_spmd
from concourse.aot_env import allow_ici_all_to_all

F32 = mybir.dt.float32
U32 = mybir.dt.uint32
AF = mybir.ActivationFunctionType
ALU = mybir.AluOpType
AX = mybir.AxisListType

NCORE = 8
B, T, D = 4, 4096, 1024
SLICE = 512
TOK = B * SLICE
NT = TOK // 128
DC = D // 128
EPS = 1e-5

NE, F = 8, 4096
FSL = F // NCORE

SUBLAYERS = (1, 2, 3, 4)
MOE_SPARSE = True
NCH = 40
NSLOT = NCH * 128
STAGE = 99
DEBUG_DUMP = ()



def _repl(v, p=128):
    v = np.ascontiguousarray(np.asarray(v), dtype=np.float32)
    return np.broadcast_to(v.reshape(1, -1), (p, v.size)).copy()


def _colsplit(v, tile=128):
    v = np.asarray(v, dtype=np.float32)
    return np.ascontiguousarray(v.reshape(-1, tile).T)


def _rope_tables():
    half = 32
    inv = 10000.0 ** (-(np.arange(half, dtype=np.float64)) * 2.0 / 64.0)
    ang = np.arange(T, dtype=np.float64)[None, :] * inv[:, None]
    out = []
    for m in (np.cos(ang), np.sin(ang)):
        hi = m.astype(np.float32)
        lo = (m - hi.astype(np.float64)).astype(np.float32)
        out.append(np.tile(hi, (4, 1)))
        out.append(np.tile(lo, (4, 1)))
    return out


def _rope_perm():
    perm = np.zeros(512, dtype=np.int64)
    for h in range(8):
        for j in range(64):
            p = (h // 4) * 256 + (j % 2) * 128 + (h % 4) * 32 + j // 2
            perm[p] = h * 64 + j
    return perm


def make_in_map(c, inp):
    hg = c % 2
    hb = c % 2
    d = {}
    x = np.asarray(inp["L0_attn_a_x"], dtype=np.float32)
    d["x0"] = np.ascontiguousarray(
        x[:, c * SLICE:(c + 1) * SLICE, :].reshape(TOK, D))
    for nm, key in (("wq", "L0_attn_a_Wq"), ("wk", "L0_attn_a_Wk"),
                    ("wv", "L0_attn_a_Wv"), ("wg", "L0_attn_a_Wg")):
        W = np.asarray(inp[key], dtype=np.float32)
        d[nm] = np.ascontiguousarray(W[:, hg * 512:(hg + 1) * 512])
    d["wgk1"] = np.ascontiguousarray(np.asarray(inp["L0_attn_a_Wgk1"], np.float32))
    d["wgk2"] = np.ascontiguousarray(
        np.asarray(inp["L0_attn_a_Wgk2"], np.float32)[:, hg * 512:(hg + 1) * 512])
    d["bgk2"] = _colsplit(np.asarray(inp["L0_attn_a_bgk2"], np.float32)
                          [hg * 512:(hg + 1) * 512])
    d["rmsw"] = _repl(inp["L0_attn_a_rms_w"])
    d["wo"] = np.ascontiguousarray(
        np.asarray(inp["L0_attn_a_Wo"], np.float32)[hg * 512:(hg + 1) * 512, :])
    d["ln1w"] = _repl(inp["L0_attn_a_ln1_w"]); d["ln1b"] = _repl(inp["L0_attn_a_ln1_b"])
    d["ln2w"] = _repl(inp["L0_attn_a_ln2_w"]); d["ln2b"] = _repl(inp["L0_attn_a_ln2_b"])
    d["w1"] = np.ascontiguousarray(np.asarray(inp["L0_attn_a_W1"], np.float32))
    d["b1c"] = _colsplit(inp["L0_attn_a_b1"])
    d["w2"] = np.ascontiguousarray(np.asarray(inp["L0_attn_a_W2"], np.float32))
    d["b2"] = _repl(inp["L0_attn_a_b2"])
    fs = slice(c * FSL, (c + 1) * FSL)
    wv = np.asarray(inp["L0_ffn_wv"], np.float32)[:, :, fs]
    w_ = np.asarray(inp["L0_ffn_w"], np.float32)[:, :, fs]
    w1m = np.asarray(inp["L0_ffn_w1"], np.float32)[:, fs, :]
    sv = np.asarray(inp["L0_ffn_sv"], np.float32)[:, 0, fs]
    s_ = np.asarray(inp["L0_ffn_s"], np.float32)[:, 0, fs]
    mwv = wv.reshape(NE, 8, 128, FSL).transpose(0, 2, 1, 3).reshape(NE * 128, 8 * FSL)
    mw = w_.reshape(NE, 8, 128, FSL).transpose(0, 2, 1, 3).reshape(NE * 128, 8 * FSL)
    svc = sv.reshape(NE, 4, 128).transpose(0, 2, 1).reshape(NE * 128, 4)
    ssc = s_.reshape(NE, 4, 128).transpose(0, 2, 1).reshape(NE * 128, 4)
    d["mwv"] = np.ascontiguousarray(np.concatenate([mwv, svc], axis=1))
    d["mw"] = np.ascontiguousarray(np.concatenate([mw, ssc], axis=1))
    d["mw1"] = np.ascontiguousarray(
        w1m.reshape(NE, 4, 128, D).transpose(0, 2, 1, 3).reshape(NE * 128, 4 * D))
    d["msv"] = np.concatenate([_colsplit(sv[e]) for e in range(NE)], axis=1)
    d["ms"] = np.concatenate([_colsplit(s_[e]) for e in range(NE)], axis=1)
    s1 = np.asarray(inp["L0_ffn_s1"], np.float32)[:, 0, :]
    d["ms1"] = np.concatenate([_repl(s1[e]) for e in range(NE)], axis=1)
    d["ms1g"] = np.ascontiguousarray(
        np.repeat(s1, 128, axis=0))
    d["msvg"] = np.ascontiguousarray(np.repeat(sv, 128, axis=0))
    d["mssg"] = np.ascontiguousarray(np.repeat(s_, 128, axis=0))
    d["rw"] = np.ascontiguousarray(np.asarray(inp["L0_ffn_router_w"], np.float32))
    p_ = np.arange(128, dtype=np.float32)[:, None]
    d["tokid"] = np.ascontiguousarray(
        (c * TOK + np.arange(NT, dtype=np.float32)[None, :] * 128 + p_))
    pr = np.zeros((128, 2 * NT), dtype=np.float32)
    for kk_ in range(2):
        for i_ in range(NT):
            pr[:, kk_ * NT + i_] = c * 2 * TOK + kk_ * TOK + i_ * 128 + p_[:, 0]
    d["pairrow"] = np.ascontiguousarray(pr)
    d["jrow"] = np.ascontiguousarray(
        (np.arange(NCH, dtype=np.float32) * 128)[None, :])
    lt = (np.arange(128)[:, None] < np.arange(128)[None, :]).astype(np.float32)
    d["ltstrict"] = np.ascontiguousarray(lt)
    d["iota8"] = np.broadcast_to(np.arange(8, dtype=np.float32), (128, 8)).copy()
    perm = _rope_perm()
    rows = hb * 512 + np.arange(512)
    d["bwq"] = np.ascontiguousarray(
        np.asarray(inp["L0_attn_b_Wq"], np.float32)[rows][perm].T)
    d["bwk"] = np.ascontiguousarray(
        np.asarray(inp["L0_attn_b_Wk"], np.float32)[rows][perm].T)
    d["bwv"] = np.ascontiguousarray(
        np.asarray(inp["L0_attn_b_Wv"], np.float32)[rows].T)
    d["bwo"] = np.ascontiguousarray(
        np.asarray(inp["L0_attn_b_Wo"], np.float32)[:, rows].T)
    ch, cl, sh, sl = _rope_tables()
    d["rch"], d["rcl"], d["rsh"], d["rsl"] = ch, cl, sh, sl
    d["ident"] = np.eye(128, dtype=np.float32)
    st = np.arange(128)
    d["trimask"] = (st[:, None] <= st[None, :]).astype(np.float32)
    dm = np.zeros((4, 128, 512), dtype=np.float32)
    for r in range(4):
        dm[r] = (np.arange(512)[None, :] <= (r * 128 + st)[:, None])
    d["diagmask"] = np.ascontiguousarray(dm.reshape(512, 512))
    d["dmneg"] = np.ascontiguousarray(((dm - 1.0) * 1e30).reshape(512, 512))
    return d



class KB:
    def __init__(self, nc, es):
        self.nc = nc
        self.t, self.v, self.s, self.g, self.y = \
            nc.tensor, nc.vector, nc.scalar, nc.gpsimd, nc.sync
        self.engines = [self.t, self.v, self.s, self.g, self.y]
        self._sems, self._cnt = {}, {}
        for nm in ("bar", "dS", "dG", "dC", "dV", "cc",
                   "pe", "ve", "se", "ge"):
            self._sems[nm] = es.enter_context(nc.semaphore(nm))
            self._cnt[nm] = 0
        self._qmap = {id(self.y): "dS", id(self.g): "dG",
                      id(self.s): "dC", id(self.v): "dV"}

    def sem(self, nm):
        return self._sems[nm]

    def inc(self, instr, nm, k=1):
        if nm in ("pe", "ve", "se", "ge"):
            eng = {"pe": self.t, "ve": self.v, "se": self.s,
                   "ge": self.g}[nm]
            eng.drain().then_inc(self._sems[nm], k)
        else:
            instr.then_inc(self._sems[nm], k)
        self._cnt[nm] += k
        return instr

    def dma(self, eng, out, in_, **kw):
        return self.inc(eng.dma_start(out=out, in_=in_, **kw),
                        self._qmap[id(eng)], 16)

    def wait_at(self, eng, nm, val):
        if val:
            eng.wait_ge(self._sems[nm], val)

    def wait(self, eng, *names):
        for nm in names:
            if self._cnt[nm]:
                eng.wait_ge(self._sems[nm], self._cnt[nm])

    def bar(self, dmas=True):
        b = self._sems["bar"]
        for e in self.engines:
            e.drain().then_inc(b, 1)
        self._cnt["bar"] += len(self.engines)
        tgt = self._cnt["bar"]
        for e in self.engines:
            e.wait_ge(b, tgt)
            if dmas:
                self.wait(e, "dS", "dG", "dC", "dV")


class Arena:
    def __init__(self, tens, ncols):
        self.t = tens
        self.n = ncols
        self.off = 0
        self.marks = []

    def take(self, cols):
        assert self.off + cols <= self.n, (self.off, cols, self.n)
        ap = self.t[:, self.off:self.off + cols]
        self.off += cols
        return ap

    def push(self):
        self.marks.append(self.off)

    def pop(self):
        self.off = self.marks.pop()


def emit_ln(k, src, dst, wsp, sc, affine=None, n_free=D):
    v, s = k.v, k.s
    eps = sc[:, 0:1]
    nmu = sc[:, 4:5]; ssq = sc[:, 5:6]; rst = sc[:, 6:7]
    inv = 1.0 / n_free
    k.wait(v, "se")
    v.reduce_sum(nmu, src, axis=AX.X)
    v.drain()
    k.inc(v.tensor_scalar_mul(nmu, nmu, -inv), "ve")
    k.wait(s, "ve")
    s.activation(out=wsp, in_=src, func=AF.Identity, bias=nmu)
    s.activation(out=dst, in_=wsp, func=AF.Square, accum_out=ssq)
    s.drain()
    s.activation(out=ssq, in_=ssq, func=AF.Identity, scale=inv, bias=eps)
    k.inc(s.activation(out=ssq, in_=ssq, func=AF.Sqrt), "se")
    k.wait(v, "se")
    k.inc(v.reciprocal(rst, ssq), "ve")
    if affine is None:
        k.inc(v.tensor_scalar_mul(dst, wsp, rst), "ve")
    else:
        w_ap, b_ap = affine
        k.inc(v.scalar_tensor_tensor(dst, wsp, rst, w_ap, ALU.mult, ALU.mult),
              "ve")
        if b_ap is not None:
            k.inc(v.tensor_tensor(dst, dst, b_ap, ALU.add), "ve")



def build_program(sublayers=None):
    sublayers = SUBLAYERS if sublayers is None else sublayers
    nc = bass.Bass("TRN2", target_bir_lowering=False, debug=False,
                   detect_race_conditions=False)
    dram = {}

    def din(name, shape, dt=F32):
        dram[name] = nc.dram_tensor(name, shape, dt, kind="ExternalInput").ap()

    def dint(name, shape, dt=F32):
        dram[name] = nc.dram_tensor(name, shape, dt).ap()

    for nm, sh in (("x0", [TOK, D]),
                   ("wq", [D, 512]), ("wk", [D, 512]), ("wv", [D, 512]),
                   ("wg", [D, 512]), ("wgk1", [D, 16]), ("wgk2", [16, 512]),
                   ("bgk2", [128, 4]), ("rmsw", [128, 256]), ("wo", [512, D]),
                   ("ln1w", [128, D]), ("ln1b", [128, D]),
                   ("ln2w", [128, D]), ("ln2b", [128, D]),
                   ("w1", [D, 3072]), ("b1c", [128, 24]),
                   ("w2", [3072, D]), ("b2", [128, D]),
                   ("mwv", [NE * 128, 8 * FSL + 4]), ("mw", [NE * 128, 8 * FSL + 4]),
                   ("mw1", [NE * 128, 4 * D]), ("ms1g", [NE * 128, D]),
                   ("msvg", [NE * 128, FSL]), ("mssg", [NE * 128, FSL]),
                   ("tokid", [128, NT]), ("pairrow", [128, 2 * NT]),
                   ("jrow", [1, NCH]), ("ltstrict", [128, 128]),
                   ("iota8", [128, 8]),
                   ("msv", [128, NE * 4]), ("ms", [128, NE * 4]),
                   ("ms1", [128, NE * D]), ("rw", [D, NE]),
                   ("bwq", [D, 512]), ("bwk", [D, 512]), ("bwv", [D, 512]),
                   ("bwo", [512, D]),
                   ("rch", [128, T]), ("rcl", [128, T]),
                   ("rsh", [128, T]), ("rsl", [128, T]),
                   ("ident", [128, 128]), ("trimask", [128, 128]),
                   ("diagmask", [512, 512]), ("dmneg", [512, 512])):
        din(nm, sh)
    yout = nc.dram_tensor("y", [TOK, D], F32, kind="ExternalOutput").ap()

    dint("xh", [TOK, D])
    dint("xh2", [TOK, D])
    dint("xh3", [TOK, D])
    dint("uh", [TOK, D])
    dint("v1h", [TOK, D])
    dint("ag_u_i", [TOK, D])
    dint("ag_u_o", [NCORE * TOK + 128, D])
    dint("ag_w_i", [TOK, NE])
    dint("ag_w_o", [NCORE * TOK, NE])
    dint("srt4", [NSLOT, 4])
    dint("cnt_i", [1, NE])
    dint("ag4", [NCORE * NSLOT, 4])
    dint("ag_cnt", [NCORE, NE])
    dint("rs_i", [2 * NCORE * TOK, D])
    dint("rs_o", [2 * TOK, D])
    dint("a2a_i", [NCORE, 4, 128, D])
    dint("a2a_o", [NCORE, 4, 128, D])
    dint("p2a_i", [NCORE, 4, 128, D])
    dint("p2a_o", [NCORE, 4, 128, D])

    es = ExitStack()
    with es:
        k = KB(nc, es)
        t, v, s, g, y = k.t, k.v, k.s, k.g, k.y

        cst = es.enter_context(nc.sbuf_tensor("cst", [128, 6144], F32))
        bigt = es.enter_context(nc.sbuf_tensor("big", [128, 45056], F32))
        A = Arena(bigt, 45056)
        pss = [es.enter_context(nc.psum_tensor("ps%d" % i, [128, 512], F32))
               for i in range(8)]

        ln1w = cst[:, 0:D]; ln1b = cst[:, D:2 * D]
        ln2w = cst[:, 2 * D:3 * D]; ln2b = cst[:, 3 * D:4 * D]
        rmsw = cst[:, 4 * D:4 * D + 256]
        ident = cst[:, 4352:4480]
        tri = cst[:, 4480:4608]
        zc = cst[:, 4608:4736]
        bgk2 = cst[:, 4736:4740]
        sc = cst[:, 4740:4804]
        b1c = cst[:, 4804:4828]
        ln_aff1 = (ln1w, ln1b)
        ln_aff2 = (ln2w, ln2b)

        for dst, srcn in ((ln1w, "ln1w"), (ln1b, "ln1b"), (ln2w, "ln2w"),
                          (ln2b, "ln2b"), (rmsw, "rmsw"), (ident, "ident"),
                          (tri, "trimask"), (bgk2, "bgk2"), (b1c, "b1c")):
            k.dma(y, dst, dram[srcn][:, :])
        g.memset(zc, 0.0)
        g.memset(sc[:, 2:64], 0.0)
        g.memset(sc[:, 0:1], EPS)
        g.memset(sc[:, 1:2], 1.0 / 16.0)
        g.memset(sc[:, 2:3], 1.0)
        k.bar()

        ctx = dict(k=k, nc=nc, dram=dram, A=A, pss=pss, sc=sc, ident=ident,
                   tri=tri, zc=zc, rmsw=rmsw, bgk2=bgk2, b1c=b1c,
                   ln_aff1=ln_aff1, ln_aff2=ln_aff2)

        xin = dram["x0"]
        if 1 in sublayers:
            emit_attn_a(ctx, xin, dram["xh"])
            xin = dram["xh"]
            k.bar()
        if 2 in sublayers:
            emit_moe(ctx, xin, dram["xh2"])
            xin = dram["xh2"]
            k.bar()
        if 3 in sublayers:
            emit_attn_b(ctx, xin, dram["xh3"])
            xin = dram["xh3"]
            k.bar()
        if 4 in sublayers:
            emit_moe(ctx, xin, dram["xh"])
            xin = dram["xh"]
            k.bar()

        A.push()
        tb = A.take(D)
        for i in range(NT):
            k.dma(y, tb, xin[i * 128:(i + 1) * 128, :])
            k.wait(y, "dS")
            k.dma(y, yout[i * 128:(i + 1) * 128, :], tb)
            k.wait(y, "dS")
        for nm in DEBUG_DUMP:
            src_ap = dram[nm]
            flat = int(np.prod(src_ap.shape))
            dbg = nc.dram_tensor("dbg_" + nm, [flat // 1024, 1024], F32,
                                 kind="ExternalOutput").ap()
            s2 = src_ap.rearrange(
                " ".join("abcd"[:len(src_ap.shape)]) + " -> (" +
                " ".join("abcd"[:len(src_ap.shape)]) + ")"
            ).rearrange("(r c) -> r c", c=1024)
            for i in range(flat // 1024 // 128):
                k.dma(y, tb, s2[i * 128:(i + 1) * 128, :])
                k.wait(y, "dS")
                k.dma(y, dbg[i * 128:(i + 1) * 128, :], tb)
                k.wait(y, "dS")
        A.pop()
        k.bar()
    return nc



def emit_attn_a(ctx, xin, xh):
    k = ctx["k"]; dram = ctx["dram"]; A = ctx["A"]; pss = ctx["pss"]
    sc = ctx["sc"]; ident = ctx["ident"]
    t, v, s, g, y = k.t, k.v, k.s, k.g, k.y
    uh, v1h = dram["uh"], dram["v1h"]
    a2a_i, a2a_o = dram["a2a_i"], dram["a2a_o"]
    p2a_i, p2a_o = dram["p2a_i"], dram["p2a_o"]

    A.push()
    xt = A.take(D); ut = A.take(D); at = A.take(D); wsp = A.take(D)
    aT = A.take(D)
    for i in range(NT):
        sq, sub = i // 4, i % 4
        k.dma(y, xt, xin[i * 128:(i + 1) * 128, :])
        k.wait(v, "dS"); k.wait(s, "dS")
        emit_ln(k, xt, ut, wsp, sc, affine=None)
        emit_ln(k, ut, at, wsp, sc, affine=ctx["ln_aff1"])
        k.wait(y, "ve")
        k.dma(y, uh[i * 128:(i + 1) * 128, :], ut)
        for dch in range(DC):
            pb = pss[dch % 2]
            k.wait(t, "ve")
            k.inc(t.transpose(pb[:, 0:128], at[:, dch * 128:(dch + 1) * 128],
                              ident), "pe")
            k.wait(v, "pe")
            k.inc(v.tensor_copy(aT[:, dch * 128:(dch + 1) * 128], pb[:, 0:128]),
                  "ve")
        k.wait(y, "ve")
        for r in (2 * sq, 2 * sq + 1):
            k.dma(y, a2a_i[r, sub, :, :], aT)
    A.pop()
    k.bar()
    if STAGE < 2:
        return

    with allow_ici_all_to_all():
        k.inc(g.collective_compute(
            "AllToAll", ALU.bypass, replica_groups=[list(range(NCORE))],
            ins=[a2a_i.opt()], outs=[a2a_o.opt()]), "cc")
    g.wait_ge(k.sem("cc"), k._cnt["cc"])
    k.bar(dmas=False)
    if STAGE < 3:
        return

    emit_gla(ctx)
    k.bar()
    if STAGE < 4:
        return

    with allow_ici_all_to_all():
        k.inc(g.collective_compute(
            "AllToAll", ALU.bypass, replica_groups=[list(range(NCORE))],
            ins=[p2a_i.opt()], outs=[p2a_o.opt()]), "cc")
    g.wait_ge(k.sem("cc"), k._cnt["cc"])
    k.bar(dmas=False)

    A.push()
    pa = A.take(D); pb_ = A.take(D); utl = A.take(D); vt = A.take(D)
    for i in range(NT):
        sq, sub = i // 4, i % 4
        k.dma(y, pa, p2a_o[2 * sq, sub, :, :])
        k.dma(y, pb_, p2a_o[2 * sq + 1, sub, :, :])
        k.dma(y, utl, uh[i * 128:(i + 1) * 128, :])
        k.wait(v, "dS")
        k.inc(v.tensor_tensor(pa, pa, pb_, ALU.add), "ve")
        for dch in range(DC):
            pb = pss[dch % 2]
            k.wait(t, "ve")
            k.inc(t.transpose(pb[:, 0:128], pa[:, dch * 128:(dch + 1) * 128],
                              ident), "pe")
            k.wait(v, "pe")
            k.inc(v.tensor_tensor(vt[:, dch * 128:(dch + 1) * 128],
                                  utl[:, dch * 128:(dch + 1) * 128],
                                  pb[:, 0:128], ALU.add), "ve")
        k.wait(y, "ve")
        k.dma(y, v1h[i * 128:(i + 1) * 128, :], vt)
    A.pop()
    k.bar()
    if STAGE < 5:
        return

    emit_ffn_a(ctx, xin, xh)


def emit_gla(ctx):
    k = ctx["k"]; dram = ctx["dram"]; A = ctx["A"]; pss = ctx["pss"]
    sc = ctx["sc"]; ident = ctx["ident"]; tri = ctx["tri"]; zc = ctx["zc"]
    rmsw = ctx["rmsw"]; bgk2 = ctx["bgk2"]
    t, v, s, g, y = k.t, k.v, k.s, k.g, k.y
    a2a_o, p2a_i = dram["a2a_o"], dram["p2a_i"]
    one16 = sc[:, 1:2]

    A.push()
    wq = A.take(8 * 512); wk = A.take(8 * 512)
    wvw = A.take(8 * 512); wgw = A.take(8 * 512)
    wo = A.take(4 * D)
    wgk1 = A.take(8 * 16)
    wgk2 = A.take(512)
    for dch in range(DC):
        k.dma(y, wq[:, dch * 512:(dch + 1) * 512],
              dram["wq"][dch * 128:(dch + 1) * 128, :])
        k.dma(y, wk[:, dch * 512:(dch + 1) * 512],
              dram["wk"][dch * 128:(dch + 1) * 128, :])
        k.dma(y, wvw[:, dch * 512:(dch + 1) * 512],
              dram["wv"][dch * 128:(dch + 1) * 128, :])
        k.dma(y, wgw[:, dch * 512:(dch + 1) * 512],
              dram["wg"][dch * 128:(dch + 1) * 128, :])
        k.dma(y, wgk1[:, dch * 16:(dch + 1) * 16],
              dram["wgk1"][dch * 128:(dch + 1) * 128, :])
    for kc in range(4):
        k.dma(y, wo[:, kc * D:(kc + 1) * D],
              dram["wo"][kc * 128:(kc + 1) * 128, :])
    k.dma(y, wgk2[0:16, :], dram["wgk2"][:, :])

    xc = A.take(4 * D)
    qT = A.take(4 * 512); kT = A.take(4 * 512)
    lsT = A.take(4 * 512)
    vts = A.take(4 * 512); gts = A.take(4 * 512)
    hst = A.take(4 * 256)
    u16 = A.take(512)
    wrk = A.take(2048)
    out_t = A.take(D)
    gsc = A.take(16)

    g.memset(hst[:, :], 0.0)
    k.bar()
    if STAGE < 3.05:
        A.pop()
        return

    for sup in range(T // 512):
        for sb_ in range(4):
            k.dma(y, xc[:, sb_ * D:(sb_ + 1) * D], a2a_o[sup, sb_, :, :])
        k.bar(dmas=True)
        if STAGE < 3.08:
            continue

        for dt_ in range(4):
            pq, pk = pss[0], pss[1]
            for ww, pp in ((wq, pq), (wk, pk)):
                for cc2 in range(4):
                    for kc in range(DC):
                        l_ = ww[:, kc * 512 + dt_ * 128: kc * 512 + (dt_ + 1) * 128]
                        r_ = xc[:, cc2 * D + kc * 128: cc2 * D + (kc + 1) * 128]
                        t.matmul(pp[:, cc2 * 128:(cc2 + 1) * 128], l_, r_,
                                 start=(kc == 0), stop=(kc == DC - 1))
            k.inc(t.drain(), "pe")
            k.wait(v, "pe")
            k.inc(v.tensor_copy(qT[:, dt_ * 512:(dt_ + 1) * 512], pq[:, :]), "ve")
            k.inc(v.tensor_copy(kT[:, dt_ * 512:(dt_ + 1) * 512], pk[:, :]), "ve")
            k.wait(t, "ve")
        if STAGE < 3.12:
            continue
        for cc2 in range(4):
            pv_, pg = pss[2], pss[3]
            for ww, pp in ((wvw, pv_), (wgw, pg)):
                for kc in range(DC):
                    l_ = xc[:, cc2 * D + kc * 128: cc2 * D + (kc + 1) * 128]
                    t.matmul(pp[:, :], l_, ww[:, kc * 512:(kc + 1) * 512],
                             start=(kc == 0), stop=(kc == DC - 1))
            k.inc(t.drain(), "pe")
            k.wait(v, "pe")
            k.inc(v.tensor_copy(vts[:, cc2 * 512:(cc2 + 1) * 512], pv_[:, :]), "ve")
            k.inc(v.tensor_copy(gts[:, cc2 * 512:(cc2 + 1) * 512], pg[:, :]), "ve")
            k.wait(t, "ve")
        if STAGE < 3.2:
            continue
        pu = pss[4]
        for cc2 in range(4):
            for kc in range(DC):
                t.matmul(pu[0:16, cc2 * 128:(cc2 + 1) * 128],
                         wgk1[:, kc * 16:(kc + 1) * 16],
                         xc[:, cc2 * D + kc * 128: cc2 * D + (kc + 1) * 128],
                         start=(kc == 0), stop=(kc == DC - 1))
        k.inc(t.drain(), "pe")
        k.wait(v, "pe")
        k.inc(v.tensor_copy(u16[0:16, :], pu[0:16, :]), "ve")
        k.wait(t, "ve")
        for dt_ in range(4):
            pz = pss[5]
            t.matmul(pz[:, :], wgk2[0:16, dt_ * 128:(dt_ + 1) * 128],
                     u16[0:16, :], start=True, stop=True)
            k.inc(t.drain(), "pe")
            k.wait(s, "pe")
            s.activation(out=wrk[:, 0:512], in_=pz[:, :], func=AF.Sigmoid,
                         bias=bgk2[:, dt_:dt_ + 1])
            k.inc(s.activation(out=lsT[:, dt_ * 512:(dt_ + 1) * 512],
                               in_=wrk[:, 0:512], func=AF.Ln), "se")
            k.wait(t, "se")
        k.bar(dmas=False)
        if STAGE < 3.3:
            continue

        for cc2 in range(4):
            pw = [pss[6], pss[7]]
            for h in range(2):
                dts = (2 * h, 2 * h + 1)
                Gt = [wrk[:, 0:128], wrk[:, 128:256]]
                E = [wrk[:, 256:384], wrk[:, 384:512]]
                En = [wrk[:, 512:640], wrk[:, 640:768]]
                Er = [wrk[:, 768:896], wrk[:, 896:1024]]
                qd = [wrk[:, 1024:1152], wrk[:, 1152:1280]]
                kda = [wrk[:, 1280:1408], wrk[:, 1408:1536]]
                kdh = [wrk[:, 1536:1664], wrk[:, 1664:1792]]
                am = wrk[:, 1792:1920]
                kdhs = [wrk[:, 1920:2048], None]
                ge16 = [gsc[:, 0:1], gsc[:, 1:2]]
                edec = [gsc[:, 2:3], gsc[:, 3:4]]
                ssq2 = gsc[:, 4:5]; rstd = gsc[:, 5:6]

                for j, dt_ in enumerate(dts):
                    lsl = lsT[:, dt_ * 512 + cc2 * 128: dt_ * 512 + (cc2 + 1) * 128]
                    k.wait(v, "se")
                    k.inc(v.tensor_tensor_scan(Gt[j], lsl, zc, 0.0,
                                               ALU.add, ALU.add), "ve")
                    k.wait(s, "ve")
                    s.activation(out=ge16[j], in_=Gt[j][:, 127:128],
                                 func=AF.Identity, scale=1.0 / 16.0)
                    s.activation(out=E[j], in_=Gt[j], func=AF.Exp,
                                 scale=1.0 / 16.0)
                    s.activation(out=En[j], in_=Gt[j], func=AF.Exp,
                                 scale=-1.0 / 16.0)
                    s.activation(out=Er[j], in_=Gt[j], func=AF.Exp,
                                 scale=-1.0 / 16.0, bias=ge16[j])
                    k.inc(s.activation(out=edec[j], in_=Gt[j][:, 127:128],
                                       func=AF.Exp, scale=1.0 / 16.0), "se")
                    k.wait(v, "se")
                    qsl = qT[:, dt_ * 512 + cc2 * 128: dt_ * 512 + (cc2 + 1) * 128]
                    ksl = kT[:, dt_ * 512 + cc2 * 128: dt_ * 512 + (cc2 + 1) * 128]
                    k.inc(v.scalar_tensor_tensor(qd[j], qsl, one16, E[j],
                                                 ALU.mult, ALU.mult), "ve")
                    k.inc(v.tensor_tensor(kda[j], ksl, En[j], ALU.mult), "ve")
                    k.inc(v.tensor_tensor(kdh[j], ksl, Er[j], ALU.mult), "ve")
                kdhs[1] = Gt[0]
                k.wait(t, "ve")

                pA = pss[0]
                t.matmul(pA[:, 0:128], kda[0], qd[0], start=True, stop=False)
                k.inc(t.matmul(pA[:, 0:128], kda[1], qd[1],
                               start=False, stop=True), "pe")
                pT = pss[1]
                k.inc(t.transpose(pT[:, 0:128], kdh[0], ident), "pe")
                k.inc(t.transpose(pT[:, 128:256], kdh[1], ident), "pe")
                k.wait(v, "pe")
                k.inc(v.tensor_tensor(am, pA[:, 0:128], tri, ALU.mult), "ve")
                k.inc(v.tensor_copy(kdhs[0], pT[:, 0:128]), "ve")
                k.inc(v.tensor_copy(kdhs[1], pT[:, 128:256]), "ve")
                k.wait(t, "ve")

                po = pss[2]
                vsl = vts[:, cc2 * 512 + h * 256: cc2 * 512 + (h + 1) * 256]
                t.matmul(po[:, 0:256], qd[0],
                         hst[:, (2 * h) * 256:(2 * h + 1) * 256],
                         start=True, stop=False)
                t.matmul(po[:, 0:256], qd[1],
                         hst[:, (2 * h + 1) * 256:(2 * h + 2) * 256],
                         start=False, stop=False)
                t.matmul(po[:, 0:256], am, vsl, start=False, stop=True)
                ph = pss[3]
                t.matmul(ph[:, 0:256], kdhs[0], vsl, start=True, stop=True)
                k.inc(t.matmul(ph[:, 256:512], kdhs[1], vsl,
                               start=True, stop=True), "pe")
                k.wait(v, "pe"); k.wait(s, "pe")
                for j in range(2):
                    k.inc(v.scalar_tensor_tensor(
                        hst[:, (2 * h + j) * 256:(2 * h + j + 1) * 256],
                        hst[:, (2 * h + j) * 256:(2 * h + j + 1) * 256],
                        edec[j], ph[:, j * 256:(j + 1) * 256],
                        ALU.mult, ALU.add), "ve")
                o2 = wrk[:, 256:512]
                k.inc(s.activation(out=o2, in_=po[:, 0:256], func=AF.Square,
                                   accum_out=ssq2), "se")
                s.drain()
                s.activation(out=ssq2, in_=ssq2, func=AF.Identity,
                             scale=1.0 / 256, bias=sc[:, 0:1])
                k.inc(s.activation(out=ssq2, in_=ssq2, func=AF.Sqrt), "se")
                k.wait(v, "se")
                k.inc(v.reciprocal(rstd, ssq2), "ve")
                gsl = gts[:, cc2 * 512 + h * 256: cc2 * 512 + (h + 1) * 256]
                sg = wrk[:, 512:768]
                k.wait(s, "ve")
                k.inc(s.activation(out=sg, in_=gsl, func=AF.Silu), "se")
                k.wait(v, "se")
                t1 = wrk[:, 768:1024]
                k.inc(v.scalar_tensor_tensor(t1, po[:, 0:256], rstd, rmsw,
                                             ALU.mult, ALU.mult), "ve")
                go = wrk[:, 1024:1280]
                k.inc(v.tensor_tensor(go, t1, sg, ALU.mult), "ve")
                k.wait(t, "ve")
                pgt = pss[4]
                k.inc(t.transpose(pgt[:, 0:128], go[:, 0:128], ident), "pe")
                k.inc(t.transpose(pgt[:, 128:256], go[:, 128:256], ident), "pe")
                k.wait(v, "pe")
                goT0 = wrk[:, 1280:1408]; goT1 = wrk[:, 1408:1536]
                k.inc(v.tensor_copy(goT0, pgt[:, 0:128]), "ve")
                k.inc(v.tensor_copy(goT1, pgt[:, 128:256]), "ve")
                k.wait(t, "ve")
                kc0 = 2 * h
                for dti in range(DC):
                    pwb = pw[dti // 4][:, (dti % 4) * 128:((dti % 4) + 1) * 128]
                    wl0 = wo[:, kc0 * D + dti * 128: kc0 * D + (dti + 1) * 128]
                    wl1 = wo[:, (kc0 + 1) * D + dti * 128:
                             (kc0 + 1) * D + (dti + 1) * 128]
                    t.matmul(pwb, wl0, goT0, start=True, stop=False)
                    t.matmul(pwb, wl1, goT1, start=False, stop=True)
                k.inc(t.drain(), "pe")
                k.wait(v, "pe")
                for dti in range(DC):
                    pwb = pw[dti // 4][:, (dti % 4) * 128:((dti % 4) + 1) * 128]
                    ot = out_t[:, dti * 128:(dti + 1) * 128]
                    if h == 0:
                        k.inc(v.tensor_copy(ot, pwb), "ve")
                    else:
                        k.inc(v.tensor_tensor(ot, ot, pwb, ALU.add), "ve")
                k.wait(t, "ve")
            k.wait(y, "ve")
            k.dma(y, p2a_i[sup, cc2, :, :], out_t)
            k.wait(t, "ve")
        k.bar(dmas=False)
    A.pop()


def emit_ffn_a(ctx, xin, xh):
    k = ctx["k"]; dram = ctx["dram"]; A = ctx["A"]; pss = ctx["pss"]
    sc = ctx["sc"]; ident = ctx["ident"]; b1c = ctx["b1c"]
    t, v, s, g, y = k.t, k.v, k.s, k.g, k.y
    v1h = dram["v1h"]
    A.push()
    v1c = A.take(4 * D)
    bT = A.take(4 * D)
    fT = A.take(24 * 512)
    w1b = A.take(2 * 8 * 128)
    w2b = A.take(2 * D)
    b2t = A.take(D)
    wsp = A.take(D)
    k.dma(y, b2t, dram["b2"][:, :])

    for sup in range(TOK // 512):
        for sb_ in range(4):
            i = sup * 4 + sb_
            k.dma(y, v1c[:, sb_ * D:(sb_ + 1) * D],
                  v1h[i * 128:(i + 1) * 128, :])
        k.bar(dmas=True)
        for sb_ in range(4):
            emit_ln(k, v1c[:, sb_ * D:(sb_ + 1) * D], wsp,
                    fT[:, 0:D], sc, affine=ctx["ln_aff2"])
            for dch in range(DC):
                pb = pss[dch % 2]
                k.wait(t, "ve")
                k.inc(t.transpose(pb[:, 0:128],
                                  wsp[:, dch * 128:(dch + 1) * 128], ident),
                      "pe")
                k.wait(v, "pe")
                k.inc(v.tensor_copy(
                    bT[:, dch * 512 + sb_ * 128: dch * 512 + (sb_ + 1) * 128],
                    pb[:, 0:128]), "ve")
            k.wait(t, "ve"); k.wait(s, "ve")
        for fc in range(24):
            buf = w1b[:, (fc % 2) * 1024:(fc % 2) * 1024 + 1024]
            k.wait(y, "pe")
            for dch in range(DC):
                k.dma(y, buf[:, dch * 128:(dch + 1) * 128],
                      dram["w1"][dch * 128:(dch + 1) * 128,
                                 fc * 128:(fc + 1) * 128])
            k.wait(t, "dS")
            pf = pss[2 + (fc % 2)]
            for dch in range(DC):
                t.matmul(pf[:, :], buf[:, dch * 128:(dch + 1) * 128],
                         bT[:, dch * 512:(dch + 1) * 512],
                         start=(dch == 0), stop=(dch == DC - 1))
            k.inc(t.drain(), "pe")
            k.wait(s, "pe"); k.wait(v, "pe")
            hb = fT[:, fc * 512:(fc + 1) * 512]
            k.inc(s.activation(out=hb, in_=pf[:, :], func=AF.Identity,
                               bias=b1c[:, fc:fc + 1]), "se")
            k.wait(v, "se")
            k.inc(v.tensor_scalar_mul(wsp[:, 0:512], hb, 0.01), "ve")
            k.inc(v.tensor_tensor(hb, hb, wsp[:, 0:512], ALU.max), "ve")
            k.wait(s, "ve")
            k.inc(s.activation(out=hb, in_=hb, func=AF.Square), "se")
            k.wait(t, "se")
        for sb_ in range(4):
            i = sup * 4 + sb_
            p0, p1 = pss[4], pss[5]
            for half, pp in ((0, p0), (1, p1)):
                for kf in range(24):
                    buf = w2b[:, (kf % 2) * 512:(kf % 2) * 512 + 512]
                    k.wait(y, "pe")
                    k.dma(y, buf, dram["w2"][kf * 128:(kf + 1) * 128,
                                             half * 512:(half + 1) * 512])
                    k.wait(t, "dS")
                    l_ = fT[:, kf * 512 + sb_ * 128: kf * 512 + (sb_ + 1) * 128]
                    k.inc(t.matmul(pp[:, :], l_, buf,
                                   start=(kf == 0), stop=(kf == 23)), "pe")
            k.inc(t.drain(), "pe")
            k.dma(y, wsp, xin[i * 128:(i + 1) * 128, :])
            k.wait(v, "pe"); k.wait(v, "dS")
            k.inc(v.tensor_tensor(wsp[:, 0:512], wsp[:, 0:512], p0[:, :],
                                  ALU.add), "ve")
            k.inc(v.tensor_tensor(wsp[:, 512:1024], wsp[:, 512:1024], p1[:, :],
                                  ALU.add), "ve")
            k.inc(v.tensor_tensor(wsp, wsp, b2t, ALU.add), "ve")
            k.inc(v.tensor_tensor(wsp, wsp, v1c[:, sb_ * D:(sb_ + 1) * D],
                                  ALU.add), "ve")
            k.wait(y, "ve")
            k.dma(y, xh[i * 128:(i + 1) * 128, :], wsp)
            k.wait(t, "ve")
    A.pop()








def emit_moe_sparse(ctx, xin, xout):
    k = ctx["k"]; dram = ctx["dram"]; A = ctx["A"]; pss = ctx["pss"]
    sc = ctx["sc"]; ident = ctx["ident"]
    t, v, s, g, y = k.t, k.v, k.s, k.g, k.y
    WR = 8 * FSL + 4

    A.push()
    for j in range(DC):
        k.dma(y, dram["ag_u_o"][NCORE * TOK:NCORE * TOK + 128,
                                j * 128:(j + 1) * 128], ctx["zc"])
    NJ = NCORE * NCH
    meta = A.take(NJ * 4)
    tokf = A.take(NJ); prf = A.take(NJ); gf_ = A.take(NJ)
    toku = A.take(NJ); pru = A.take(NJ)
    widxf = A.take(NJ); widxu = A.take(NJ)
    jrow = A.take(NCH)
    cnr = A.take(8); padr = A.take(8); cumx = A.take(8); tmp8 = A.take(8)
    tmp40 = A.take(NCH)
    ech = A.take(NCH); iotu = A.take(1); iot = A.take(1); onesr = A.take(128)
    k.dma(y, meta, dram["ag4"].rearrange("(a p) f -> p a f", p=128))
    k.dma(y, jrow[0:1, :], dram["jrow"][:, :])
    g.iota(iotu.bitcast(U32), pattern=[[1, 1]], base=0, channel_multiplier=1)
    v.memset(onesr[0:1, :], 1.0)
    k.bar()
    mview = meta.rearrange("p (a f) -> p a f", f=4)
    v.tensor_copy(tokf, mview[:, :, 0])
    v.tensor_copy(prf, mview[:, :, 1])
    v.tensor_copy(gf_, mview[:, :, 2])
    v.drain()
    v.tensor_copy(iot, iotu.bitcast(U32))
    v.tensor_copy(toku.bitcast(U32), tokf)
    v.tensor_copy(pru.bitcast(U32), prf)
    k.inc(v.drain(), "ve")
    for r in range(NCORE):
        k.wait(y, "ve")
        k.dma(y, cnr[0:1, :], dram["ag_cnt"][r:r + 1, :])
        k.wait(v, "dS")
        v.tensor_scalar(padr[0:1, :], cnr[0:1, :], 63.5, None, ALU.add)
        v.drain()
        v.tensor_scalar(padr[0:1, :], padr[0:1, :], 1.0 / 128.0, None, ALU.mult)
        v.drain()
        v.tensor_scalar(padr[0:1, :], padr[0:1, :], 8388608.0, None, ALU.add)
        v.drain()
        v.tensor_scalar(padr[0:1, :], padr[0:1, :], 8388608.0, None,
                        ALU.subtract)
        v.drain()
        v.tensor_scalar(padr[0:1, :], padr[0:1, :], 128.0, None, ALU.mult)
        v.drain()
        v.tensor_tensor_scan(cumx[0:1, :], padr[0:1, :], padr[0:1, :],
                             0.0, ALU.add, ALU.bypass)
        v.drain()
        v.tensor_tensor(cumx[0:1, :], cumx[0:1, :], padr[0:1, :], ALU.subtract)
        v.drain()
        v.memset(ech[0:1, :], 0.0)
        v.drain()
        for e in range(NE):
            v.tensor_scalar(tmp40[0:1, :], jrow[0:1, :],
                            cumx[0:1, e:e + 1], None, ALU.is_ge)
            v.drain()
            v.tensor_tensor(ech[0:1, :], ech[0:1, :], tmp40[0:1, :],
                            ALU.add)
            v.drain()
        v.tensor_scalar(ech[0:1, :], ech[0:1, :], -1.0, None, ALU.add)
        v.drain()
        k.inc(v.tensor_scalar(ech[0:1, :], ech[0:1, :], 128.0, None,
                              ALU.mult), "ve")
        k.wait(t, "ve")
        pm = pss[6]
        t.matmul(pm[:, 0:NCH], onesr[0:1, :], ech[0:1, :],
                 start=True, stop=True)
        k.inc(t.drain(), "pe")
        k.wait(v, "pe")
        k.inc(v.tensor_scalar(widxf[:, r * NCH:(r + 1) * NCH], pm[:, 0:NCH],
                              iot[:, 0:1], None, ALU.add), "ve")
    v.tensor_copy(widxu.bitcast(U32), widxf)
    k.inc(v.drain(), "ve")
    k.bar()

    wvb = [A.take(WR), A.take(WR)]
    wb_ = [A.take(WR), A.take(WR)]
    w1b = [A.take(4 * D), A.take(4 * D)]
    s1b = [A.take(D), A.take(D)]
    svb = [A.take(FSL), A.take(FSL)]
    ssb = [A.take(FSL), A.take(FSL)]
    xgb = [A.take(D), A.take(D)]
    xTb = [A.take(D), A.take(D)]
    vT = A.take(FSL); yT = A.take(FSL); ygl = A.take(FSL)
    hb = A.take(FSL); hTb = A.take(FSL)
    ott = [A.take(D), A.take(D)]
    breg = g.alloc_register()
    g.reg_mov(breg, 2 * NCORE * TOK - 1)
    pe_par = [0, 0]
    ve_par = [0, 0]
    sc_par = [0, 0]
    for J in range(NCORE * NCH):
        par = J % 2
        k.wait_at(g, "pe", pe_par[par])
        k.wait_at(g, "ve", ve_par[par])
        for dst, srcn, idx in ((xgb[par], "ag_u_o", toku),
                               (wvb[par], "mwv", widxu),
                               (wb_[par], "mw", widxu),
                               (w1b[par], "mw1", widxu),
                               (s1b[par], "ms1g", widxu),
                               (svb[par], "msvg", widxu),
                               (ssb[par], "mssg", widxu)):
            k.inc(g.indirect_dma_start(
                out=dst[:, :], out_offset=None, in_=dram[srcn][:, :],
                in_offset=bass.IndirectOffsetOnAxis(
                    ap=idx[:, J:J + 1].bitcast(U32), axis=0)), "dG", 16)
        k.wait_at(t, "dG", k._cnt["dG"])
        k.wait(t, "ve")
        for dch in range(DC):
            pb = pss[6 + dch // 4]
            t.transpose(pb[:, (dch % 4) * 128:(dch % 4 + 1) * 128],
                        xgb[par][:, dch * 128:(dch + 1) * 128], ident)
        k.inc(t.drain(), "pe")
        k.wait(v, "pe")
        v.tensor_copy(xTb[par][:, 0:512], pss[6][:, :])
        k.inc(v.tensor_copy(xTb[par][:, 512:1024], pss[7][:, :]), "ve")
        k.wait(t, "ve")
        for pj, src_w in ((4, wvb[par]), (5, wb_[par])):
            for kc in range(DC):
                t.matmul(pss[pj][:, :],
                         xTb[par][:, kc * 128:(kc + 1) * 128],
                         src_w[:, kc * FSL:(kc + 1) * FSL],
                         start=(kc == 0), stop=(kc == DC - 1))
        k.inc(t.drain(), "pe")
        k.wait(v, "pe")
        v.tensor_tensor(vT, pss[4][:, :], svb[par], ALU.mult)
        k.inc(v.tensor_tensor(yT, pss[5][:, :], ssb[par], ALU.mult), "ve")
        k.wait(s, "ve")
        k.inc(s.activation(out=ygl, in_=yT, func=AF.Gelu_apprx_tanh), "se")
        k.wait(v, "se")
        k.inc(v.tensor_tensor(hb, vT, ygl, ALU.mult), "ve")
        k.wait(t, "ve")
        for kf in range(4):
            t.transpose(pss[6][:, kf * 128:(kf + 1) * 128],
                        hb[:, kf * 128:(kf + 1) * 128], ident)
        k.inc(t.drain(), "pe")
        k.wait(v, "pe")
        k.inc(v.tensor_copy(hTb, pss[6][:, :]), "ve")
        k.wait(t, "ve")
        for half in range(2):
            ph_ = pss[2 + half]
            for kf in range(4):
                t.matmul(ph_[:, :],
                         hTb[:, kf * 128:(kf + 1) * 128],
                         w1b[par][:, kf * D + half * 512:
                                  kf * D + half * 512 + 512],
                         start=(kf == 0), stop=(kf == 3))
        k.inc(t.drain(), "pe")
        pe_par[par] = k._cnt["pe"]
        k.wait(v, "pe")
        k.wait_at(v, "dG", sc_par[par])
        for half in range(2):
            ph_ = pss[2 + half]
            o_ = ott[par][:, half * 512:(half + 1) * 512]
            v.tensor_tensor(o_, ph_[:, :],
                            s1b[par][:, half * 512:(half + 1) * 512],
                            ALU.mult)
        k.inc(v.tensor_scalar(ott[par][:, :], ott[par][:, :],
                              gf_[:, J:J + 1], None, ALU.mult), "ve")
        ve_par[par] = k._cnt["ve"]
        k.wait(g, "ve")
        k.inc(g.indirect_dma_start(
            out=dram["rs_i"][:, :],
            out_offset=bass.IndirectOffsetOnAxis(ap=pru[:, J:J + 1].bitcast(U32), axis=0),
            in_=ott[par][:, :], in_offset=None,
            bounds_check=breg, oob_is_err=False), "dG", 16)
        sc_par[par] = k._cnt["dG"]
        k.wait(t, "ve")
    A.pop()
    k.bar()

    with allow_ici_all_to_all():
        k.inc(g.collective_compute(
            "ReduceScatter", ALU.add, replica_groups=[list(range(NCORE))],
            ins=[dram["rs_i"].opt()], outs=[dram["rs_o"].opt()]), "cc")
    g.wait_ge(k.sem("cc"), k._cnt["cc"])
    k.bar(dmas=False)

    A.push()
    xa = A.take(D); ra = A.take(D); rb = A.take(D)
    for i in range(NT):
        k.dma(y, xa, xin[i * 128:(i + 1) * 128, :])
        k.dma(y, ra, dram["rs_o"][i * 128:(i + 1) * 128, :])
        k.dma(y, rb, dram["rs_o"][TOK + i * 128:TOK + (i + 1) * 128, :])
        k.wait(v, "dS")
        v.tensor_tensor(xa, xa, ra, ALU.add)
        k.inc(v.tensor_tensor(xa, xa, rb, ALU.add), "ve")
        k.wait(y, "ve")
        k.dma(y, xout[i * 128:(i + 1) * 128, :], xa)
    A.pop()



def emit_attn_b(ctx, xin, xout):
    k = ctx["k"]; dram = ctx["dram"]; A = ctx["A"]; pss = ctx["pss"]
    sc = ctx["sc"]; ident = ctx["ident"]
    t, v, s, g, y = k.t, k.v, k.s, k.g, k.y
    a2a_i, a2a_o = dram["a2a_i"], dram["a2a_o"]
    p2a_i, p2a_o = dram["p2a_i"], dram["p2a_o"]

    A.push()
    xt = A.take(D); ut = A.take(D); aT = A.take(D)
    for i in range(NT):
        sq, sub = i // 4, i % 4
        k.dma(y, xt, xin[i * 128:(i + 1) * 128, :])
        k.wait(v, "dS"); k.wait(s, "dS")
        emit_ln(k, xt, ut, aT, sc, affine=None)
        for dch in range(DC):
            pb = pss[dch % 2]
            k.wait(t, "ve")
            k.inc(t.transpose(pb[:, 0:128], ut[:, dch * 128:(dch + 1) * 128],
                              ident), "pe")
            k.wait(v, "pe")
            k.inc(v.tensor_copy(aT[:, dch * 128:(dch + 1) * 128], pb[:, 0:128]),
                  "ve")
        k.wait(y, "ve")
        for r in (2 * sq, 2 * sq + 1):
            k.dma(y, a2a_i[r, sub, :, :], aT)
    A.pop()
    k.bar()

    with allow_ici_all_to_all():
        k.inc(g.collective_compute(
            "AllToAll", ALU.bypass, replica_groups=[list(range(NCORE))],
            ins=[a2a_i.opt()], outs=[a2a_o.opt()]), "cc")
    g.wait_ge(k.sem("cc"), k._cnt["cc"])
    k.bar(dmas=False)

    A.push()
    qh = A.take(2 * T)
    kh = A.take(2 * T)
    vt_ = A.take(8 * 1024)
    xc = A.take(4 * D)
    rt = A.take(4 * 512)
    wq_ = A.take(8 * 256); wk_ = A.take(8 * 256); wv_ = A.take(8 * 256)
    wo_ = A.take(2 * D)
    rp1 = A.take(512); rp2 = A.take(512)
    dneg = A.take(4 * 512)
    out_t = A.take(D); out_t2 = A.take(D)
    Pt = A.take(512)
    oacc = A.take(128)
    fsc = A.take(16)
    for r_ in range(4):
        k.dma(y, dneg[:, r_ * 512:(r_ + 1) * 512],
              dram["dmneg"][r_ * 128:(r_ + 1) * 128, :])
    k.bar()

    for wave in range(2):
        wc = wave * 256
        for dch in range(DC):
            k.dma(y, wq_[:, dch * 256:(dch + 1) * 256],
                  dram["bwq"][dch * 128:(dch + 1) * 128, wc:wc + 256])
            k.dma(y, wk_[:, dch * 256:(dch + 1) * 256],
                  dram["bwk"][dch * 128:(dch + 1) * 128, wc:wc + 256])
            k.dma(y, wv_[:, dch * 256:(dch + 1) * 256],
                  dram["bwv"][dch * 128:(dch + 1) * 128, wc:wc + 256])
        for kc in range(2):
            k.dma(y, wo_[:, kc * D:(kc + 1) * D],
                  dram["bwo"][wc + kc * 128: wc + (kc + 1) * 128, :])
        k.bar()

        for sup in range(T // 512):
            k.wait(y, "pe"); k.wait(y, "ve"); k.wait(y, "se")
            for sb_ in range(4):
                k.dma(y, xc[:, sb_ * D:(sb_ + 1) * D], a2a_o[sup, sb_, :, :])
            for tb in range(4):
                k.dma(y, rt[:, tb * 512:(tb + 1) * 512],
                      dram[("rch", "rcl", "rsh", "rsl")[tb]]
                      [:, sup * 512:(sup + 1) * 512])
            k.bar(dmas=True)
            CH = rt[:, 0 * 512:1 * 512]
            CL = rt[:, 1 * 512:2 * 512]
            SH = rt[:, 2 * 512:3 * 512]
            SL = rt[:, 3 * 512:4 * 512]
            for (ws, dst, isq) in ((wq_, qh, True), (wk_, kh, False)):
                px = [pss[0], pss[1]]
                for dt_ in (0, 1):
                    for cc2 in range(4):
                        for kc in range(DC):
                            l_ = ws[:, kc * 256 + dt_ * 128:
                                    kc * 256 + (dt_ + 1) * 128]
                            r_ = xc[:, cc2 * D + kc * 128:
                                    cc2 * D + (kc + 1) * 128]
                            t.matmul(px[dt_][:, cc2 * 128:(cc2 + 1) * 128],
                                     l_, r_, start=(kc == 0),
                                     stop=(kc == DC - 1))
                k.inc(t.drain(), "pe")
                k.wait(v, "pe")
                x1, x2 = px[0], px[1]
                k.inc(v.tensor_tensor(rp1, x1[:, :], CH, ALU.mult), "ve")
                v.tensor_tensor(rp2, x2[:, :], SH, ALU.mult)
                v.tensor_tensor(rp1, rp1, rp2, ALU.subtract)
                v.tensor_tensor(rp2, x1[:, :], CL, ALU.mult)
                v.tensor_tensor(rp1, rp1, rp2, ALU.add)
                v.tensor_tensor(rp2, x2[:, :], SL, ALU.mult)
                k.inc(v.tensor_tensor(rp1, rp1, rp2, ALU.subtract), "ve")
                for h in range(4):
                    k.inc(v.tensor_copy(
                        dst[(h % 2) * 64:(h % 2) * 64 + 32,
                            (h // 2) * T + sup * 512:(h // 2) * T + (sup + 1) * 512],
                        rp1[32 * h:32 * h + 32, :]), "ve")
                v.tensor_tensor(rp1, x1[:, :], SH, ALU.mult)
                v.tensor_tensor(rp2, x2[:, :], CH, ALU.mult)
                v.tensor_tensor(rp1, rp1, rp2, ALU.add)
                v.tensor_tensor(rp2, x1[:, :], SL, ALU.mult)
                v.tensor_tensor(rp1, rp1, rp2, ALU.add)
                v.tensor_tensor(rp2, x2[:, :], CL, ALU.mult)
                k.inc(v.tensor_tensor(rp1, rp1, rp2, ALU.add), "ve")
                for h in range(4):
                    k.inc(v.tensor_copy(
                        dst[(h % 2) * 64 + 32:(h % 2) * 64 + 64,
                            (h // 2) * T + sup * 512:(h // 2) * T + (sup + 1) * 512],
                        rp1[32 * h:32 * h + 32, :]), "ve")
                k.wait(t, "ve")
            if DEBUG_DUMP and wave == 0 and sup == 0:
                for j in range(4):
                    k.dma(y, dram["rs_i"][(24 + j) * 128:(25 + j) * 128, :],
                          xc[:, j * D:(j + 1) * D])
                for j in range(2):
                    k.dma(y, dram["rs_i"][(28 + j) * 128:(29 + j) * 128, :],
                          wv_[:, j * D:(j + 1) * D])
                k.wait(y, "dS")
            for cc2 in range(4):
                pv_ = pss[2]
                for kc in range(DC):
                    t.matmul(pv_[:, 0:256],
                             xc[:, cc2 * D + kc * 128: cc2 * D + (kc + 1) * 128],
                             wv_[:, kc * 256:(kc + 1) * 256],
                             start=(kc == 0), stop=(kc == DC - 1))
                k.inc(t.drain(), "pe")
                k.wait(v, "pe")
                k.inc(v.tensor_copy(
                    vt_[:, (sup * 4 + cc2) * 256:(sup * 4 + cc2 + 1) * 256],
                    pv_[:, 0:256]), "ve")
                k.wait(t, "ve")
        k.bar()
        if DEBUG_DUMP and wave == 0:
            for j in range(8):
                k.dma(y, dram["rs_i"][j * 128:(j + 1) * 128, :],
                      vt_[:, j * D:(j + 1) * D])
                k.dma(y, dram["rs_i"][(8 + j) * 128:(9 + j) * 128, :],
                      qh[:, j * D:(j + 1) * D])
                k.dma(y, dram["rs_i"][(16 + j) * 128:(17 + j) * 128, :],
                      kh[:, j * D:(j + 1) * D])
            k.wait(y, "dS")

        mneg = fsc[:, 8:9]
        g.memset(mneg, -1e30)
        k.bar(dmas=False)
        for qt in range(NT * 2):
            for hp in range(2):
                macc = [fsc[:, 0:1], fsc[:, 1:2]]
                lacc = [fsc[:, 2:3], fsc[:, 3:4]]
                nmr = fsc[:, 4:5]
                nnm8 = fsc[:, 5:6]
                alph = fsc[:, 6:7]
                rsum = fsc[:, 7:8]
                rl = fsc[:, 9:10]
                k.wait(v, "se"); k.wait(v, "pe")
                v.memset(oacc[:, :], 0.0)
                for ha in range(2):
                    v.tensor_copy(macc[ha], mneg)
                    v.memset(lacc[ha], 0.0)
                k.inc(v.drain(), "ve")
                nsb = qt // 4 + 1
                for sb_ in range(nsb):
                    diag = (sb_ == qt // 4)
                    for ha in range(2):
                        pS = pss[ha]
                        k.wait(t, "ve")
                        k.inc(t.matmul(
                            pS[:, :],
                            qh[64 * ha:64 * ha + 64,
                               hp * T + qt * 128:hp * T + (qt + 1) * 128],
                            kh[64 * ha:64 * ha + 64,
                               hp * T + sb_ * 512:hp * T + (sb_ + 1) * 512],
                            start=True, stop=True), "pe")
                        k.wait(v, "pe")
                        Ssrc = pS[:, :]
                        if diag:
                            r_ = qt % 4
                            k.inc(v.tensor_tensor(
                                Pt, pS[:, :], dneg[:, r_ * 512:(r_ + 1) * 512],
                                ALU.add), "ve")
                            Ssrc = Pt
                        v.reduce_max(nmr, Ssrc, axis=AX.X)
                        v.drain()
                        v.tensor_tensor(nmr, nmr, macc[ha], ALU.max)
                        v.drain()
                        k.inc(v.tensor_scalar_mul(nnm8, nmr, -0.125), "ve")
                        k.wait(s, "ve")
                        s.activation(out=alph, in_=macc[ha], func=AF.Exp,
                                     scale=0.125, bias=nnm8)
                        s.drain()
                        k.inc(s.activation(out=Pt, in_=Ssrc, func=AF.Exp,
                                           scale=0.125, bias=nnm8,
                                           accum_out=rsum), "se")
                        k.wait(v, "se")
                        v.tensor_copy(macc[ha], nmr)
                        v.drain()
                        v.scalar_tensor_tensor(lacc[ha], lacc[ha], alph,
                                               rsum, ALU.mult, ALU.add)
                        k.inc(v.drain(), "ve")
                        pT = pss[2]
                        for i4 in range(4):
                            k.wait(t, "ve")
                            k.inc(t.transpose(pT[:, i4 * 128:(i4 + 1) * 128],
                                              Pt[:, i4 * 128:(i4 + 1) * 128],
                                              ident), "pe")
                        k.wait(v, "pe")
                        for i4 in range(4):
                            k.inc(v.tensor_copy(
                                Pt[:, i4 * 128:(i4 + 1) * 128],
                                pT[:, i4 * 128:(i4 + 1) * 128]), "ve")
                        k.wait(t, "ve")
                        po = pss[3]
                        for i4 in range(4):
                            t.matmul(po[:, 0:64],
                                     Pt[:, i4 * 128:(i4 + 1) * 128],
                                     vt_[:, (sb_ * 4 + i4) * 256 +
                                         (2 * hp + ha) * 64:
                                         (sb_ * 4 + i4) * 256 +
                                         (2 * hp + ha) * 64 + 64],
                                     start=(i4 == 0), stop=(i4 == 3))
                        k.inc(t.drain(), "pe")
                        k.wait(v, "pe")
                        k.inc(v.scalar_tensor_tensor(
                            oacc[:, ha * 64:(ha + 1) * 64],
                            oacc[:, ha * 64:(ha + 1) * 64],
                            alph, po[:, 0:64], ALU.mult, ALU.add), "ve")
                for ha in range(2):
                    v.reciprocal(rl, lacc[ha])
                    v.drain()
                    k.inc(v.tensor_scalar_mul(oacc[:, ha * 64:(ha + 1) * 64],
                                              oacc[:, ha * 64:(ha + 1) * 64],
                                              rl), "ve")
                k.wait(t, "ve")
                pT = pss[4]
                k.inc(t.transpose(pT[:, 0:128], oacc[:, 0:128], ident), "pe")
                k.wait(v, "pe")
                k.inc(v.tensor_copy(out_t2[:, 0:128], pT[:, 0:128]), "ve")
                k.wait(t, "ve")
                pw = [pss[5], pss[6]]
                for dti in range(DC):
                    pwb = pw[dti // 4][:, (dti % 4) * 128:((dti % 4) + 1) * 128]
                    t.matmul(pwb, wo_[:, hp * D + dti * 128:
                                      hp * D + (dti + 1) * 128],
                             out_t2[:, 0:128], start=True, stop=True)
                k.inc(t.drain(), "pe")
                k.wait(v, "pe")
                for dti in range(DC):
                    pwb = pw[dti // 4][:, (dti % 4) * 128:((dti % 4) + 1) * 128]
                    ot = out_t[:, dti * 128:(dti + 1) * 128]
                    if hp == 0:
                        k.inc(v.tensor_copy(ot, pwb), "ve")
                    else:
                        k.inc(v.tensor_tensor(ot, ot, pwb, ALU.add), "ve")
                k.wait(t, "ve")
            sup_, cc2_ = qt // 4, qt % 4
            if wave == 0:
                k.wait(y, "ve")
                k.dma(y, p2a_i[sup_, cc2_, :, :], out_t)
            else:
                k.wait(y, "pe")
                k.dma(y, out_t2, p2a_i[sup_, cc2_, :, :])
                k.wait(v, "dS")
                k.inc(v.tensor_tensor(out_t, out_t, out_t2, ALU.add), "ve")
                k.wait(y, "ve")
                k.dma(y, p2a_i[sup_, cc2_, :, :], out_t)
    A.pop()
    k.bar()

    with allow_ici_all_to_all():
        k.inc(g.collective_compute(
            "AllToAll", ALU.bypass, replica_groups=[list(range(NCORE))],
            ins=[p2a_i.opt()], outs=[p2a_o.opt()]), "cc")
    g.wait_ge(k.sem("cc"), k._cnt["cc"])
    k.bar(dmas=False)

    A.push()
    pa = A.take(D); pb_ = A.take(D); xtl = A.take(D); vt2 = A.take(D)
    for i in range(NT):
        sq, sub = i // 4, i % 4
        k.dma(y, pa, p2a_o[2 * sq, sub, :, :])
        k.dma(y, pb_, p2a_o[2 * sq + 1, sub, :, :])
        k.dma(y, xtl, xin[i * 128:(i + 1) * 128, :])
        k.wait(v, "dS")
        k.inc(v.tensor_tensor(pa, pa, pb_, ALU.add), "ve")
        for dch in range(DC):
            pb = pss[dch % 2]
            k.wait(t, "ve")
            k.inc(t.transpose(pb[:, 0:128], pa[:, dch * 128:(dch + 1) * 128],
                              ident), "pe")
            k.wait(v, "pe")
            k.inc(v.tensor_tensor(vt2[:, dch * 128:(dch + 1) * 128],
                                  xtl[:, dch * 128:(dch + 1) * 128],
                                  pb[:, 0:128], ALU.add), "ve")
        k.wait(y, "ve")
        k.dma(y, xout[i * 128:(i + 1) * 128, :], vt2)
    A.pop()



def emit_moe_sort(ctx, A, SEL1c, SEL2c, G1c, G2c, cntrow):
    k = ctx["k"]; dram = ctx["dram"]; pss = ctx["pss"]; sc = ctx["sc"]
    t, v, s, g, y = k.t, k.v, k.s, k.g, k.y
    lt = A.take(128)
    onesr = A.take(128)
    tokc = A.take(NT)
    prc = A.take(2 * NT)
    s12 = A.take(8)
    pre = A.take(8)
    brow = A.take(8); base = A.take(8); runt = A.take(8); pad = A.take(8)
    sl1 = A.take(1); sl2 = A.take(1)
    su = A.take(2)
    stg4 = A.take(8)
    dflt = A.take(8)
    sbreg = g.alloc_register()
    g.reg_mov(sbreg, NSLOT - 1)
    k.dma(y, lt, dram["ltstrict"][:, :])
    k.dma(y, tokc, dram["tokid"][:, :])
    k.dma(y, prc, dram["pairrow"][:, :])
    v.memset(stg4, 0.0)
    v.memset(onesr[0:1, :], 1.0)
    v.memset(runt[0:1, :], 0.0)
    k.inc(v.drain(), "ve")
    k.wait(v, "ve")
    v.tensor_scalar(pad[0:1, :], cntrow[0:1, :], 63.5, None, ALU.add)
    v.drain()
    v.tensor_scalar(pad[0:1, :], pad[0:1, :], 1.0 / 128.0, None, ALU.mult)
    v.drain()
    v.tensor_scalar(pad[0:1, :], pad[0:1, :], 8388608.0, None, ALU.add)
    v.drain()
    v.tensor_scalar(pad[0:1, :], pad[0:1, :], 8388608.0, None, ALU.subtract)
    v.drain()
    v.tensor_scalar(pad[0:1, :], pad[0:1, :], 128.0, None, ALU.mult)
    v.drain()
    v.tensor_tensor_scan(base[0:1, :], pad[0:1, :], pad[0:1, :],
                         0.0, ALU.add, ALU.bypass)
    v.drain()
    v.tensor_tensor(base[0:1, :], base[0:1, :], pad[0:1, :], ALU.subtract)
    k.inc(v.drain(), "ve")
    k.wait(y, "ve")
    k.dma(y, dram["cnt_i"][:, :], cntrow[0:1, :])
    g.memset(dflt[:, 0:1], float(NCORE * TOK))
    g.memset(dflt[:, 1:2], 1048576.0)
    g.memset(dflt[:, 2:4], 0.0)
    k.inc(g.drain(), "ge")
    k.wait(y, "ge")
    for a_ in range(NCH):
        k.dma(y, dram["srt4"][a_ * 128:(a_ + 1) * 128, :], dflt[:, 0:4])
    k.bar(dmas=True)
    for i in range(NT):
        se1 = SEL1c[:, i * 8:(i + 1) * 8]
        se2 = SEL2c[:, i * 8:(i + 1) * 8]
        k.wait(v, "ge")
        v.tensor_tensor(s12, se1, se2, ALU.add)
        v.tensor_tensor(brow[0:1, :], runt[0:1, :], base[0:1, :], ALU.add)
        k.inc(v.drain(), "ve")
        k.wait(t, "ve")
        pp = pss[6]
        t.matmul(pp[:, 0:8], lt, s12, start=True, stop=False)
        t.matmul(pp[:, 0:8], onesr[0:1, :], brow[0:1, :],
                 start=False, stop=True)
        pc_ = pss[7]
        t.matmul(pc_[0:1, 0:8], sc[:, 2:3], s12, start=True, stop=True)
        k.inc(t.drain(), "pe")
        k.wait(v, "pe")
        v.tensor_copy(pre, pp[:, 0:8])
        v.tensor_tensor(runt[0:1, :], runt[0:1, :], pc_[0:1, 0:8], ALU.add)
        v.drain()
        v.tensor_tensor(s12, se1, pre, ALU.mult)
        v.drain()
        v.reduce_sum(sl1, s12, axis=AX.X)
        v.drain()
        v.tensor_tensor(s12, se2, pre, ALU.mult)
        v.drain()
        v.reduce_sum(sl2, s12, axis=AX.X)
        v.drain()
        k.wait_at(v, "dG", k._cnt["dG"])
        v.tensor_copy(su[:, 0:1].bitcast(U32), sl1)
        v.tensor_copy(su[:, 1:2].bitcast(U32), sl2)
        v.drain()
        for kk_, gc in ((0, G1c), (1, G2c)):
            st4 = stg4[:, kk_ * 4:(kk_ + 1) * 4]
            v.tensor_copy(st4[:, 0:1], tokc[:, i:i + 1])
            v.tensor_copy(st4[:, 1:2], prc[:, kk_ * NT + i:kk_ * NT + i + 1])
            v.tensor_copy(st4[:, 2:3], gc[:, i:i + 1])
        k.inc(v.drain(), "ve")
        k.wait(t, "ve")
        k.wait(g, "ve")
        for kk_ in (0, 1):
            idx = su[:, kk_:kk_ + 1].bitcast(U32)
            k.inc(g.indirect_dma_start(
                out=dram["srt4"][:, :],
                out_offset=bass.IndirectOffsetOnAxis(ap=idx, axis=0),
                in_=stg4[:, kk_ * 4:(kk_ + 1) * 4], in_offset=None,
                bounds_check=sbreg, oob_is_err=False), "dG", 16)
    k.bar()


def emit_moe(ctx, xin, xout):
    k = ctx["k"]; dram = ctx["dram"]; A = ctx["A"]; pss = ctx["pss"]
    sc = ctx["sc"]; ident = ctx["ident"]; zc = ctx["zc"]
    t, v, s, g, y = k.t, k.v, k.s, k.g, k.y
    GTOK = NCORE * TOK

    A.push()
    xt = A.take(D); ut = A.take(D); wsp = A.take(D); uT = A.take(D)
    rws = A.take(64)
    lg = A.take(8)
    pr = A.take(8); e1 = A.take(8); pm_ = A.take(8); sels = A.take(8)
    c1 = A.take(8); wmt = A.take(8)
    rsc = A.take(8)
    SEL1c = A.take(NT * 8); SEL2c = A.take(NT * 8)
    G1c = A.take(NT); G2c = A.take(NT)
    cntrow = A.take(8)
    for dch in range(DC):
        k.dma(y, rws[:, dch * 8:(dch + 1) * 8],
              dram["rw"][dch * 128:(dch + 1) * 128, :])
    v.memset(cntrow[0:1, :], 0.0)
    k.bar()
    for i in range(NT):
        k.dma(y, xt, xin[i * 128:(i + 1) * 128, :])
        k.wait(v, "dS"); k.wait(s, "dS")
        emit_ln(k, xt, ut, wsp, sc, affine=None)
        k.wait(y, "ve")
        k.dma(y, dram["ag_u_i"][i * 128:(i + 1) * 128, :], ut)
        for dch in range(DC):
            pb = pss[dch % 2]
            k.wait(t, "ve")
            k.inc(t.transpose(pb[:, 0:128], ut[:, dch * 128:(dch + 1) * 128],
                              ident), "pe")
            k.wait(v, "pe")
            k.inc(v.tensor_copy(uT[:, dch * 128:(dch + 1) * 128], pb[:, 0:128]),
                  "ve")
        k.wait(t, "ve")
        pl = pss[2]
        for dch in range(DC):
            t.matmul(pl[:, 0:8], uT[:, dch * 128:(dch + 1) * 128],
                     rws[:, dch * 8:(dch + 1) * 8],
                     start=(dch == 0), stop=(dch == DC - 1))
        k.inc(t.drain(), "pe")
        k.wait(v, "pe")
        k.inc(v.tensor_copy(lg, pl[:, 0:8]), "ve")
        rmax = rsc[:, 0:1]; nmax = rsc[:, 1:2]; ssum = rsc[:, 2:3]
        rs_ = rsc[:, 3:4]; m1 = rsc[:, 4:5]; m2 = rsc[:, 5:6]
        v.reduce_max(rmax, lg, axis=AX.X)
        v.drain()
        k.inc(v.tensor_scalar_mul(nmax, rmax, -1.0), "ve")
        k.wait(s, "ve")
        s.activation(out=pr, in_=lg, func=AF.Exp, bias=nmax, accum_out=ssum)
        k.inc(s.drain(), "se")
        k.wait(v, "se")
        v.reciprocal(rs_, ssum)
        v.drain()
        v.tensor_scalar_mul(pr, pr, rs_)
        v.drain()
        v.reduce_max(m1, pr, axis=AX.X)
        v.drain()
        v.tensor_scalar(e1, pr, m1, None, ALU.is_ge)
        v.drain()
        v.tensor_tensor_scan(c1, e1, zc[:, 0:8], 0.0, ALU.add, ALU.add)
        v.drain()
        v.tensor_scalar(c1, c1, 1.0, None, ALU.is_le)
        v.drain()
        v.tensor_tensor(sels, e1, c1, ALU.mult)
        v.drain()
        if MOE_SPARSE:
            v.tensor_copy(SEL1c[:, i * 8:(i + 1) * 8], sels)
            v.tensor_copy(G1c[:, i:i + 1], m1)
            v.drain()
        v.tensor_tensor(pm_, pr, sels, ALU.mult)
        v.drain()
        v.tensor_tensor(pm_, pr, pm_, ALU.subtract)
        v.drain()
        v.reduce_max(m2, pm_, axis=AX.X)
        v.drain()
        v.tensor_scalar(e1, pm_, m2, None, ALU.is_ge)
        v.drain()
        v.tensor_tensor_scan(c1, e1, zc[:, 0:8], 0.0, ALU.add, ALU.add)
        v.drain()
        v.tensor_scalar(c1, c1, 1.0, None, ALU.is_le)
        v.drain()
        v.tensor_tensor(e1, e1, c1, ALU.mult)
        v.drain()
        if MOE_SPARSE:
            v.tensor_copy(SEL2c[:, i * 8:(i + 1) * 8], e1)
            v.tensor_copy(G2c[:, i:i + 1], m2)
            v.drain()
        v.tensor_tensor(sels, sels, e1, ALU.add)
        v.drain()
        k.inc(v.tensor_tensor(wmt, pr, sels, ALU.mult), "ve")
        if MOE_SPARSE:
            k.wait(t, "ve")
            pc_ = pss[5]
            t.matmul(pc_[0:1, 0:8], sc[:, 2:3], sels, start=True, stop=True)
            k.inc(t.drain(), "pe")
            k.wait(v, "pe")
            k.inc(v.tensor_tensor(cntrow[0:1, :], cntrow[0:1, :],
                                  pc_[0:1, 0:8], ALU.add), "ve")
            v.drain()
        k.wait(y, "ve")
        if not MOE_SPARSE:
            k.dma(y, dram["ag_w_i"][i * 128:(i + 1) * 128, :], wmt)
    if MOE_SPARSE:
        emit_moe_sort(ctx, A, SEL1c, SEL2c, G1c, G2c, cntrow)
    A.pop()
    k.bar()

    with allow_ici_all_to_all():
        k.inc(g.collective_compute(
            "AllGather", ALU.bypass, replica_groups=[list(range(NCORE))],
            ins=[dram["ag_u_i"].opt()],
            outs=[dram["ag_u_o"][0:NCORE * TOK, :].opt()]), "cc")
        if MOE_SPARSE:
            for a_, b_ in (("srt4", "ag4"), ("cnt_i", "ag_cnt")):
                k.inc(g.collective_compute(
                    "AllGather", ALU.bypass,
                    replica_groups=[list(range(NCORE))],
                    ins=[dram[a_].opt()], outs=[dram[b_].opt()]), "cc")
        else:
            k.inc(g.collective_compute(
                "AllGather", ALU.bypass, replica_groups=[list(range(NCORE))],
                ins=[dram["ag_w_i"].opt()], outs=[dram["ag_w_o"].opt()]), "cc")
    g.wait_ge(k.sem("cc"), k._cnt["cc"])
    k.bar(dmas=False)
    if MOE_SPARSE:
        emit_moe_sparse(ctx, xin, xout)
        return

    A.push()
    msv = A.take(NE * 4); ms_ = A.take(NE * 4)
    k.dma(y, msv, dram["msv"][:, :])
    k.dma(y, ms_, dram["ms"][:, :])
    ctx["msv"], ctx["ms"] = msv, ms_
    k.bar()
    xT = A.take(8 * 512)
    vT = A.take(4 * 512)
    yT = A.take(4 * 512)
    acc = A.take(4 * D)
    wm = A.take(32)
    xtl = A.take(D)
    wvb = [A.take(8 * 512), A.take(8 * 512)]
    wb_ = [A.take(8 * 512), A.take(8 * 512)]
    w1b = [A.take(4 * D), A.take(4 * D)]
    ms1b = [A.take(D), A.take(D)]
    pe_at_parity = [0, 0]
    ve_at_parity = [0, 0]
    for scg in range(GTOK // 512):
        for sb_ in range(4):
            gi = scg * 4 + sb_
            k.dma(y, xtl, dram["ag_u_o"][gi * 128:(gi + 1) * 128, :])
            k.dma(y, wm[:, sb_ * 8:(sb_ + 1) * 8],
                  dram["ag_w_o"][gi * 128:(gi + 1) * 128, :])
            k.wait(t, "dS")
            for dch in range(DC):
                pb = pss[6 + dch % 2]
                k.wait(t, "ve")
                k.inc(t.transpose(pb[:, 0:128], xtl[:, dch * 128:(dch + 1) * 128],
                                  ident), "pe")
                k.wait(v, "pe")
                k.inc(v.tensor_copy(xT[:, dch * 512 + sb_ * 128:
                                       dch * 512 + (sb_ + 1) * 128],
                                    pb[:, 0:128]), "ve")
            k.wait(t, "ve")
            k.wait(y, "ve")
        for e in range(NE):
            par = e % 2
            k.wait_at(y, "pe", pe_at_parity[par])
            k.wait_at(y, "ve", ve_at_parity[par])
            k.dma(y, wvb[par], dram["mwv"][e * 128:(e + 1) * 128, :])
            k.dma(y, wb_[par], dram["mw"][e * 128:(e + 1) * 128, :])
            k.dma(y, w1b[par], dram["mw1"][e * 128:(e + 1) * 128, :])
            k.dma(y, ms1b[par], dram["ms1"][:, e * D:(e + 1) * D])
            k.wait(t, "dS")
            for src_w, dst, scl, fn in (
                    (wvb[par], vT, "msv", AF.Identity),
                    (wb_[par], yT, "ms", AF.Gelu_apprx_tanh)):
                for ft in range(4):
                    pf = pss[ft % 2]
                    for kc in range(DC):
                        t.matmul(pf[:, :],
                                 src_w[:, kc * 512 + ft * 128:
                                       kc * 512 + (ft + 1) * 128],
                                 xT[:, kc * 512:(kc + 1) * 512],
                                 start=(kc == 0), stop=(kc == DC - 1))
                    k.inc(t.drain(), "pe")
                    k.wait(s, "pe")
                    k.inc(s.activation(out=dst[:, ft * 512:(ft + 1) * 512],
                                       in_=pf[:, :], func=fn,
                                       scale=ctx[scl][:, e * 4 + ft:e * 4 + ft + 1]),
                          "se")
                    k.wait(t, "se")
            k.wait(v, "se")
            for ft in range(4):
                k.inc(v.tensor_tensor(vT[:, ft * 512:(ft + 1) * 512],
                                      vT[:, ft * 512:(ft + 1) * 512],
                                      yT[:, ft * 512:(ft + 1) * 512],
                                      ALU.mult), "ve")
            k.wait(t, "ve")
            for sb_ in range(4):
                for half in range(2):
                    ph_ = pss[2 + half]
                    for kf in range(4):
                        t.matmul(ph_[:, :],
                                 vT[:, kf * 512 + sb_ * 128:
                                    kf * 512 + (sb_ + 1) * 128],
                                 w1b[par][:, kf * D + half * 512:
                                          kf * D + half * 512 + 512],
                                 start=(kf == 0), stop=(kf == 3))
                k.inc(t.drain(), "pe")
                if sb_ == 3:
                    pe_at_parity[par] = k._cnt["pe"]
                k.wait(v, "pe")
                gate = wm[:, sb_ * 8 + e: sb_ * 8 + e + 1]
                for half in range(2):
                    ph_ = pss[2 + half]
                    tmp = xtl[:, half * 512:(half + 1) * 512]
                    a_ = acc[:, sb_ * D + half * 512: sb_ * D + half * 512 + 512]
                    v.tensor_tensor(tmp, ph_[:, :],
                                    ms1b[par][:, half * 512:(half + 1) * 512],
                                    ALU.mult)
                    if e == 0:
                        k.inc(v.tensor_scalar_mul(a_, tmp, gate), "ve")
                    else:
                        k.inc(v.scalar_tensor_tensor(a_, tmp, gate, a_,
                                                     ALU.mult, ALU.add), "ve")
                k.wait(t, "ve")
            ve_at_parity[par] = k._cnt["ve"]
        k.wait(y, "ve")
        for sb_ in range(4):
            gi = scg * 4 + sb_
            k.dma(y, dram["rs_i"][gi * 128:(gi + 1) * 128, :],
                  acc[:, sb_ * D:(sb_ + 1) * D])
    A.pop()
    k.bar()

    with allow_ici_all_to_all():
        k.inc(g.collective_compute(
            "ReduceScatter", ALU.add, replica_groups=[list(range(NCORE))],
            ins=[dram["rs_i"][0:NCORE * TOK, :].opt()],
            outs=[dram["rs_o"][0:TOK, :].opt()]), "cc")
    g.wait_ge(k.sem("cc"), k._cnt["cc"])
    k.bar(dmas=False)

    A.push()
    xa = A.take(D); ra = A.take(D)
    for i in range(NT):
        k.dma(y, xa, xin[i * 128:(i + 1) * 128, :])
        k.dma(y, ra, dram["rs_o"][i * 128:(i + 1) * 128, :])
        k.wait(v, "dS")
        k.inc(v.tensor_tensor(xa, xa, ra, ALU.add), "ve")
        k.wait(y, "ve")
        k.dma(y, xout[i * 128:(i + 1) * 128, :], xa)
    A.pop()



def kernel(**inputs):
    nc = build_program()
    in_maps = [make_in_map(c, inputs) for c in range(NCORE)]
    res = run_bass_kernel_spmd(nc, in_maps, core_ids=list(range(NCORE)))
    out = np.zeros((B, T, D), dtype=np.float32)
    for c in range(NCORE):
        yc = np.asarray(res.results[c]["y"]).reshape(B, SLICE, D)
        out[:, c * SLICE:(c + 1) * SLICE, :] = yc
    return out


# version 04 (final)
# speedup vs baseline: 1.5997x; 1.1775x over previous
import numpy as np
from contextlib import ExitStack

import concourse.bass as bass
import concourse.mybir as mybir
from concourse.bass_utils import run_bass_kernel_spmd
from concourse.aot_env import allow_ici_all_to_all

F32 = mybir.dt.float32
U32 = mybir.dt.uint32
AF = mybir.ActivationFunctionType
ALU = mybir.AluOpType
AX = mybir.AxisListType

NCORE = 8
B, T, D = 4, 4096, 1024
SLICE = 512
TOK = B * SLICE
NT = TOK // 128
DC = D // 128
EPS = 1e-5

NE, F = 8, 4096
FSL = F // NCORE

SUBLAYERS = (1, 2, 3, 4)
MOE_SPARSE = True
NCH = 40
NSLOT = NCH * 128
STAGE = 99
DEBUG_DUMP = ()



def _repl(v, p=128):
    v = np.ascontiguousarray(np.asarray(v), dtype=np.float32)
    return np.broadcast_to(v.reshape(1, -1), (p, v.size)).copy()


def _colsplit(v, tile=128):
    v = np.asarray(v, dtype=np.float32)
    return np.ascontiguousarray(v.reshape(-1, tile).T)


def _rope_tables():
    half = 32
    inv = 10000.0 ** (-(np.arange(half, dtype=np.float64)) * 2.0 / 64.0)
    ang = np.arange(T, dtype=np.float64)[None, :] * inv[:, None]
    out = []
    for m in (np.cos(ang), np.sin(ang)):
        hi = m.astype(np.float32)
        lo = (m - hi.astype(np.float64)).astype(np.float32)
        out.append(np.tile(hi, (4, 1)))
        out.append(np.tile(lo, (4, 1)))
    return out


def _rope_perm():
    perm = np.zeros(512, dtype=np.int64)
    for h in range(8):
        for j in range(64):
            p = (h // 4) * 256 + (j % 2) * 128 + (h % 4) * 32 + j // 2
            perm[p] = h * 64 + j
    return perm


def make_in_map(c, inp):
    hg = c % 2
    hb = c % 2
    d = {}
    x = np.asarray(inp["L0_attn_a_x"], dtype=np.float32)
    d["x0"] = np.ascontiguousarray(
        x[:, c * SLICE:(c + 1) * SLICE, :].reshape(TOK, D))
    for nm, key in (("wq", "L0_attn_a_Wq"), ("wk", "L0_attn_a_Wk"),
                    ("wv", "L0_attn_a_Wv"), ("wg", "L0_attn_a_Wg")):
        W = np.asarray(inp[key], dtype=np.float32)
        d[nm] = np.ascontiguousarray(W[:, hg * 512:(hg + 1) * 512])
    d["wgk1"] = np.ascontiguousarray(np.asarray(inp["L0_attn_a_Wgk1"], np.float32))
    d["wgk2"] = np.ascontiguousarray(
        np.asarray(inp["L0_attn_a_Wgk2"], np.float32)[:, hg * 512:(hg + 1) * 512])
    d["bgk2"] = _colsplit(np.asarray(inp["L0_attn_a_bgk2"], np.float32)
                          [hg * 512:(hg + 1) * 512])
    d["rmsw"] = _repl(inp["L0_attn_a_rms_w"])
    d["wo"] = np.ascontiguousarray(
        np.asarray(inp["L0_attn_a_Wo"], np.float32)[hg * 512:(hg + 1) * 512, :])
    d["ln1w"] = _repl(inp["L0_attn_a_ln1_w"]); d["ln1b"] = _repl(inp["L0_attn_a_ln1_b"])
    d["ln2w"] = _repl(inp["L0_attn_a_ln2_w"]); d["ln2b"] = _repl(inp["L0_attn_a_ln2_b"])
    d["w1"] = np.ascontiguousarray(np.asarray(inp["L0_attn_a_W1"], np.float32))
    d["b1c"] = _colsplit(inp["L0_attn_a_b1"])
    d["w2"] = np.ascontiguousarray(np.asarray(inp["L0_attn_a_W2"], np.float32))
    d["b2"] = _repl(inp["L0_attn_a_b2"])
    fs = slice(c * FSL, (c + 1) * FSL)
    wv = np.asarray(inp["L0_ffn_wv"], np.float32)[:, :, fs]
    w_ = np.asarray(inp["L0_ffn_w"], np.float32)[:, :, fs]
    w1m = np.asarray(inp["L0_ffn_w1"], np.float32)[:, fs, :]
    sv = np.asarray(inp["L0_ffn_sv"], np.float32)[:, 0, fs]
    s_ = np.asarray(inp["L0_ffn_s"], np.float32)[:, 0, fs]
    mwv = wv.reshape(NE, 8, 128, FSL).transpose(0, 2, 1, 3).reshape(NE * 128, 8 * FSL)
    mw = w_.reshape(NE, 8, 128, FSL).transpose(0, 2, 1, 3).reshape(NE * 128, 8 * FSL)
    svc = sv.reshape(NE, 4, 128).transpose(0, 2, 1).reshape(NE * 128, 4)
    ssc = s_.reshape(NE, 4, 128).transpose(0, 2, 1).reshape(NE * 128, 4)
    d["mwv"] = np.ascontiguousarray(np.concatenate([mwv, svc], axis=1))
    d["mw"] = np.ascontiguousarray(np.concatenate([mw, ssc], axis=1))
    d["mw1"] = np.ascontiguousarray(
        w1m.reshape(NE, 4, 128, D).transpose(0, 2, 1, 3).reshape(NE * 128, 4 * D))
    d["msv"] = np.concatenate([_colsplit(sv[e]) for e in range(NE)], axis=1)
    d["ms"] = np.concatenate([_colsplit(s_[e]) for e in range(NE)], axis=1)
    s1 = np.asarray(inp["L0_ffn_s1"], np.float32)[:, 0, :]
    d["ms1"] = np.concatenate([_repl(s1[e]) for e in range(NE)], axis=1)
    d["ms1g"] = np.ascontiguousarray(
        np.repeat(s1, 128, axis=0))
    d["msvg"] = np.ascontiguousarray(np.repeat(sv, 128, axis=0))
    d["mssg"] = np.ascontiguousarray(np.repeat(s_, 128, axis=0))
    d["rw"] = np.ascontiguousarray(np.asarray(inp["L0_ffn_router_w"], np.float32))
    p_ = np.arange(128, dtype=np.float32)[:, None]
    d["tokid"] = np.ascontiguousarray(
        (c * TOK + np.arange(NT, dtype=np.float32)[None, :] * 128 + p_))
    pr = np.zeros((128, 2 * NT), dtype=np.float32)
    for kk_ in range(2):
        for i_ in range(NT):
            pr[:, kk_ * NT + i_] = c * 2 * TOK + kk_ * TOK + i_ * 128 + p_[:, 0]
    d["pairrow"] = np.ascontiguousarray(pr)
    d["jrow"] = np.ascontiguousarray(
        (np.arange(NCH, dtype=np.float32) * 128)[None, :])
    lt = (np.arange(128)[:, None] < np.arange(128)[None, :]).astype(np.float32)
    d["ltstrict"] = np.ascontiguousarray(lt)
    d["iota8"] = np.broadcast_to(np.arange(8, dtype=np.float32), (128, 8)).copy()
    perm = _rope_perm()
    rows = hb * 512 + np.arange(512)
    d["bwq"] = np.ascontiguousarray(
        np.asarray(inp["L0_attn_b_Wq"], np.float32)[rows][perm].T)
    d["bwk"] = np.ascontiguousarray(
        np.asarray(inp["L0_attn_b_Wk"], np.float32)[rows][perm].T)
    d["bwv"] = np.ascontiguousarray(
        np.asarray(inp["L0_attn_b_Wv"], np.float32)[rows].T)
    d["bwo"] = np.ascontiguousarray(
        np.asarray(inp["L0_attn_b_Wo"], np.float32)[:, rows].T)
    ch, cl, sh, sl = _rope_tables()
    d["rch"], d["rcl"], d["rsh"], d["rsl"] = ch, cl, sh, sl
    d["ident"] = np.eye(128, dtype=np.float32)
    st = np.arange(128)
    d["trimask"] = (st[:, None] <= st[None, :]).astype(np.float32)
    dm = np.zeros((4, 128, 512), dtype=np.float32)
    for r in range(4):
        dm[r] = (np.arange(512)[None, :] <= (r * 128 + st)[:, None])
    d["diagmask"] = np.ascontiguousarray(dm.reshape(512, 512))
    d["dmneg"] = np.ascontiguousarray(((dm - 1.0) * 1e30).reshape(512, 512))
    return d



class KB:
    def __init__(self, nc, es):
        self.nc = nc
        self.t, self.v, self.s, self.g, self.y = \
            nc.tensor, nc.vector, nc.scalar, nc.gpsimd, nc.sync
        self.engines = [self.t, self.v, self.s, self.g, self.y]
        self._sems, self._cnt = {}, {}
        for nm in ("bar", "dS", "dG", "dC", "dV", "cc",
                   "pe", "ve", "se", "ge"):
            self._sems[nm] = es.enter_context(nc.semaphore(nm))
            self._cnt[nm] = 0
        self._qmap = {id(self.y): "dS", id(self.g): "dG",
                      id(self.s): "dC", id(self.v): "dV"}

    def sem(self, nm):
        return self._sems[nm]

    def inc(self, instr, nm, k=1):
        if nm in ("pe", "ve", "se", "ge"):
            eng = {"pe": self.t, "ve": self.v, "se": self.s,
                   "ge": self.g}[nm]
            eng.drain().then_inc(self._sems[nm], k)
        else:
            instr.then_inc(self._sems[nm], k)
        self._cnt[nm] += k
        return instr

    def dma(self, eng, out, in_, **kw):
        return self.inc(eng.dma_start(out=out, in_=in_, **kw),
                        self._qmap[id(eng)], 16)

    def wait_at(self, eng, nm, val):
        if val:
            eng.wait_ge(self._sems[nm], val)

    def wait(self, eng, *names):
        for nm in names:
            if self._cnt[nm]:
                eng.wait_ge(self._sems[nm], self._cnt[nm])

    def bar(self, dmas=True):
        b = self._sems["bar"]
        for e in self.engines:
            e.drain().then_inc(b, 1)
        self._cnt["bar"] += len(self.engines)
        tgt = self._cnt["bar"]
        for e in self.engines:
            e.wait_ge(b, tgt)
            if dmas:
                self.wait(e, "dS", "dG", "dC", "dV")


class Arena:
    def __init__(self, tens, ncols):
        self.t = tens
        self.n = ncols
        self.off = 0
        self.marks = []

    def take(self, cols):
        assert self.off + cols <= self.n, (self.off, cols, self.n)
        ap = self.t[:, self.off:self.off + cols]
        self.off += cols
        return ap

    def push(self):
        self.marks.append(self.off)

    def pop(self):
        self.off = self.marks.pop()


def emit_ln(k, src, dst, wsp, sc, affine=None, n_free=D):
    v, s = k.v, k.s
    eps = sc[:, 0:1]
    nmu = sc[:, 4:5]; ssq = sc[:, 5:6]; rst = sc[:, 6:7]
    inv = 1.0 / n_free
    k.wait(v, "se")
    v.reduce_sum(nmu, src, axis=AX.X)
    v.drain()
    k.inc(v.tensor_scalar_mul(nmu, nmu, -inv), "ve")
    k.wait(s, "ve")
    s.activation(out=wsp, in_=src, func=AF.Identity, bias=nmu)
    s.activation(out=dst, in_=wsp, func=AF.Square, accum_out=ssq)
    s.drain()
    s.activation(out=ssq, in_=ssq, func=AF.Identity, scale=inv, bias=eps)
    k.inc(s.activation(out=ssq, in_=ssq, func=AF.Sqrt), "se")
    k.wait(v, "se")
    k.inc(v.reciprocal(rst, ssq), "ve")
    if affine is None:
        k.inc(v.tensor_scalar_mul(dst, wsp, rst), "ve")
    else:
        w_ap, b_ap = affine
        k.inc(v.scalar_tensor_tensor(dst, wsp, rst, w_ap, ALU.mult, ALU.mult),
              "ve")
        if b_ap is not None:
            k.inc(v.tensor_tensor(dst, dst, b_ap, ALU.add), "ve")



def build_program(sublayers=None):
    sublayers = SUBLAYERS if sublayers is None else sublayers
    nc = bass.Bass("TRN2", target_bir_lowering=False, debug=False,
                   detect_race_conditions=False)
    dram = {}

    def din(name, shape, dt=F32):
        dram[name] = nc.dram_tensor(name, shape, dt, kind="ExternalInput").ap()

    def dint(name, shape, dt=F32):
        dram[name] = nc.dram_tensor(name, shape, dt).ap()

    for nm, sh in (("x0", [TOK, D]),
                   ("wq", [D, 512]), ("wk", [D, 512]), ("wv", [D, 512]),
                   ("wg", [D, 512]), ("wgk1", [D, 16]), ("wgk2", [16, 512]),
                   ("bgk2", [128, 4]), ("rmsw", [128, 256]), ("wo", [512, D]),
                   ("ln1w", [128, D]), ("ln1b", [128, D]),
                   ("ln2w", [128, D]), ("ln2b", [128, D]),
                   ("w1", [D, 3072]), ("b1c", [128, 24]),
                   ("w2", [3072, D]), ("b2", [128, D]),
                   ("mwv", [NE * 128, 8 * FSL + 4]), ("mw", [NE * 128, 8 * FSL + 4]),
                   ("mw1", [NE * 128, 4 * D]), ("ms1g", [NE * 128, D]),
                   ("msvg", [NE * 128, FSL]), ("mssg", [NE * 128, FSL]),
                   ("tokid", [128, NT]), ("pairrow", [128, 2 * NT]),
                   ("jrow", [1, NCH]), ("ltstrict", [128, 128]),
                   ("iota8", [128, 8]),
                   ("msv", [128, NE * 4]), ("ms", [128, NE * 4]),
                   ("ms1", [128, NE * D]), ("rw", [D, NE]),
                   ("bwq", [D, 512]), ("bwk", [D, 512]), ("bwv", [D, 512]),
                   ("bwo", [512, D]),
                   ("rch", [128, T]), ("rcl", [128, T]),
                   ("rsh", [128, T]), ("rsl", [128, T]),
                   ("ident", [128, 128]), ("trimask", [128, 128]),
                   ("diagmask", [512, 512]), ("dmneg", [512, 512])):
        din(nm, sh)
    yout = nc.dram_tensor("y", [TOK, D], F32, kind="ExternalOutput").ap()

    dint("xh", [TOK, D])
    dint("xh2", [TOK, D])
    dint("xh3", [TOK, D])
    dint("uh", [TOK, D])
    dint("v1h", [TOK, D])
    dint("ag_u_i", [TOK, D])
    dint("ag_u_o", [NCORE * TOK + 128, D])
    dint("ag_w_i", [TOK, NE])
    dint("ag_w_o", [NCORE * TOK, NE])
    dint("srt4", [NSLOT, 4])
    dint("cnt_i", [1, NE])
    dint("ag4", [NCORE * NSLOT, 4])
    dint("ag_cnt", [NCORE, NE])
    dint("rs_i", [2 * NCORE * TOK, D])
    dint("rs_o", [2 * TOK, D])
    dint("a2a_i", [NCORE, 4, 128, D])
    dint("a2a_o", [NCORE, 4, 128, D])
    dint("p2a_i", [NCORE, 4, 128, D])
    dint("p2a_o", [NCORE, 4, 128, D])

    es = ExitStack()
    with es:
        k = KB(nc, es)
        t, v, s, g, y = k.t, k.v, k.s, k.g, k.y

        cst = es.enter_context(nc.sbuf_tensor("cst", [128, 6144], F32))
        bigt = es.enter_context(nc.sbuf_tensor("big", [128, 45056], F32))
        A = Arena(bigt, 45056)
        pss = [es.enter_context(nc.psum_tensor("ps%d" % i, [128, 512], F32))
               for i in range(8)]

        ln1w = cst[:, 0:D]; ln1b = cst[:, D:2 * D]
        ln2w = cst[:, 2 * D:3 * D]; ln2b = cst[:, 3 * D:4 * D]
        rmsw = cst[:, 4 * D:4 * D + 256]
        ident = cst[:, 4352:4480]
        tri = cst[:, 4480:4608]
        zc = cst[:, 4608:4736]
        bgk2 = cst[:, 4736:4740]
        sc = cst[:, 4740:4804]
        b1c = cst[:, 4804:4828]
        ln_aff1 = (ln1w, ln1b)
        ln_aff2 = (ln2w, ln2b)

        for dst, srcn in ((ln1w, "ln1w"), (ln1b, "ln1b"), (ln2w, "ln2w"),
                          (ln2b, "ln2b"), (rmsw, "rmsw"), (ident, "ident"),
                          (tri, "trimask"), (bgk2, "bgk2"), (b1c, "b1c")):
            k.dma(y, dst, dram[srcn][:, :])
        g.memset(zc, 0.0)
        g.memset(sc[:, 2:64], 0.0)
        g.memset(sc[:, 0:1], EPS)
        g.memset(sc[:, 1:2], 1.0 / 16.0)
        g.memset(sc[:, 2:3], 1.0)
        k.bar()

        ctx = dict(k=k, nc=nc, dram=dram, A=A, pss=pss, sc=sc, ident=ident,
                   tri=tri, zc=zc, rmsw=rmsw, bgk2=bgk2, b1c=b1c,
                   ln_aff1=ln_aff1, ln_aff2=ln_aff2)

        xin = dram["x0"]
        if 1 in sublayers:
            emit_attn_a(ctx, xin, dram["xh"])
            xin = dram["xh"]
            k.bar()
        if 2 in sublayers:
            emit_moe(ctx, xin, dram["xh2"])
            xin = dram["xh2"]
            k.bar()
        if 3 in sublayers:
            emit_attn_b(ctx, xin, dram["xh3"])
            xin = dram["xh3"]
            k.bar()
        if 4 in sublayers:
            emit_moe(ctx, xin, dram["xh"])
            xin = dram["xh"]
            k.bar()

        A.push()
        tb = A.take(D)
        for i in range(NT):
            k.dma(y, tb, xin[i * 128:(i + 1) * 128, :])
            k.wait(y, "dS")
            k.dma(y, yout[i * 128:(i + 1) * 128, :], tb)
            k.wait(y, "dS")
        for nm in DEBUG_DUMP:
            src_ap = dram[nm]
            flat = int(np.prod(src_ap.shape))
            dbg = nc.dram_tensor("dbg_" + nm, [flat // 1024, 1024], F32,
                                 kind="ExternalOutput").ap()
            s2 = src_ap.rearrange(
                " ".join("abcd"[:len(src_ap.shape)]) + " -> (" +
                " ".join("abcd"[:len(src_ap.shape)]) + ")"
            ).rearrange("(r c) -> r c", c=1024)
            for i in range(flat // 1024 // 128):
                k.dma(y, tb, s2[i * 128:(i + 1) * 128, :])
                k.wait(y, "dS")
                k.dma(y, dbg[i * 128:(i + 1) * 128, :], tb)
                k.wait(y, "dS")
        A.pop()
        k.bar()
    return nc



def emit_attn_a(ctx, xin, xh):
    k = ctx["k"]; dram = ctx["dram"]; A = ctx["A"]; pss = ctx["pss"]
    sc = ctx["sc"]; ident = ctx["ident"]
    t, v, s, g, y = k.t, k.v, k.s, k.g, k.y
    uh, v1h = dram["uh"], dram["v1h"]
    a2a_i, a2a_o = dram["a2a_i"], dram["a2a_o"]
    p2a_i, p2a_o = dram["p2a_i"], dram["p2a_o"]

    A.push()
    xt = A.take(D); ut = A.take(D); at = A.take(D); wsp = A.take(D)
    aT = A.take(D)
    for i in range(NT):
        sq, sub = i // 4, i % 4
        k.dma(y, xt, xin[i * 128:(i + 1) * 128, :])
        k.wait(v, "dS"); k.wait(s, "dS")
        emit_ln(k, xt, ut, wsp, sc, affine=None)
        emit_ln(k, ut, at, wsp, sc, affine=ctx["ln_aff1"])
        k.wait(y, "ve")
        k.dma(y, uh[i * 128:(i + 1) * 128, :], ut)
        for dch in range(DC):
            pb = pss[dch % 2]
            k.wait(t, "ve")
            k.inc(t.transpose(pb[:, 0:128], at[:, dch * 128:(dch + 1) * 128],
                              ident), "pe")
            k.wait(v, "pe")
            k.inc(v.tensor_copy(aT[:, dch * 128:(dch + 1) * 128], pb[:, 0:128]),
                  "ve")
        k.wait(y, "ve")
        for r in (2 * sq, 2 * sq + 1):
            k.dma(y, a2a_i[r, sub, :, :], aT)
    A.pop()
    k.bar()
    if STAGE < 2:
        return

    with allow_ici_all_to_all():
        k.inc(g.collective_compute(
            "AllToAll", ALU.bypass, replica_groups=[list(range(NCORE))],
            ins=[a2a_i.opt()], outs=[a2a_o.opt()]), "cc")
    g.wait_ge(k.sem("cc"), k._cnt["cc"])
    k.bar(dmas=False)
    if STAGE < 3:
        return

    emit_gla(ctx)
    k.bar()
    if STAGE < 4:
        return

    with allow_ici_all_to_all():
        k.inc(g.collective_compute(
            "AllToAll", ALU.bypass, replica_groups=[list(range(NCORE))],
            ins=[p2a_i.opt()], outs=[p2a_o.opt()]), "cc")
    g.wait_ge(k.sem("cc"), k._cnt["cc"])
    k.bar(dmas=False)

    A.push()
    pa = A.take(D); pb_ = A.take(D); utl = A.take(D); vt = A.take(D)
    for i in range(NT):
        sq, sub = i // 4, i % 4
        k.dma(y, pa, p2a_o[2 * sq, sub, :, :])
        k.dma(y, pb_, p2a_o[2 * sq + 1, sub, :, :])
        k.dma(y, utl, uh[i * 128:(i + 1) * 128, :])
        k.wait(v, "dS")
        k.inc(v.tensor_tensor(pa, pa, pb_, ALU.add), "ve")
        for dch in range(DC):
            pb = pss[dch % 2]
            k.wait(t, "ve")
            k.inc(t.transpose(pb[:, 0:128], pa[:, dch * 128:(dch + 1) * 128],
                              ident), "pe")
            k.wait(v, "pe")
            k.inc(v.tensor_tensor(vt[:, dch * 128:(dch + 1) * 128],
                                  utl[:, dch * 128:(dch + 1) * 128],
                                  pb[:, 0:128], ALU.add), "ve")
        k.wait(y, "ve")
        k.dma(y, v1h[i * 128:(i + 1) * 128, :], vt)
    A.pop()
    k.bar()
    if STAGE < 5:
        return

    emit_ffn_a(ctx, xin, xh)


def emit_gla(ctx):
    k = ctx["k"]; dram = ctx["dram"]; A = ctx["A"]; pss = ctx["pss"]
    sc = ctx["sc"]; ident = ctx["ident"]; tri = ctx["tri"]; zc = ctx["zc"]
    rmsw = ctx["rmsw"]; bgk2 = ctx["bgk2"]
    t, v, s, g, y = k.t, k.v, k.s, k.g, k.y
    a2a_o, p2a_i = dram["a2a_o"], dram["p2a_i"]
    one16 = sc[:, 1:2]

    A.push()
    wq = A.take(8 * 512); wk = A.take(8 * 512)
    wvw = A.take(8 * 512); wgw = A.take(8 * 512)
    wo = A.take(4 * D)
    wgk1 = A.take(8 * 16)
    wgk2 = A.take(512)
    for dch in range(DC):
        k.dma(y, wq[:, dch * 512:(dch + 1) * 512],
              dram["wq"][dch * 128:(dch + 1) * 128, :])
        k.dma(y, wk[:, dch * 512:(dch + 1) * 512],
              dram["wk"][dch * 128:(dch + 1) * 128, :])
        k.dma(y, wvw[:, dch * 512:(dch + 1) * 512],
              dram["wv"][dch * 128:(dch + 1) * 128, :])
        k.dma(y, wgw[:, dch * 512:(dch + 1) * 512],
              dram["wg"][dch * 128:(dch + 1) * 128, :])
        k.dma(y, wgk1[:, dch * 16:(dch + 1) * 16],
              dram["wgk1"][dch * 128:(dch + 1) * 128, :])
    for kc in range(4):
        k.dma(y, wo[:, kc * D:(kc + 1) * D],
              dram["wo"][kc * 128:(kc + 1) * 128, :])
    k.dma(y, wgk2[0:16, :], dram["wgk2"][:, :])

    xc = A.take(4 * D)
    qT = A.take(4 * 512); kT = A.take(4 * 512)
    lsT = A.take(4 * 512)
    vts = A.take(4 * 512); gts = A.take(4 * 512)
    hst = A.take(4 * 256)
    u16 = A.take(512)
    wrk = A.take(2048)
    out_t = A.take(D)
    gsc = A.take(16)

    g.memset(hst[:, :], 0.0)
    k.bar()
    if STAGE < 3.05:
        A.pop()
        return

    for sup in range(T // 512):
        for sb_ in range(4):
            k.dma(y, xc[:, sb_ * D:(sb_ + 1) * D], a2a_o[sup, sb_, :, :])
        k.bar(dmas=True)
        if STAGE < 3.08:
            continue

        for dt_ in range(4):
            pq, pk = pss[0], pss[1]
            for ww, pp in ((wq, pq), (wk, pk)):
                for cc2 in range(4):
                    for kc in range(DC):
                        l_ = ww[:, kc * 512 + dt_ * 128: kc * 512 + (dt_ + 1) * 128]
                        r_ = xc[:, cc2 * D + kc * 128: cc2 * D + (kc + 1) * 128]
                        t.matmul(pp[:, cc2 * 128:(cc2 + 1) * 128], l_, r_,
                                 start=(kc == 0), stop=(kc == DC - 1))
            k.inc(t.drain(), "pe")
            k.wait(v, "pe")
            k.inc(v.tensor_copy(qT[:, dt_ * 512:(dt_ + 1) * 512], pq[:, :]), "ve")
            k.inc(v.tensor_copy(kT[:, dt_ * 512:(dt_ + 1) * 512], pk[:, :]), "ve")
            k.wait(t, "ve")
        if STAGE < 3.12:
            continue
        for cc2 in range(4):
            pv_, pg = pss[2], pss[3]
            for ww, pp in ((wvw, pv_), (wgw, pg)):
                for kc in range(DC):
                    l_ = xc[:, cc2 * D + kc * 128: cc2 * D + (kc + 1) * 128]
                    t.matmul(pp[:, :], l_, ww[:, kc * 512:(kc + 1) * 512],
                             start=(kc == 0), stop=(kc == DC - 1))
            k.inc(t.drain(), "pe")
            k.wait(v, "pe")
            k.inc(v.tensor_copy(vts[:, cc2 * 512:(cc2 + 1) * 512], pv_[:, :]), "ve")
            k.inc(v.tensor_copy(gts[:, cc2 * 512:(cc2 + 1) * 512], pg[:, :]), "ve")
            k.wait(t, "ve")
        if STAGE < 3.2:
            continue
        pu = pss[4]
        for cc2 in range(4):
            for kc in range(DC):
                t.matmul(pu[0:16, cc2 * 128:(cc2 + 1) * 128],
                         wgk1[:, kc * 16:(kc + 1) * 16],
                         xc[:, cc2 * D + kc * 128: cc2 * D + (kc + 1) * 128],
                         start=(kc == 0), stop=(kc == DC - 1))
        k.inc(t.drain(), "pe")
        k.wait(v, "pe")
        k.inc(v.tensor_copy(u16[0:16, :], pu[0:16, :]), "ve")
        k.wait(t, "ve")
        for dt_ in range(4):
            pz = pss[5]
            t.matmul(pz[:, :], wgk2[0:16, dt_ * 128:(dt_ + 1) * 128],
                     u16[0:16, :], start=True, stop=True)
            k.inc(t.drain(), "pe")
            k.wait(s, "pe")
            s.activation(out=wrk[:, 0:512], in_=pz[:, :], func=AF.Sigmoid,
                         bias=bgk2[:, dt_:dt_ + 1])
            k.inc(s.activation(out=lsT[:, dt_ * 512:(dt_ + 1) * 512],
                               in_=wrk[:, 0:512], func=AF.Ln), "se")
            k.wait(t, "se")
        k.bar(dmas=False)
        if STAGE < 3.3:
            continue

        for cc2 in range(4):
            pw = [pss[6], pss[7]]
            for h in range(2):
                dts = (2 * h, 2 * h + 1)
                Gt = [wrk[:, 0:128], wrk[:, 128:256]]
                E = [wrk[:, 256:384], wrk[:, 384:512]]
                En = [wrk[:, 512:640], wrk[:, 640:768]]
                Er = [wrk[:, 768:896], wrk[:, 896:1024]]
                qd = [wrk[:, 1024:1152], wrk[:, 1152:1280]]
                kda = [wrk[:, 1280:1408], wrk[:, 1408:1536]]
                kdh = [wrk[:, 1536:1664], wrk[:, 1664:1792]]
                am = wrk[:, 1792:1920]
                kdhs = [wrk[:, 1920:2048], None]
                ge16 = [gsc[:, 0:1], gsc[:, 1:2]]
                edec = [gsc[:, 2:3], gsc[:, 3:4]]
                ssq2 = gsc[:, 4:5]; rstd = gsc[:, 5:6]

                for j, dt_ in enumerate(dts):
                    lsl = lsT[:, dt_ * 512 + cc2 * 128: dt_ * 512 + (cc2 + 1) * 128]
                    k.wait(v, "se")
                    k.inc(v.tensor_tensor_scan(Gt[j], lsl, zc, 0.0,
                                               ALU.add, ALU.add), "ve")
                    k.wait(s, "ve")
                    s.activation(out=ge16[j], in_=Gt[j][:, 127:128],
                                 func=AF.Identity, scale=1.0 / 16.0)
                    s.activation(out=E[j], in_=Gt[j], func=AF.Exp,
                                 scale=1.0 / 16.0)
                    s.activation(out=En[j], in_=Gt[j], func=AF.Exp,
                                 scale=-1.0 / 16.0)
                    s.activation(out=Er[j], in_=Gt[j], func=AF.Exp,
                                 scale=-1.0 / 16.0, bias=ge16[j])
                    k.inc(s.activation(out=edec[j], in_=Gt[j][:, 127:128],
                                       func=AF.Exp, scale=1.0 / 16.0), "se")
                    k.wait(v, "se")
                    qsl = qT[:, dt_ * 512 + cc2 * 128: dt_ * 512 + (cc2 + 1) * 128]
                    ksl = kT[:, dt_ * 512 + cc2 * 128: dt_ * 512 + (cc2 + 1) * 128]
                    k.inc(v.scalar_tensor_tensor(qd[j], qsl, one16, E[j],
                                                 ALU.mult, ALU.mult), "ve")
                    k.inc(v.tensor_tensor(kda[j], ksl, En[j], ALU.mult), "ve")
                    k.inc(v.tensor_tensor(kdh[j], ksl, Er[j], ALU.mult), "ve")
                kdhs[1] = Gt[0]
                k.wait(t, "ve")

                pA = pss[0]
                t.matmul(pA[:, 0:128], kda[0], qd[0], start=True, stop=False)
                k.inc(t.matmul(pA[:, 0:128], kda[1], qd[1],
                               start=False, stop=True), "pe")
                pT = pss[1]
                k.inc(t.transpose(pT[:, 0:128], kdh[0], ident), "pe")
                k.inc(t.transpose(pT[:, 128:256], kdh[1], ident), "pe")
                k.wait(v, "pe")
                k.inc(v.tensor_tensor(am, pA[:, 0:128], tri, ALU.mult), "ve")
                k.inc(v.tensor_copy(kdhs[0], pT[:, 0:128]), "ve")
                k.inc(v.tensor_copy(kdhs[1], pT[:, 128:256]), "ve")
                k.wait(t, "ve")

                po = pss[2]
                vsl = vts[:, cc2 * 512 + h * 256: cc2 * 512 + (h + 1) * 256]
                t.matmul(po[:, 0:256], qd[0],
                         hst[:, (2 * h) * 256:(2 * h + 1) * 256],
                         start=True, stop=False)
                t.matmul(po[:, 0:256], qd[1],
                         hst[:, (2 * h + 1) * 256:(2 * h + 2) * 256],
                         start=False, stop=False)
                t.matmul(po[:, 0:256], am, vsl, start=False, stop=True)
                ph = pss[3]
                t.matmul(ph[:, 0:256], kdhs[0], vsl, start=True, stop=True)
                k.inc(t.matmul(ph[:, 256:512], kdhs[1], vsl,
                               start=True, stop=True), "pe")
                k.wait(v, "pe"); k.wait(s, "pe")
                for j in range(2):
                    k.inc(v.scalar_tensor_tensor(
                        hst[:, (2 * h + j) * 256:(2 * h + j + 1) * 256],
                        hst[:, (2 * h + j) * 256:(2 * h + j + 1) * 256],
                        edec[j], ph[:, j * 256:(j + 1) * 256],
                        ALU.mult, ALU.add), "ve")
                o2 = wrk[:, 256:512]
                k.inc(s.activation(out=o2, in_=po[:, 0:256], func=AF.Square,
                                   accum_out=ssq2), "se")
                s.drain()
                s.activation(out=ssq2, in_=ssq2, func=AF.Identity,
                             scale=1.0 / 256, bias=sc[:, 0:1])
                k.inc(s.activation(out=ssq2, in_=ssq2, func=AF.Sqrt), "se")
                k.wait(v, "se")
                k.inc(v.reciprocal(rstd, ssq2), "ve")
                gsl = gts[:, cc2 * 512 + h * 256: cc2 * 512 + (h + 1) * 256]
                sg = wrk[:, 512:768]
                k.wait(s, "ve")
                k.inc(s.activation(out=sg, in_=gsl, func=AF.Silu), "se")
                k.wait(v, "se")
                t1 = wrk[:, 768:1024]
                k.inc(v.scalar_tensor_tensor(t1, po[:, 0:256], rstd, rmsw,
                                             ALU.mult, ALU.mult), "ve")
                go = wrk[:, 1024:1280]
                k.inc(v.tensor_tensor(go, t1, sg, ALU.mult), "ve")
                k.wait(t, "ve")
                pgt = pss[4]
                k.inc(t.transpose(pgt[:, 0:128], go[:, 0:128], ident), "pe")
                k.inc(t.transpose(pgt[:, 128:256], go[:, 128:256], ident), "pe")
                k.wait(v, "pe")
                goT0 = wrk[:, 1280:1408]; goT1 = wrk[:, 1408:1536]
                k.inc(v.tensor_copy(goT0, pgt[:, 0:128]), "ve")
                k.inc(v.tensor_copy(goT1, pgt[:, 128:256]), "ve")
                k.wait(t, "ve")
                kc0 = 2 * h
                for dti in range(DC):
                    pwb = pw[dti // 4][:, (dti % 4) * 128:((dti % 4) + 1) * 128]
                    wl0 = wo[:, kc0 * D + dti * 128: kc0 * D + (dti + 1) * 128]
                    wl1 = wo[:, (kc0 + 1) * D + dti * 128:
                             (kc0 + 1) * D + (dti + 1) * 128]
                    t.matmul(pwb, wl0, goT0, start=True, stop=False)
                    t.matmul(pwb, wl1, goT1, start=False, stop=True)
                k.inc(t.drain(), "pe")
                k.wait(v, "pe")
                for dti in range(DC):
                    pwb = pw[dti // 4][:, (dti % 4) * 128:((dti % 4) + 1) * 128]
                    ot = out_t[:, dti * 128:(dti + 1) * 128]
                    if h == 0:
                        k.inc(v.tensor_copy(ot, pwb), "ve")
                    else:
                        k.inc(v.tensor_tensor(ot, ot, pwb, ALU.add), "ve")
                k.wait(t, "ve")
            k.wait(y, "ve")
            k.dma(y, p2a_i[sup, cc2, :, :], out_t)
            k.wait(t, "ve")
        k.bar(dmas=False)
    A.pop()


def emit_ffn_a(ctx, xin, xh):
    k = ctx["k"]; dram = ctx["dram"]; A = ctx["A"]; pss = ctx["pss"]
    sc = ctx["sc"]; ident = ctx["ident"]; b1c = ctx["b1c"]
    t, v, s, g, y = k.t, k.v, k.s, k.g, k.y
    v1h = dram["v1h"]
    A.push()
    v1c = A.take(4 * D)
    bT = A.take(4 * D)
    fT = A.take(24 * 512)
    w1b = A.take(2 * 8 * 128)
    w2b = A.take(2 * D)
    b2t = A.take(D)
    wsp = A.take(D)
    k.dma(y, b2t, dram["b2"][:, :])

    for sup in range(TOK // 512):
        for sb_ in range(4):
            i = sup * 4 + sb_
            k.dma(y, v1c[:, sb_ * D:(sb_ + 1) * D],
                  v1h[i * 128:(i + 1) * 128, :])
        k.bar(dmas=True)
        for sb_ in range(4):
            emit_ln(k, v1c[:, sb_ * D:(sb_ + 1) * D], wsp,
                    fT[:, 0:D], sc, affine=ctx["ln_aff2"])
            for dch in range(DC):
                pb = pss[dch % 2]
                k.wait(t, "ve")
                k.inc(t.transpose(pb[:, 0:128],
                                  wsp[:, dch * 128:(dch + 1) * 128], ident),
                      "pe")
                k.wait(v, "pe")
                k.inc(v.tensor_copy(
                    bT[:, dch * 512 + sb_ * 128: dch * 512 + (sb_ + 1) * 128],
                    pb[:, 0:128]), "ve")
            k.wait(t, "ve"); k.wait(s, "ve")
        for fc in range(24):
            buf = w1b[:, (fc % 2) * 1024:(fc % 2) * 1024 + 1024]
            k.wait(y, "pe")
            for dch in range(DC):
                k.dma(y, buf[:, dch * 128:(dch + 1) * 128],
                      dram["w1"][dch * 128:(dch + 1) * 128,
                                 fc * 128:(fc + 1) * 128])
            k.wait(t, "dS")
            pf = pss[2 + (fc % 2)]
            for dch in range(DC):
                t.matmul(pf[:, :], buf[:, dch * 128:(dch + 1) * 128],
                         bT[:, dch * 512:(dch + 1) * 512],
                         start=(dch == 0), stop=(dch == DC - 1))
            k.inc(t.drain(), "pe")
            k.wait(s, "pe"); k.wait(v, "pe")
            hb = fT[:, fc * 512:(fc + 1) * 512]
            k.inc(s.activation(out=hb, in_=pf[:, :], func=AF.Identity,
                               bias=b1c[:, fc:fc + 1]), "se")
            k.wait(v, "se")
            k.inc(v.tensor_scalar_mul(wsp[:, 0:512], hb, 0.01), "ve")
            k.inc(v.tensor_tensor(hb, hb, wsp[:, 0:512], ALU.max), "ve")
            k.wait(s, "ve")
            k.inc(s.activation(out=hb, in_=hb, func=AF.Square), "se")
            k.wait(t, "se")
        for sb_ in range(4):
            i = sup * 4 + sb_
            p0, p1 = pss[4], pss[5]
            for half, pp in ((0, p0), (1, p1)):
                for kf in range(24):
                    buf = w2b[:, (kf % 2) * 512:(kf % 2) * 512 + 512]
                    k.wait(y, "pe")
                    k.dma(y, buf, dram["w2"][kf * 128:(kf + 1) * 128,
                                             half * 512:(half + 1) * 512])
                    k.wait(t, "dS")
                    l_ = fT[:, kf * 512 + sb_ * 128: kf * 512 + (sb_ + 1) * 128]
                    k.inc(t.matmul(pp[:, :], l_, buf,
                                   start=(kf == 0), stop=(kf == 23)), "pe")
            k.inc(t.drain(), "pe")
            k.dma(y, wsp, xin[i * 128:(i + 1) * 128, :])
            k.wait(v, "pe"); k.wait(v, "dS")
            k.inc(v.tensor_tensor(wsp[:, 0:512], wsp[:, 0:512], p0[:, :],
                                  ALU.add), "ve")
            k.inc(v.tensor_tensor(wsp[:, 512:1024], wsp[:, 512:1024], p1[:, :],
                                  ALU.add), "ve")
            k.inc(v.tensor_tensor(wsp, wsp, b2t, ALU.add), "ve")
            k.inc(v.tensor_tensor(wsp, wsp, v1c[:, sb_ * D:(sb_ + 1) * D],
                                  ALU.add), "ve")
            k.wait(y, "ve")
            k.dma(y, xh[i * 128:(i + 1) * 128, :], wsp)
            k.wait(t, "ve")
    A.pop()








def emit_moe_sparse(ctx, xin, xout):
    k = ctx["k"]; dram = ctx["dram"]; A = ctx["A"]; pss = ctx["pss"]
    sc = ctx["sc"]; ident = ctx["ident"]
    t, v, s, g, y = k.t, k.v, k.s, k.g, k.y
    WR = 8 * FSL + 4

    A.push()
    for j in range(DC):
        k.dma(y, dram["ag_u_o"][NCORE * TOK:NCORE * TOK + 128,
                                j * 128:(j + 1) * 128], ctx["zc"])
    NJ = NCORE * NCH
    meta = A.take(NJ * 4)
    tokf = A.take(NJ); prf = A.take(NJ); gf_ = A.take(NJ)
    toku = A.take(NJ); pru = A.take(NJ)
    widxf = A.take(NJ); widxu = A.take(NJ)
    jrow = A.take(NCH)
    cnr = A.take(8); padr = A.take(8); cumx = A.take(8); tmp8 = A.take(8)
    tmp40 = A.take(NCH)
    ech = A.take(NCH); iotu = A.take(1); iot = A.take(1); onesr = A.take(128)
    k.dma(y, meta, dram["ag4"].rearrange("(a p) f -> p a f", p=128))
    k.dma(y, jrow[0:1, :], dram["jrow"][:, :])
    g.iota(iotu.bitcast(U32), pattern=[[1, 1]], base=0, channel_multiplier=1)
    v.memset(onesr[0:1, :], 1.0)
    k.bar()
    mview = meta.rearrange("p (a f) -> p a f", f=4)
    v.tensor_copy(tokf, mview[:, :, 0])
    v.tensor_copy(prf, mview[:, :, 1])
    v.tensor_copy(gf_, mview[:, :, 2])
    v.drain()
    v.tensor_copy(iot, iotu.bitcast(U32))
    v.tensor_copy(toku.bitcast(U32), tokf)
    v.tensor_copy(pru.bitcast(U32), prf)
    k.inc(v.drain(), "ve")
    for r in range(NCORE):
        k.wait(y, "ve")
        k.dma(y, cnr[0:1, :], dram["ag_cnt"][r:r + 1, :])
        k.wait(v, "dS")
        v.tensor_scalar(padr[0:1, :], cnr[0:1, :], 63.5, None, ALU.add)
        v.drain()
        v.tensor_scalar(padr[0:1, :], padr[0:1, :], 1.0 / 128.0, None, ALU.mult)
        v.drain()
        v.tensor_scalar(padr[0:1, :], padr[0:1, :], 8388608.0, None, ALU.add)
        v.drain()
        v.tensor_scalar(padr[0:1, :], padr[0:1, :], 8388608.0, None,
                        ALU.subtract)
        v.drain()
        v.tensor_scalar(padr[0:1, :], padr[0:1, :], 128.0, None, ALU.mult)
        v.drain()
        v.tensor_tensor_scan(cumx[0:1, :], padr[0:1, :], padr[0:1, :],
                             0.0, ALU.add, ALU.bypass)
        v.drain()
        v.tensor_tensor(cumx[0:1, :], cumx[0:1, :], padr[0:1, :], ALU.subtract)
        v.drain()
        v.memset(ech[0:1, :], 0.0)
        v.drain()
        for e in range(NE):
            v.tensor_scalar(tmp40[0:1, :], jrow[0:1, :],
                            cumx[0:1, e:e + 1], None, ALU.is_ge)
            v.drain()
            v.tensor_tensor(ech[0:1, :], ech[0:1, :], tmp40[0:1, :],
                            ALU.add)
            v.drain()
        v.tensor_scalar(ech[0:1, :], ech[0:1, :], -1.0, None, ALU.add)
        v.drain()
        k.inc(v.tensor_scalar(ech[0:1, :], ech[0:1, :], 128.0, None,
                              ALU.mult), "ve")
        k.wait(t, "ve")
        pm = pss[6]
        t.matmul(pm[:, 0:NCH], onesr[0:1, :], ech[0:1, :],
                 start=True, stop=True)
        k.inc(t.drain(), "pe")
        k.wait(v, "pe")
        k.inc(v.tensor_scalar(widxf[:, r * NCH:(r + 1) * NCH], pm[:, 0:NCH],
                              iot[:, 0:1], None, ALU.add), "ve")
    v.tensor_copy(widxu.bitcast(U32), widxf)
    k.inc(v.drain(), "ve")
    k.bar()

    wvb = [A.take(WR), A.take(WR)]
    wb_ = [A.take(WR), A.take(WR)]
    w1b = [A.take(4 * D), A.take(4 * D)]
    s1b = [A.take(D), A.take(D)]
    svb = [A.take(FSL), A.take(FSL)]
    ssb = [A.take(FSL), A.take(FSL)]
    xgb = [A.take(D), A.take(D)]
    xTb = [A.take(D), A.take(D)]
    vT = A.take(FSL); yT = A.take(FSL); ygl = A.take(FSL)
    hb = [A.take(FSL), A.take(FSL)]
    hTb = A.take(FSL)
    ott = [A.take(D), A.take(D)]
    breg = g.alloc_register()
    g.reg_mov(breg, 2 * NCORE * TOK - 1)

    NJ_ = NCORE * NCH
    pe_s1 = [0, 0]; pe_s2 = [0, 0]
    dg_xgb = [0, 0]; dg_wv = [0, 0]; dg_w1 = [0, 0]
    ve_xcopy = [0, 0]; ve_s2evac = [0, 0]; ve_h = [0, 0]
    ve_hTcopy = 0; ve_s4evac = 0
    sc_par = [0, 0]

    def gidma(dst, srcn, idx, J):
        k.inc(g.indirect_dma_start(
            out=dst[:, :], out_offset=None, in_=dram[srcn][:, :],
            in_offset=bass.IndirectOffsetOnAxis(
                ap=idx[:, J:J + 1].bitcast(U32), axis=0)), "dG", 16)

    for i in range(NJ_ + 2):
        a, b, c = i, i - 1, i - 2
        pa, pb_, pc = a % 2, b % 2, c % 2
        if a < NJ_:
            k.wait_at(g, "pe", pe_s1[pa])
            gidma(xgb[pa], "ag_u_o", toku, a)
            dg_xgb[pa] = k._cnt["dG"]
            k.wait_at(g, "pe", pe_s2[pa])
            k.wait_at(g, "ve", ve_s2evac[pa])
            gidma(wvb[pa], "mwv", widxu, a)
            gidma(wb_[pa], "mw", widxu, a)
            gidma(svb[pa], "msvg", widxu, a)
            gidma(ssb[pa], "mssg", widxu, a)
            dg_wv[pa] = k._cnt["dG"]
        if a < NJ_:
            k.wait_at(t, "dG", dg_xgb[pa])
            k.wait_at(t, "ve", ve_xcopy[1 - pa])
            for dch in range(DC):
                pbk = pss[6 + dch // 4]
                t.transpose(pbk[:, (dch % 4) * 128:(dch % 4 + 1) * 128],
                            xgb[pa][:, dch * 128:(dch + 1) * 128], ident)
            k.inc(t.drain(), "pe")
            pe_s1[pa] = k._cnt["pe"]
            k.wait_at(v, "pe", pe_s1[pa])
            v.tensor_copy(xTb[pa][:, 0:512], pss[6][:, :])
            k.inc(v.tensor_copy(xTb[pa][:, 512:1024], pss[7][:, :]), "ve")
            ve_xcopy[pa] = k._cnt["ve"]
        if 0 <= b < NJ_:
            k.wait_at(t, "dG", dg_wv[pb_])
            k.wait_at(t, "ve", ve_xcopy[pb_])
            k.wait_at(t, "ve", ve_s2evac[1 - pb_])
            for pj, src_w in ((4, wvb[pb_]), (5, wb_[pb_])):
                for kc in range(DC):
                    t.matmul(pss[pj][:, :],
                             xTb[pb_][:, kc * 128:(kc + 1) * 128],
                             src_w[:, kc * FSL:(kc + 1) * FSL],
                             start=(kc == 0), stop=(kc == DC - 1))
            k.inc(t.drain(), "pe")
            pe_s2[pb_] = k._cnt["pe"]
            k.wait_at(v, "pe", pe_s2[pb_])
            v.tensor_tensor(vT, pss[4][:, :], svb[pb_], ALU.mult)
            k.inc(v.tensor_tensor(yT, pss[5][:, :], ssb[pb_], ALU.mult), "ve")
            ve_s2evac[pb_] = k._cnt["ve"]
            k.wait_at(s, "ve", ve_s2evac[pb_])
            k.inc(s.activation(out=ygl, in_=yT, func=AF.Gelu_apprx_tanh), "se")
            k.wait(v, "se")
            k.inc(v.tensor_tensor(hb[pb_], vT, ygl, ALU.mult), "ve")
            ve_h[pb_] = k._cnt["ve"]
        if 0 <= c < NJ_:
            k.wait_at(t, "ve", ve_h[pc])
            k.wait_at(t, "ve", ve_hTcopy)
            for kf in range(4):
                t.transpose(pss[0][:, kf * 128:(kf + 1) * 128],
                            hb[pc][:, kf * 128:(kf + 1) * 128], ident)
            k.inc(t.drain(), "pe")
            k.wait_at(v, "pe", k._cnt["pe"])
            k.inc(v.tensor_copy(hTb, pss[0][:, :]), "ve")
            ve_hTcopy = k._cnt["ve"]
            k.wait_at(t, "dG", dg_w1[pc])
            k.wait_at(t, "ve", ve_hTcopy)
            k.wait_at(t, "ve", ve_s4evac)
            for half in range(2):
                ph_ = pss[2 + half]
                for kf in range(4):
                    t.matmul(ph_[:, :],
                             hTb[:, kf * 128:(kf + 1) * 128],
                             w1b[pc][:, kf * D + half * 512:
                                     kf * D + half * 512 + 512],
                             start=(kf == 0), stop=(kf == 3))
            k.inc(t.drain(), "pe")
            k.wait_at(v, "pe", k._cnt["pe"])
            k.wait_at(v, "dG", sc_par[pc])
            for half in range(2):
                ph_ = pss[2 + half]
                o_ = ott[pc][:, half * 512:(half + 1) * 512]
                v.tensor_tensor(o_, ph_[:, :],
                                s1b[pc][:, half * 512:(half + 1) * 512],
                                ALU.mult)
            k.inc(v.tensor_scalar(ott[pc][:, :], ott[pc][:, :],
                                  gf_[:, c:c + 1], None, ALU.mult), "ve")
            ve_s4evac = k._cnt["ve"]
            k.wait(g, "ve")
            k.inc(g.indirect_dma_start(
                out=dram["rs_i"][:, :],
                out_offset=bass.IndirectOffsetOnAxis(
                    ap=pru[:, c:c + 1].bitcast(U32), axis=0),
                in_=ott[pc][:, :], in_offset=None,
                bounds_check=breg, oob_is_err=False), "dG", 16)
            sc_par[pc] = k._cnt["dG"]
        if a < NJ_:
            k.wait(g, "pe")
            k.wait(g, "ve")
            gidma(w1b[pa], "mw1", widxu, a)
            gidma(s1b[pa], "ms1g", widxu, a)
            dg_w1[pa] = k._cnt["dG"]
    A.pop()
    k.bar()

    with allow_ici_all_to_all():
        k.inc(g.collective_compute(
            "ReduceScatter", ALU.add, replica_groups=[list(range(NCORE))],
            ins=[dram["rs_i"].opt()], outs=[dram["rs_o"].opt()]), "cc")
    g.wait_ge(k.sem("cc"), k._cnt["cc"])
    k.bar(dmas=False)

    A.push()
    xa = A.take(D); ra = A.take(D); rb = A.take(D)
    for i in range(NT):
        k.dma(y, xa, xin[i * 128:(i + 1) * 128, :])
        k.dma(y, ra, dram["rs_o"][i * 128:(i + 1) * 128, :])
        k.dma(y, rb, dram["rs_o"][TOK + i * 128:TOK + (i + 1) * 128, :])
        k.wait(v, "dS")
        v.tensor_tensor(xa, xa, ra, ALU.add)
        k.inc(v.tensor_tensor(xa, xa, rb, ALU.add), "ve")
        k.wait(y, "ve")
        k.dma(y, xout[i * 128:(i + 1) * 128, :], xa)
    A.pop()



def emit_attn_b(ctx, xin, xout):
    k = ctx["k"]; dram = ctx["dram"]; A = ctx["A"]; pss = ctx["pss"]
    sc = ctx["sc"]; ident = ctx["ident"]
    t, v, s, g, y = k.t, k.v, k.s, k.g, k.y
    a2a_i, a2a_o = dram["a2a_i"], dram["a2a_o"]
    p2a_i, p2a_o = dram["p2a_i"], dram["p2a_o"]

    A.push()
    xt = A.take(D); ut = A.take(D); aT = A.take(D)
    for i in range(NT):
        sq, sub = i // 4, i % 4
        k.dma(y, xt, xin[i * 128:(i + 1) * 128, :])
        k.wait(v, "dS"); k.wait(s, "dS")
        emit_ln(k, xt, ut, aT, sc, affine=None)
        for dch in range(DC):
            pb = pss[dch % 2]
            k.wait(t, "ve")
            k.inc(t.transpose(pb[:, 0:128], ut[:, dch * 128:(dch + 1) * 128],
                              ident), "pe")
            k.wait(v, "pe")
            k.inc(v.tensor_copy(aT[:, dch * 128:(dch + 1) * 128], pb[:, 0:128]),
                  "ve")
        k.wait(y, "ve")
        for r in (2 * sq, 2 * sq + 1):
            k.dma(y, a2a_i[r, sub, :, :], aT)
    A.pop()
    k.bar()

    with allow_ici_all_to_all():
        k.inc(g.collective_compute(
            "AllToAll", ALU.bypass, replica_groups=[list(range(NCORE))],
            ins=[a2a_i.opt()], outs=[a2a_o.opt()]), "cc")
    g.wait_ge(k.sem("cc"), k._cnt["cc"])
    k.bar(dmas=False)

    A.push()
    qh = A.take(2 * T)
    kh = A.take(2 * T)
    vt_ = A.take(8 * 1024)
    xc = A.take(4 * D)
    rt = A.take(4 * 512)
    wq_ = A.take(8 * 256); wk_ = A.take(8 * 256); wv_ = A.take(8 * 256)
    wo_ = A.take(2 * D)
    rp1 = A.take(512); rp2 = A.take(512)
    dneg = A.take(4 * 512)
    out_t = A.take(D); out_t2 = A.take(D)
    Pt = A.take(512)
    oacc = A.take(128)
    fsc = A.take(16)
    for r_ in range(4):
        k.dma(y, dneg[:, r_ * 512:(r_ + 1) * 512],
              dram["dmneg"][r_ * 128:(r_ + 1) * 128, :])
    k.bar()

    for wave in range(2):
        wc = wave * 256
        for dch in range(DC):
            k.dma(y, wq_[:, dch * 256:(dch + 1) * 256],
                  dram["bwq"][dch * 128:(dch + 1) * 128, wc:wc + 256])
            k.dma(y, wk_[:, dch * 256:(dch + 1) * 256],
                  dram["bwk"][dch * 128:(dch + 1) * 128, wc:wc + 256])
            k.dma(y, wv_[:, dch * 256:(dch + 1) * 256],
                  dram["bwv"][dch * 128:(dch + 1) * 128, wc:wc + 256])
        for kc in range(2):
            k.dma(y, wo_[:, kc * D:(kc + 1) * D],
                  dram["bwo"][wc + kc * 128: wc + (kc + 1) * 128, :])
        k.bar()

        for sup in range(T // 512):
            k.wait(y, "pe"); k.wait(y, "ve"); k.wait(y, "se")
            for sb_ in range(4):
                k.dma(y, xc[:, sb_ * D:(sb_ + 1) * D], a2a_o[sup, sb_, :, :])
            for tb in range(4):
                k.dma(y, rt[:, tb * 512:(tb + 1) * 512],
                      dram[("rch", "rcl", "rsh", "rsl")[tb]]
                      [:, sup * 512:(sup + 1) * 512])
            k.bar(dmas=True)
            CH = rt[:, 0 * 512:1 * 512]
            CL = rt[:, 1 * 512:2 * 512]
            SH = rt[:, 2 * 512:3 * 512]
            SL = rt[:, 3 * 512:4 * 512]
            for (ws, dst, isq) in ((wq_, qh, True), (wk_, kh, False)):
                px = [pss[0], pss[1]]
                for dt_ in (0, 1):
                    for cc2 in range(4):
                        for kc in range(DC):
                            l_ = ws[:, kc * 256 + dt_ * 128:
                                    kc * 256 + (dt_ + 1) * 128]
                            r_ = xc[:, cc2 * D + kc * 128:
                                    cc2 * D + (kc + 1) * 128]
                            t.matmul(px[dt_][:, cc2 * 128:(cc2 + 1) * 128],
                                     l_, r_, start=(kc == 0),
                                     stop=(kc == DC - 1))
                k.inc(t.drain(), "pe")
                k.wait(v, "pe")
                x1, x2 = px[0], px[1]
                k.inc(v.tensor_tensor(rp1, x1[:, :], CH, ALU.mult), "ve")
                v.tensor_tensor(rp2, x2[:, :], SH, ALU.mult)
                v.tensor_tensor(rp1, rp1, rp2, ALU.subtract)
                v.tensor_tensor(rp2, x1[:, :], CL, ALU.mult)
                v.tensor_tensor(rp1, rp1, rp2, ALU.add)
                v.tensor_tensor(rp2, x2[:, :], SL, ALU.mult)
                k.inc(v.tensor_tensor(rp1, rp1, rp2, ALU.subtract), "ve")
                for h in range(4):
                    k.inc(v.tensor_copy(
                        dst[(h % 2) * 64:(h % 2) * 64 + 32,
                            (h // 2) * T + sup * 512:(h // 2) * T + (sup + 1) * 512],
                        rp1[32 * h:32 * h + 32, :]), "ve")
                v.tensor_tensor(rp1, x1[:, :], SH, ALU.mult)
                v.tensor_tensor(rp2, x2[:, :], CH, ALU.mult)
                v.tensor_tensor(rp1, rp1, rp2, ALU.add)
                v.tensor_tensor(rp2, x1[:, :], SL, ALU.mult)
                v.tensor_tensor(rp1, rp1, rp2, ALU.add)
                v.tensor_tensor(rp2, x2[:, :], CL, ALU.mult)
                k.inc(v.tensor_tensor(rp1, rp1, rp2, ALU.add), "ve")
                for h in range(4):
                    k.inc(v.tensor_copy(
                        dst[(h % 2) * 64 + 32:(h % 2) * 64 + 64,
                            (h // 2) * T + sup * 512:(h // 2) * T + (sup + 1) * 512],
                        rp1[32 * h:32 * h + 32, :]), "ve")
                k.wait(t, "ve")
            if DEBUG_DUMP and wave == 0 and sup == 0:
                for j in range(4):
                    k.dma(y, dram["rs_i"][(24 + j) * 128:(25 + j) * 128, :],
                          xc[:, j * D:(j + 1) * D])
                for j in range(2):
                    k.dma(y, dram["rs_i"][(28 + j) * 128:(29 + j) * 128, :],
                          wv_[:, j * D:(j + 1) * D])
                k.wait(y, "dS")
            for cc2 in range(4):
                pv_ = pss[2]
                for kc in range(DC):
                    t.matmul(pv_[:, 0:256],
                             xc[:, cc2 * D + kc * 128: cc2 * D + (kc + 1) * 128],
                             wv_[:, kc * 256:(kc + 1) * 256],
                             start=(kc == 0), stop=(kc == DC - 1))
                k.inc(t.drain(), "pe")
                k.wait(v, "pe")
                k.inc(v.tensor_copy(
                    vt_[:, (sup * 4 + cc2) * 256:(sup * 4 + cc2 + 1) * 256],
                    pv_[:, 0:256]), "ve")
                k.wait(t, "ve")
        k.bar()
        if DEBUG_DUMP and wave == 0:
            for j in range(8):
                k.dma(y, dram["rs_i"][j * 128:(j + 1) * 128, :],
                      vt_[:, j * D:(j + 1) * D])
                k.dma(y, dram["rs_i"][(8 + j) * 128:(9 + j) * 128, :],
                      qh[:, j * D:(j + 1) * D])
                k.dma(y, dram["rs_i"][(16 + j) * 128:(17 + j) * 128, :],
                      kh[:, j * D:(j + 1) * D])
            k.wait(y, "dS")

        mneg = fsc[:, 8:9]
        g.memset(mneg, -1e30)
        k.bar(dmas=False)
        for qt in range(NT * 2):
            for hp in range(2):
                macc = [fsc[:, 0:1], fsc[:, 1:2]]
                lacc = [fsc[:, 2:3], fsc[:, 3:4]]
                nmr = fsc[:, 4:5]
                nnm8 = fsc[:, 5:6]
                alph = fsc[:, 6:7]
                rsum = fsc[:, 7:8]
                rl = fsc[:, 9:10]
                k.wait(v, "se"); k.wait(v, "pe")
                v.memset(oacc[:, :], 0.0)
                for ha in range(2):
                    v.tensor_copy(macc[ha], mneg)
                    v.memset(lacc[ha], 0.0)
                k.inc(v.drain(), "ve")
                nsb = qt // 4 + 1
                for sb_ in range(nsb):
                    diag = (sb_ == qt // 4)
                    for ha in range(2):
                        pS = pss[ha]
                        k.wait(t, "ve")
                        k.inc(t.matmul(
                            pS[:, :],
                            qh[64 * ha:64 * ha + 64,
                               hp * T + qt * 128:hp * T + (qt + 1) * 128],
                            kh[64 * ha:64 * ha + 64,
                               hp * T + sb_ * 512:hp * T + (sb_ + 1) * 512],
                            start=True, stop=True), "pe")
                        k.wait(v, "pe")
                        Ssrc = pS[:, :]
                        if diag:
                            r_ = qt % 4
                            k.inc(v.tensor_tensor(
                                Pt, pS[:, :], dneg[:, r_ * 512:(r_ + 1) * 512],
                                ALU.add), "ve")
                            Ssrc = Pt
                        v.reduce_max(nmr, Ssrc, axis=AX.X)
                        v.drain()
                        v.tensor_tensor(nmr, nmr, macc[ha], ALU.max)
                        v.drain()
                        k.inc(v.tensor_scalar_mul(nnm8, nmr, -0.125), "ve")
                        k.wait(s, "ve")
                        s.activation(out=alph, in_=macc[ha], func=AF.Exp,
                                     scale=0.125, bias=nnm8)
                        s.drain()
                        k.inc(s.activation(out=Pt, in_=Ssrc, func=AF.Exp,
                                           scale=0.125, bias=nnm8,
                                           accum_out=rsum), "se")
                        k.wait(v, "se")
                        v.tensor_copy(macc[ha], nmr)
                        v.drain()
                        v.scalar_tensor_tensor(lacc[ha], lacc[ha], alph,
                                               rsum, ALU.mult, ALU.add)
                        k.inc(v.drain(), "ve")
                        pT = pss[2]
                        for i4 in range(4):
                            k.wait(t, "ve")
                            k.inc(t.transpose(pT[:, i4 * 128:(i4 + 1) * 128],
                                              Pt[:, i4 * 128:(i4 + 1) * 128],
                                              ident), "pe")
                        k.wait(v, "pe")
                        for i4 in range(4):
                            k.inc(v.tensor_copy(
                                Pt[:, i4 * 128:(i4 + 1) * 128],
                                pT[:, i4 * 128:(i4 + 1) * 128]), "ve")
                        k.wait(t, "ve")
                        po = pss[3]
                        for i4 in range(4):
                            t.matmul(po[:, 0:64],
                                     Pt[:, i4 * 128:(i4 + 1) * 128],
                                     vt_[:, (sb_ * 4 + i4) * 256 +
                                         (2 * hp + ha) * 64:
                                         (sb_ * 4 + i4) * 256 +
                                         (2 * hp + ha) * 64 + 64],
                                     start=(i4 == 0), stop=(i4 == 3))
                        k.inc(t.drain(), "pe")
                        k.wait(v, "pe")
                        k.inc(v.scalar_tensor_tensor(
                            oacc[:, ha * 64:(ha + 1) * 64],
                            oacc[:, ha * 64:(ha + 1) * 64],
                            alph, po[:, 0:64], ALU.mult, ALU.add), "ve")
                for ha in range(2):
                    v.reciprocal(rl, lacc[ha])
                    v.drain()
                    k.inc(v.tensor_scalar_mul(oacc[:, ha * 64:(ha + 1) * 64],
                                              oacc[:, ha * 64:(ha + 1) * 64],
                                              rl), "ve")
                k.wait(t, "ve")
                pT = pss[4]
                k.inc(t.transpose(pT[:, 0:128], oacc[:, 0:128], ident), "pe")
                k.wait(v, "pe")
                k.inc(v.tensor_copy(out_t2[:, 0:128], pT[:, 0:128]), "ve")
                k.wait(t, "ve")
                pw = [pss[5], pss[6]]
                for dti in range(DC):
                    pwb = pw[dti // 4][:, (dti % 4) * 128:((dti % 4) + 1) * 128]
                    t.matmul(pwb, wo_[:, hp * D + dti * 128:
                                      hp * D + (dti + 1) * 128],
                             out_t2[:, 0:128], start=True, stop=True)
                k.inc(t.drain(), "pe")
                k.wait(v, "pe")
                for dti in range(DC):
                    pwb = pw[dti // 4][:, (dti % 4) * 128:((dti % 4) + 1) * 128]
                    ot = out_t[:, dti * 128:(dti + 1) * 128]
                    if hp == 0:
                        k.inc(v.tensor_copy(ot, pwb), "ve")
                    else:
                        k.inc(v.tensor_tensor(ot, ot, pwb, ALU.add), "ve")
                k.wait(t, "ve")
            sup_, cc2_ = qt // 4, qt % 4
            if wave == 0:
                k.wait(y, "ve")
                k.dma(y, p2a_i[sup_, cc2_, :, :], out_t)
            else:
                k.wait(y, "pe")
                k.dma(y, out_t2, p2a_i[sup_, cc2_, :, :])
                k.wait(v, "dS")
                k.inc(v.tensor_tensor(out_t, out_t, out_t2, ALU.add), "ve")
                k.wait(y, "ve")
                k.dma(y, p2a_i[sup_, cc2_, :, :], out_t)
    A.pop()
    k.bar()

    with allow_ici_all_to_all():
        k.inc(g.collective_compute(
            "AllToAll", ALU.bypass, replica_groups=[list(range(NCORE))],
            ins=[p2a_i.opt()], outs=[p2a_o.opt()]), "cc")
    g.wait_ge(k.sem("cc"), k._cnt["cc"])
    k.bar(dmas=False)

    A.push()
    pa = A.take(D); pb_ = A.take(D); xtl = A.take(D); vt2 = A.take(D)
    for i in range(NT):
        sq, sub = i // 4, i % 4
        k.dma(y, pa, p2a_o[2 * sq, sub, :, :])
        k.dma(y, pb_, p2a_o[2 * sq + 1, sub, :, :])
        k.dma(y, xtl, xin[i * 128:(i + 1) * 128, :])
        k.wait(v, "dS")
        k.inc(v.tensor_tensor(pa, pa, pb_, ALU.add), "ve")
        for dch in range(DC):
            pb = pss[dch % 2]
            k.wait(t, "ve")
            k.inc(t.transpose(pb[:, 0:128], pa[:, dch * 128:(dch + 1) * 128],
                              ident), "pe")
            k.wait(v, "pe")
            k.inc(v.tensor_tensor(vt2[:, dch * 128:(dch + 1) * 128],
                                  xtl[:, dch * 128:(dch + 1) * 128],
                                  pb[:, 0:128], ALU.add), "ve")
        k.wait(y, "ve")
        k.dma(y, xout[i * 128:(i + 1) * 128, :], vt2)
    A.pop()



def emit_moe_sort(ctx, A, SEL1c, SEL2c, G1c, G2c, cntrow):
    k = ctx["k"]; dram = ctx["dram"]; pss = ctx["pss"]; sc = ctx["sc"]
    t, v, s, g, y = k.t, k.v, k.s, k.g, k.y
    lt = A.take(128)
    onesr = A.take(128)
    tokc = A.take(NT)
    prc = A.take(2 * NT)
    s12 = A.take(8)
    pre = A.take(8)
    brow = A.take(8); base = A.take(8); runt = A.take(8); pad = A.take(8)
    sl1 = A.take(1); sl2 = A.take(1)
    su = A.take(2)
    stg4 = A.take(8)
    dflt = A.take(8)
    sbreg = g.alloc_register()
    g.reg_mov(sbreg, NSLOT - 1)
    k.dma(y, lt, dram["ltstrict"][:, :])
    k.dma(y, tokc, dram["tokid"][:, :])
    k.dma(y, prc, dram["pairrow"][:, :])
    v.memset(stg4, 0.0)
    v.memset(onesr[0:1, :], 1.0)
    v.memset(runt[0:1, :], 0.0)
    k.inc(v.drain(), "ve")
    k.wait(v, "ve")
    v.tensor_scalar(pad[0:1, :], cntrow[0:1, :], 63.5, None, ALU.add)
    v.drain()
    v.tensor_scalar(pad[0:1, :], pad[0:1, :], 1.0 / 128.0, None, ALU.mult)
    v.drain()
    v.tensor_scalar(pad[0:1, :], pad[0:1, :], 8388608.0, None, ALU.add)
    v.drain()
    v.tensor_scalar(pad[0:1, :], pad[0:1, :], 8388608.0, None, ALU.subtract)
    v.drain()
    v.tensor_scalar(pad[0:1, :], pad[0:1, :], 128.0, None, ALU.mult)
    v.drain()
    v.tensor_tensor_scan(base[0:1, :], pad[0:1, :], pad[0:1, :],
                         0.0, ALU.add, ALU.bypass)
    v.drain()
    v.tensor_tensor(base[0:1, :], base[0:1, :], pad[0:1, :], ALU.subtract)
    k.inc(v.drain(), "ve")
    k.wait(y, "ve")
    k.dma(y, dram["cnt_i"][:, :], cntrow[0:1, :])
    g.memset(dflt[:, 0:1], float(NCORE * TOK))
    g.memset(dflt[:, 1:2], 1048576.0)
    g.memset(dflt[:, 2:4], 0.0)
    k.inc(g.drain(), "ge")
    k.wait(y, "ge")
    for a_ in range(NCH):
        k.dma(y, dram["srt4"][a_ * 128:(a_ + 1) * 128, :], dflt[:, 0:4])
    k.bar(dmas=True)
    for i in range(NT):
        se1 = SEL1c[:, i * 8:(i + 1) * 8]
        se2 = SEL2c[:, i * 8:(i + 1) * 8]
        k.wait(v, "ge")
        v.tensor_tensor(s12, se1, se2, ALU.add)
        v.tensor_tensor(brow[0:1, :], runt[0:1, :], base[0:1, :], ALU.add)
        k.inc(v.drain(), "ve")
        k.wait(t, "ve")
        pp = pss[6]
        t.matmul(pp[:, 0:8], lt, s12, start=True, stop=False)
        t.matmul(pp[:, 0:8], onesr[0:1, :], brow[0:1, :],
                 start=False, stop=True)
        pc_ = pss[7]
        t.matmul(pc_[0:1, 0:8], sc[:, 2:3], s12, start=True, stop=True)
        k.inc(t.drain(), "pe")
        k.wait(v, "pe")
        v.tensor_copy(pre, pp[:, 0:8])
        v.tensor_tensor(runt[0:1, :], runt[0:1, :], pc_[0:1, 0:8], ALU.add)
        v.drain()
        v.tensor_tensor(s12, se1, pre, ALU.mult)
        v.drain()
        v.reduce_sum(sl1, s12, axis=AX.X)
        v.drain()
        v.tensor_tensor(s12, se2, pre, ALU.mult)
        v.drain()
        v.reduce_sum(sl2, s12, axis=AX.X)
        v.drain()
        k.wait_at(v, "dG", k._cnt["dG"])
        v.tensor_copy(su[:, 0:1].bitcast(U32), sl1)
        v.tensor_copy(su[:, 1:2].bitcast(U32), sl2)
        v.drain()
        for kk_, gc in ((0, G1c), (1, G2c)):
            st4 = stg4[:, kk_ * 4:(kk_ + 1) * 4]
            v.tensor_copy(st4[:, 0:1], tokc[:, i:i + 1])
            v.tensor_copy(st4[:, 1:2], prc[:, kk_ * NT + i:kk_ * NT + i + 1])
            v.tensor_copy(st4[:, 2:3], gc[:, i:i + 1])
        k.inc(v.drain(), "ve")
        k.wait(t, "ve")
        k.wait(g, "ve")
        for kk_ in (0, 1):
            idx = su[:, kk_:kk_ + 1].bitcast(U32)
            k.inc(g.indirect_dma_start(
                out=dram["srt4"][:, :],
                out_offset=bass.IndirectOffsetOnAxis(ap=idx, axis=0),
                in_=stg4[:, kk_ * 4:(kk_ + 1) * 4], in_offset=None,
                bounds_check=sbreg, oob_is_err=False), "dG", 16)
    k.bar()


def emit_moe(ctx, xin, xout):
    k = ctx["k"]; dram = ctx["dram"]; A = ctx["A"]; pss = ctx["pss"]
    sc = ctx["sc"]; ident = ctx["ident"]; zc = ctx["zc"]
    t, v, s, g, y = k.t, k.v, k.s, k.g, k.y
    GTOK = NCORE * TOK

    A.push()
    xt = A.take(D); ut = A.take(D); wsp = A.take(D); uT = A.take(D)
    rws = A.take(64)
    lg = A.take(8)
    pr = A.take(8); e1 = A.take(8); pm_ = A.take(8); sels = A.take(8)
    c1 = A.take(8); wmt = A.take(8)
    rsc = A.take(8)
    SEL1c = A.take(NT * 8); SEL2c = A.take(NT * 8)
    G1c = A.take(NT); G2c = A.take(NT)
    cntrow = A.take(8)
    for dch in range(DC):
        k.dma(y, rws[:, dch * 8:(dch + 1) * 8],
              dram["rw"][dch * 128:(dch + 1) * 128, :])
    v.memset(cntrow[0:1, :], 0.0)
    k.bar()
    for i in range(NT):
        k.dma(y, xt, xin[i * 128:(i + 1) * 128, :])
        k.wait(v, "dS"); k.wait(s, "dS")
        emit_ln(k, xt, ut, wsp, sc, affine=None)
        k.wait(y, "ve")
        k.dma(y, dram["ag_u_i"][i * 128:(i + 1) * 128, :], ut)
        for dch in range(DC):
            pb = pss[dch % 2]
            k.wait(t, "ve")
            k.inc(t.transpose(pb[:, 0:128], ut[:, dch * 128:(dch + 1) * 128],
                              ident), "pe")
            k.wait(v, "pe")
            k.inc(v.tensor_copy(uT[:, dch * 128:(dch + 1) * 128], pb[:, 0:128]),
                  "ve")
        k.wait(t, "ve")
        pl = pss[2]
        for dch in range(DC):
            t.matmul(pl[:, 0:8], uT[:, dch * 128:(dch + 1) * 128],
                     rws[:, dch * 8:(dch + 1) * 8],
                     start=(dch == 0), stop=(dch == DC - 1))
        k.inc(t.drain(), "pe")
        k.wait(v, "pe")
        k.inc(v.tensor_copy(lg, pl[:, 0:8]), "ve")
        rmax = rsc[:, 0:1]; nmax = rsc[:, 1:2]; ssum = rsc[:, 2:3]
        rs_ = rsc[:, 3:4]; m1 = rsc[:, 4:5]; m2 = rsc[:, 5:6]
        v.reduce_max(rmax, lg, axis=AX.X)
        v.drain()
        k.inc(v.tensor_scalar_mul(nmax, rmax, -1.0), "ve")
        k.wait(s, "ve")
        s.activation(out=pr, in_=lg, func=AF.Exp, bias=nmax, accum_out=ssum)
        k.inc(s.drain(), "se")
        k.wait(v, "se")
        v.reciprocal(rs_, ssum)
        v.drain()
        v.tensor_scalar_mul(pr, pr, rs_)
        v.drain()
        v.reduce_max(m1, pr, axis=AX.X)
        v.drain()
        v.tensor_scalar(e1, pr, m1, None, ALU.is_ge)
        v.drain()
        v.tensor_tensor_scan(c1, e1, zc[:, 0:8], 0.0, ALU.add, ALU.add)
        v.drain()
        v.tensor_scalar(c1, c1, 1.0, None, ALU.is_le)
        v.drain()
        v.tensor_tensor(sels, e1, c1, ALU.mult)
        v.drain()
        if MOE_SPARSE:
            v.tensor_copy(SEL1c[:, i * 8:(i + 1) * 8], sels)
            v.tensor_copy(G1c[:, i:i + 1], m1)
            v.drain()
        v.tensor_tensor(pm_, pr, sels, ALU.mult)
        v.drain()
        v.tensor_tensor(pm_, pr, pm_, ALU.subtract)
        v.drain()
        v.reduce_max(m2, pm_, axis=AX.X)
        v.drain()
        v.tensor_scalar(e1, pm_, m2, None, ALU.is_ge)
        v.drain()
        v.tensor_tensor_scan(c1, e1, zc[:, 0:8], 0.0, ALU.add, ALU.add)
        v.drain()
        v.tensor_scalar(c1, c1, 1.0, None, ALU.is_le)
        v.drain()
        v.tensor_tensor(e1, e1, c1, ALU.mult)
        v.drain()
        if MOE_SPARSE:
            v.tensor_copy(SEL2c[:, i * 8:(i + 1) * 8], e1)
            v.tensor_copy(G2c[:, i:i + 1], m2)
            v.drain()
        v.tensor_tensor(sels, sels, e1, ALU.add)
        v.drain()
        k.inc(v.tensor_tensor(wmt, pr, sels, ALU.mult), "ve")
        if MOE_SPARSE:
            k.wait(t, "ve")
            pc_ = pss[5]
            t.matmul(pc_[0:1, 0:8], sc[:, 2:3], sels, start=True, stop=True)
            k.inc(t.drain(), "pe")
            k.wait(v, "pe")
            k.inc(v.tensor_tensor(cntrow[0:1, :], cntrow[0:1, :],
                                  pc_[0:1, 0:8], ALU.add), "ve")
            v.drain()
        k.wait(y, "ve")
        if not MOE_SPARSE:
            k.dma(y, dram["ag_w_i"][i * 128:(i + 1) * 128, :], wmt)
    if MOE_SPARSE:
        emit_moe_sort(ctx, A, SEL1c, SEL2c, G1c, G2c, cntrow)
    A.pop()
    k.bar()

    with allow_ici_all_to_all():
        k.inc(g.collective_compute(
            "AllGather", ALU.bypass, replica_groups=[list(range(NCORE))],
            ins=[dram["ag_u_i"].opt()],
            outs=[dram["ag_u_o"][0:NCORE * TOK, :].opt()]), "cc")
        if MOE_SPARSE:
            for a_, b_ in (("srt4", "ag4"), ("cnt_i", "ag_cnt")):
                k.inc(g.collective_compute(
                    "AllGather", ALU.bypass,
                    replica_groups=[list(range(NCORE))],
                    ins=[dram[a_].opt()], outs=[dram[b_].opt()]), "cc")
        else:
            k.inc(g.collective_compute(
                "AllGather", ALU.bypass, replica_groups=[list(range(NCORE))],
                ins=[dram["ag_w_i"].opt()], outs=[dram["ag_w_o"].opt()]), "cc")
    g.wait_ge(k.sem("cc"), k._cnt["cc"])
    k.bar(dmas=False)
    if MOE_SPARSE:
        emit_moe_sparse(ctx, xin, xout)
        return

    A.push()
    msv = A.take(NE * 4); ms_ = A.take(NE * 4)
    k.dma(y, msv, dram["msv"][:, :])
    k.dma(y, ms_, dram["ms"][:, :])
    ctx["msv"], ctx["ms"] = msv, ms_
    k.bar()
    xT = A.take(8 * 512)
    vT = A.take(4 * 512)
    yT = A.take(4 * 512)
    acc = A.take(4 * D)
    wm = A.take(32)
    xtl = A.take(D)
    wvb = [A.take(8 * 512), A.take(8 * 512)]
    wb_ = [A.take(8 * 512), A.take(8 * 512)]
    w1b = [A.take(4 * D), A.take(4 * D)]
    ms1b = [A.take(D), A.take(D)]
    pe_at_parity = [0, 0]
    ve_at_parity = [0, 0]
    for scg in range(GTOK // 512):
        for sb_ in range(4):
            gi = scg * 4 + sb_
            k.dma(y, xtl, dram["ag_u_o"][gi * 128:(gi + 1) * 128, :])
            k.dma(y, wm[:, sb_ * 8:(sb_ + 1) * 8],
                  dram["ag_w_o"][gi * 128:(gi + 1) * 128, :])
            k.wait(t, "dS")
            for dch in range(DC):
                pb = pss[6 + dch % 2]
                k.wait(t, "ve")
                k.inc(t.transpose(pb[:, 0:128], xtl[:, dch * 128:(dch + 1) * 128],
                                  ident), "pe")
                k.wait(v, "pe")
                k.inc(v.tensor_copy(xT[:, dch * 512 + sb_ * 128:
                                       dch * 512 + (sb_ + 1) * 128],
                                    pb[:, 0:128]), "ve")
            k.wait(t, "ve")
            k.wait(y, "ve")
        for e in range(NE):
            par = e % 2
            k.wait_at(y, "pe", pe_at_parity[par])
            k.wait_at(y, "ve", ve_at_parity[par])
            k.dma(y, wvb[par], dram["mwv"][e * 128:(e + 1) * 128, :])
            k.dma(y, wb_[par], dram["mw"][e * 128:(e + 1) * 128, :])
            k.dma(y, w1b[par], dram["mw1"][e * 128:(e + 1) * 128, :])
            k.dma(y, ms1b[par], dram["ms1"][:, e * D:(e + 1) * D])
            k.wait(t, "dS")
            for src_w, dst, scl, fn in (
                    (wvb[par], vT, "msv", AF.Identity),
                    (wb_[par], yT, "ms", AF.Gelu_apprx_tanh)):
                for ft in range(4):
                    pf = pss[ft % 2]
                    for kc in range(DC):
                        t.matmul(pf[:, :],
                                 src_w[:, kc * 512 + ft * 128:
                                       kc * 512 + (ft + 1) * 128],
                                 xT[:, kc * 512:(kc + 1) * 512],
                                 start=(kc == 0), stop=(kc == DC - 1))
                    k.inc(t.drain(), "pe")
                    k.wait(s, "pe")
                    k.inc(s.activation(out=dst[:, ft * 512:(ft + 1) * 512],
                                       in_=pf[:, :], func=fn,
                                       scale=ctx[scl][:, e * 4 + ft:e * 4 + ft + 1]),
                          "se")
                    k.wait(t, "se")
            k.wait(v, "se")
            for ft in range(4):
                k.inc(v.tensor_tensor(vT[:, ft * 512:(ft + 1) * 512],
                                      vT[:, ft * 512:(ft + 1) * 512],
                                      yT[:, ft * 512:(ft + 1) * 512],
                                      ALU.mult), "ve")
            k.wait(t, "ve")
            for sb_ in range(4):
                for half in range(2):
                    ph_ = pss[2 + half]
                    for kf in range(4):
                        t.matmul(ph_[:, :],
                                 vT[:, kf * 512 + sb_ * 128:
                                    kf * 512 + (sb_ + 1) * 128],
                                 w1b[par][:, kf * D + half * 512:
                                          kf * D + half * 512 + 512],
                                 start=(kf == 0), stop=(kf == 3))
                k.inc(t.drain(), "pe")
                if sb_ == 3:
                    pe_at_parity[par] = k._cnt["pe"]
                k.wait(v, "pe")
                gate = wm[:, sb_ * 8 + e: sb_ * 8 + e + 1]
                for half in range(2):
                    ph_ = pss[2 + half]
                    tmp = xtl[:, half * 512:(half + 1) * 512]
                    a_ = acc[:, sb_ * D + half * 512: sb_ * D + half * 512 + 512]
                    v.tensor_tensor(tmp, ph_[:, :],
                                    ms1b[par][:, half * 512:(half + 1) * 512],
                                    ALU.mult)
                    if e == 0:
                        k.inc(v.tensor_scalar_mul(a_, tmp, gate), "ve")
                    else:
                        k.inc(v.scalar_tensor_tensor(a_, tmp, gate, a_,
                                                     ALU.mult, ALU.add), "ve")
                k.wait(t, "ve")
            ve_at_parity[par] = k._cnt["ve"]
        k.wait(y, "ve")
        for sb_ in range(4):
            gi = scg * 4 + sb_
            k.dma(y, dram["rs_i"][gi * 128:(gi + 1) * 128, :],
                  acc[:, sb_ * D:(sb_ + 1) * D])
    A.pop()
    k.bar()

    with allow_ici_all_to_all():
        k.inc(g.collective_compute(
            "ReduceScatter", ALU.add, replica_groups=[list(range(NCORE))],
            ins=[dram["rs_i"][0:NCORE * TOK, :].opt()],
            outs=[dram["rs_o"][0:TOK, :].opt()]), "cc")
    g.wait_ge(k.sem("cc"), k._cnt["cc"])
    k.bar(dmas=False)

    A.push()
    xa = A.take(D); ra = A.take(D)
    for i in range(NT):
        k.dma(y, xa, xin[i * 128:(i + 1) * 128, :])
        k.dma(y, ra, dram["rs_o"][i * 128:(i + 1) * 128, :])
        k.wait(v, "dS")
        k.inc(v.tensor_tensor(xa, xa, ra, ALU.add), "ve")
        k.wait(y, "ve")
        k.dma(y, xout[i * 128:(i + 1) * 128, :], xa)
    A.pop()



def kernel(**inputs):
    nc = build_program()
    in_maps = [make_in_map(c, inputs) for c in range(NCORE)]
    res = run_bass_kernel_spmd(nc, in_maps, core_ids=list(range(NCORE)))
    out = np.zeros((B, T, D), dtype=np.float32)
    for c in range(NCORE):
        yc = np.asarray(res.results[c]["y"]).reshape(B, SLICE, D)
        out[:, c * SLICE:(c + 1) * SLICE, :] = yc
    return out


# version 05
# speedup vs baseline: 1.6383x; 1.0241x over previous
import numpy as np
from contextlib import ExitStack

import concourse.bass as bass
import concourse.mybir as mybir
from concourse.bass_utils import run_bass_kernel_spmd
from concourse.aot_env import allow_ici_all_to_all

F32 = mybir.dt.float32
U32 = mybir.dt.uint32
AF = mybir.ActivationFunctionType
ALU = mybir.AluOpType
AX = mybir.AxisListType

NCORE = 8
B, T, D = 4, 4096, 1024
SLICE = 512
TOK = B * SLICE
NT = TOK // 128
DC = D // 128
EPS = 1e-5

NE, F = 8, 4096
FSL = F // NCORE

SUBLAYERS = (1, 2, 3, 4)
MOE_SPARSE = True
NCH = 40
NSLOT = NCH * 128
STAGE = 99
DEBUG_DUMP = ()



def _repl(v, p=128):
    v = np.ascontiguousarray(np.asarray(v), dtype=np.float32)
    return np.broadcast_to(v.reshape(1, -1), (p, v.size)).copy()


def _colsplit(v, tile=128):
    v = np.asarray(v, dtype=np.float32)
    return np.ascontiguousarray(v.reshape(-1, tile).T)


def _rope_tables():
    half = 32
    inv = 10000.0 ** (-(np.arange(half, dtype=np.float64)) * 2.0 / 64.0)
    ang = np.arange(T, dtype=np.float64)[None, :] * inv[:, None]
    out = []
    for m in (np.cos(ang), np.sin(ang)):
        hi = m.astype(np.float32)
        lo = (m - hi.astype(np.float64)).astype(np.float32)
        out.append(np.tile(hi, (4, 1)))
        out.append(np.tile(lo, (4, 1)))
    return out


def _rope_perm():
    perm = np.zeros(512, dtype=np.int64)
    for h in range(8):
        for j in range(64):
            p = (h // 4) * 256 + (j % 2) * 128 + (h % 4) * 32 + j // 2
            perm[p] = h * 64 + j
    return perm


def make_in_map(c, inp):
    hg = c % 2
    hb = c % 2
    d = {}
    x = np.asarray(inp["L0_attn_a_x"], dtype=np.float32)
    d["x0"] = np.ascontiguousarray(
        x[:, c * SLICE:(c + 1) * SLICE, :].reshape(TOK, D))
    for nm, key in (("wq", "L0_attn_a_Wq"), ("wk", "L0_attn_a_Wk"),
                    ("wv", "L0_attn_a_Wv"), ("wg", "L0_attn_a_Wg")):
        W = np.asarray(inp[key], dtype=np.float32)
        d[nm] = np.ascontiguousarray(W[:, hg * 512:(hg + 1) * 512])
    d["wgk1"] = np.ascontiguousarray(np.asarray(inp["L0_attn_a_Wgk1"], np.float32))
    d["wgk2"] = np.ascontiguousarray(
        np.asarray(inp["L0_attn_a_Wgk2"], np.float32)[:, hg * 512:(hg + 1) * 512])
    d["bgk2"] = _colsplit(np.asarray(inp["L0_attn_a_bgk2"], np.float32)
                          [hg * 512:(hg + 1) * 512])
    d["rmsw"] = _repl(inp["L0_attn_a_rms_w"])
    d["wo"] = np.ascontiguousarray(
        np.asarray(inp["L0_attn_a_Wo"], np.float32)[hg * 512:(hg + 1) * 512, :])
    d["ln1w"] = _repl(inp["L0_attn_a_ln1_w"]); d["ln1b"] = _repl(inp["L0_attn_a_ln1_b"])
    d["ln2w"] = _repl(inp["L0_attn_a_ln2_w"]); d["ln2b"] = _repl(inp["L0_attn_a_ln2_b"])
    d["w1"] = np.ascontiguousarray(np.asarray(inp["L0_attn_a_W1"], np.float32))
    d["b1c"] = _colsplit(inp["L0_attn_a_b1"])
    d["w2"] = np.ascontiguousarray(np.asarray(inp["L0_attn_a_W2"], np.float32))
    d["b2"] = _repl(inp["L0_attn_a_b2"])
    fs = slice(c * FSL, (c + 1) * FSL)
    wv = np.asarray(inp["L0_ffn_wv"], np.float32)[:, :, fs]
    w_ = np.asarray(inp["L0_ffn_w"], np.float32)[:, :, fs]
    w1m = np.asarray(inp["L0_ffn_w1"], np.float32)[:, fs, :]
    sv = np.asarray(inp["L0_ffn_sv"], np.float32)[:, 0, fs]
    s_ = np.asarray(inp["L0_ffn_s"], np.float32)[:, 0, fs]
    mwv = wv.reshape(NE, 8, 128, FSL).transpose(0, 2, 1, 3).reshape(NE * 128, 8 * FSL)
    mw = w_.reshape(NE, 8, 128, FSL).transpose(0, 2, 1, 3).reshape(NE * 128, 8 * FSL)
    svc = sv.reshape(NE, 4, 128).transpose(0, 2, 1).reshape(NE * 128, 4)
    ssc = s_.reshape(NE, 4, 128).transpose(0, 2, 1).reshape(NE * 128, 4)
    d["mwv"] = np.ascontiguousarray(np.concatenate([mwv, svc], axis=1))
    d["mw"] = np.ascontiguousarray(np.concatenate([mw, ssc], axis=1))
    d["mw1"] = np.ascontiguousarray(
        w1m.reshape(NE, 4, 128, D).transpose(0, 2, 1, 3).reshape(NE * 128, 4 * D))
    d["msv"] = np.concatenate([_colsplit(sv[e]) for e in range(NE)], axis=1)
    d["ms"] = np.concatenate([_colsplit(s_[e]) for e in range(NE)], axis=1)
    s1 = np.asarray(inp["L0_ffn_s1"], np.float32)[:, 0, :]
    d["ms1"] = np.concatenate([_repl(s1[e]) for e in range(NE)], axis=1)
    d["ms1g"] = np.ascontiguousarray(
        np.repeat(s1, 128, axis=0))
    d["msvg"] = np.ascontiguousarray(np.repeat(sv, 128, axis=0))
    d["mssg"] = np.ascontiguousarray(np.repeat(s_, 128, axis=0))
    d["rw"] = np.ascontiguousarray(np.asarray(inp["L0_ffn_router_w"], np.float32))
    p_ = np.arange(128, dtype=np.float32)[:, None]
    d["tokid"] = np.ascontiguousarray(
        (c * TOK + np.arange(NT, dtype=np.float32)[None, :] * 128 + p_))
    pr = np.zeros((128, 2 * NT), dtype=np.float32)
    for kk_ in range(2):
        for i_ in range(NT):
            pr[:, kk_ * NT + i_] = c * 2 * TOK + kk_ * TOK + i_ * 128 + p_[:, 0]
    d["pairrow"] = np.ascontiguousarray(pr)
    d["jrow"] = np.ascontiguousarray(
        (np.arange(NCH, dtype=np.float32) * 128)[None, :])
    lt = (np.arange(128)[:, None] < np.arange(128)[None, :]).astype(np.float32)
    d["ltstrict"] = np.ascontiguousarray(lt)
    d["iota8"] = np.broadcast_to(np.arange(8, dtype=np.float32), (128, 8)).copy()
    perm = _rope_perm()
    rows = hb * 512 + np.arange(512)
    d["bwq"] = np.ascontiguousarray(
        np.asarray(inp["L0_attn_b_Wq"], np.float32)[rows][perm].T)
    d["bwk"] = np.ascontiguousarray(
        np.asarray(inp["L0_attn_b_Wk"], np.float32)[rows][perm].T)
    d["bwv"] = np.ascontiguousarray(
        np.asarray(inp["L0_attn_b_Wv"], np.float32)[rows].T)
    d["bwo"] = np.ascontiguousarray(
        np.asarray(inp["L0_attn_b_Wo"], np.float32)[:, rows].T)
    ch, cl, sh, sl = _rope_tables()
    d["rch"], d["rcl"], d["rsh"], d["rsl"] = ch, cl, sh, sl
    d["ident"] = np.eye(128, dtype=np.float32)
    st = np.arange(128)
    d["trimask"] = (st[:, None] <= st[None, :]).astype(np.float32)
    dm = np.zeros((4, 128, 512), dtype=np.float32)
    for r in range(4):
        dm[r] = (np.arange(512)[None, :] <= (r * 128 + st)[:, None])
    d["diagmask"] = np.ascontiguousarray(dm.reshape(512, 512))
    d["dmneg"] = np.ascontiguousarray(((dm - 1.0) * 1e30).reshape(512, 512))
    return d



class KB:
    def __init__(self, nc, es):
        self.nc = nc
        self.t, self.v, self.s, self.g, self.y = \
            nc.tensor, nc.vector, nc.scalar, nc.gpsimd, nc.sync
        self.engines = [self.t, self.v, self.s, self.g, self.y]
        self._sems, self._cnt = {}, {}
        for nm in ("bar", "dS", "dG", "dC", "dV", "cc",
                   "pe", "ve", "se", "ge"):
            self._sems[nm] = es.enter_context(nc.semaphore(nm))
            self._cnt[nm] = 0
        self._qmap = {id(self.y): "dS", id(self.g): "dG",
                      id(self.s): "dC", id(self.v): "dV"}

    def sem(self, nm):
        return self._sems[nm]

    def inc(self, instr, nm, k=1):
        if nm in ("pe", "ve", "se", "ge"):
            eng = {"pe": self.t, "ve": self.v, "se": self.s,
                   "ge": self.g}[nm]
            eng.drain().then_inc(self._sems[nm], k)
        else:
            instr.then_inc(self._sems[nm], k)
        self._cnt[nm] += k
        return instr

    def dma(self, eng, out, in_, **kw):
        return self.inc(eng.dma_start(out=out, in_=in_, **kw),
                        self._qmap[id(eng)], 16)

    def wait_at(self, eng, nm, val):
        if val:
            eng.wait_ge(self._sems[nm], val)

    def wait(self, eng, *names):
        for nm in names:
            if self._cnt[nm]:
                eng.wait_ge(self._sems[nm], self._cnt[nm])

    def bar(self, dmas=True):
        b = self._sems["bar"]
        for e in self.engines:
            e.drain().then_inc(b, 1)
        self._cnt["bar"] += len(self.engines)
        tgt = self._cnt["bar"]
        for e in self.engines:
            e.wait_ge(b, tgt)
            if dmas:
                self.wait(e, "dS", "dG", "dC", "dV")


class Arena:
    def __init__(self, tens, ncols):
        self.t = tens
        self.n = ncols
        self.off = 0
        self.marks = []

    def take(self, cols):
        assert self.off + cols <= self.n, (self.off, cols, self.n)
        ap = self.t[:, self.off:self.off + cols]
        self.off += cols
        return ap

    def push(self):
        self.marks.append(self.off)

    def pop(self):
        self.off = self.marks.pop()


def emit_ln(k, src, dst, wsp, sc, affine=None, n_free=D):
    v, s = k.v, k.s
    eps = sc[:, 0:1]
    nmu = sc[:, 4:5]; ssq = sc[:, 5:6]; rst = sc[:, 6:7]
    inv = 1.0 / n_free
    k.wait(v, "se")
    v.reduce_sum(nmu, src, axis=AX.X)
    v.drain()
    k.inc(v.tensor_scalar_mul(nmu, nmu, -inv), "ve")
    k.wait(s, "ve")
    s.activation(out=wsp, in_=src, func=AF.Identity, bias=nmu)
    s.activation(out=dst, in_=wsp, func=AF.Square, accum_out=ssq)
    s.drain()
    s.activation(out=ssq, in_=ssq, func=AF.Identity, scale=inv, bias=eps)
    k.inc(s.activation(out=ssq, in_=ssq, func=AF.Sqrt), "se")
    k.wait(v, "se")
    k.inc(v.reciprocal(rst, ssq), "ve")
    if affine is None:
        k.inc(v.tensor_scalar_mul(dst, wsp, rst), "ve")
    else:
        w_ap, b_ap = affine
        k.inc(v.scalar_tensor_tensor(dst, wsp, rst, w_ap, ALU.mult, ALU.mult),
              "ve")
        if b_ap is not None:
            k.inc(v.tensor_tensor(dst, dst, b_ap, ALU.add), "ve")



def build_program(sublayers=None):
    sublayers = SUBLAYERS if sublayers is None else sublayers
    nc = bass.Bass("TRN2", target_bir_lowering=False, debug=False,
                   detect_race_conditions=False)
    dram = {}

    def din(name, shape, dt=F32):
        dram[name] = nc.dram_tensor(name, shape, dt, kind="ExternalInput").ap()

    def dint(name, shape, dt=F32):
        dram[name] = nc.dram_tensor(name, shape, dt).ap()

    for nm, sh in (("x0", [TOK, D]),
                   ("wq", [D, 512]), ("wk", [D, 512]), ("wv", [D, 512]),
                   ("wg", [D, 512]), ("wgk1", [D, 16]), ("wgk2", [16, 512]),
                   ("bgk2", [128, 4]), ("rmsw", [128, 256]), ("wo", [512, D]),
                   ("ln1w", [128, D]), ("ln1b", [128, D]),
                   ("ln2w", [128, D]), ("ln2b", [128, D]),
                   ("w1", [D, 3072]), ("b1c", [128, 24]),
                   ("w2", [3072, D]), ("b2", [128, D]),
                   ("mwv", [NE * 128, 8 * FSL + 4]), ("mw", [NE * 128, 8 * FSL + 4]),
                   ("mw1", [NE * 128, 4 * D]), ("ms1g", [NE * 128, D]),
                   ("msvg", [NE * 128, FSL]), ("mssg", [NE * 128, FSL]),
                   ("tokid", [128, NT]), ("pairrow", [128, 2 * NT]),
                   ("jrow", [1, NCH]), ("ltstrict", [128, 128]),
                   ("iota8", [128, 8]),
                   ("msv", [128, NE * 4]), ("ms", [128, NE * 4]),
                   ("ms1", [128, NE * D]), ("rw", [D, NE]),
                   ("bwq", [D, 512]), ("bwk", [D, 512]), ("bwv", [D, 512]),
                   ("bwo", [512, D]),
                   ("rch", [128, T]), ("rcl", [128, T]),
                   ("rsh", [128, T]), ("rsl", [128, T]),
                   ("ident", [128, 128]), ("trimask", [128, 128]),
                   ("diagmask", [512, 512]), ("dmneg", [512, 512])):
        din(nm, sh)
    yout = nc.dram_tensor("y", [TOK, D], F32, kind="ExternalOutput").ap()

    dint("xh", [TOK, D])
    dint("xh2", [TOK, D])
    dint("xh3", [TOK, D])
    dint("uh", [TOK, D])
    dint("v1h", [TOK, D])
    dint("ag_u_i", [TOK, D])
    dint("ag_u_o", [NCORE * TOK + 128, D])
    dint("ag_w_i", [TOK, NE])
    dint("ag_w_o", [NCORE * TOK, NE])
    dint("srt4", [NSLOT, 4])
    dint("cnt_i", [1, NE])
    dint("ag4", [NCORE * NSLOT, 4])
    dint("ag_cnt", [NCORE, NE])
    dint("rs_i", [2 * NCORE * TOK, D])
    dint("rs_o", [2 * TOK, D])
    dint("a2a_i", [NCORE, 4, 128, D])
    dint("a2a_o", [NCORE, 4, 128, D])
    dint("p2a_i", [NCORE, 4, 128, D])
    dint("p2a_o", [NCORE, 4, 128, D])

    es = ExitStack()
    with es:
        k = KB(nc, es)
        t, v, s, g, y = k.t, k.v, k.s, k.g, k.y

        cst = es.enter_context(nc.sbuf_tensor("cst", [128, 6144], F32))
        bigt = es.enter_context(nc.sbuf_tensor("big", [128, 45056], F32))
        A = Arena(bigt, 45056)
        pss = [es.enter_context(nc.psum_tensor("ps%d" % i, [128, 512], F32))
               for i in range(8)]

        ln1w = cst[:, 0:D]; ln1b = cst[:, D:2 * D]
        ln2w = cst[:, 2 * D:3 * D]; ln2b = cst[:, 3 * D:4 * D]
        rmsw = cst[:, 4 * D:4 * D + 256]
        ident = cst[:, 4352:4480]
        tri = cst[:, 4480:4608]
        zc = cst[:, 4608:4736]
        bgk2 = cst[:, 4736:4740]
        sc = cst[:, 4740:4804]
        b1c = cst[:, 4804:4828]
        ln_aff1 = (ln1w, ln1b)
        ln_aff2 = (ln2w, ln2b)

        for dst, srcn in ((ln1w, "ln1w"), (ln1b, "ln1b"), (ln2w, "ln2w"),
                          (ln2b, "ln2b"), (rmsw, "rmsw"), (ident, "ident"),
                          (tri, "trimask"), (bgk2, "bgk2"), (b1c, "b1c")):
            k.dma(y, dst, dram[srcn][:, :])
        g.memset(zc, 0.0)
        g.memset(sc[:, 2:64], 0.0)
        g.memset(sc[:, 0:1], EPS)
        g.memset(sc[:, 1:2], 1.0 / 16.0)
        g.memset(sc[:, 2:3], 1.0)
        k.bar()

        ctx = dict(k=k, nc=nc, dram=dram, A=A, pss=pss, sc=sc, ident=ident,
                   tri=tri, zc=zc, rmsw=rmsw, bgk2=bgk2, b1c=b1c,
                   ln_aff1=ln_aff1, ln_aff2=ln_aff2)

        xin = dram["x0"]
        if 1 in sublayers:
            emit_attn_a(ctx, xin, dram["xh"])
            xin = dram["xh"]
            k.bar()
        if 2 in sublayers:
            emit_moe(ctx, xin, dram["xh2"])
            xin = dram["xh2"]
            k.bar()
        if 3 in sublayers:
            emit_attn_b(ctx, xin, dram["xh3"])
            xin = dram["xh3"]
            k.bar()
        if 4 in sublayers:
            emit_moe(ctx, xin, dram["xh"])
            xin = dram["xh"]
            k.bar()

        A.push()
        tb = A.take(D)
        for i in range(NT):
            k.dma(y, tb, xin[i * 128:(i + 1) * 128, :])
            k.wait(y, "dS")
            k.dma(y, yout[i * 128:(i + 1) * 128, :], tb)
            k.wait(y, "dS")
        for nm in DEBUG_DUMP:
            src_ap = dram[nm]
            flat = int(np.prod(src_ap.shape))
            dbg = nc.dram_tensor("dbg_" + nm, [flat // 1024, 1024], F32,
                                 kind="ExternalOutput").ap()
            s2 = src_ap.rearrange(
                " ".join("abcd"[:len(src_ap.shape)]) + " -> (" +
                " ".join("abcd"[:len(src_ap.shape)]) + ")"
            ).rearrange("(r c) -> r c", c=1024)
            for i in range(flat // 1024 // 128):
                k.dma(y, tb, s2[i * 128:(i + 1) * 128, :])
                k.wait(y, "dS")
                k.dma(y, dbg[i * 128:(i + 1) * 128, :], tb)
                k.wait(y, "dS")
        A.pop()
        k.bar()
    return nc



def emit_attn_a(ctx, xin, xh):
    k = ctx["k"]; dram = ctx["dram"]; A = ctx["A"]; pss = ctx["pss"]
    sc = ctx["sc"]; ident = ctx["ident"]
    t, v, s, g, y = k.t, k.v, k.s, k.g, k.y
    uh, v1h = dram["uh"], dram["v1h"]
    a2a_i, a2a_o = dram["a2a_i"], dram["a2a_o"]
    p2a_i, p2a_o = dram["p2a_i"], dram["p2a_o"]

    A.push()
    xt = A.take(D); ut = A.take(D); at = A.take(D); wsp = A.take(D)
    aT = A.take(D)
    for i in range(NT):
        sq, sub = i // 4, i % 4
        k.dma(y, xt, xin[i * 128:(i + 1) * 128, :])
        k.wait(v, "dS"); k.wait(s, "dS")
        emit_ln(k, xt, ut, wsp, sc, affine=None)
        emit_ln(k, ut, at, wsp, sc, affine=ctx["ln_aff1"])
        k.wait(y, "ve")
        k.dma(y, uh[i * 128:(i + 1) * 128, :], ut)
        k.wait(t, "ve")
        for dch in range(DC):
            t.transpose(pss[dch // 4][:, (dch % 4) * 128:(dch % 4 + 1) * 128],
                        at[:, dch * 128:(dch + 1) * 128], ident)
        k.inc(t.drain(), "pe")
        k.wait(v, "pe")
        v.tensor_copy(aT[:, 0:512], pss[0][:, :])
        k.inc(v.tensor_copy(aT[:, 512:1024], pss[1][:, :]), "ve")
        k.wait(y, "ve")
        for r in (2 * sq, 2 * sq + 1):
            k.dma(y, a2a_i[r, sub, :, :], aT)
    A.pop()
    k.bar()
    if STAGE < 2:
        return

    with allow_ici_all_to_all():
        k.inc(g.collective_compute(
            "AllToAll", ALU.bypass, replica_groups=[list(range(NCORE))],
            ins=[a2a_i.opt()], outs=[a2a_o.opt()]), "cc")
    g.wait_ge(k.sem("cc"), k._cnt["cc"])
    k.bar(dmas=False)
    if STAGE < 3:
        return

    emit_gla(ctx)
    k.bar()
    if STAGE < 4:
        return

    with allow_ici_all_to_all():
        k.inc(g.collective_compute(
            "AllToAll", ALU.bypass, replica_groups=[list(range(NCORE))],
            ins=[p2a_i.opt()], outs=[p2a_o.opt()]), "cc")
    g.wait_ge(k.sem("cc"), k._cnt["cc"])
    k.bar(dmas=False)

    A.push()
    pa = A.take(D); pb_ = A.take(D); utl = A.take(D); vt = A.take(D)
    for i in range(NT):
        sq, sub = i // 4, i % 4
        k.dma(y, pa, p2a_o[2 * sq, sub, :, :])
        k.dma(y, pb_, p2a_o[2 * sq + 1, sub, :, :])
        k.dma(y, utl, uh[i * 128:(i + 1) * 128, :])
        k.wait(v, "dS")
        k.inc(v.tensor_tensor(pa, pa, pb_, ALU.add), "ve")
        k.wait(t, "ve")
        for dch in range(DC):
            t.transpose(pss[dch // 4][:, (dch % 4) * 128:(dch % 4 + 1) * 128],
                        pa[:, dch * 128:(dch + 1) * 128], ident)
        k.inc(t.drain(), "pe")
        k.wait(v, "pe")
        v.tensor_tensor(vt[:, 0:512], utl[:, 0:512], pss[0][:, :],
                        ALU.add)
        k.inc(v.tensor_tensor(vt[:, 512:1024], utl[:, 512:1024],
                              pss[1][:, :], ALU.add), "ve")
        k.wait(y, "ve")
        k.dma(y, v1h[i * 128:(i + 1) * 128, :], vt)
    A.pop()
    k.bar()
    if STAGE < 5:
        return

    emit_ffn_a(ctx, xin, xh)


def emit_gla(ctx):
    k = ctx["k"]; dram = ctx["dram"]; A = ctx["A"]; pss = ctx["pss"]
    sc = ctx["sc"]; ident = ctx["ident"]; tri = ctx["tri"]; zc = ctx["zc"]
    rmsw = ctx["rmsw"]; bgk2 = ctx["bgk2"]
    t, v, s, g, y = k.t, k.v, k.s, k.g, k.y
    a2a_o, p2a_i = dram["a2a_o"], dram["p2a_i"]
    one16 = sc[:, 1:2]

    A.push()
    wq = A.take(8 * 512); wk = A.take(8 * 512)
    wvw = A.take(8 * 512); wgw = A.take(8 * 512)
    wo = A.take(4 * D)
    wgk1 = A.take(8 * 16)
    wgk2 = A.take(512)
    for dch in range(DC):
        k.dma(y, wq[:, dch * 512:(dch + 1) * 512],
              dram["wq"][dch * 128:(dch + 1) * 128, :])
        k.dma(y, wk[:, dch * 512:(dch + 1) * 512],
              dram["wk"][dch * 128:(dch + 1) * 128, :])
        k.dma(y, wvw[:, dch * 512:(dch + 1) * 512],
              dram["wv"][dch * 128:(dch + 1) * 128, :])
        k.dma(y, wgw[:, dch * 512:(dch + 1) * 512],
              dram["wg"][dch * 128:(dch + 1) * 128, :])
        k.dma(y, wgk1[:, dch * 16:(dch + 1) * 16],
              dram["wgk1"][dch * 128:(dch + 1) * 128, :])
    for kc in range(4):
        k.dma(y, wo[:, kc * D:(kc + 1) * D],
              dram["wo"][kc * 128:(kc + 1) * 128, :])
    k.dma(y, wgk2[0:16, :], dram["wgk2"][:, :])

    xc = A.take(4 * D)
    qT = A.take(4 * 512); kT = A.take(4 * 512)
    lsT = A.take(4 * 512)
    vts = A.take(4 * 512); gts = A.take(4 * 512)
    hst = A.take(4 * 256)
    u16 = A.take(512)
    wrk = A.take(2048)
    out_t = A.take(D)
    gsc = A.take(16)

    g.memset(hst[:, :], 0.0)
    k.bar()
    if STAGE < 3.05:
        A.pop()
        return

    for sup in range(T // 512):
        for sb_ in range(4):
            k.dma(y, xc[:, sb_ * D:(sb_ + 1) * D], a2a_o[sup, sb_, :, :])
        k.bar(dmas=True)
        if STAGE < 3.08:
            continue

        for dt_ in range(4):
            pq, pk = pss[0], pss[1]
            for ww, pp in ((wq, pq), (wk, pk)):
                for cc2 in range(4):
                    for kc in range(DC):
                        l_ = ww[:, kc * 512 + dt_ * 128: kc * 512 + (dt_ + 1) * 128]
                        r_ = xc[:, cc2 * D + kc * 128: cc2 * D + (kc + 1) * 128]
                        t.matmul(pp[:, cc2 * 128:(cc2 + 1) * 128], l_, r_,
                                 start=(kc == 0), stop=(kc == DC - 1))
            k.inc(t.drain(), "pe")
            k.wait(v, "pe")
            k.inc(v.tensor_copy(qT[:, dt_ * 512:(dt_ + 1) * 512], pq[:, :]), "ve")
            k.inc(v.tensor_copy(kT[:, dt_ * 512:(dt_ + 1) * 512], pk[:, :]), "ve")
            k.wait(t, "ve")
        if STAGE < 3.12:
            continue
        for cc2 in range(4):
            pv_, pg = pss[2], pss[3]
            for ww, pp in ((wvw, pv_), (wgw, pg)):
                for kc in range(DC):
                    l_ = xc[:, cc2 * D + kc * 128: cc2 * D + (kc + 1) * 128]
                    t.matmul(pp[:, :], l_, ww[:, kc * 512:(kc + 1) * 512],
                             start=(kc == 0), stop=(kc == DC - 1))
            k.inc(t.drain(), "pe")
            k.wait(v, "pe")
            k.inc(v.tensor_copy(vts[:, cc2 * 512:(cc2 + 1) * 512], pv_[:, :]), "ve")
            k.inc(v.tensor_copy(gts[:, cc2 * 512:(cc2 + 1) * 512], pg[:, :]), "ve")
            k.wait(t, "ve")
        if STAGE < 3.2:
            continue
        pu = pss[4]
        for cc2 in range(4):
            for kc in range(DC):
                t.matmul(pu[0:16, cc2 * 128:(cc2 + 1) * 128],
                         wgk1[:, kc * 16:(kc + 1) * 16],
                         xc[:, cc2 * D + kc * 128: cc2 * D + (kc + 1) * 128],
                         start=(kc == 0), stop=(kc == DC - 1))
        k.inc(t.drain(), "pe")
        k.wait(v, "pe")
        k.inc(v.tensor_copy(u16[0:16, :], pu[0:16, :]), "ve")
        k.wait(t, "ve")
        for dt_ in range(4):
            pz = pss[5]
            t.matmul(pz[:, :], wgk2[0:16, dt_ * 128:(dt_ + 1) * 128],
                     u16[0:16, :], start=True, stop=True)
            k.inc(t.drain(), "pe")
            k.wait(s, "pe")
            s.activation(out=wrk[:, 0:512], in_=pz[:, :], func=AF.Sigmoid,
                         bias=bgk2[:, dt_:dt_ + 1])
            k.inc(s.activation(out=lsT[:, dt_ * 512:(dt_ + 1) * 512],
                               in_=wrk[:, 0:512], func=AF.Ln), "se")
            k.wait(t, "se")
        k.bar(dmas=False)
        if STAGE < 3.3:
            continue

        for cc2 in range(4):
            pw = [pss[6], pss[7]]
            for h in range(2):
                dts = (2 * h, 2 * h + 1)
                Gt = [wrk[:, 0:128], wrk[:, 128:256]]
                E = [wrk[:, 256:384], wrk[:, 384:512]]
                En = [wrk[:, 512:640], wrk[:, 640:768]]
                Er = [wrk[:, 768:896], wrk[:, 896:1024]]
                qd = [wrk[:, 1024:1152], wrk[:, 1152:1280]]
                kda = [wrk[:, 1280:1408], wrk[:, 1408:1536]]
                kdh = [wrk[:, 1536:1664], wrk[:, 1664:1792]]
                am = wrk[:, 1792:1920]
                kdhs = [wrk[:, 1920:2048], None]
                ge16 = [gsc[:, 0:1], gsc[:, 1:2]]
                edec = [gsc[:, 2:3], gsc[:, 3:4]]
                ssq2 = gsc[:, 4:5]; rstd = gsc[:, 5:6]

                for j, dt_ in enumerate(dts):
                    lsl = lsT[:, dt_ * 512 + cc2 * 128: dt_ * 512 + (cc2 + 1) * 128]
                    k.wait(v, "se")
                    k.inc(v.tensor_tensor_scan(Gt[j], lsl, zc, 0.0,
                                               ALU.add, ALU.add), "ve")
                    k.wait(s, "ve")
                    s.activation(out=ge16[j], in_=Gt[j][:, 127:128],
                                 func=AF.Identity, scale=1.0 / 16.0)
                    s.activation(out=E[j], in_=Gt[j], func=AF.Exp,
                                 scale=1.0 / 16.0)
                    s.activation(out=En[j], in_=Gt[j], func=AF.Exp,
                                 scale=-1.0 / 16.0)
                    s.activation(out=Er[j], in_=Gt[j], func=AF.Exp,
                                 scale=-1.0 / 16.0, bias=ge16[j])
                    k.inc(s.activation(out=edec[j], in_=Gt[j][:, 127:128],
                                       func=AF.Exp, scale=1.0 / 16.0), "se")
                    k.wait(v, "se")
                    qsl = qT[:, dt_ * 512 + cc2 * 128: dt_ * 512 + (cc2 + 1) * 128]
                    ksl = kT[:, dt_ * 512 + cc2 * 128: dt_ * 512 + (cc2 + 1) * 128]
                    k.inc(v.scalar_tensor_tensor(qd[j], qsl, one16, E[j],
                                                 ALU.mult, ALU.mult), "ve")
                    k.inc(v.tensor_tensor(kda[j], ksl, En[j], ALU.mult), "ve")
                    k.inc(v.tensor_tensor(kdh[j], ksl, Er[j], ALU.mult), "ve")
                kdhs[1] = Gt[0]
                k.wait(t, "ve")

                pA = pss[0]
                t.matmul(pA[:, 0:128], kda[0], qd[0], start=True, stop=False)
                k.inc(t.matmul(pA[:, 0:128], kda[1], qd[1],
                               start=False, stop=True), "pe")
                pT = pss[1]
                t.transpose(pT[:, 0:128], kdh[0], ident)
                k.inc(t.transpose(pT[:, 128:256], kdh[1], ident), "pe")
                k.wait(v, "pe")
                k.inc(v.tensor_tensor(am, pA[:, 0:128], tri, ALU.mult), "ve")
                k.inc(v.tensor_copy(kdhs[0], pT[:, 0:128]), "ve")
                k.inc(v.tensor_copy(kdhs[1], pT[:, 128:256]), "ve")
                k.wait(t, "ve")

                po = pss[2]
                vsl = vts[:, cc2 * 512 + h * 256: cc2 * 512 + (h + 1) * 256]
                t.matmul(po[:, 0:256], qd[0],
                         hst[:, (2 * h) * 256:(2 * h + 1) * 256],
                         start=True, stop=False)
                t.matmul(po[:, 0:256], qd[1],
                         hst[:, (2 * h + 1) * 256:(2 * h + 2) * 256],
                         start=False, stop=False)
                t.matmul(po[:, 0:256], am, vsl, start=False, stop=True)
                ph = pss[3]
                t.matmul(ph[:, 0:256], kdhs[0], vsl, start=True, stop=True)
                k.inc(t.matmul(ph[:, 256:512], kdhs[1], vsl,
                               start=True, stop=True), "pe")
                k.wait(v, "pe"); k.wait(s, "pe")
                for j in range(2):
                    k.inc(v.scalar_tensor_tensor(
                        hst[:, (2 * h + j) * 256:(2 * h + j + 1) * 256],
                        hst[:, (2 * h + j) * 256:(2 * h + j + 1) * 256],
                        edec[j], ph[:, j * 256:(j + 1) * 256],
                        ALU.mult, ALU.add), "ve")
                o2 = wrk[:, 256:512]
                k.inc(s.activation(out=o2, in_=po[:, 0:256], func=AF.Square,
                                   accum_out=ssq2), "se")
                s.drain()
                s.activation(out=ssq2, in_=ssq2, func=AF.Identity,
                             scale=1.0 / 256, bias=sc[:, 0:1])
                k.inc(s.activation(out=ssq2, in_=ssq2, func=AF.Sqrt), "se")
                k.wait(v, "se")
                k.inc(v.reciprocal(rstd, ssq2), "ve")
                gsl = gts[:, cc2 * 512 + h * 256: cc2 * 512 + (h + 1) * 256]
                sg = wrk[:, 512:768]
                k.wait(s, "ve")
                k.inc(s.activation(out=sg, in_=gsl, func=AF.Silu), "se")
                k.wait(v, "se")
                t1 = wrk[:, 768:1024]
                k.inc(v.scalar_tensor_tensor(t1, po[:, 0:256], rstd, rmsw,
                                             ALU.mult, ALU.mult), "ve")
                go = wrk[:, 1024:1280]
                k.inc(v.tensor_tensor(go, t1, sg, ALU.mult), "ve")
                k.wait(t, "ve")
                pgt = pss[4]
                t.transpose(pgt[:, 0:128], go[:, 0:128], ident)
                k.inc(t.transpose(pgt[:, 128:256], go[:, 128:256], ident), "pe")
                k.wait(v, "pe")
                goT0 = wrk[:, 1280:1408]; goT1 = wrk[:, 1408:1536]
                k.inc(v.tensor_copy(goT0, pgt[:, 0:128]), "ve")
                k.inc(v.tensor_copy(goT1, pgt[:, 128:256]), "ve")
                k.wait(t, "ve")
                kc0 = 2 * h
                for dti in range(DC):
                    pwb = pw[dti // 4][:, (dti % 4) * 128:((dti % 4) + 1) * 128]
                    wl0 = wo[:, kc0 * D + dti * 128: kc0 * D + (dti + 1) * 128]
                    wl1 = wo[:, (kc0 + 1) * D + dti * 128:
                             (kc0 + 1) * D + (dti + 1) * 128]
                    t.matmul(pwb, wl0, goT0, start=True, stop=False)
                    t.matmul(pwb, wl1, goT1, start=False, stop=True)
                k.inc(t.drain(), "pe")
                k.wait(v, "pe")
                for dti in range(DC):
                    pwb = pw[dti // 4][:, (dti % 4) * 128:((dti % 4) + 1) * 128]
                    ot = out_t[:, dti * 128:(dti + 1) * 128]
                    if h == 0:
                        k.inc(v.tensor_copy(ot, pwb), "ve")
                    else:
                        k.inc(v.tensor_tensor(ot, ot, pwb, ALU.add), "ve")
                k.wait(t, "ve")
            k.wait(y, "ve")
            k.dma(y, p2a_i[sup, cc2, :, :], out_t)
            k.wait(t, "ve")
        k.bar(dmas=False)
    A.pop()


def emit_ffn_a(ctx, xin, xh):
    k = ctx["k"]; dram = ctx["dram"]; A = ctx["A"]; pss = ctx["pss"]
    sc = ctx["sc"]; ident = ctx["ident"]; b1c = ctx["b1c"]
    t, v, s, g, y = k.t, k.v, k.s, k.g, k.y
    v1h = dram["v1h"]
    A.push()
    v1c = A.take(4 * D)
    bT = A.take(4 * D)
    fT = A.take(24 * 512)
    w1b = A.take(2 * 8 * 128)
    w2b = A.take(2 * D)
    b2t = A.take(D)
    wsp = A.take(D)
    k.dma(y, b2t, dram["b2"][:, :])

    for sup in range(TOK // 512):
        for sb_ in range(4):
            i = sup * 4 + sb_
            k.dma(y, v1c[:, sb_ * D:(sb_ + 1) * D],
                  v1h[i * 128:(i + 1) * 128, :])
        k.bar(dmas=True)
        for sb_ in range(4):
            emit_ln(k, v1c[:, sb_ * D:(sb_ + 1) * D], wsp,
                    fT[:, 0:D], sc, affine=ctx["ln_aff2"])
            for dch in range(DC):
                pb = pss[dch % 2]
                k.wait(t, "ve")
                k.inc(t.transpose(pb[:, 0:128],
                                  wsp[:, dch * 128:(dch + 1) * 128], ident),
                      "pe")
                k.wait(v, "pe")
                k.inc(v.tensor_copy(
                    bT[:, dch * 512 + sb_ * 128: dch * 512 + (sb_ + 1) * 128],
                    pb[:, 0:128]), "ve")
            k.wait(t, "ve"); k.wait(s, "ve")
        for fc in range(24):
            buf = w1b[:, (fc % 2) * 1024:(fc % 2) * 1024 + 1024]
            k.wait(y, "pe")
            for dch in range(DC):
                k.dma(y, buf[:, dch * 128:(dch + 1) * 128],
                      dram["w1"][dch * 128:(dch + 1) * 128,
                                 fc * 128:(fc + 1) * 128])
            k.wait(t, "dS")
            pf = pss[2 + (fc % 2)]
            for dch in range(DC):
                t.matmul(pf[:, :], buf[:, dch * 128:(dch + 1) * 128],
                         bT[:, dch * 512:(dch + 1) * 512],
                         start=(dch == 0), stop=(dch == DC - 1))
            k.inc(t.drain(), "pe")
            k.wait(s, "pe"); k.wait(v, "pe")
            hb = fT[:, fc * 512:(fc + 1) * 512]
            k.inc(s.activation(out=hb, in_=pf[:, :], func=AF.Identity,
                               bias=b1c[:, fc:fc + 1]), "se")
            k.wait(v, "se")
            k.inc(v.tensor_scalar_mul(wsp[:, 0:512], hb, 0.01), "ve")
            k.inc(v.tensor_tensor(hb, hb, wsp[:, 0:512], ALU.max), "ve")
            k.wait(s, "ve")
            k.inc(s.activation(out=hb, in_=hb, func=AF.Square), "se")
            k.wait(t, "se")
        for sb_ in range(4):
            i = sup * 4 + sb_
            p0, p1 = pss[4], pss[5]
            for half, pp in ((0, p0), (1, p1)):
                for kf in range(24):
                    buf = w2b[:, (kf % 2) * 512:(kf % 2) * 512 + 512]
                    k.wait(y, "pe")
                    k.dma(y, buf, dram["w2"][kf * 128:(kf + 1) * 128,
                                             half * 512:(half + 1) * 512])
                    k.wait(t, "dS")
                    l_ = fT[:, kf * 512 + sb_ * 128: kf * 512 + (sb_ + 1) * 128]
                    k.inc(t.matmul(pp[:, :], l_, buf,
                                   start=(kf == 0), stop=(kf == 23)), "pe")
            k.inc(t.drain(), "pe")
            k.dma(y, wsp, xin[i * 128:(i + 1) * 128, :])
            k.wait(v, "pe"); k.wait(v, "dS")
            k.inc(v.tensor_tensor(wsp[:, 0:512], wsp[:, 0:512], p0[:, :],
                                  ALU.add), "ve")
            k.inc(v.tensor_tensor(wsp[:, 512:1024], wsp[:, 512:1024], p1[:, :],
                                  ALU.add), "ve")
            k.inc(v.tensor_tensor(wsp, wsp, b2t, ALU.add), "ve")
            k.inc(v.tensor_tensor(wsp, wsp, v1c[:, sb_ * D:(sb_ + 1) * D],
                                  ALU.add), "ve")
            k.wait(y, "ve")
            k.dma(y, xh[i * 128:(i + 1) * 128, :], wsp)
            k.wait(t, "ve")
    A.pop()








def emit_moe_sparse(ctx, xin, xout):
    k = ctx["k"]; dram = ctx["dram"]; A = ctx["A"]; pss = ctx["pss"]
    sc = ctx["sc"]; ident = ctx["ident"]
    t, v, s, g, y = k.t, k.v, k.s, k.g, k.y
    WR = 8 * FSL + 4

    A.push()
    for j in range(DC):
        k.dma(y, dram["ag_u_o"][NCORE * TOK:NCORE * TOK + 128,
                                j * 128:(j + 1) * 128], ctx["zc"])
    NJ = NCORE * NCH
    meta = A.take(NJ * 4)
    tokf = A.take(NJ); prf = A.take(NJ); gf_ = A.take(NJ)
    toku = A.take(NJ); pru = A.take(NJ)
    widxf = A.take(NJ); widxu = A.take(NJ)
    jrow = A.take(NCH)
    cnr = A.take(8); padr = A.take(8); cumx = A.take(8); tmp8 = A.take(8)
    tmp40 = A.take(NCH)
    ech = A.take(NCH); iotu = A.take(1); iot = A.take(1); onesr = A.take(128)
    k.dma(y, meta, dram["ag4"].rearrange("(a p) f -> p a f", p=128))
    k.dma(y, jrow[0:1, :], dram["jrow"][:, :])
    g.iota(iotu.bitcast(U32), pattern=[[1, 1]], base=0, channel_multiplier=1)
    v.memset(onesr[0:1, :], 1.0)
    k.bar()
    mview = meta.rearrange("p (a f) -> p a f", f=4)
    v.tensor_copy(tokf, mview[:, :, 0])
    v.tensor_copy(prf, mview[:, :, 1])
    v.tensor_copy(gf_, mview[:, :, 2])
    v.drain()
    v.tensor_copy(iot, iotu.bitcast(U32))
    v.tensor_copy(toku.bitcast(U32), tokf)
    v.tensor_copy(pru.bitcast(U32), prf)
    k.inc(v.drain(), "ve")
    for r in range(NCORE):
        k.wait(y, "ve")
        k.dma(y, cnr[0:1, :], dram["ag_cnt"][r:r + 1, :])
        k.wait(v, "dS")
        v.tensor_scalar(padr[0:1, :], cnr[0:1, :], 63.5, None, ALU.add)
        v.drain()
        v.tensor_scalar(padr[0:1, :], padr[0:1, :], 1.0 / 128.0, None, ALU.mult)
        v.drain()
        v.tensor_scalar(padr[0:1, :], padr[0:1, :], 8388608.0, None, ALU.add)
        v.drain()
        v.tensor_scalar(padr[0:1, :], padr[0:1, :], 8388608.0, None,
                        ALU.subtract)
        v.drain()
        v.tensor_scalar(padr[0:1, :], padr[0:1, :], 128.0, None, ALU.mult)
        v.drain()
        v.tensor_tensor_scan(cumx[0:1, :], padr[0:1, :], padr[0:1, :],
                             0.0, ALU.add, ALU.bypass)
        v.drain()
        v.tensor_tensor(cumx[0:1, :], cumx[0:1, :], padr[0:1, :], ALU.subtract)
        v.drain()
        v.memset(ech[0:1, :], 0.0)
        v.drain()
        for e in range(NE):
            v.tensor_scalar(tmp40[0:1, :], jrow[0:1, :],
                            cumx[0:1, e:e + 1], None, ALU.is_ge)
            v.drain()
            v.tensor_tensor(ech[0:1, :], ech[0:1, :], tmp40[0:1, :],
                            ALU.add)
            v.drain()
        v.tensor_scalar(ech[0:1, :], ech[0:1, :], -1.0, None, ALU.add)
        v.drain()
        k.inc(v.tensor_scalar(ech[0:1, :], ech[0:1, :], 128.0, None,
                              ALU.mult), "ve")
        k.wait(t, "ve")
        pm = pss[6]
        t.matmul(pm[:, 0:NCH], onesr[0:1, :], ech[0:1, :],
                 start=True, stop=True)
        k.inc(t.drain(), "pe")
        k.wait(v, "pe")
        k.inc(v.tensor_scalar(widxf[:, r * NCH:(r + 1) * NCH], pm[:, 0:NCH],
                              iot[:, 0:1], None, ALU.add), "ve")
    v.tensor_copy(widxu.bitcast(U32), widxf)
    k.inc(v.drain(), "ve")
    k.bar()

    wvb = [A.take(WR), A.take(WR)]
    wb_ = [A.take(WR), A.take(WR)]
    w1b = [A.take(4 * D), A.take(4 * D)]
    s1b = [A.take(D), A.take(D)]
    svb = [A.take(FSL), A.take(FSL)]
    ssb = [A.take(FSL), A.take(FSL)]
    xgb = [A.take(D), A.take(D)]
    xTb = [A.take(D), A.take(D)]
    vT = A.take(FSL); yT = A.take(FSL); ygl = A.take(FSL)
    hb = [A.take(FSL), A.take(FSL)]
    hTb = A.take(FSL)
    ott = [A.take(D), A.take(D)]
    breg = g.alloc_register()
    g.reg_mov(breg, 2 * NCORE * TOK - 1)

    NJ_ = NCORE * NCH
    pe_s1 = [0, 0]; pe_s2 = [0, 0]
    dg_xgb = [0, 0]; dg_wv = [0, 0]; dg_w1 = [0, 0]
    ve_xcopy = [0, 0]; ve_s2evac = [0, 0]; ve_h = [0, 0]
    ve_hTcopy = 0; ve_s4evac = 0
    sc_par = [0, 0]

    def gidma(dst, srcn, idx, J):
        k.inc(g.indirect_dma_start(
            out=dst[:, :], out_offset=None, in_=dram[srcn][:, :],
            in_offset=bass.IndirectOffsetOnAxis(
                ap=idx[:, J:J + 1].bitcast(U32), axis=0)), "dG", 16)

    for i in range(NJ_ + 2):
        a, b, c = i, i - 1, i - 2
        pa, pb_, pc = a % 2, b % 2, c % 2
        if a < NJ_:
            k.wait_at(g, "pe", pe_s1[pa])
            gidma(xgb[pa], "ag_u_o", toku, a)
            dg_xgb[pa] = k._cnt["dG"]
            k.wait_at(g, "pe", pe_s2[pa])
            k.wait_at(g, "ve", ve_s2evac[pa])
            gidma(wvb[pa], "mwv", widxu, a)
            gidma(wb_[pa], "mw", widxu, a)
            gidma(svb[pa], "msvg", widxu, a)
            gidma(ssb[pa], "mssg", widxu, a)
            dg_wv[pa] = k._cnt["dG"]
        if a < NJ_:
            k.wait_at(t, "dG", dg_xgb[pa])
            k.wait_at(t, "ve", ve_xcopy[1 - pa])
            for dch in range(DC):
                pbk = pss[6 + dch // 4]
                t.transpose(pbk[:, (dch % 4) * 128:(dch % 4 + 1) * 128],
                            xgb[pa][:, dch * 128:(dch + 1) * 128], ident)
            k.inc(t.drain(), "pe")
            pe_s1[pa] = k._cnt["pe"]
            k.wait_at(v, "pe", pe_s1[pa])
            v.tensor_copy(xTb[pa][:, 0:512], pss[6][:, :])
            k.inc(v.tensor_copy(xTb[pa][:, 512:1024], pss[7][:, :]), "ve")
            ve_xcopy[pa] = k._cnt["ve"]
        if 0 <= b < NJ_:
            k.wait_at(t, "dG", dg_wv[pb_])
            k.wait_at(t, "ve", ve_xcopy[pb_])
            k.wait_at(t, "ve", ve_s2evac[1 - pb_])
            for pj, src_w in ((4, wvb[pb_]), (5, wb_[pb_])):
                for kc in range(DC):
                    t.matmul(pss[pj][:, :],
                             xTb[pb_][:, kc * 128:(kc + 1) * 128],
                             src_w[:, kc * FSL:(kc + 1) * FSL],
                             start=(kc == 0), stop=(kc == DC - 1))
            k.inc(t.drain(), "pe")
            pe_s2[pb_] = k._cnt["pe"]
            k.wait_at(v, "pe", pe_s2[pb_])
            v.tensor_tensor(vT, pss[4][:, :], svb[pb_], ALU.mult)
            k.inc(v.tensor_tensor(yT, pss[5][:, :], ssb[pb_], ALU.mult), "ve")
            ve_s2evac[pb_] = k._cnt["ve"]
            k.wait_at(s, "ve", ve_s2evac[pb_])
            k.inc(s.activation(out=ygl, in_=yT, func=AF.Gelu_apprx_tanh), "se")
            k.wait(v, "se")
            k.inc(v.tensor_tensor(hb[pb_], vT, ygl, ALU.mult), "ve")
            ve_h[pb_] = k._cnt["ve"]
        if 0 <= c < NJ_:
            k.wait_at(t, "ve", ve_h[pc])
            k.wait_at(t, "ve", ve_hTcopy)
            for kf in range(4):
                t.transpose(pss[0][:, kf * 128:(kf + 1) * 128],
                            hb[pc][:, kf * 128:(kf + 1) * 128], ident)
            k.inc(t.drain(), "pe")
            k.wait_at(v, "pe", k._cnt["pe"])
            k.inc(v.tensor_copy(hTb, pss[0][:, :]), "ve")
            ve_hTcopy = k._cnt["ve"]
            k.wait_at(t, "dG", dg_w1[pc])
            k.wait_at(t, "ve", ve_hTcopy)
            k.wait_at(t, "ve", ve_s4evac)
            for half in range(2):
                ph_ = pss[2 + half]
                for kf in range(4):
                    t.matmul(ph_[:, :],
                             hTb[:, kf * 128:(kf + 1) * 128],
                             w1b[pc][:, kf * D + half * 512:
                                     kf * D + half * 512 + 512],
                             start=(kf == 0), stop=(kf == 3))
            k.inc(t.drain(), "pe")
            k.wait_at(v, "pe", k._cnt["pe"])
            k.wait_at(v, "dG", sc_par[pc])
            for half in range(2):
                ph_ = pss[2 + half]
                o_ = ott[pc][:, half * 512:(half + 1) * 512]
                v.tensor_tensor(o_, ph_[:, :],
                                s1b[pc][:, half * 512:(half + 1) * 512],
                                ALU.mult)
            k.inc(v.tensor_scalar(ott[pc][:, :], ott[pc][:, :],
                                  gf_[:, c:c + 1], None, ALU.mult), "ve")
            ve_s4evac = k._cnt["ve"]
            k.wait(g, "ve")
            k.inc(g.indirect_dma_start(
                out=dram["rs_i"][:, :],
                out_offset=bass.IndirectOffsetOnAxis(
                    ap=pru[:, c:c + 1].bitcast(U32), axis=0),
                in_=ott[pc][:, :], in_offset=None,
                bounds_check=breg, oob_is_err=False), "dG", 16)
            sc_par[pc] = k._cnt["dG"]
        if a < NJ_:
            k.wait(g, "pe")
            k.wait(g, "ve")
            gidma(w1b[pa], "mw1", widxu, a)
            gidma(s1b[pa], "ms1g", widxu, a)
            dg_w1[pa] = k._cnt["dG"]
    A.pop()
    k.bar()

    with allow_ici_all_to_all():
        k.inc(g.collective_compute(
            "ReduceScatter", ALU.add, replica_groups=[list(range(NCORE))],
            ins=[dram["rs_i"].opt()], outs=[dram["rs_o"].opt()]), "cc")
    g.wait_ge(k.sem("cc"), k._cnt["cc"])
    k.bar(dmas=False)

    A.push()
    xa = A.take(D); ra = A.take(D); rb = A.take(D)
    for i in range(NT):
        k.dma(y, xa, xin[i * 128:(i + 1) * 128, :])
        k.dma(y, ra, dram["rs_o"][i * 128:(i + 1) * 128, :])
        k.dma(y, rb, dram["rs_o"][TOK + i * 128:TOK + (i + 1) * 128, :])
        k.wait(v, "dS")
        v.tensor_tensor(xa, xa, ra, ALU.add)
        k.inc(v.tensor_tensor(xa, xa, rb, ALU.add), "ve")
        k.wait(y, "ve")
        k.dma(y, xout[i * 128:(i + 1) * 128, :], xa)
    A.pop()



def emit_attn_b(ctx, xin, xout):
    k = ctx["k"]; dram = ctx["dram"]; A = ctx["A"]; pss = ctx["pss"]
    sc = ctx["sc"]; ident = ctx["ident"]
    t, v, s, g, y = k.t, k.v, k.s, k.g, k.y
    a2a_i, a2a_o = dram["a2a_i"], dram["a2a_o"]
    p2a_i, p2a_o = dram["p2a_i"], dram["p2a_o"]

    A.push()
    xt = A.take(D); ut = A.take(D); aT = A.take(D)
    for i in range(NT):
        sq, sub = i // 4, i % 4
        k.dma(y, xt, xin[i * 128:(i + 1) * 128, :])
        k.wait(v, "dS"); k.wait(s, "dS")
        emit_ln(k, xt, ut, aT, sc, affine=None)
        k.wait(t, "ve")
        for dch in range(DC):
            t.transpose(pss[dch // 4][:, (dch % 4) * 128:(dch % 4 + 1) * 128],
                        ut[:, dch * 128:(dch + 1) * 128], ident)
        k.inc(t.drain(), "pe")
        k.wait(v, "pe")
        v.tensor_copy(aT[:, 0:512], pss[0][:, :])
        k.inc(v.tensor_copy(aT[:, 512:1024], pss[1][:, :]), "ve")
        k.wait(y, "ve")
        for r in (2 * sq, 2 * sq + 1):
            k.dma(y, a2a_i[r, sub, :, :], aT)
    A.pop()
    k.bar()

    with allow_ici_all_to_all():
        k.inc(g.collective_compute(
            "AllToAll", ALU.bypass, replica_groups=[list(range(NCORE))],
            ins=[a2a_i.opt()], outs=[a2a_o.opt()]), "cc")
    g.wait_ge(k.sem("cc"), k._cnt["cc"])
    k.bar(dmas=False)

    A.push()
    qh = A.take(2 * T)
    kh = A.take(2 * T)
    vt_ = A.take(8 * 1024)
    xc = A.take(4 * D)
    rt = A.take(4 * 512)
    wq_ = A.take(8 * 256); wk_ = A.take(8 * 256); wv_ = A.take(8 * 256)
    wo_ = A.take(2 * D)
    rp1 = A.take(512); rp2 = A.take(512)
    dneg = A.take(4 * 512)
    out_t = A.take(D); out_t2 = A.take(D)
    Pt = A.take(512)
    oacc = A.take(128)
    fsc = A.take(16)
    for r_ in range(4):
        k.dma(y, dneg[:, r_ * 512:(r_ + 1) * 512],
              dram["dmneg"][r_ * 128:(r_ + 1) * 128, :])
    k.bar()

    for wave in range(2):
        wc = wave * 256
        for dch in range(DC):
            k.dma(y, wq_[:, dch * 256:(dch + 1) * 256],
                  dram["bwq"][dch * 128:(dch + 1) * 128, wc:wc + 256])
            k.dma(y, wk_[:, dch * 256:(dch + 1) * 256],
                  dram["bwk"][dch * 128:(dch + 1) * 128, wc:wc + 256])
            k.dma(y, wv_[:, dch * 256:(dch + 1) * 256],
                  dram["bwv"][dch * 128:(dch + 1) * 128, wc:wc + 256])
        for kc in range(2):
            k.dma(y, wo_[:, kc * D:(kc + 1) * D],
                  dram["bwo"][wc + kc * 128: wc + (kc + 1) * 128, :])
        k.bar()

        for sup in range(T // 512):
            k.wait(y, "pe"); k.wait(y, "ve"); k.wait(y, "se")
            for sb_ in range(4):
                k.dma(y, xc[:, sb_ * D:(sb_ + 1) * D], a2a_o[sup, sb_, :, :])
            for tb in range(4):
                k.dma(y, rt[:, tb * 512:(tb + 1) * 512],
                      dram[("rch", "rcl", "rsh", "rsl")[tb]]
                      [:, sup * 512:(sup + 1) * 512])
            k.bar(dmas=True)
            CH = rt[:, 0 * 512:1 * 512]
            CL = rt[:, 1 * 512:2 * 512]
            SH = rt[:, 2 * 512:3 * 512]
            SL = rt[:, 3 * 512:4 * 512]
            for (ws, dst, isq) in ((wq_, qh, True), (wk_, kh, False)):
                px = [pss[0], pss[1]]
                for dt_ in (0, 1):
                    for cc2 in range(4):
                        for kc in range(DC):
                            l_ = ws[:, kc * 256 + dt_ * 128:
                                    kc * 256 + (dt_ + 1) * 128]
                            r_ = xc[:, cc2 * D + kc * 128:
                                    cc2 * D + (kc + 1) * 128]
                            t.matmul(px[dt_][:, cc2 * 128:(cc2 + 1) * 128],
                                     l_, r_, start=(kc == 0),
                                     stop=(kc == DC - 1))
                k.inc(t.drain(), "pe")
                k.wait(v, "pe")
                x1, x2 = px[0], px[1]
                k.inc(v.tensor_tensor(rp1, x1[:, :], CH, ALU.mult), "ve")
                v.tensor_tensor(rp2, x2[:, :], SH, ALU.mult)
                v.tensor_tensor(rp1, rp1, rp2, ALU.subtract)
                v.tensor_tensor(rp2, x1[:, :], CL, ALU.mult)
                v.tensor_tensor(rp1, rp1, rp2, ALU.add)
                v.tensor_tensor(rp2, x2[:, :], SL, ALU.mult)
                k.inc(v.tensor_tensor(rp1, rp1, rp2, ALU.subtract), "ve")
                for h in range(4):
                    k.inc(v.tensor_copy(
                        dst[(h % 2) * 64:(h % 2) * 64 + 32,
                            (h // 2) * T + sup * 512:(h // 2) * T + (sup + 1) * 512],
                        rp1[32 * h:32 * h + 32, :]), "ve")
                v.tensor_tensor(rp1, x1[:, :], SH, ALU.mult)
                v.tensor_tensor(rp2, x2[:, :], CH, ALU.mult)
                v.tensor_tensor(rp1, rp1, rp2, ALU.add)
                v.tensor_tensor(rp2, x1[:, :], SL, ALU.mult)
                v.tensor_tensor(rp1, rp1, rp2, ALU.add)
                v.tensor_tensor(rp2, x2[:, :], CL, ALU.mult)
                k.inc(v.tensor_tensor(rp1, rp1, rp2, ALU.add), "ve")
                for h in range(4):
                    k.inc(v.tensor_copy(
                        dst[(h % 2) * 64 + 32:(h % 2) * 64 + 64,
                            (h // 2) * T + sup * 512:(h // 2) * T + (sup + 1) * 512],
                        rp1[32 * h:32 * h + 32, :]), "ve")
                k.wait(t, "ve")
            if DEBUG_DUMP and wave == 0 and sup == 0:
                for j in range(4):
                    k.dma(y, dram["rs_i"][(24 + j) * 128:(25 + j) * 128, :],
                          xc[:, j * D:(j + 1) * D])
                for j in range(2):
                    k.dma(y, dram["rs_i"][(28 + j) * 128:(29 + j) * 128, :],
                          wv_[:, j * D:(j + 1) * D])
                k.wait(y, "dS")
            for cc2 in range(4):
                pv_ = pss[2]
                for kc in range(DC):
                    t.matmul(pv_[:, 0:256],
                             xc[:, cc2 * D + kc * 128: cc2 * D + (kc + 1) * 128],
                             wv_[:, kc * 256:(kc + 1) * 256],
                             start=(kc == 0), stop=(kc == DC - 1))
                k.inc(t.drain(), "pe")
                k.wait(v, "pe")
                k.inc(v.tensor_copy(
                    vt_[:, (sup * 4 + cc2) * 256:(sup * 4 + cc2 + 1) * 256],
                    pv_[:, 0:256]), "ve")
                k.wait(t, "ve")
        k.bar()
        if DEBUG_DUMP and wave == 0:
            for j in range(8):
                k.dma(y, dram["rs_i"][j * 128:(j + 1) * 128, :],
                      vt_[:, j * D:(j + 1) * D])
                k.dma(y, dram["rs_i"][(8 + j) * 128:(9 + j) * 128, :],
                      qh[:, j * D:(j + 1) * D])
                k.dma(y, dram["rs_i"][(16 + j) * 128:(17 + j) * 128, :],
                      kh[:, j * D:(j + 1) * D])
            k.wait(y, "dS")

        mneg = fsc[:, 8:9]
        g.memset(mneg, -1e30)
        k.bar(dmas=False)
        for qt in range(NT * 2):
            for hp in range(2):
                macc = [fsc[:, 0:1], fsc[:, 1:2]]
                lacc = [fsc[:, 2:3], fsc[:, 3:4]]
                nmr = fsc[:, 4:5]
                nnm8 = fsc[:, 5:6]
                alph = fsc[:, 6:7]
                rsum = fsc[:, 7:8]
                rl = fsc[:, 9:10]
                k.wait(v, "se"); k.wait(v, "pe")
                v.memset(oacc[:, :], 0.0)
                for ha in range(2):
                    v.tensor_copy(macc[ha], mneg)
                    v.memset(lacc[ha], 0.0)
                k.inc(v.drain(), "ve")
                nsb = qt // 4 + 1
                for sb_ in range(nsb):
                    diag = (sb_ == qt // 4)
                    for ha in range(2):
                        pS = pss[ha]
                        k.wait(t, "ve")
                        k.inc(t.matmul(
                            pS[:, :],
                            qh[64 * ha:64 * ha + 64,
                               hp * T + qt * 128:hp * T + (qt + 1) * 128],
                            kh[64 * ha:64 * ha + 64,
                               hp * T + sb_ * 512:hp * T + (sb_ + 1) * 512],
                            start=True, stop=True), "pe")
                        k.wait(v, "pe")
                        Ssrc = pS[:, :]
                        if diag:
                            r_ = qt % 4
                            k.inc(v.tensor_tensor(
                                Pt, pS[:, :], dneg[:, r_ * 512:(r_ + 1) * 512],
                                ALU.add), "ve")
                            Ssrc = Pt
                        v.reduce_max(nmr, Ssrc, axis=AX.X)
                        v.drain()
                        v.tensor_tensor(nmr, nmr, macc[ha], ALU.max)
                        v.drain()
                        k.inc(v.tensor_scalar_mul(nnm8, nmr, -0.125), "ve")
                        k.wait(s, "ve")
                        s.activation(out=alph, in_=macc[ha], func=AF.Exp,
                                     scale=0.125, bias=nnm8)
                        s.drain()
                        k.inc(s.activation(out=Pt, in_=Ssrc, func=AF.Exp,
                                           scale=0.125, bias=nnm8,
                                           accum_out=rsum), "se")
                        k.wait(v, "se")
                        v.tensor_copy(macc[ha], nmr)
                        v.drain()
                        v.scalar_tensor_tensor(lacc[ha], lacc[ha], alph,
                                               rsum, ALU.mult, ALU.add)
                        k.inc(v.drain(), "ve")
                        pT = pss[2]
                        k.wait(t, "ve")
                        for i4 in range(4):
                            t.transpose(pT[:, i4 * 128:(i4 + 1) * 128],
                                        Pt[:, i4 * 128:(i4 + 1) * 128],
                                        ident)
                        k.inc(t.drain(), "pe")
                        k.wait(v, "pe")
                        for i4 in range(4):
                            k.inc(v.tensor_copy(
                                Pt[:, i4 * 128:(i4 + 1) * 128],
                                pT[:, i4 * 128:(i4 + 1) * 128]), "ve")
                        k.wait(t, "ve")
                        po = pss[3]
                        for i4 in range(4):
                            t.matmul(po[:, 0:64],
                                     Pt[:, i4 * 128:(i4 + 1) * 128],
                                     vt_[:, (sb_ * 4 + i4) * 256 +
                                         (2 * hp + ha) * 64:
                                         (sb_ * 4 + i4) * 256 +
                                         (2 * hp + ha) * 64 + 64],
                                     start=(i4 == 0), stop=(i4 == 3))
                        k.inc(t.drain(), "pe")
                        k.wait(v, "pe")
                        k.inc(v.scalar_tensor_tensor(
                            oacc[:, ha * 64:(ha + 1) * 64],
                            oacc[:, ha * 64:(ha + 1) * 64],
                            alph, po[:, 0:64], ALU.mult, ALU.add), "ve")
                for ha in range(2):
                    v.reciprocal(rl, lacc[ha])
                    v.drain()
                    k.inc(v.tensor_scalar_mul(oacc[:, ha * 64:(ha + 1) * 64],
                                              oacc[:, ha * 64:(ha + 1) * 64],
                                              rl), "ve")
                k.wait(t, "ve")
                pT = pss[4]
                k.inc(t.transpose(pT[:, 0:128], oacc[:, 0:128], ident), "pe")
                k.wait(v, "pe")
                k.inc(v.tensor_copy(out_t2[:, 0:128], pT[:, 0:128]), "ve")
                k.wait(t, "ve")
                pw = [pss[5], pss[6]]
                for dti in range(DC):
                    pwb = pw[dti // 4][:, (dti % 4) * 128:((dti % 4) + 1) * 128]
                    t.matmul(pwb, wo_[:, hp * D + dti * 128:
                                      hp * D + (dti + 1) * 128],
                             out_t2[:, 0:128], start=True, stop=True)
                k.inc(t.drain(), "pe")
                k.wait(v, "pe")
                for dti in range(DC):
                    pwb = pw[dti // 4][:, (dti % 4) * 128:((dti % 4) + 1) * 128]
                    ot = out_t[:, dti * 128:(dti + 1) * 128]
                    if hp == 0:
                        k.inc(v.tensor_copy(ot, pwb), "ve")
                    else:
                        k.inc(v.tensor_tensor(ot, ot, pwb, ALU.add), "ve")
                k.wait(t, "ve")
            sup_, cc2_ = qt // 4, qt % 4
            if wave == 0:
                k.wait(y, "ve")
                k.dma(y, p2a_i[sup_, cc2_, :, :], out_t)
            else:
                k.wait(y, "pe")
                k.dma(y, out_t2, p2a_i[sup_, cc2_, :, :])
                k.wait(v, "dS")
                k.inc(v.tensor_tensor(out_t, out_t, out_t2, ALU.add), "ve")
                k.wait(y, "ve")
                k.dma(y, p2a_i[sup_, cc2_, :, :], out_t)
    A.pop()
    k.bar()

    with allow_ici_all_to_all():
        k.inc(g.collective_compute(
            "AllToAll", ALU.bypass, replica_groups=[list(range(NCORE))],
            ins=[p2a_i.opt()], outs=[p2a_o.opt()]), "cc")
    g.wait_ge(k.sem("cc"), k._cnt["cc"])
    k.bar(dmas=False)

    A.push()
    pa = A.take(D); pb_ = A.take(D); xtl = A.take(D); vt2 = A.take(D)
    for i in range(NT):
        sq, sub = i // 4, i % 4
        k.dma(y, pa, p2a_o[2 * sq, sub, :, :])
        k.dma(y, pb_, p2a_o[2 * sq + 1, sub, :, :])
        k.dma(y, xtl, xin[i * 128:(i + 1) * 128, :])
        k.wait(v, "dS")
        k.inc(v.tensor_tensor(pa, pa, pb_, ALU.add), "ve")
        k.wait(t, "ve")
        for dch in range(DC):
            t.transpose(pss[dch // 4][:, (dch % 4) * 128:(dch % 4 + 1) * 128],
                        pa[:, dch * 128:(dch + 1) * 128], ident)
        k.inc(t.drain(), "pe")
        k.wait(v, "pe")
        v.tensor_tensor(vt2[:, 0:512], xtl[:, 0:512], pss[0][:, :],
                        ALU.add)
        k.inc(v.tensor_tensor(vt2[:, 512:1024], xtl[:, 512:1024],
                              pss[1][:, :], ALU.add), "ve")
        k.wait(y, "ve")
        k.dma(y, xout[i * 128:(i + 1) * 128, :], vt2)
    A.pop()



def emit_moe_sort(ctx, A, SEL1c, SEL2c, G1c, G2c, cntrow):
    k = ctx["k"]; dram = ctx["dram"]; pss = ctx["pss"]; sc = ctx["sc"]
    t, v, s, g, y = k.t, k.v, k.s, k.g, k.y
    lt = A.take(128)
    onesr = A.take(128)
    tokc = A.take(NT)
    prc = A.take(2 * NT)
    s12 = A.take(8)
    pre = A.take(8)
    brow = A.take(8); base = A.take(8); runt = A.take(8); pad = A.take(8)
    sl1 = A.take(1); sl2 = A.take(1)
    su = A.take(2)
    stg4 = A.take(8)
    dflt = A.take(8)
    sbreg = g.alloc_register()
    g.reg_mov(sbreg, NSLOT - 1)
    k.dma(y, lt, dram["ltstrict"][:, :])
    k.dma(y, tokc, dram["tokid"][:, :])
    k.dma(y, prc, dram["pairrow"][:, :])
    v.memset(stg4, 0.0)
    v.memset(onesr[0:1, :], 1.0)
    v.memset(runt[0:1, :], 0.0)
    k.inc(v.drain(), "ve")
    k.wait(v, "ve")
    v.tensor_scalar(pad[0:1, :], cntrow[0:1, :], 63.5, None, ALU.add)
    v.drain()
    v.tensor_scalar(pad[0:1, :], pad[0:1, :], 1.0 / 128.0, None, ALU.mult)
    v.drain()
    v.tensor_scalar(pad[0:1, :], pad[0:1, :], 8388608.0, None, ALU.add)
    v.drain()
    v.tensor_scalar(pad[0:1, :], pad[0:1, :], 8388608.0, None, ALU.subtract)
    v.drain()
    v.tensor_scalar(pad[0:1, :], pad[0:1, :], 128.0, None, ALU.mult)
    v.drain()
    v.tensor_tensor_scan(base[0:1, :], pad[0:1, :], pad[0:1, :],
                         0.0, ALU.add, ALU.bypass)
    v.drain()
    v.tensor_tensor(base[0:1, :], base[0:1, :], pad[0:1, :], ALU.subtract)
    k.inc(v.drain(), "ve")
    k.wait(y, "ve")
    k.dma(y, dram["cnt_i"][:, :], cntrow[0:1, :])
    g.memset(dflt[:, 0:1], float(NCORE * TOK))
    g.memset(dflt[:, 1:2], 1048576.0)
    g.memset(dflt[:, 2:4], 0.0)
    k.inc(g.drain(), "ge")
    k.wait(y, "ge")
    for a_ in range(NCH):
        k.dma(y, dram["srt4"][a_ * 128:(a_ + 1) * 128, :], dflt[:, 0:4])
    k.bar(dmas=True)
    for i in range(NT):
        se1 = SEL1c[:, i * 8:(i + 1) * 8]
        se2 = SEL2c[:, i * 8:(i + 1) * 8]
        k.wait(v, "ge")
        v.tensor_tensor(s12, se1, se2, ALU.add)
        v.tensor_tensor(brow[0:1, :], runt[0:1, :], base[0:1, :], ALU.add)
        k.inc(v.drain(), "ve")
        k.wait(t, "ve")
        pp = pss[6]
        t.matmul(pp[:, 0:8], lt, s12, start=True, stop=False)
        t.matmul(pp[:, 0:8], onesr[0:1, :], brow[0:1, :],
                 start=False, stop=True)
        pc_ = pss[7]
        t.matmul(pc_[0:1, 0:8], sc[:, 2:3], s12, start=True, stop=True)
        k.inc(t.drain(), "pe")
        k.wait(v, "pe")
        v.tensor_copy(pre, pp[:, 0:8])
        v.tensor_tensor(runt[0:1, :], runt[0:1, :], pc_[0:1, 0:8], ALU.add)
        v.drain()
        v.tensor_tensor(s12, se1, pre, ALU.mult)
        v.drain()
        v.reduce_sum(sl1, s12, axis=AX.X)
        v.drain()
        v.tensor_tensor(s12, se2, pre, ALU.mult)
        v.drain()
        v.reduce_sum(sl2, s12, axis=AX.X)
        v.drain()
        k.wait_at(v, "dG", k._cnt["dG"])
        v.tensor_copy(su[:, 0:1].bitcast(U32), sl1)
        v.tensor_copy(su[:, 1:2].bitcast(U32), sl2)
        v.drain()
        for kk_, gc in ((0, G1c), (1, G2c)):
            st4 = stg4[:, kk_ * 4:(kk_ + 1) * 4]
            v.tensor_copy(st4[:, 0:1], tokc[:, i:i + 1])
            v.tensor_copy(st4[:, 1:2], prc[:, kk_ * NT + i:kk_ * NT + i + 1])
            v.tensor_copy(st4[:, 2:3], gc[:, i:i + 1])
        k.inc(v.drain(), "ve")
        k.wait(t, "ve")
        k.wait(g, "ve")
        for kk_ in (0, 1):
            idx = su[:, kk_:kk_ + 1].bitcast(U32)
            k.inc(g.indirect_dma_start(
                out=dram["srt4"][:, :],
                out_offset=bass.IndirectOffsetOnAxis(ap=idx, axis=0),
                in_=stg4[:, kk_ * 4:(kk_ + 1) * 4], in_offset=None,
                bounds_check=sbreg, oob_is_err=False), "dG", 16)
    k.bar()


def emit_moe(ctx, xin, xout):
    k = ctx["k"]; dram = ctx["dram"]; A = ctx["A"]; pss = ctx["pss"]
    sc = ctx["sc"]; ident = ctx["ident"]; zc = ctx["zc"]
    t, v, s, g, y = k.t, k.v, k.s, k.g, k.y
    GTOK = NCORE * TOK

    A.push()
    xt = A.take(D); ut = A.take(D); wsp = A.take(D); uT = A.take(D)
    rws = A.take(64)
    lg = A.take(8)
    pr = A.take(8); e1 = A.take(8); pm_ = A.take(8); sels = A.take(8)
    c1 = A.take(8); wmt = A.take(8)
    rsc = A.take(8)
    SEL1c = A.take(NT * 8); SEL2c = A.take(NT * 8)
    G1c = A.take(NT); G2c = A.take(NT)
    cntrow = A.take(8)
    for dch in range(DC):
        k.dma(y, rws[:, dch * 8:(dch + 1) * 8],
              dram["rw"][dch * 128:(dch + 1) * 128, :])
    v.memset(cntrow[0:1, :], 0.0)
    k.bar()
    for i in range(NT):
        k.dma(y, xt, xin[i * 128:(i + 1) * 128, :])
        k.wait(v, "dS"); k.wait(s, "dS")
        emit_ln(k, xt, ut, wsp, sc, affine=None)
        k.wait(y, "ve")
        k.dma(y, dram["ag_u_i"][i * 128:(i + 1) * 128, :], ut)
        k.wait(t, "ve")
        for dch in range(DC):
            t.transpose(pss[dch // 4][:, (dch % 4) * 128:(dch % 4 + 1) * 128],
                        ut[:, dch * 128:(dch + 1) * 128], ident)
        k.inc(t.drain(), "pe")
        k.wait(v, "pe")
        v.tensor_copy(uT[:, 0:512], pss[0][:, :])
        k.inc(v.tensor_copy(uT[:, 512:1024], pss[1][:, :]), "ve")
        k.wait(t, "ve")
        pl = pss[2]
        for dch in range(DC):
            t.matmul(pl[:, 0:8], uT[:, dch * 128:(dch + 1) * 128],
                     rws[:, dch * 8:(dch + 1) * 8],
                     start=(dch == 0), stop=(dch == DC - 1))
        k.inc(t.drain(), "pe")
        k.wait(v, "pe")
        k.inc(v.tensor_copy(lg, pl[:, 0:8]), "ve")
        rmax = rsc[:, 0:1]; nmax = rsc[:, 1:2]; ssum = rsc[:, 2:3]
        rs_ = rsc[:, 3:4]; m1 = rsc[:, 4:5]; m2 = rsc[:, 5:6]
        v.reduce_max(rmax, lg, axis=AX.X)
        v.drain()
        k.inc(v.tensor_scalar_mul(nmax, rmax, -1.0), "ve")
        k.wait(s, "ve")
        s.activation(out=pr, in_=lg, func=AF.Exp, bias=nmax, accum_out=ssum)
        k.inc(s.drain(), "se")
        k.wait(v, "se")
        v.reciprocal(rs_, ssum)
        v.drain()
        v.tensor_scalar_mul(pr, pr, rs_)
        v.drain()
        v.reduce_max(m1, pr, axis=AX.X)
        v.drain()
        v.tensor_scalar(e1, pr, m1, None, ALU.is_ge)
        v.drain()
        v.tensor_tensor_scan(c1, e1, zc[:, 0:8], 0.0, ALU.add, ALU.add)
        v.drain()
        v.tensor_scalar(c1, c1, 1.0, None, ALU.is_le)
        v.drain()
        v.tensor_tensor(sels, e1, c1, ALU.mult)
        v.drain()
        if MOE_SPARSE:
            v.tensor_copy(SEL1c[:, i * 8:(i + 1) * 8], sels)
            v.tensor_copy(G1c[:, i:i + 1], m1)
            v.drain()
        v.tensor_tensor(pm_, pr, sels, ALU.mult)
        v.drain()
        v.tensor_tensor(pm_, pr, pm_, ALU.subtract)
        v.drain()
        v.reduce_max(m2, pm_, axis=AX.X)
        v.drain()
        v.tensor_scalar(e1, pm_, m2, None, ALU.is_ge)
        v.drain()
        v.tensor_tensor_scan(c1, e1, zc[:, 0:8], 0.0, ALU.add, ALU.add)
        v.drain()
        v.tensor_scalar(c1, c1, 1.0, None, ALU.is_le)
        v.drain()
        v.tensor_tensor(e1, e1, c1, ALU.mult)
        v.drain()
        if MOE_SPARSE:
            v.tensor_copy(SEL2c[:, i * 8:(i + 1) * 8], e1)
            v.tensor_copy(G2c[:, i:i + 1], m2)
            v.drain()
        v.tensor_tensor(sels, sels, e1, ALU.add)
        v.drain()
        k.inc(v.tensor_tensor(wmt, pr, sels, ALU.mult), "ve")
        if MOE_SPARSE:
            k.wait(t, "ve")
            pc_ = pss[5]
            t.matmul(pc_[0:1, 0:8], sc[:, 2:3], sels, start=True, stop=True)
            k.inc(t.drain(), "pe")
            k.wait(v, "pe")
            k.inc(v.tensor_tensor(cntrow[0:1, :], cntrow[0:1, :],
                                  pc_[0:1, 0:8], ALU.add), "ve")
            v.drain()
        k.wait(y, "ve")
        if not MOE_SPARSE:
            k.dma(y, dram["ag_w_i"][i * 128:(i + 1) * 128, :], wmt)
    if MOE_SPARSE:
        emit_moe_sort(ctx, A, SEL1c, SEL2c, G1c, G2c, cntrow)
    A.pop()
    k.bar()

    with allow_ici_all_to_all():
        k.inc(g.collective_compute(
            "AllGather", ALU.bypass, replica_groups=[list(range(NCORE))],
            ins=[dram["ag_u_i"].opt()],
            outs=[dram["ag_u_o"][0:NCORE * TOK, :].opt()]), "cc")
        if MOE_SPARSE:
            for a_, b_ in (("srt4", "ag4"), ("cnt_i", "ag_cnt")):
                k.inc(g.collective_compute(
                    "AllGather", ALU.bypass,
                    replica_groups=[list(range(NCORE))],
                    ins=[dram[a_].opt()], outs=[dram[b_].opt()]), "cc")
        else:
            k.inc(g.collective_compute(
                "AllGather", ALU.bypass, replica_groups=[list(range(NCORE))],
                ins=[dram["ag_w_i"].opt()], outs=[dram["ag_w_o"].opt()]), "cc")
    g.wait_ge(k.sem("cc"), k._cnt["cc"])
    k.bar(dmas=False)
    if MOE_SPARSE:
        emit_moe_sparse(ctx, xin, xout)
        return

    A.push()
    msv = A.take(NE * 4); ms_ = A.take(NE * 4)
    k.dma(y, msv, dram["msv"][:, :])
    k.dma(y, ms_, dram["ms"][:, :])
    ctx["msv"], ctx["ms"] = msv, ms_
    k.bar()
    xT = A.take(8 * 512)
    vT = A.take(4 * 512)
    yT = A.take(4 * 512)
    acc = A.take(4 * D)
    wm = A.take(32)
    xtl = A.take(D)
    wvb = [A.take(8 * 512), A.take(8 * 512)]
    wb_ = [A.take(8 * 512), A.take(8 * 512)]
    w1b = [A.take(4 * D), A.take(4 * D)]
    ms1b = [A.take(D), A.take(D)]
    pe_at_parity = [0, 0]
    ve_at_parity = [0, 0]
    for scg in range(GTOK // 512):
        for sb_ in range(4):
            gi = scg * 4 + sb_
            k.dma(y, xtl, dram["ag_u_o"][gi * 128:(gi + 1) * 128, :])
            k.dma(y, wm[:, sb_ * 8:(sb_ + 1) * 8],
                  dram["ag_w_o"][gi * 128:(gi + 1) * 128, :])
            k.wait(t, "dS")
            for dch in range(DC):
                pb = pss[6 + dch % 2]
                k.wait(t, "ve")
                k.inc(t.transpose(pb[:, 0:128], xtl[:, dch * 128:(dch + 1) * 128],
                                  ident), "pe")
                k.wait(v, "pe")
                k.inc(v.tensor_copy(xT[:, dch * 512 + sb_ * 128:
                                       dch * 512 + (sb_ + 1) * 128],
                                    pb[:, 0:128]), "ve")
            k.wait(t, "ve")
            k.wait(y, "ve")
        for e in range(NE):
            par = e % 2
            k.wait_at(y, "pe", pe_at_parity[par])
            k.wait_at(y, "ve", ve_at_parity[par])
            k.dma(y, wvb[par], dram["mwv"][e * 128:(e + 1) * 128, :])
            k.dma(y, wb_[par], dram["mw"][e * 128:(e + 1) * 128, :])
            k.dma(y, w1b[par], dram["mw1"][e * 128:(e + 1) * 128, :])
            k.dma(y, ms1b[par], dram["ms1"][:, e * D:(e + 1) * D])
            k.wait(t, "dS")
            for src_w, dst, scl, fn in (
                    (wvb[par], vT, "msv", AF.Identity),
                    (wb_[par], yT, "ms", AF.Gelu_apprx_tanh)):
                for ft in range(4):
                    pf = pss[ft % 2]
                    for kc in range(DC):
                        t.matmul(pf[:, :],
                                 src_w[:, kc * 512 + ft * 128:
                                       kc * 512 + (ft + 1) * 128],
                                 xT[:, kc * 512:(kc + 1) * 512],
                                 start=(kc == 0), stop=(kc == DC - 1))
                    k.inc(t.drain(), "pe")
                    k.wait(s, "pe")
                    k.inc(s.activation(out=dst[:, ft * 512:(ft + 1) * 512],
                                       in_=pf[:, :], func=fn,
                                       scale=ctx[scl][:, e * 4 + ft:e * 4 + ft + 1]),
                          "se")
                    k.wait(t, "se")
            k.wait(v, "se")
            for ft in range(4):
                k.inc(v.tensor_tensor(vT[:, ft * 512:(ft + 1) * 512],
                                      vT[:, ft * 512:(ft + 1) * 512],
                                      yT[:, ft * 512:(ft + 1) * 512],
                                      ALU.mult), "ve")
            k.wait(t, "ve")
            for sb_ in range(4):
                for half in range(2):
                    ph_ = pss[2 + half]
                    for kf in range(4):
                        t.matmul(ph_[:, :],
                                 vT[:, kf * 512 + sb_ * 128:
                                    kf * 512 + (sb_ + 1) * 128],
                                 w1b[par][:, kf * D + half * 512:
                                          kf * D + half * 512 + 512],
                                 start=(kf == 0), stop=(kf == 3))
                k.inc(t.drain(), "pe")
                if sb_ == 3:
                    pe_at_parity[par] = k._cnt["pe"]
                k.wait(v, "pe")
                gate = wm[:, sb_ * 8 + e: sb_ * 8 + e + 1]
                for half in range(2):
                    ph_ = pss[2 + half]
                    tmp = xtl[:, half * 512:(half + 1) * 512]
                    a_ = acc[:, sb_ * D + half * 512: sb_ * D + half * 512 + 512]
                    v.tensor_tensor(tmp, ph_[:, :],
                                    ms1b[par][:, half * 512:(half + 1) * 512],
                                    ALU.mult)
                    if e == 0:
                        k.inc(v.tensor_scalar_mul(a_, tmp, gate), "ve")
                    else:
                        k.inc(v.scalar_tensor_tensor(a_, tmp, gate, a_,
                                                     ALU.mult, ALU.add), "ve")
                k.wait(t, "ve")
            ve_at_parity[par] = k._cnt["ve"]
        k.wait(y, "ve")
        for sb_ in range(4):
            gi = scg * 4 + sb_
            k.dma(y, dram["rs_i"][gi * 128:(gi + 1) * 128, :],
                  acc[:, sb_ * D:(sb_ + 1) * D])
    A.pop()
    k.bar()

    with allow_ici_all_to_all():
        k.inc(g.collective_compute(
            "ReduceScatter", ALU.add, replica_groups=[list(range(NCORE))],
            ins=[dram["rs_i"][0:NCORE * TOK, :].opt()],
            outs=[dram["rs_o"][0:TOK, :].opt()]), "cc")
    g.wait_ge(k.sem("cc"), k._cnt["cc"])
    k.bar(dmas=False)

    A.push()
    xa = A.take(D); ra = A.take(D)
    for i in range(NT):
        k.dma(y, xa, xin[i * 128:(i + 1) * 128, :])
        k.dma(y, ra, dram["rs_o"][i * 128:(i + 1) * 128, :])
        k.wait(v, "dS")
        k.inc(v.tensor_tensor(xa, xa, ra, ALU.add), "ve")
        k.wait(y, "ve")
        k.dma(y, xout[i * 128:(i + 1) * 128, :], xa)
    A.pop()



def kernel(**inputs):
    nc = build_program()
    in_maps = [make_in_map(c, inputs) for c in range(NCORE)]
    res = run_bass_kernel_spmd(nc, in_maps, core_ids=list(range(NCORE)))
    out = np.zeros((B, T, D), dtype=np.float32)
    for c in range(NCORE):
        yc = np.asarray(res.results[c]["y"]).reshape(B, SLICE, D)
        out[:, c * SLICE:(c + 1) * SLICE, :] = yc
    return out
